# version 24
# speedup vs baseline: 1.0836x; 1.0287x over previous
import numpy as np
import ml_dtypes
from contextlib import ExitStack

import concourse.bass as bass
import concourse.mybir as mybir
from concourse.bass_utils import run_bass_kernel_spmd

F32 = mybir.dt.float32
BF16 = mybir.dt.bfloat16
F16 = mybir.dt.float16
I32 = mybir.dt.int32
AF = mybir.ActivationFunctionType
ALU = mybir.AluOpType
AX = mybir.AxisListType.X
BF = ml_dtypes.bfloat16

NCORE = 8
B, S, D = 4, 2048, 2048
NTOK = B * S
NOWN = NTOK // NCORE
NH = 16
Q_LORA = 512
KV_LORA = 512
NOPE = 128
ROPE = 64
QK_HD = NOPE + ROPE
V_HD = 128
I0 = 5632
F1 = 8192
NC_D = D // 128

ENGS = ("sync", "tensor", "scalar", "vector", "gpsimd")


class PB:

    def __init__(self, nc, ctx):
        self.nc = nc
        self.ops = {e: [] for e in ENGS}
        self.barsem = {e: ctx.enter_context(nc.semaphore("bar_" + e)) for e in ENGS}
        self.barcnt = {e: 0 for e in ENGS}
        self.dsem = {}
        self.dcount = {}
        self._ctx = ctx
        for q in ("sync", "gpsimd", "scalar"):
            for t in ("d", "w", "a", "s", "t", "w2", "a2", "t2"):
                key = (q, t)
                self.dsem[key] = ctx.enter_context(nc.semaphore(f"dsem_{q}_{t}"))
                self.dcount[key] = 0
        self.lastbox = {e: None for e in ENGS}
        self.lastbar = {e: None for e in ENGS}
        self.pending = {e: [] for e in ENGS}

    def emit(self, eng, fn):
        box = [None]

        def run(h):
            box[0] = fn(h)
        self.ops[eng].append(run)
        self.lastbox[eng] = box
        self.lastbar[eng] = None
        self.pending[eng] = []
        return box

    def dma(self, queue, fn, tag="d"):
        key = (queue, tag)
        sem = self.dsem[key]

        def run(h):
            return fn(h).then_inc(sem, 16)
        self.ops[queue].append(run)
        self.dcount[key] += 16
        self.lastbox[queue] = None
        self.lastbar[queue] = None
        self.pending[queue] = []
        return self.dcount[key]

    def wait_dma(self, eng, queue, tag="d", upto=None):
        key = (queue, tag)
        n = self.dcount[key] if upto is None else upto
        if n == 0:
            return
        sem = self.dsem[key]
        self.ops[eng].append(lambda h: h.wait_ge(sem, n))
        self.pending[eng].append((sem, n))

    def mark(self, eng):
        box = self.lastbox[eng]
        assert box is not None, f"mark({eng}) with no instruction"
        if self.lastbar[eng] is None:
            self.barcnt[eng] += 1
            self.lastbar[eng] = self.barcnt[eng]
            sem = self.barsem[eng]

            def attach(h, b=box, sem=sem):
                b[0].then_inc(sem, 1)
            self.ops[eng].append(attach)
        return (self.barsem[eng], self.barcnt[eng], tuple(self.pending[eng]))

    def wait_mark(self, eng, token):
        sem, n, pend = token
        self.ops[eng].append(lambda h, sem=sem, n=n: h.wait_ge(sem, n))
        for (s2, n2) in pend:
            self.ops[eng].append(lambda h, s2=s2, n2=n2: h.wait_ge(s2, n2))
            self.pending[eng].append((s2, n2))

    def barrier(self, producers, consumers):
        waits = []
        fwd = []
        for e in producers:
            fwd.extend(self.pending[e])
            box = self.lastbox[e]
            if box is None:
                if self.barcnt[e] > 0:
                    waits.append((e, self.barcnt[e]))
                continue
            if self.lastbar[e] is None:
                self.barcnt[e] += 1
                self.lastbar[e] = self.barcnt[e]
                sem = self.barsem[e]

                def attach(h, b=box, sem=sem):
                    b[0].then_inc(sem, 1)
                self.ops[e].append(attach)
            waits.append((e, self.barcnt[e]))
        fmax = {}
        for (sem, n) in fwd:
            k = id(sem)
            if k not in fmax or fmax[k][1] < n:
                fmax[k] = (sem, n)
        for c in consumers:
            for (e, n) in waits:
                self.ops[c].append(
                    lambda h, sem=self.barsem[e], n=n: h.wait_ge(sem, n))
            for (sem, n) in fmax.values():
                self.ops[c].append(lambda h, sem=sem, n=n: h.wait_ge(sem, n))
                self.pending[c].append((sem, n))

    def finalize(self, block):
        @block.sync
        def _(h):
            for f in self.ops["sync"]:
                f(h)

        @block.tensor
        def _(h):
            for f in self.ops["tensor"]:
                f(h)

        @block.scalar
        def _(h):
            for f in self.ops["scalar"]:
                f(h)

        @block.vector
        def _(h):
            for f in self.ops["vector"]:
                f(h)

        @block.gpsimd
        def _(h):
            for f in self.ops["gpsimd"]:
                f(h)



def build_qkv1():
    nc = bass.Bass("TRN2", target_bir_lowering=False)
    xT_d = nc.dram_tensor("xT", [D, NOWN], BF16, kind="ExternalInput").ap()
    wq_d = nc.dram_tensor("wq", [D, D], BF16, kind="ExternalInput").ap()
    wk_d = nc.dram_tensor("wk", [D, D], BF16, kind="ExternalInput").ap()
    wv_d = nc.dram_tensor("wv", [D, D], BF16, kind="ExternalInput").ap()
    bq_d = nc.dram_tensor("bq", [128, 16], F32, kind="ExternalInput").ap()
    bk_d = nc.dram_tensor("bk", [128, 16], F32, kind="ExternalInput").ap()
    bv_d = nc.dram_tensor("bv", [1, D], F32, kind="ExternalInput").ap()
    qT_o = nc.dram_tensor("qT", [D, NOWN], BF16, kind="ExternalOutput").ap()
    kT_o = nc.dram_tensor("kT", [D, NOWN], BF16, kind="ExternalOutput").ap()
    v_o = nc.dram_tensor("v", [NOWN, D], BF16, kind="ExternalOutput").ap()

    with ExitStack() as ctx:
        e = ctx.enter_context
        xT = e(nc.sbuf_tensor("xT_s", [128, NC_D * NOWN], BF16))
        wa = e(nc.sbuf_tensor("w_a", [128, NC_D * D], BF16))
        bc = e(nc.sbuf_tensor("bc", [128, 16], F32))
        bvr = e(nc.sbuf_tensor("bvr", [1, D], F32))
        bvr_b = e(nc.sbuf_tensor("bvr_b", [1, D], BF16))
        bvb = e(nc.sbuf_tensor("bvb", [128, D], BF16))
        ones1 = e(nc.sbuf_tensor("ones1", [1, 128], BF16))
        outT = e(nc.sbuf_tensor("outT", [128, NC_D * NOWN], BF16))
        ps = [e(nc.psum_tensor(f"ps{i}", [128, 512], F32)) for i in range(2)]
        pbv = e(nc.psum_tensor("pbv", [128, 512], F32))
        block = e(nc.Block())
        pb = PB(nc, ctx)

        for c in range(NC_D):
            pb.dma("sync", tag="a", fn=lambda h, c=c: h.dma_start(
                out=xT[:, c * NOWN:(c + 1) * NOWN], in_=xT_d[c * 128:(c + 1) * 128, :]))
        pb.dma("sync", tag="a", fn=lambda h: h.dma_start(out=bvr[:], in_=bv_d))
        x_loaded = pb.dcount[("sync", "a")]

        pb.emit("vector", lambda h: h.memset(ones1[:], 1.0))
        pb.wait_dma("vector", "sync", tag="a", upto=x_loaded)
        pb.emit("vector", lambda h: h.tensor_copy(out=bvr_b[:], in_=bvr[:]))
        pb.barrier(["vector"], ["tensor"])
        for s4 in range(4):
            pb.emit("tensor", lambda h, s4=s4: nc.tensor.matmul(
                out=pbv[:, 0:512], lhsT=ones1[:], rhs=bvr_b[:, s4 * 512:(s4 + 1) * 512],
                start=True, stop=True))
            pb.barrier(["tensor"], ["vector"])
            pb.emit("vector", lambda h, s4=s4: h.tensor_copy(
                out=bvb[:, s4 * 512:(s4 + 1) * 512], in_=pbv[:, 0:512]))
            pb.barrier(["vector"], ["tensor"])

        def proj_T(w_dram, bias_dram, out_dram, first):
            if not first:
                pb.barrier(["tensor"], ["gpsimd"])
                pb.barrier(["scalar"], ["sync"])
            for c in range(NC_D):
                pb.dma("gpsimd", tag="w", fn=lambda h, c=c: h.dma_start(
                    out=wa[:, c * D:(c + 1) * D], in_=w_dram[c * 128:(c + 1) * 128, :]))
            pb.dma("sync", tag="a", fn=lambda h: h.dma_start(out=bc[:, 0:16], in_=bias_dram))
            if not first:
                pb.wait_dma("scalar", "scalar", tag="s")
            pb.wait_dma("tensor", "gpsimd", tag="w")
            pb.wait_dma("scalar", "sync", tag="a")
            for ft in range(NC_D):
                for qh in range(2):
                    p = ps[(ft * 2 + qh) % 2]
                    for c in range(NC_D):
                        pb.emit("tensor", lambda h, ft=ft, qh=qh, c=c, p=p: nc.tensor.matmul(
                            out=p[:, 0:512],
                            lhsT=wa[:, c * D + ft * 128: c * D + (ft + 1) * 128],
                            rhs=xT[:, c * NOWN + qh * 512: c * NOWN + (qh + 1) * 512],
                            start=(c == 0), stop=(c == NC_D - 1)))
                    pb.barrier(["tensor"], ["scalar"])
                    pb.emit("scalar", lambda h, ft=ft, qh=qh, p=p: h.activation(
                        out=outT[:, ft * NOWN + qh * 512: ft * NOWN + qh * 512 + 512],
                        in_=p[:, 0:512], func=AF.Identity, bias=bc[:, ft:ft + 1], scale=1.0))
                    pb.barrier(["scalar"], ["tensor"])
            pb.barrier(["scalar"], ["scalar"])
            for c in range(NC_D):
                pb.dma("scalar", tag="s", fn=lambda h, c=c, od=out_dram: h.dma_start(
                    out=od[c * 128:(c + 1) * 128, :],
                    in_=outT[:, c * NOWN:(c + 1) * NOWN]))

        proj_T(wq_d, bq_d, qT_o, True)
        proj_T(wk_d, bk_d, kT_o, False)

        pb.barrier(["tensor"], ["gpsimd"])
        for c in range(NC_D):
            pb.dma("gpsimd", tag="w", fn=lambda h, c=c: h.dma_start(
                out=wa[:, c * D:(c + 1) * D], in_=wv_d[c * 128:(c + 1) * 128, :]))
        pb.wait_dma("scalar", "scalar", tag="s")
        pb.barrier(["scalar"], ["vector"])
        pb.wait_dma("tensor", "gpsimd", tag="w")
        for kt in range(8):
            for s4 in range(4):
                p = ps[(kt * 4 + s4) % 2]
                for c in range(NC_D):
                    pb.emit("tensor", lambda h, kt=kt, s4=s4, c=c, p=p: nc.tensor.matmul(
                        out=p[:, 0:512],
                        lhsT=xT[:, c * NOWN + kt * 128: c * NOWN + (kt + 1) * 128],
                        rhs=wa[:, c * D + s4 * 512: c * D + (s4 + 1) * 512],
                        start=(c == 0), stop=(c == NC_D - 1)))
                pb.barrier(["tensor"], ["vector"])
                pb.emit("vector", lambda h, kt=kt, s4=s4, p=p: h.tensor_tensor(
                    out=outT[:, kt * D + s4 * 512: kt * D + (s4 + 1) * 512],
                    in0=p[:, 0:512], in1=bvb[:, s4 * 512:(s4 + 1) * 512], op=ALU.add))
                pb.barrier(["vector"], ["tensor"])
        pb.barrier(["vector"], ["gpsimd"])
        for kt in range(8):
            pb.dma("gpsimd", tag="s", fn=lambda h, kt=kt: h.dma_start(
                out=v_o[kt * 128:(kt + 1) * 128, :], in_=outT[:, kt * D:(kt + 1) * D]))


        pb.wait_dma("scalar", "scalar", tag="s")
        pb.wait_dma("gpsimd", "gpsimd", tag="s")
        pb.finalize(block)
    return nc


def _bcols(b):
    n = b.shape[0] // 128
    return np.ascontiguousarray(np.asarray(b, dtype=np.float32).reshape(n, 128).T)






SHIFT1 = 12.0
ISQ_HD = 1.0 / float(np.sqrt(128.0))


def build_attn1():
    nc = bass.Bass("TRN2", target_bir_lowering=False)
    qT_d = nc.dram_tensor("qT", [D, NOWN], BF16, kind="ExternalInput").ap()
    kT_d = nc.dram_tensor("kTf", [D, S], BF16, kind="ExternalInput").ap()
    vblk_d = nc.dram_tensor("vblk", [NH, 128, S], BF16, kind="ExternalInput").ap()
    wo_d = nc.dram_tensor("wo", [D, D], BF16, kind="ExternalInput").ap()
    bo_d = nc.dram_tensor("bo", [1, D], F32, kind="ExternalInput").ap()
    xres_d = nc.dram_tensor("xres", [NOWN, D], BF16, kind="ExternalInput").ap()
    out_d = nc.dram_tensor("out", [NOWN, D], BF16, kind="ExternalOutput").ap()

    NKT = S // 128

    with ExitStack() as ctx:
        e = ctx.enter_context
        qT = e(nc.sbuf_tensor("qT_s", [128, NH * NOWN], BF16))
        pt = e(nc.sbuf_tensor("pt_s", [128, NKT * NOWN], BF16))
        ao = e(nc.sbuf_tensor("ao_s", [128, NH * NOWN], BF16))
        wos = e(nc.sbuf_tensor("wos", [128, NC_D * D], BF16))
        kv = e(nc.sbuf_tensor("kv_s", [128, 2 * S], BF16))
        xr = e(nc.sbuf_tensor("xr_s", [128, D], BF16))
        sout = e(nc.sbuf_tensor("sout", [128, D], BF16))
        bor = e(nc.sbuf_tensor("bor", [1, D], F32))
        bor_b = e(nc.sbuf_tensor("bor_b", [1, D], BF16))
        bob = e(nc.sbuf_tensor("bob", [128, D], BF16))
        ones1 = e(nc.sbuf_tensor("ones1", [1, 128], BF16))
        onesc = e(nc.sbuf_tensor("onesc", [128, 1], BF16))
        rrow = e(nc.sbuf_tensor("rrow", [1, NOWN], BF16))
        rrow32 = e(nc.sbuf_tensor("rrow32", [1, NOWN], F32))
        rbc = e(nc.sbuf_tensor("rbc", [128, NOWN], BF16))
        cols = e(nc.sbuf_tensor("cols", [128, 8], F32))
        shcol = e(nc.sbuf_tensor("shcol", [128, 1], F32))
        sc = e(nc.psum_tensor("sc", [128, 1024], F32))
        po = e(nc.psum_tensor("po", [128, 1024], F32))
        sums = e(nc.psum_tensor("sums", [1, 1024], F32))
        block = e(nc.Block())
        pb = PB(nc, ctx)

        for c in range(NC_D):
            pb.dma("sync", tag="a", fn=lambda h, c=c: h.dma_start(
                out=qT[:, c * NOWN:(c + 1) * NOWN], in_=qT_d[c * 128:(c + 1) * 128, :]))
        pb.dma("sync", tag="a", fn=lambda h: h.dma_start(out=bor[:], in_=bo_d))
        for c in range(NC_D):
            pb.dma("gpsimd", tag="w", fn=lambda h, c=c: h.dma_start(
                out=wos[:, c * D:(c + 1) * D], in_=wo_d[c * 128:(c + 1) * 128, :]))
        pb.emit("vector", lambda h: h.memset(ones1[:], 1.0))
        pb.emit("vector", lambda h: h.memset(onesc[:], 1.0))
        pb.emit("vector", lambda h: h.memset(shcol[:], -SHIFT1))
        pb.barrier(["vector"], ["scalar"])
        pb.wait_dma("vector", "sync", tag="a")
        pb.emit("vector", lambda h: h.tensor_copy(out=bor_b[:], in_=bor[:]))
        pb.barrier(["vector"], ["tensor"])
        for s4 in range(4):
            pb.emit("tensor", lambda h, s4=s4: nc.tensor.matmul(
                out=sc[:, 0:512], lhsT=ones1[:], rhs=bor_b[:, s4 * 512:(s4 + 1) * 512],
                start=True, stop=True))
            pb.barrier(["tensor"], ["vector"])
            pb.emit("vector", lambda h, s4=s4: h.tensor_copy(
                out=bob[:, s4 * 512:(s4 + 1) * 512], in_=sc[:, 0:512]))
            pb.barrier(["vector"], ["tensor"])

        for hh in range(NH):
            if hh > 0:
                pb.barrier(["tensor"], ["gpsimd"])
            pb.dma("gpsimd", tag="w", fn=lambda h, hh=hh: h.dma_start(
                out=kv[:, 0:S], in_=kT_d[hh * 128:(hh + 1) * 128, :]))
            pb.dma("gpsimd", tag="w", fn=lambda h, hh=hh: h.dma_start(
                out=kv[:, S:2 * S], in_=vblk_d[hh, :, :]))
            pb.wait_dma("tensor", "gpsimd", tag="w")
            for kt in range(NKT):
                for qh in range(2):
                    pb.emit("tensor", lambda h, hh=hh, kt=kt, qh=qh: nc.tensor.matmul(
                        out=sc[:, qh * 512:qh * 512 + 512],
                        lhsT=kv[:, kt * 128:(kt + 1) * 128],
                        rhs=qT[:, hh * NOWN + qh * 512: hh * NOWN + qh * 512 + 512],
                        start=True, stop=True))
                pb.barrier(["tensor"], ["scalar"])
                pb.emit("scalar", lambda h, kt=kt: h.activation(
                    out=pt[:, kt * NOWN:(kt + 1) * NOWN], in_=sc[:, 0:1024],
                    func=AF.Exp, bias=shcol[:, 0:1], scale=ISQ_HD))
                pb.barrier(["scalar"], ["tensor"])
                for qh in range(2):
                    pb.emit("tensor", lambda h, kt=kt, qh=qh: nc.tensor.matmul(
                        out=sums[0:1, qh * 512:qh * 512 + 512], lhsT=onesc[:, 0:1],
                        rhs=pt[:, kt * NOWN + qh * 512: kt * NOWN + qh * 512 + 512],
                        start=(kt == 0), stop=(kt == NKT - 1)))
                    pb.emit("tensor", lambda h, kt=kt, qh=qh: nc.tensor.matmul(
                        out=po[:, qh * 512:qh * 512 + 512],
                        lhsT=kv[:, S + kt * 128: S + (kt + 1) * 128],
                        rhs=pt[:, kt * NOWN + qh * 512: kt * NOWN + qh * 512 + 512],
                        start=(kt == 0), stop=(kt == NKT - 1)))
            pb.barrier(["tensor"], ["vector"])
            pb.barrier(["vector"], ["vector"])
            pb.emit("vector", lambda h: h.reciprocal(out=rrow32[:, 0:512], in_=sums[0:1, 0:512]))
            pb.emit("vector", lambda h: h.reciprocal(out=rrow32[:, 512:1024], in_=sums[0:1, 512:1024]))
            pb.barrier(["vector"], ["vector"])
            pb.emit("vector", lambda h: h.tensor_copy(out=rrow[:, 0:1024], in_=rrow32[:, 0:1024]))
            pb.barrier(["vector"], ["tensor"])
            for qh in range(2):
                pb.emit("tensor", lambda h, qh=qh: nc.tensor.matmul(
                    out=sc[:, qh * 512:qh * 512 + 512], lhsT=ones1[:],
                    rhs=rrow[:, qh * 512:qh * 512 + 512], start=True, stop=True))
            pb.barrier(["tensor"], ["vector"])
            pb.emit("vector", lambda h: h.tensor_copy(out=rbc[:, 0:1024], in_=sc[:, 0:1024]))
            pb.barrier(["vector"], ["vector"])
            for qh in range(2):
                pb.emit("vector", lambda h, hh=hh, qh=qh: h.tensor_tensor(
                    out=ao[:, hh * NOWN + qh * 512: hh * NOWN + qh * 512 + 512],
                    in0=po[:, qh * 512:qh * 512 + 512],
                    in1=rbc[:, qh * 512:qh * 512 + 512], op=ALU.mult))
            pb.barrier(["vector"], ["tensor"])

        pb.wait_dma("tensor", "gpsimd", tag="w")
        for qt in range(8):
            pb.barrier(["vector"], ["sync"])
            pb.dma("sync", tag="a", fn=lambda h, qt=qt: h.dma_start(
                out=xr[:, :], in_=xres_d[qt * 128:(qt + 1) * 128, :]))
            for s4 in range(4):
                for c in range(NC_D):
                    pb.emit("tensor", lambda h, qt=qt, s4=s4, c=c: nc.tensor.matmul(
                        out=sc[:, (s4 % 2) * 512:(s4 % 2) * 512 + 512],
                        lhsT=ao[:, c * NOWN + qt * 128: c * NOWN + (qt + 1) * 128],
                        rhs=wos[:, c * D + s4 * 512: c * D + (s4 + 1) * 512],
                        start=(c == 0), stop=(c == NC_D - 1)))
                pb.barrier(["tensor"], ["vector"])
                if s4 == 0:
                    pb.wait_dma("vector", "sync", tag="a")
                pb.emit("vector", lambda h, qt=qt, s4=s4: h.tensor_tensor(
                    out=sout[:, s4 * 512:(s4 + 1) * 512],
                    in0=sc[:, (s4 % 2) * 512:(s4 % 2) * 512 + 512],
                    in1=bob[:, s4 * 512:(s4 + 1) * 512], op=ALU.add))
                pb.barrier(["vector"], ["vector"])
                pb.emit("vector", lambda h, qt=qt, s4=s4: h.tensor_tensor(
                    out=sout[:, s4 * 512:(s4 + 1) * 512],
                    in0=sout[:, s4 * 512:(s4 + 1) * 512],
                    in1=xr[:, s4 * 512:(s4 + 1) * 512], op=ALU.add))
                pb.barrier(["vector"], ["tensor"])
            _emit_ln(pb, sout, pt, cols, qt, out_d)
        pb.wait_dma("gpsimd", "gpsimd", tag="s")
        pb.wait_dma("sync", "sync", tag="a")
        pb.finalize(block)
    return nc


def _emit_ln(pb, s_sb, scratch, cols, qt, out_d):
    nc = pb.nc
    V, SC = "vector", "scalar"
    pb.wait_dma(SC, "gpsimd", tag="s")
    pb.wait_dma(V, "gpsimd", tag="s")
    pb.barrier([V], [V])
    pb.emit(V, lambda h: h.reduce_sum(out=cols[:, 0:1], in_=s_sb[:, 0:D], axis=AX))
    pb.barrier([V], [SC, V])
    pb.emit(SC, lambda h: h.activation(
        out=scratch[:, 0:D], in_=s_sb[:, 0:D], func=AF.Square))
    pb.emit(V, lambda h: h.tensor_scalar_mul(out=cols[:, 2:3], in0=cols[:, 0:1],
                                             scalar1=1.0 / D))
    pb.barrier([SC, V], [V])
    pb.emit(V, lambda h: h.reduce_sum(out=cols[:, 1:2], in_=scratch[:, 0:D], axis=AX))
    pb.emit(V, lambda h: h.tensor_tensor(out=cols[:, 4:5], in0=cols[:, 2:3],
                                         in1=cols[:, 2:3], op=ALU.mult))
    pb.barrier([V], [V])
    pb.emit(V, lambda h: h.tensor_scalar(out=cols[:, 3:4], in0=cols[:, 1:2],
                                         scalar1=1.0 / D, scalar2=None,
                                         op0=ALU.mult))
    pb.barrier([V], [V])
    pb.emit(V, lambda h: h.tensor_tensor(out=cols[:, 4:5], in0=cols[:, 3:4],
                                         in1=cols[:, 4:5], op=ALU.subtract))
    pb.barrier([V], [V])
    pb.emit(V, lambda h: h.tensor_scalar_add(out=cols[:, 5:6], in0=cols[:, 4:5],
                                             scalar1=1e-5))
    pb.barrier([V], [SC])
    pb.emit(SC, lambda h: h.activation(out=cols[:, 5:6], in_=cols[:, 5:6],
                                       func=AF.Sqrt))
    pb.barrier([SC], [V])
    pb.emit(V, lambda h: h.reciprocal(out=cols[:, 6:7], in_=cols[:, 5:6]))
    pb.barrier([V], [V])
    pb.emit(V, lambda h: h.tensor_scalar(
        out=scratch[:, 0:D], in0=s_sb[:, 0:D], scalar1=cols[:, 2:3],
        scalar2=cols[:, 6:7], op0=ALU.subtract, op1=ALU.mult))
    pb.barrier([V], ["gpsimd"])
    pb.dma("gpsimd", tag="s", fn=lambda h: h.dma_start(
        out=out_d[qt * 128:(qt + 1) * 128, :], in_=scratch[:, 0:D]))



SHIFT0 = 6.0


def build_attn0():
    nc = bass.Bass("TRN2", target_bir_lowering=False)
    qnT_d = nc.dram_tensor("qnT", [NH * NOPE, NOWN], BF16, kind="ExternalInput").ap()
    qrT_d = nc.dram_tensor("qrT", [NH * ROPE, NOWN], BF16, kind="ExternalInput").ap()
    ckvT_d = nc.dram_tensor("ckvT", [KV_LORA + ROPE, S], BF16, kind="ExternalInput").ap()
    wkvb_d = nc.dram_tensor("wkvb", [KV_LORA, NH * 256], F32, kind="ExternalInput").ap()
    bkv_d = nc.dram_tensor("bkv", [128, 32], F32, kind="ExternalInput").ap()
    bvrow_d = nc.dram_tensor("bvrow", [1, NH * V_HD], F32, kind="ExternalInput").ap()
    wo_d = nc.dram_tensor("wo", [D, D], F32, kind="ExternalInput").ap()
    bo_d = nc.dram_tensor("bo", [1, D], F32, kind="ExternalInput").ap()
    xres_d = nc.dram_tensor("xres", [NOWN, D], F32, kind="ExternalInput").ap()
    out_d = nc.dram_tensor("out", [NOWN, D], BF16, kind="ExternalOutput").ap()

    NKT = S // 128
    NCC = KV_LORA // 128

    with ExitStack() as ctx:
        e = ctx.enter_context
        qnT = e(nc.sbuf_tensor("qnT_s", [128, NH * NOWN], BF16))
        qrT = e(nc.sbuf_tensor("qrT_s", [128, 8 * NOWN], BF16))
        ckv = e(nc.sbuf_tensor("ckv_s", [128, 5 * S], BF16))
        wkv = e(nc.sbuf_tensor("wkv_s", [128, NCC * 4096], BF16))
        pt = e(nc.sbuf_tensor("pt_s", [128, NKT * NOWN], BF16))
        ao = e(nc.sbuf_tensor("ao_s", [128, NH * NOWN], BF16))
        kh = e(nc.sbuf_tensor("kh_s", [128, S], BF16))
        vh = e(nc.sbuf_tensor("vh_s", [128, S], BF16))
        bkc = e(nc.sbuf_tensor("bkc", [128, 32], F32))
        bvrb = e(nc.sbuf_tensor("bvrb", [1, 2048], BF16))
        bvb = e(nc.sbuf_tensor("bvb", [128, 2048], BF16))
        borb = e(nc.sbuf_tensor("borb", [1, D], BF16))
        bob = e(nc.sbuf_tensor("bob", [128, D], BF16))
        ones1 = e(nc.sbuf_tensor("ones1", [1, 128], BF16))
        onesc = e(nc.sbuf_tensor("onesc", [128, 1], BF16))
        shcol = e(nc.sbuf_tensor("shcol", [128, 1], F32))
        rrow = e(nc.sbuf_tensor("rrow", [1, NOWN], BF16))
        rrow32 = e(nc.sbuf_tensor("rrow32", [1, NOWN], F32))
        rbc = e(nc.sbuf_tensor("rbc", [128, NOWN], BF16))
        cols = e(nc.sbuf_tensor("cols", [128, 8], F32))
        sc = e(nc.psum_tensor("sc", [128, 1024], F32))
        po = e(nc.psum_tensor("po", [128, 1024], F32))
        sums = e(nc.psum_tensor("sums", [1, 1024], F32))
        pk = e(nc.psum_tensor("pk", [128, 1024], F32))
        block = e(nc.Block())
        pb = PB(nc, ctx)

        for c in range(NH):
            pb.dma("sync", tag="a", fn=lambda h, c=c: h.dma_start(
                out=qnT[:, c * NOWN:(c + 1) * NOWN], in_=qnT_d[c * 128:(c + 1) * 128, :]))
        for c in range(8):
            pb.dma("sync", tag="a", fn=lambda h, c=c: h.dma_start(
                out=qrT[:, c * NOWN:(c + 1) * NOWN], in_=qrT_d[c * 128:(c + 1) * 128, :]))
        for c in range(4):
            pb.dma("sync", tag="a", fn=lambda h, c=c: h.dma_start(
                out=ckv[:, c * S:(c + 1) * S], in_=ckvT_d[c * 128:(c + 1) * 128, :]))
        pb.dma("sync", tag="a", fn=lambda h: h.dma_start(
            out=ckv[0:64, 4 * S:5 * S], in_=ckvT_d[512:576, :]))
        pb.dma("sync", tag="a", fn=lambda h: h.dma_start(
            out=ckv[64:128, 4 * S:5 * S], in_=ckvT_d[512:576, :]))
        pb.dma("sync", tag="a", fn=lambda h: h.dma_start(out=bkc[:], in_=bkv_d))
        pb.dma("gpsimd", tag="a", fn=lambda h: h.dma_start(out=bvrb[:], in_=bvrow_d))
        pb.dma("gpsimd", tag="a", fn=lambda h: h.dma_start(out=borb[:], in_=bo_d))
        for c in range(NCC):
            pb.dma("gpsimd", tag="w", fn=lambda h, c=c: h.dma_start(
                out=wkv[:, c * 4096:(c + 1) * 4096], in_=wkvb_d[c * 128:(c + 1) * 128, :]))
        pb.emit("vector", lambda h: h.memset(ones1[:], 1.0))
        pb.emit("vector", lambda h: h.memset(onesc[:], 1.0))
        pb.emit("vector", lambda h: h.memset(shcol[:], -SHIFT0))
        pb.barrier(["vector"], ["scalar"])
        pb.wait_dma("vector", "sync", tag="a")
        pb.barrier(["vector"], ["tensor"])
        pb.wait_dma("tensor", "gpsimd", tag="w")
        for (row, dst) in ((bvrb, bvb), (borb, bob)):
            for s4 in range(4):
                pb.emit("tensor", lambda h, s4=s4, row=row: nc.tensor.matmul(
                    out=pk[:, 0:512], lhsT=ones1[:], rhs=row[:, s4 * 512:(s4 + 1) * 512],
                    start=True, stop=True))
                pb.barrier(["tensor"], ["vector"])
                pb.emit("vector", lambda h, s4=s4, dst=dst: h.tensor_copy(
                    out=dst[:, s4 * 512:(s4 + 1) * 512], in_=pk[:, 0:512]))
                pb.barrier(["vector"], ["tensor"])

        pb.wait_dma("tensor", "gpsimd", tag="w")
        pb.wait_dma("scalar", "sync", tag="a")

        for hh in range(NH):
            if hh > 0:
                pb.barrier(["tensor"], ["scalar", "vector"])
            for ks in range(4):
                for c in range(NCC):
                    pb.emit("tensor", lambda h, hh=hh, ks=ks, c=c: nc.tensor.matmul(
                        out=pk[:, (ks % 2) * 512:(ks % 2) * 512 + 512],
                        lhsT=wkv[:, c * 4096 + hh * 256: c * 4096 + hh * 256 + 128],
                        rhs=ckv[:, c * S + ks * 512: c * S + ks * 512 + 512],
                        start=(c == 0), stop=(c == NCC - 1)))
                pb.barrier(["tensor"], ["scalar"])
                pb.emit("scalar", lambda h, hh=hh, ks=ks: h.activation(
                    out=kh[:, ks * 512:(ks + 1) * 512],
                    in_=pk[:, (ks % 2) * 512:(ks % 2) * 512 + 512],
                    func=AF.Identity, bias=bkc[:, 2 * hh:2 * hh + 1], scale=1.0))
                pb.barrier(["scalar"], ["tensor"])
            for kt in range(NKT):
                for c in range(NCC):
                    pb.emit("tensor", lambda h, hh=hh, kt=kt, c=c: nc.tensor.matmul(
                        out=pk[:, (kt % 4) * 128 + 512:(kt % 4) * 128 + 640],
                        lhsT=ckv[:, c * S + kt * 128: c * S + (kt + 1) * 128],
                        rhs=wkv[:, c * 4096 + hh * 256 + 128: c * 4096 + hh * 256 + 256],
                        start=(c == 0), stop=(c == NCC - 1)))
                pb.barrier(["tensor"], ["vector"])
                pb.emit("vector", lambda h, hh=hh, kt=kt: h.tensor_tensor(
                    out=vh[:, kt * 128:(kt + 1) * 128],
                    in0=pk[:, (kt % 4) * 128 + 512:(kt % 4) * 128 + 640],
                    in1=bvb[:, hh * 128:(hh + 1) * 128], op=ALU.add))
                pb.barrier(["vector"], ["tensor"])
            pb.barrier(["scalar", "vector"], ["tensor"])
            for kt in range(NKT):
                for qh in range(2):
                    pb.emit("tensor", lambda h, hh=hh, kt=kt, qh=qh: nc.tensor.matmul(
                        out=sc[:, qh * 512:qh * 512 + 512],
                        lhsT=kh[:, kt * 128:(kt + 1) * 128],
                        rhs=qnT[:, hh * NOWN + qh * 512: hh * NOWN + qh * 512 + 512],
                        start=True, stop=False))
                    pb.emit("tensor", lambda h, hh=hh, kt=kt, qh=qh: nc.tensor.matmul(
                        out=sc[:, qh * 512:qh * 512 + 512],
                        lhsT=ckv[64 * (hh % 2):64 * (hh % 2) + 64,
                                 4 * S + kt * 128: 4 * S + (kt + 1) * 128],
                        rhs=qrT[64 * (hh % 2):64 * (hh % 2) + 64,
                                (hh // 2) * NOWN + qh * 512: (hh // 2) * NOWN + qh * 512 + 512],
                        start=False, stop=True))
                pb.barrier(["tensor"], ["scalar"])
                pb.emit("scalar", lambda h, kt=kt: h.activation(
                    out=pt[:, kt * NOWN:(kt + 1) * NOWN], in_=sc[:, 0:1024],
                    func=AF.Exp, bias=shcol[:, 0:1], scale=ISQ_HD))
                pb.barrier(["scalar"], ["tensor"])
                for qh in range(2):
                    pb.emit("tensor", lambda h, kt=kt, qh=qh: nc.tensor.matmul(
                        out=sums[0:1, qh * 512:qh * 512 + 512], lhsT=onesc[:, 0:1],
                        rhs=pt[:, kt * NOWN + qh * 512: kt * NOWN + qh * 512 + 512],
                        start=(kt == 0), stop=(kt == NKT - 1)))
                    pb.emit("tensor", lambda h, kt=kt, qh=qh: nc.tensor.matmul(
                        out=po[:, qh * 512:qh * 512 + 512],
                        lhsT=vh[:, kt * 128:(kt + 1) * 128],
                        rhs=pt[:, kt * NOWN + qh * 512: kt * NOWN + qh * 512 + 512],
                        start=(kt == 0), stop=(kt == NKT - 1)))
            pb.barrier(["tensor"], ["vector"])
            pb.barrier(["vector"], ["vector"])
            pb.emit("vector", lambda h: h.reciprocal(out=rrow32[:, 0:512], in_=sums[0:1, 0:512]))
            pb.emit("vector", lambda h: h.reciprocal(out=rrow32[:, 512:1024], in_=sums[0:1, 512:1024]))
            pb.barrier(["vector"], ["vector"])
            pb.emit("vector", lambda h: h.tensor_copy(out=rrow[:, 0:1024], in_=rrow32[:, 0:1024]))
            pb.barrier(["vector"], ["tensor"])
            for qh in range(2):
                pb.emit("tensor", lambda h, qh=qh: nc.tensor.matmul(
                    out=sc[:, qh * 512:qh * 512 + 512], lhsT=ones1[:],
                    rhs=rrow[:, qh * 512:qh * 512 + 512], start=True, stop=True))
            pb.barrier(["tensor"], ["vector"])
            pb.emit("vector", lambda h: h.tensor_copy(out=rbc[:, 0:1024], in_=sc[:, 0:1024]))
            pb.barrier(["vector"], ["vector"])
            for qh in range(2):
                pb.emit("vector", lambda h, hh=hh, qh=qh: h.tensor_tensor(
                    out=ao[:, hh * NOWN + qh * 512: hh * NOWN + qh * 512 + 512],
                    in0=po[:, qh * 512:qh * 512 + 512],
                    in1=rbc[:, qh * 512:qh * 512 + 512], op=ALU.mult))
            pb.barrier(["vector"], ["tensor"])

        wos = qnT
        for s4 in range(4):
            pb.barrier(["tensor"], ["gpsimd"])
            for c in range(NC_D):
                pb.dma("gpsimd", tag="w", fn=lambda h, c=c, s4=s4: h.dma_start(
                    out=wos[:, c * 512:(c + 1) * 512],
                    in_=wo_d[c * 128:(c + 1) * 128, s4 * 512:(s4 + 1) * 512]))
            pb.wait_dma("tensor", "gpsimd", tag="w")
            for qt in range(8):
                p_half = (qt % 2) * 512
                for c in range(NC_D):
                    pb.emit("tensor", lambda h, qt=qt, s4=s4, c=c, p_half=p_half: nc.tensor.matmul(
                        out=sc[:, p_half:p_half + 512],
                        lhsT=ao[:, c * NOWN + qt * 128: c * NOWN + (qt + 1) * 128],
                        rhs=wos[:, c * 512:(c + 1) * 512],
                        start=(c == 0), stop=(c == NC_D - 1)))
                pb.barrier(["tensor"], ["vector"])
                pb.emit("vector", lambda h, qt=qt, s4=s4, p_half=p_half: h.tensor_tensor(
                    out=pt[:, qt * 2048 + s4 * 512: qt * 2048 + s4 * 512 + 512],
                    in0=sc[:, p_half:p_half + 512],
                    in1=bob[:, s4 * 512:(s4 + 1) * 512], op=ALU.add))
                pb.barrier(["vector"], ["tensor"])

        for qt in range(8):
            pb.barrier(["vector"], ["gpsimd"])
            pb.dma("gpsimd", tag="a", fn=lambda h, qt=qt: h.dma_start(
                out=vh[:, 0:D], in_=xres_d[qt * 128:(qt + 1) * 128, :]))
            pb.wait_dma("vector", "gpsimd", tag="a")
            pb.barrier(["vector"], ["vector"])
            pb.emit("vector", lambda h, qt=qt: h.tensor_tensor(
                out=pt[:, qt * 2048: qt * 2048 + 2048],
                in0=pt[:, qt * 2048: qt * 2048 + 2048],
                in1=vh[:, 0:D], op=ALU.add))
            _emit_ln2(pb, pt, qt * 2048, kh, cols, qt, out_d)
        pb.wait_dma("gpsimd", "gpsimd", tag="s")
        pb.wait_dma("sync", "sync", tag="a")
        pb.finalize(block)
    return nc


def _emit_ln2(pb, s_sb, s_off, scratch, cols, qt, out_d):
    nc = pb.nc
    V, SC = "vector", "scalar"
    pb.wait_dma(SC, "gpsimd", tag="s")
    pb.wait_dma(V, "gpsimd", tag="s")
    pb.barrier([V], [V])
    pb.emit(V, lambda h: h.reduce_sum(out=cols[:, 0:1], in_=s_sb[:, s_off:s_off + D], axis=AX))
    pb.barrier([V], [SC, V])
    pb.emit(SC, lambda h: h.activation(
        out=scratch[:, 0:D], in_=s_sb[:, s_off:s_off + D], func=AF.Square))
    pb.emit(V, lambda h: h.tensor_scalar_mul(out=cols[:, 2:3], in0=cols[:, 0:1],
                                             scalar1=1.0 / D))
    pb.barrier([SC, V], [V])
    pb.emit(V, lambda h: h.reduce_sum(out=cols[:, 1:2], in_=scratch[:, 0:D], axis=AX))
    pb.emit(V, lambda h: h.tensor_tensor(out=cols[:, 4:5], in0=cols[:, 2:3],
                                         in1=cols[:, 2:3], op=ALU.mult))
    pb.barrier([V], [V])
    pb.emit(V, lambda h: h.tensor_scalar(out=cols[:, 3:4], in0=cols[:, 1:2],
                                         scalar1=1.0 / D, scalar2=None, op0=ALU.mult))
    pb.barrier([V], [V])
    pb.emit(V, lambda h: h.tensor_tensor(out=cols[:, 4:5], in0=cols[:, 3:4],
                                         in1=cols[:, 4:5], op=ALU.subtract))
    pb.barrier([V], [V])
    pb.emit(V, lambda h: h.tensor_scalar_add(out=cols[:, 5:6], in0=cols[:, 4:5],
                                             scalar1=1e-5))
    pb.barrier([V], [SC])
    pb.emit(SC, lambda h: h.activation(out=cols[:, 5:6], in_=cols[:, 5:6], func=AF.Sqrt))
    pb.barrier([SC], [V])
    pb.emit(V, lambda h: h.reciprocal(out=cols[:, 6:7], in_=cols[:, 5:6]))
    pb.barrier([V], [V])
    pb.emit(V, lambda h: h.tensor_scalar(
        out=scratch[:, 0:D], in0=s_sb[:, s_off:s_off + D], scalar1=cols[:, 2:3],
        scalar2=cols[:, 6:7], op0=ALU.subtract, op1=ALU.mult))
    pb.barrier([V], ["gpsimd"])
    pb.dma("gpsimd", tag="s", fn=lambda h: h.dma_start(
        out=out_d[qt * 128:(qt + 1) * 128, :], in_=scratch[:, 0:D]))



def build_ffn0():
    nc = bass.Bass("TRN2", target_bir_lowering=False)
    x_d = nc.dram_tensor("x", [NOWN, D], BF16, kind="ExternalInput").ap()
    xT_d = nc.dram_tensor("xT", [D, NOWN], BF16, kind="ExternalInput").ap()
    wgu_d = nc.dram_tensor("wgu", [44, NC_D, 128, 256], BF16, kind="ExternalInput").ap()
    wd_d = nc.dram_tensor("wd", [I0, D], BF16, kind="ExternalInput").ap()
    out_d = nc.dram_tensor("out", [NOWN, D], BF16, kind="ExternalOutput").ap()
    NSP = 44

    with ExitStack() as ctx:
        e = ctx.enter_context
        xT = e(nc.sbuf_tensor("xT_s", [128, NC_D * NOWN], BF16))
        hb = e(nc.sbuf_tensor("hb_s", [128, NSP * NOWN], BF16))
        wg = e(nc.sbuf_tensor("wg_s", [128, NC_D * 256], BF16))
        wd = e(nc.sbuf_tensor("wd_s", [128, D], BF16))
        tg = e(nc.sbuf_tensor("tg_s", [128, NOWN], BF16))
        yb = e(nc.sbuf_tensor("yb_s", [128, D], BF16))
        xc = e(nc.sbuf_tensor("xc_s", [128, D], BF16))
        scr = e(nc.sbuf_tensor("scr_s", [128, D], BF16))
        cols = e(nc.sbuf_tensor("cols", [128, 12], F32))
        pgu = e(nc.psum_tensor("pgu", [128, 2048], F32))
        pd = e(nc.psum_tensor("pd", [128, 2048], F32))
        pg = pgu
        pd2 = pgu
        block = e(nc.Block())
        pb = PB(nc, ctx)

        for c in range(NC_D):
            pb.dma("sync", tag="a", fn=lambda h, c=c: h.dma_start(
                out=xT[:, c * NOWN:(c + 1) * NOWN], in_=xT_d[c * 128:(c + 1) * 128, :]))
        pb.wait_dma("tensor", "sync", tag="a")

        for s in range(NSP):
            if s > 0:
                pb.barrier(["tensor"], ["gpsimd"])
            for cc in range(NC_D):
                pb.dma("gpsimd", tag="w", fn=lambda h, s=s, cc=cc: h.dma_start(
                    out=wg[:, cc * 256:(cc + 1) * 256], in_=wgu_d[s, cc]))
            pb.wait_dma("tensor", "gpsimd", tag="w")
            for qh in range(2):
                for cc in range(NC_D):
                    pb.emit("tensor", lambda h, s=s, qh=qh, cc=cc: nc.tensor.matmul(
                        out=pg[:, qh * 512:qh * 512 + 512],
                        lhsT=wg[:, cc * 256:cc * 256 + 128],
                        rhs=xT[:, cc * NOWN + qh * 512: cc * NOWN + qh * 512 + 512],
                        start=(cc == 0), stop=(cc == NC_D - 1)))
                for cc in range(NC_D):
                    pb.emit("tensor", lambda h, s=s, qh=qh, cc=cc: nc.tensor.matmul(
                        out=pgu[:, 1024 + qh * 512:1024 + qh * 512 + 512],
                        lhsT=wg[:, cc * 256 + 128:cc * 256 + 256],
                        rhs=xT[:, cc * NOWN + qh * 512: cc * NOWN + qh * 512 + 512],
                        start=(cc == 0), stop=(cc == NC_D - 1)))
            pb.barrier(["tensor"], ["scalar"])
            pb.emit("scalar", lambda h: h.activation(
                out=tg[:, 0:NOWN], in_=pg[:, 0:NOWN], func=AF.Silu))
            pb.barrier(["scalar"], ["vector"])
            pb.emit("vector", lambda h, s=s: h.tensor_tensor(
                out=hb[:, s * NOWN:(s + 1) * NOWN], in0=pgu[:, 1024:1024 + NOWN],
                in1=tg[:, 0:NOWN], op=ALU.mult))
            pb.barrier(["vector"], ["tensor"])

        for qtp in range(4):
            for ic in range(NSP):
                pb.barrier(["tensor"], ["gpsimd"])
                pb.dma("gpsimd", tag="w", fn=lambda h, ic=ic: h.dma_start(
                    out=wd[:, :], in_=wd_d[ic * 128:(ic + 1) * 128, :]))
                pb.wait_dma("tensor", "gpsimd", tag="w")
                for qt2 in range(2):
                    p = pd if qt2 == 0 else pd2
                    qt = qtp * 2 + qt2
                    for os in range(4):
                        pb.emit("tensor", lambda h, ic=ic, qt=qt, os=os, p=p: nc.tensor.matmul(
                            out=p[:, os * 512:(os + 1) * 512],
                            lhsT=hb[:, ic * NOWN + qt * 128: ic * NOWN + (qt + 1) * 128],
                            rhs=wd[:, os * 512:(os + 1) * 512],
                            start=(ic == 0), stop=(ic == NSP - 1)))
            for qt2 in range(2):
                p = pd if qt2 == 0 else pd2
                qt = qtp * 2 + qt2
                pb.barrier(["tensor"], ["vector", "gpsimd"])
                pb.wait_dma("vector", "gpsimd", tag="a")
                pb.barrier(["vector"], ["vector", "gpsimd"])
                pb.dma("gpsimd", tag="a", fn=lambda h, qt=qt: h.dma_start(
                    out=xc[:, :], in_=x_d[qt * 128:(qt + 1) * 128, :]))
                pb.emit("vector", lambda h, p=p: h.tensor_copy(out=yb[:, 0:D], in_=p[:, 0:D]))
                pb.barrier(["vector"], ["scalar", "vector"])
                pb.wait_dma("scalar", "gpsimd", tag="s")
                pb.emit("scalar", lambda h: h.activation(out=scr[:, 0:D], in_=yb[:, 0:D],
                                                         func=AF.Square))
                pb.barrier(["scalar"], ["vector"])
                pb.emit("vector", lambda h: h.reduce_sum(out=cols[:, 8:9], in_=scr[:, 0:D], axis=AX))
                pb.barrier(["vector"], ["vector"])
                pb.emit("vector", lambda h: h.tensor_scalar(
                    out=cols[:, 9:10], in0=cols[:, 8:9], scalar1=1.0 / D, scalar2=1e-6,
                    op0=ALU.mult, op1=ALU.add))
                pb.barrier(["vector"], ["scalar"])
                pb.emit("scalar", lambda h: h.activation(out=cols[:, 9:10], in_=cols[:, 9:10],
                                                         func=AF.Sqrt))
                pb.barrier(["scalar"], ["vector"])
                pb.emit("vector", lambda h: h.reciprocal(out=cols[:, 10:11], in_=cols[:, 9:10]))
                pb.wait_dma("vector", "gpsimd", tag="a")
                pb.barrier(["vector"], ["vector"])
                pb.emit("vector", lambda h: h.scalar_tensor_tensor(
                    out=yb[:, 0:D], in0=yb[:, 0:D], scalar=cols[:, 10:11], in1=xc[:, 0:D],
                    op0=ALU.mult, op1=ALU.add))
                _emit_ln2(pb, yb, 0, scr, cols, qt, out_d)
        pb.wait_dma("gpsimd", "gpsimd", tag="s")
        pb.finalize(block)
    return nc



FFN1_DEBUG = False


def build_ffn1():
    nc = bass.Bass("TRN2", target_bir_lowering=False)
    _ikind = "ExternalOutput" if FFN1_DEBUG else "Internal"
    x_d = nc.dram_tensor("x", [NOWN, D], BF16, kind="ExternalInput").ap()
    wgu_d = nc.dram_tensor("wgu", [16, NC_D, 128, 1024], BF16, kind="ExternalInput").ap()
    wd_d = nc.dram_tensor("wd", [F1, D], BF16, kind="ExternalInput").ap()
    sgu_d = nc.dram_tensor("sgu", [1, 1], F32, kind="ExternalInput").ap()
    swd_d = nc.dram_tensor("swd", [1, 1], F32, kind="ExternalInput").ap()
    out_d = nc.dram_tensor("out", [NOWN, D], F32, kind="ExternalOutput").ap()
    xq_dram = nc.dram_tensor("xq_i", [NOWN, D], BF16, kind=_ikind).ap()
    hq_dram = nc.dram_tensor("hq_i", [NOWN, F1], BF16, kind=_ikind).ap()
    NIC = F1 // 128

    with ExitStack() as ctx:
        e = ctx.enter_context
        x3 = e(nc.sbuf_tensor("x3_s", [128, 8 * D], BF16))
        xqT = e(nc.sbuf_tensor("xqT_s", [128, NC_D * NOWN], BF16))
        wgs = e(nc.sbuf_tensor("wgs_s", [128, NC_D * 1024], BF16))
        hqt = e(nc.sbuf_tensor("hqt_s", [128, NOWN], BF16))
        wdc = e(nc.sbuf_tensor("wdc_s", [128, D], BF16))
        f32a = e(nc.sbuf_tensor("f32a", [128, D], F32))
        f32b = e(nc.sbuf_tensor("f32b", [128, D], F32))
        bfa = e(nc.sbuf_tensor("bfa", [128, F1 // 4], BF16))
        cols = e(nc.sbuf_tensor("colsq", [128, 28], F32))
        sca = e(nc.sbuf_tensor("sca", [128, 2], F32))
        ones1 = e(nc.sbuf_tensor("ones1q", [1, 128], BF16))
        srow = e(nc.sbuf_tensor("srow", [1, 2], F32))
        srowb = e(nc.sbuf_tensor("srowb", [1, 2], BF16))
        pg = e(nc.psum_tensor("pgq", [128, 2048], F32))
        pd = e(nc.psum_tensor("pdq", [128, 2048], F32))
        pd2 = pg
        block = e(nc.Block())
        pb = PB(nc, ctx)

        for c in range(8):
            pb.dma("sync", tag="a", fn=lambda h, c=c: h.dma_start(
                out=x3[:, c * D:(c + 1) * D], in_=x_d[c * 128:(c + 1) * 128, :]))
        pb.dma("sync", tag="a", fn=lambda h: h.dma_start(out=srow[0:1, 0:1], in_=sgu_d))
        pb.dma("sync", tag="a", fn=lambda h: h.dma_start(out=srow[0:1, 1:2], in_=swd_d))
        pb.emit("vector", lambda h: h.memset(ones1[:], 1.0))
        pb.wait_dma("vector", "sync", tag="a")
        pb.emit("vector", lambda h: h.tensor_copy(out=srowb[:], in_=srow[:]))
        pb.barrier(["vector"], ["tensor"])
        pb.emit("tensor", lambda h: nc.tensor.matmul(
            out=pg[:, 0:2], lhsT=ones1[:], rhs=srowb[0:1, 0:2], start=True, stop=True))
        pb.barrier(["tensor"], ["vector"])
        pb.emit("vector", lambda h: h.tensor_copy(out=sca[:, 0:2], in_=pg[:, 0:2]))
        pb.barrier(["vector"], ["vector", "tensor"])

        for qt in range(8):
            pb.emit("vector", lambda h, qt=qt: h.reduce_max(
                out=cols[:, qt:qt + 1], in_=x3[:, qt * D:(qt + 1) * D], axis=AX,
                apply_absolute_value=True))
        pb.barrier(["vector"], ["vector"])
        pb.emit("vector", lambda h: h.tensor_scalar_max(out=cols[:, 0:8], in0=cols[:, 0:8],
                                                        scalar1=1e-5))
        pb.barrier(["vector"], ["vector"])
        pb.emit("vector", lambda h: h.tensor_scalar(
            out=cols[:, 8:16], in0=cols[:, 0:8], scalar1=sca[:, 0:1], scalar2=1.0 / 127,
            op0=ALU.mult, op1=ALU.mult))
        pb.emit("vector", lambda h: h.reciprocal(out=cols[:, 16:24], in_=cols[:, 0:8]))
        pb.barrier(["vector"], ["vector"])
        pb.emit("vector", lambda h: h.tensor_scalar_mul(out=cols[:, 16:24], in0=cols[:, 16:24],
                                                        scalar1=127.0))
        pb.barrier(["vector"], ["vector"])
        for qt in range(8):
            pb.wait_dma("vector", "gpsimd", tag="s")
            pb.emit("vector", lambda h, qt=qt: h.tensor_scalar_mul(
                out=f32a[:, 0:D], in0=x3[:, qt * D:(qt + 1) * D], scalar1=cols[:, 16 + qt:17 + qt]))
            pb.barrier(["vector"], ["vector"])
            iv = f32b.bitcast(I32)
            pb.emit("vector", lambda h, iv=iv: h.tensor_copy(out=iv, in_=f32a[:, 0:D]))
            pb.barrier(["vector"], ["vector"])
            pb.emit("vector", lambda h, iv=iv: h.tensor_copy(out=bfa[:, 0:D], in_=iv))
            pb.barrier(["vector"], ["gpsimd"])
            pb.dma("gpsimd", tag="s", fn=lambda h, qt=qt: h.dma_start(
                out=xq_dram[qt * 128:(qt + 1) * 128, :], in_=bfa[:, 0:D]))
        pb.wait_dma("sync", "gpsimd", tag="s")
        for c in range(NC_D):
            pb.dma("sync", tag="t", fn=lambda h, c=c: h.dma_start(
                out=xqT[:, c * NOWN:(c + 1) * NOWN],
                in_=xq_dram[:, c * 128:(c + 1) * 128], transpose=True))
        pb.wait_dma("tensor", "sync", tag="t")

        for s in range(16):
            pb.barrier(["tensor"], ["gpsimd"])
            for cc in range(NC_D):
                pb.dma("gpsimd", tag="w", fn=lambda h, s=s, cc=cc: h.dma_start(
                    out=wgs[:, cc * 1024:(cc + 1) * 1024], in_=wgu_d[s, cc]))
            pb.wait_dma("tensor", "gpsimd", tag="w")
            for qt in range(8):
                for cc in range(NC_D):
                    pb.emit("tensor", lambda h, s=s, qt=qt, cc=cc: nc.tensor.matmul(
                        out=pg[:, 0:512],
                        lhsT=xqT[:, cc * NOWN + qt * 128: cc * NOWN + (qt + 1) * 128],
                        rhs=wgs[:, cc * 1024:cc * 1024 + 512],
                        start=(cc == 0), stop=(cc == NC_D - 1)))
                for cc in range(NC_D):
                    pb.emit("tensor", lambda h, s=s, qt=qt, cc=cc: nc.tensor.matmul(
                        out=pg[:, 512:1024],
                        lhsT=xqT[:, cc * NOWN + qt * 128: cc * NOWN + (qt + 1) * 128],
                        rhs=wgs[:, cc * 1024 + 512:cc * 1024 + 1024],
                        start=(cc == 0), stop=(cc == NC_D - 1)))
                pb.barrier(["tensor"], ["vector"])
                pb.barrier(["vector"], ["vector"])
                pb.emit("vector", lambda h, qt=qt: h.tensor_scalar_mul(
                    out=f32a[:, 0:512], in0=pg[:, 0:512], scalar1=cols[:, 8 + qt:9 + qt]))
                pb.emit("vector", lambda h, qt=qt: h.tensor_scalar_mul(
                    out=f32a[:, 512:1024], in0=pg[:, 512:1024], scalar1=cols[:, 8 + qt:9 + qt]))
                pb.barrier(["vector"], ["scalar", "tensor"])
                pb.emit("scalar", lambda h: h.activation(
                    out=f32b[:, 0:512], in_=f32a[:, 0:512], func=AF.Silu))
                pb.barrier(["scalar"], ["vector"])
                pb.wait_dma("vector", "gpsimd", tag="s")
                pb.emit("vector", lambda h: h.tensor_tensor(
                    out=bfa[:, 0:512], in0=f32b[:, 0:512], in1=f32a[:, 512:1024],
                    op=ALU.mult))
                pb.barrier(["vector"], ["vector"])
                pb.emit("vector", lambda h, s=s, qt=qt: h.reduce_max(
                    out=cols[:, 24:25], in_=bfa[:, 0:512], axis=AX,
                    apply_absolute_value=True))
                pb.barrier(["vector"], ["vector"])
                if s == 0:
                    pb.emit("vector", lambda h, qt=qt: h.tensor_copy(
                        out=cols[:, qt:qt + 1], in_=cols[:, 24:25]))
                else:
                    pb.emit("vector", lambda h, qt=qt: h.tensor_tensor(
                        out=cols[:, qt:qt + 1], in0=cols[:, qt:qt + 1],
                        in1=cols[:, 24:25], op=ALU.max))
                pb.barrier(["vector"], ["gpsimd"])
                pb.dma("gpsimd", tag="s", fn=lambda h, s=s, qt=qt: h.dma_start(
                    out=hq_dram[qt * 128:(qt + 1) * 128, s * 512:(s + 1) * 512],
                    in_=bfa[:, 0:512]))

        pb.barrier(["vector"], ["vector"])
        pb.emit("vector", lambda h: h.tensor_scalar_max(out=cols[:, 0:8], in0=cols[:, 0:8],
                                                        scalar1=1e-5))
        pb.barrier(["vector"], ["vector"])
        pb.emit("vector", lambda h: h.tensor_scalar(
            out=cols[:, 8:16], in0=cols[:, 0:8], scalar1=sca[:, 1:2], scalar2=1.0 / 127,
            op0=ALU.mult, op1=ALU.mult))
    
        pb.emit("vector", lambda h: h.reciprocal(out=cols[:, 16:24], in_=cols[:, 0:8]))
        pb.barrier(["vector"], ["vector"])
        pb.emit("vector", lambda h: h.tensor_scalar_mul(out=cols[:, 16:24], in0=cols[:, 16:24],
                                                        scalar1=127.0))
        pb.barrier(["vector"], ["vector"])
        for qt in range(8):
            for hf in range(4):
                pb.wait_dma("gpsimd", "gpsimd", tag="s")
                pb.barrier(["vector"], ["gpsimd"])
                pb.dma("gpsimd", tag="a", fn=lambda h, qt=qt, hf=hf: h.dma_start(
                    out=bfa[:, 0:2048],
                    in_=hq_dram[qt * 128:(qt + 1) * 128, hf * 2048:(hf + 1) * 2048]))
                pb.wait_dma("vector", "gpsimd", tag="a")
                pb.emit("vector", lambda h, qt=qt: h.tensor_scalar_mul(
                    out=f32a[:, 0:2048], in0=bfa[:, 0:2048], scalar1=cols[:, 16 + qt:17 + qt]))
                pb.barrier(["vector"], ["vector"])
                iv = f32b.bitcast(I32)
                pb.emit("vector", lambda h, iv=iv: h.tensor_copy(out=iv, in_=f32a[:, 0:2048]))
                pb.barrier(["vector"], ["vector"])
                pb.emit("vector", lambda h, iv=iv: h.tensor_copy(out=bfa[:, 0:2048], in_=iv))
                pb.barrier(["vector"], ["gpsimd"])
                pb.dma("gpsimd", tag="s", fn=lambda h, qt=qt, hf=hf: h.dma_start(
                    out=hq_dram[qt * 128:(qt + 1) * 128, hf * 2048:(hf + 1) * 2048],
                    in_=bfa[:, 0:2048]))
        pb.wait_dma("sync", "gpsimd", tag="s")

        for qtp in range(4):
            for ic in range(NIC):
                pb.barrier(["tensor"], ["sync", "gpsimd"])
                pb.dma("gpsimd", tag="w", fn=lambda h, ic=ic: h.dma_start(
                    out=wdc[:, :], in_=wd_d[ic * 128:(ic + 1) * 128, :]))
                pb.dma("sync", tag="t", fn=lambda h, ic=ic: h.dma_start(
                    out=hqt[:, 0:NOWN], in_=hq_dram[:, ic * 128:(ic + 1) * 128],
                    transpose=True))
                pb.wait_dma("tensor", "gpsimd", tag="w")
                pb.wait_dma("tensor", "sync", tag="t")
                for qt2 in range(2):
                    p = pd if qt2 == 0 else pd2
                    qt = qtp * 2 + qt2
                    for os in range(4):
                        pb.emit("tensor", lambda h, ic=ic, qt=qt, os=os, p=p: nc.tensor.matmul(
                            out=p[:, os * 512:(os + 1) * 512],
                            lhsT=hqt[:, qt * 128:(qt + 1) * 128],
                            rhs=wdc[:, os * 512:(os + 1) * 512],
                            start=(ic == 0), stop=(ic == NIC - 1)))
            for qt2 in range(2):
                p = pd if qt2 == 0 else pd2
                qt = qtp * 2 + qt2
                pb.barrier(["tensor"], ["vector"])
                pb.barrier(["vector"], ["vector"])
                pb.emit("vector", lambda h, qt=qt, p=p: h.scalar_tensor_tensor(
                    out=f32a[:, 0:D], in0=p[:, 0:D], scalar=cols[:, 8 + qt:9 + qt],
                    in1=x3[:, qt * D:(qt + 1) * D], op0=ALU.mult, op1=ALU.add))
                _emit_ln2(pb, f32a, 0, f32b, cols, qt, out_d)
        pb.wait_dma("gpsimd", "gpsimd", tag="s")
        pb.finalize(block)
    return nc



def build_prepA():
    nc = bass.Bass("TRN2", target_bir_lowering=False)
    wgu_d = nc.dram_tensor("wgu_sl", [D, D], F32, kind="ExternalInput").ap()
    wd_d = nc.dram_tensor("wd_sl", [1024, D], F32, kind="ExternalInput").ap()
    out_d = nc.dram_tensor("partial", [1, 2], F32, kind="ExternalOutput").ap()

    with ExitStack() as ctx:
        e = ctx.enter_context
        wt = e(nc.sbuf_tensor("wt", [128, 2 * D], F32))
        acc = e(nc.sbuf_tensor("acc", [128, 4], F32))
        onesc = e(nc.sbuf_tensor("onescp", [128, 1], F32))
        res = e(nc.sbuf_tensor("res", [1, 2], F32))
        ps = e(nc.psum_tensor("psp", [1, 2], F32))
        block = e(nc.Block())
        pb = PB(nc, ctx)

        pb.emit("vector", lambda h: h.memset(acc[:, 0:4], 0.0))
        pb.emit("vector", lambda h: h.memset(onesc[:], 1.0))
        pb.barrier(["vector"], ["vector"])
        for j, (src, nt, col) in enumerate(((wgu_d, 16, 0), (wd_d, 8, 1))):
            for t in range(nt):
                buf = (t % 2) * D
                pb.wait_dma("vector", "sync", tag="a")
                pb.barrier(["vector"], ["sync"])
                pb.dma("sync", tag="a", fn=lambda h, t=t, src=src, buf=buf: h.dma_start(
                    out=wt[:, buf:buf + D], in_=src[t * 128:(t + 1) * 128, :]))
                pb.wait_dma("vector", "sync", tag="a")
                pb.emit("vector", lambda h, buf=buf: h.reduce_sum(
                    out=acc[:, 2:3], in_=wt[:, buf:buf + D], axis=AX,
                    apply_absolute_value=True))
                pb.barrier(["vector"], ["vector"])
                pb.emit("vector", lambda h, col=col: h.tensor_tensor(
                    out=acc[:, col:col + 1], in0=acc[:, col:col + 1],
                    in1=acc[:, 2:3], op=ALU.add))
                pb.barrier(["vector"], ["vector"])
        pb.barrier(["vector"], ["tensor"])
        for j in range(2):
            pb.emit("tensor", lambda h, j=j: nc.tensor.matmul(
                out=ps[0:1, j:j + 1], lhsT=acc[:, j:j + 1], rhs=onesc[:, 0:1],
                start=True, stop=True))
        pb.barrier(["tensor"], ["vector"])
        pb.emit("vector", lambda h: h.tensor_copy(out=res[:], in_=ps[0:1, 0:2]))
        pb.barrier(["vector"], ["gpsimd"])
        pb.dma("gpsimd", tag="s", fn=lambda h: h.dma_start(out=out_d, in_=res[:]))
        pb.wait_dma("gpsimd", "gpsimd", tag="s")
        pb.finalize(block)
    return nc



def build_prepB():
    nc = bass.Bass("TRN2", target_bir_lowering=False)
    parts_d = nc.dram_tensor("parts", [8, 2], F32, kind="ExternalInput").ap()
    wgu1_d = nc.dram_tensor("wgu1_sl", [D, D], F32, kind="ExternalInput").ap()
    wd1_d = nc.dram_tensor("wd1_sl", [1024, D], F32, kind="ExternalInput").ap()
    wgu1q_o = nc.dram_tensor("wgu1q_sl", [D, D], BF16, kind="ExternalOutput").ap()
    wd1q_o = nc.dram_tensor("wd1q_sl", [1024, D], BF16, kind="ExternalOutput").ap()
    sgu_o = nc.dram_tensor("sgu", [1, 1], F32, kind="ExternalOutput").ap()
    swd_o = nc.dram_tensor("swd", [1, 1], F32, kind="ExternalOutput").ap()
    pgu_d = nc.dram_tensor("pguT_sl", [256, 1408], I32, kind="ExternalInput").ap()
    sgu0_d = nc.dram_tensor("sguT_sl", [16, 1408], F16, kind="ExternalInput").ap()
    pdn_d = nc.dram_tensor("pdnT_sl", [704, 256], I32, kind="ExternalInput").ap()
    sdn_d = nc.dram_tensor("sdnT_sl", [44, 256], F16, kind="ExternalInput").ap()
    exp8_d = nc.dram_tensor("exp8", [8, 128], BF16, kind="ExternalInput").ap()
    wgu0_o = nc.dram_tensor("wgu0_sl", [D, 1408], BF16, kind="ExternalOutput").ap()
    wdn0_o = nc.dram_tensor("wdn0_sl", [I0, 256], BF16, kind="ExternalOutput").ap()
    exp4_d = nc.dram_tensor("exp4", [4, 128], BF16, kind="ExternalInput").ap()
    cq_d = [nc.dram_tensor(f"codes{i}", [256, D], BF16, kind="ExternalInput").ap()
            for i in range(4)]
    sq_d = [nc.dram_tensor(f"mxs{i}", [8, D], F32, kind="ExternalInput").ap()
            for i in range(4)]
    dq_o = [nc.dram_tensor(f"deq{i}", [256, D], BF16, kind="ExternalOutput").ap()
            for i in range(4)]

    with ExitStack() as ctx:
        e = ctx.enter_context
        wt = e(nc.sbuf_tensor("wtp", [128, D], F32))
        f32s = e(nc.sbuf_tensor("f32sp", [128, D], F32))
        bfo = e(nc.sbuf_tensor("bfop", [128, 2048], BF16))
        pk = e(nc.sbuf_tensor("pkp", [128, 1408], I32))
        nib = e(nc.sbuf_tensor("nibp", [128, 1408], I32))
        nbf = e(nc.sbuf_tensor("nbfp", [128, 1408], BF16))
        d1 = e(nc.sbuf_tensor("d1p", [128, 1408], BF16))
        d2 = e(nc.sbuf_tensor("d2p", [128, 1408], BF16))
        d3 = e(nc.sbuf_tensor("d3p", [128, 1408], BF16))
        srep = e(nc.sbuf_tensor("srepp", [128, 1408], BF16))
        srow = e(nc.sbuf_tensor("srowp", [16, 2048], BF16))
        expt = e(nc.sbuf_tensor("exptp", [8, 128], BF16))
        prt = e(nc.sbuf_tensor("prtp", [8, 2], F32))
        onesc = e(nc.sbuf_tensor("onescb", [8, 1], F32))
        ones1 = e(nc.sbuf_tensor("ones1b", [1, 128], F32))
        sc2 = e(nc.sbuf_tensor("sc2", [1, 4], F32))
        scc = e(nc.sbuf_tensor("sccb", [128, 2], F32))
        pps = e(nc.psum_tensor("ppsb", [128, 2048], F32))
        block = e(nc.Block())
        pb = PB(nc, ctx)

        pb.dma("sync", tag="a", fn=lambda h: h.dma_start(out=prt[:], in_=parts_d))
        pb.emit("vector", lambda h: h.memset(onesc[:], 1.0))
        pb.emit("vector", lambda h: h.memset(ones1[:], 1.0))
        pb.wait_dma("tensor", "sync", tag="a")
        pb.barrier(["vector"], ["tensor"])
        for j in range(2):
            pb.emit("tensor", lambda h, j=j: nc.tensor.matmul(
                out=pps[0:1, j:j + 1], lhsT=prt[:, j:j + 1], rhs=onesc[:, 0:1],
                start=True, stop=True))
        pb.barrier(["tensor"], ["vector"])
        pb.emit("vector", lambda h: h.tensor_scalar(
            out=sc2[0:1, 0:1], in0=pps[0:1, 0:1], scalar1=1.0 / (2 * F1 * D), scalar2=1e-5,
            op0=ALU.mult, op1=ALU.max))
        pb.emit("vector", lambda h: h.tensor_scalar(
            out=sc2[0:1, 1:2], in0=pps[0:1, 1:2], scalar1=1.0 / (D * F1), scalar2=1e-5,
            op0=ALU.mult, op1=ALU.max))
        pb.barrier(["vector"], ["vector", "gpsimd"])
        pb.emit("vector", lambda h: h.reciprocal(out=sc2[0:1, 2:4], in_=sc2[0:1, 0:2]))
        pb.barrier(["vector"], ["tensor", "gpsimd"])
        pb.dma("gpsimd", tag="s", fn=lambda h: h.dma_start(out=sgu_o, in_=sc2[0:1, 0:1]))
        pb.dma("gpsimd", tag="s", fn=lambda h: h.dma_start(out=swd_o, in_=sc2[0:1, 1:2]))
        pb.emit("tensor", lambda h: nc.tensor.matmul(
            out=pps[:, 4:6], lhsT=ones1[0:1, :], rhs=sc2[0:1, 2:4], start=True, stop=True))
        pb.barrier(["tensor"], ["vector"])
        pb.emit("vector", lambda h: h.tensor_copy(out=scc[:, 0:2], in_=pps[:, 4:6]))
        pb.barrier(["vector"], ["scalar", "tensor"])

        def quant_tiles(src, dst, nt, col):
            for t in range(nt):
                pb.wait_dma("scalar", "sync", tag="a")
                pb.barrier(["vector"], ["sync"])
                pb.wait_dma("sync", "gpsimd", tag="s")
                pb.dma("sync", tag="a", fn=lambda h, t=t, src=src: h.dma_start(
                    out=wt[:, :], in_=src[t * 128:(t + 1) * 128, :]))
                pb.wait_dma("scalar", "sync", tag="a")
                pb.emit("scalar", lambda h, col=col: h.activation(
                    out=f32s[:, 0:D], in_=wt[:, 0:D], func=AF.Identity,
                    scale=scc[:, col:col + 1]))
                pb.barrier(["scalar"], ["vector"])
                pb.emit("vector", lambda h: h.tensor_scalar(
                    out=f32s[:, 0:D], in0=f32s[:, 0:D], scalar1=1.0, scalar2=-1.0,
                    op0=ALU.min, op1=ALU.max))
                pb.barrier(["vector"], ["vector"])
                iv = wt.ap().bitcast(I32)
                pb.emit("vector", lambda h, iv=iv: h.tensor_copy(out=iv, in_=f32s[:, 0:D]))
                pb.barrier(["vector"], ["vector"])
                pb.emit("vector", lambda h, iv=iv: h.tensor_copy(out=bfo[:, 0:D], in_=iv))
                pb.barrier(["vector"], ["gpsimd", "scalar"])
                pb.dma("gpsimd", tag="s", fn=lambda h, t=t, dst=dst: h.dma_start(
                    out=dst[t * 128:(t + 1) * 128, :], in_=bfo[:, 0:D]))
                pb.wait_dma("vector", "gpsimd", tag="s")
                pb.barrier(["vector"], ["vector"])

        quant_tiles(wgu1_d, wgu1q_o, 16, 0)
        quant_tiles(wd1_d, wd1q_o, 8, 1)

        def decode(packed_d, scales_d, out_view, n_pchunks, owid, ngrp_tot, last_rows):
            pb.wait_dma("vector", "sync", tag="a")
            pb.barrier(["vector"], ["sync"])
            pb.dma("sync", tag="a", fn=lambda h: h.dma_start(out=expt[:], in_=exp8_d))
            pb.wait_dma("tensor", "sync", tag="a")
            for ch in range(n_pchunks):
                rows = last_rows if ch == n_pchunks - 1 else 128
                ngrp = (rows + 15) // 16
                pb.barrier(["tensor"], ["gpsimd"])
                pb.dma("gpsimd", tag="a", fn=lambda h, ch=ch, ngrp=ngrp: h.dma_start(
                    out=srow[0:ngrp, 0:owid],
                    in_=scales_d[8 * ch:8 * ch + ngrp, 0:owid]))
                pb.wait_dma("tensor", "gpsimd", tag="a")
                g0 = 0
                pb.barrier(["vector"], ["tensor"])
                nmm = (owid + 511) // 512
                for mi in range(nmm):
                    wid = min(512, owid - mi * 512)
                    pb.emit("tensor", lambda h, mi=mi, wid=wid, g0=g0, ngrp=ngrp, rows=rows:
                            nc.tensor.matmul(
                                out=pps[0:rows, mi * 512:mi * 512 + wid],
                                lhsT=expt[0:ngrp, 0:rows],
                                rhs=srow[g0:g0 + ngrp, mi * 512:mi * 512 + wid],
                                start=True, stop=True))
                pb.barrier(["tensor"], ["scalar"])
                pb.emit("scalar", lambda h, owid=owid, rows=rows: h.activation(
                    out=srep[0:rows, 0:owid], in_=pps[0:rows, 0:owid],
                    func=AF.Identity, scale=0.5))
                pb.barrier(["scalar"], ["sync", "vector"])
                pb.dma("sync", tag="a", fn=lambda h, ch=ch, rows=rows, owid=owid:
                       h.dma_start(out=pk[0:rows, 0:owid],
                                   in_=packed_d[ch * 128:ch * 128 + rows, 0:owid]))
                pb.wait_dma("vector", "sync", tag="a")
                for k in range(8):
                    V = "vector"
                    pb.wait_dma(V, "gpsimd", tag="s")
                    pb.barrier([V], [V])
                    pb.emit(V, lambda h, k=k, rows=rows, owid=owid: h.tensor_scalar(
                        out=nib[0:rows, 0:owid], in0=pk[0:rows, 0:owid],
                        scalar1=4 * k, scalar2=15,
                        op0=ALU.logical_shift_right, op1=ALU.bitwise_and))
                    pb.barrier([V], [V])
                    pb.emit(V, lambda h, rows=rows, owid=owid: h.tensor_copy(
                        out=nbf[0:rows, 0:owid], in_=nib[0:rows, 0:owid]))
                    pb.barrier([V], [V])
                    pb.emit(V, lambda h, rows=rows, owid=owid: h.tensor_scalar(
                        out=d1[0:rows, 0:owid], in0=nbf[0:rows, 0:owid],
                        scalar1=8.0, scalar2=-2.0, op0=ALU.is_ge, op1=ALU.mult))
                    pb.barrier([V], [V])
                    pb.emit(V, lambda h, rows=rows, owid=owid: h.scalar_tensor_tensor(
                        out=d2[0:rows, 0:owid], in0=d1[0:rows, 0:owid], scalar=4.0,
                        in1=nbf[0:rows, 0:owid], op0=ALU.mult, op1=ALU.add))
                    pb.barrier([V], [V])
                    pb.emit(V, lambda h, rows=rows, owid=owid: h.tensor_scalar(
                        out=d3[0:rows, 0:owid], in0=d2[0:rows, 0:owid],
                        scalar1=2.0, scalar2=-4.0, op0=ALU.mult, op1=ALU.add))
                    pb.barrier([V], [V])
                    pb.emit(V, lambda h, rows=rows, owid=owid: h.tensor_tensor(
                        out=d3[0:rows, 0:owid], in0=d2[0:rows, 0:owid],
                        in1=d3[0:rows, 0:owid], op=ALU.max))
                    pb.barrier([V], [V])
                    pb.emit(V, lambda h, rows=rows, owid=owid: h.tensor_scalar(
                        out=d2[0:rows, 0:owid], in0=d2[0:rows, 0:owid],
                        scalar1=4.0, scalar2=-16.0, op0=ALU.mult, op1=ALU.add))
                    pb.barrier([V], [V])
                    pb.emit(V, lambda h, rows=rows, owid=owid: h.tensor_tensor(
                        out=d3[0:rows, 0:owid], in0=d3[0:rows, 0:owid],
                        in1=d2[0:rows, 0:owid], op=ALU.max))
                    pb.barrier([V], [V])
                    pb.emit(V, lambda h, rows=rows, owid=owid: h.scalar_tensor_tensor(
                        out=d1[0:rows, 0:owid], in0=d1[0:rows, 0:owid], scalar=1.0,
                        in1=d3[0:rows, 0:owid], op0=ALU.add, op1=ALU.mult))
                    pb.barrier([V], [V])
                    pb.emit(V, lambda h, rows=rows, owid=owid: h.tensor_tensor(
                        out=d1[0:rows, 0:owid], in0=d1[0:rows, 0:owid],
                        in1=srep[0:rows, 0:owid], op=ALU.mult))
                    pb.barrier([V], ["gpsimd"])
                    pb.dma("gpsimd", tag="s", fn=lambda h, ch=ch, k=k, rows=rows, owid=owid:
                           h.dma_start(out=out_view[k, ch * 128:ch * 128 + rows, 0:owid],
                                       in_=d1[0:rows, 0:owid]))

        wgu0_view = wgu0_o.rearrange("(cp k) o -> k cp o", k=8)
        decode(pgu_d, sgu0_d, wgu0_view, 2, 1408, 16, 128)
        wdn0_view = wdn0_o.rearrange("(cp k) o -> k cp o", k=8)
        decode(pdn_d, sdn_d, wdn0_view, 6, 256, 44, 64)

        pb.barrier(["tensor"], ["sync"])
        pb.dma("sync", tag="a", fn=lambda h: h.dma_start(out=expt[0:4, :], in_=exp4_d))
        for i in range(4):
            for ch in range(2):
                pb.wait_dma("vector", "sync", tag="a")
                pb.barrier(["vector", "tensor"], ["sync", "gpsimd"])
                pb.dma("gpsimd", tag="a", fn=lambda h, i=i, ch=ch: h.dma_start(
                    out=bfo[:, 0:D], in_=cq_d[i][ch * 128:(ch + 1) * 128, :]))
                pb.dma("sync", tag="a", fn=lambda h, i=i, ch=ch: h.dma_start(
                    out=wt[0:4, 0:D], in_=sq_d[i][4 * ch:4 * ch + 4, :]))
                pb.wait_dma("vector", "sync", tag="a")
                pb.emit("vector", lambda h: h.tensor_copy(out=srow[0:4, 0:D], in_=wt[0:4, 0:D]))
                pb.barrier(["vector"], ["tensor"])
                pb.wait_dma("tensor", "sync", tag="a")
                for mi in range(4):
                    pb.emit("tensor", lambda h, mi=mi: nc.tensor.matmul(
                        out=pps[:, mi * 512:(mi + 1) * 512], lhsT=expt[0:4, :],
                        rhs=srow[0:4, mi * 512:(mi + 1) * 512], start=True, stop=True))
                pb.barrier(["tensor"], ["vector"])
                pb.wait_dma("vector", "gpsimd", tag="a")
                pb.wait_dma("vector", "gpsimd", tag="s")
                pb.emit("vector", lambda h: h.tensor_tensor(
                    out=bfo[:, 0:D], in0=pps[:, 0:D], in1=bfo[:, 0:D], op=ALU.mult))
                pb.barrier(["vector"], ["gpsimd"])
                pb.dma("gpsimd", tag="s", fn=lambda h, i=i, ch=ch: h.dma_start(
                    out=dq_o[i][ch * 128:(ch + 1) * 128, :], in_=bfo[:, 0:D]))
        pb.wait_dma("gpsimd", "gpsimd", tag="s")
        pb.wait_dma("sync", "gpsimd", tag="s")
        pb.finalize(block)
    return nc



def build_qkv0():
    nc = bass.Bass("TRN2", target_bir_lowering=False)
    xT_d = nc.dram_tensor("xT", [D, NOWN], F32, kind="ExternalInput").ap()
    wqa_d = nc.dram_tensor("wqa", [D, Q_LORA], F32, kind="ExternalInput").ap()
    bqa_d = nc.dram_tensor("bqa", [128, 4], F32, kind="ExternalInput").ap()
    wqb_d = nc.dram_tensor("wqb_p", [Q_LORA, 4096], F32, kind="ExternalInput").ap()
    bqbn_d = nc.dram_tensor("bqbn", [128, 16], F32, kind="ExternalInput").ap()
    bqbr_d = nc.dram_tensor("bqbr", [64, 32], F32, kind="ExternalInput").ap()
    wkva_d = nc.dram_tensor("wkva_p", [D, 640], F32, kind="ExternalInput").ap()
    bkva_d = nc.dram_tensor("bkva", [128, 6], F32, kind="ExternalInput").ap()
    cos_d = nc.dram_tensor("cosT", [32, NOWN], F32, kind="ExternalInput").ap()
    sin_d = nc.dram_tensor("sinT", [32, NOWN], F32, kind="ExternalInput").ap()
    exp32_d = nc.dram_tensor("exp32", [32, 128], BF16, kind="ExternalInput").ap()
    qnT_o = nc.dram_tensor("qnT", [NH * NOPE, NOWN], BF16, kind="ExternalOutput").ap()
    qrT_o = nc.dram_tensor("qrT", [NH * ROPE, NOWN], BF16, kind="ExternalOutput").ap()
    ckvT_o = nc.dram_tensor("ckvT", [KV_LORA + ROPE, NOWN], BF16, kind="ExternalOutput").ap()

    with ExitStack() as ctx:
        e = ctx.enter_context
        xT = e(nc.sbuf_tensor("xTq", [128, NC_D * NOWN], BF16))
        wqa = e(nc.sbuf_tensor("wqaq", [128, NC_D * Q_LORA], BF16))
        wqb = e(nc.sbuf_tensor("wqbq", [128, 4 * 4096], BF16))
        wkva = e(nc.sbuf_tensor("wkvaq", [128, NC_D * 640], BF16))
        qlat = e(nc.sbuf_tensor("qlatq", [128, 4 * NOWN], BF16))
        ckvs = e(nc.sbuf_tensor("ckvsq", [128, 5 * NOWN], BF16))
        qn = e(nc.sbuf_tensor("qnq", [128, NH * NOWN], BF16))
        qrp = e(nc.sbuf_tensor("qrpq", [64, NOWN], BF16))
        qrsw = e(nc.sbuf_tensor("qrswq", [64, NOWN], BF16))
        kpesw = e(nc.sbuf_tensor("kpeswq", [64, NOWN], BF16))
        qro = e(nc.sbuf_tensor("qroq", [64, NOWN], BF16))
        cs2 = e(nc.sbuf_tensor("cs2q", [64, 2 * NOWN], BF16))
        csr = e(nc.sbuf_tensor("csrq", [32, 2 * NOWN], BF16))
        bqa = e(nc.sbuf_tensor("bqaq", [128, 4], F32))
        bqbn = e(nc.sbuf_tensor("bqbnq", [128, 16], F32))
        bqbr = e(nc.sbuf_tensor("bqbrq", [64, 32], F32))
        bkva = e(nc.sbuf_tensor("bkvaq", [128, 6], F32))
        exp32 = e(nc.sbuf_tensor("exp32q", [32, 128], BF16))
        psA = e(nc.psum_tensor("psAq", [128, 1024], F32))
        psB = e(nc.psum_tensor("psBq", [128, 1024], F32))
        block = e(nc.Block())
        pb = PB(nc, ctx)

        for c in range(NC_D):
            pb.dma("gpsimd", tag="a", fn=lambda h, c=c: h.dma_start(
                out=xT[:, c * NOWN:(c + 1) * NOWN], in_=xT_d[c * 128:(c + 1) * 128, :]))
        for c in range(NC_D):
            pb.dma("gpsimd", tag="w", fn=lambda h, c=c: h.dma_start(
                out=wqa[:, c * Q_LORA:(c + 1) * Q_LORA], in_=wqa_d[c * 128:(c + 1) * 128, :]))
        for c in range(4):
            pb.dma("gpsimd", tag="w", fn=lambda h, c=c: h.dma_start(
                out=wqb[:, c * 4096:(c + 1) * 4096], in_=wqb_d[c * 128:(c + 1) * 128, :]))
        for c in range(NC_D):
            pb.dma("gpsimd", tag="w", fn=lambda h, c=c: h.dma_start(
                out=wkva[:, c * 640:(c + 1) * 640], in_=wkva_d[c * 128:(c + 1) * 128, :]))
        pb.dma("gpsimd", tag="w", fn=lambda h: h.dma_start(out=csr[:, 0:NOWN], in_=cos_d))
        pb.dma("gpsimd", tag="w", fn=lambda h: h.dma_start(out=csr[:, NOWN:2 * NOWN], in_=sin_d))
        pb.dma("sync", tag="a", fn=lambda h: h.dma_start(out=bqa[:], in_=bqa_d))
        pb.dma("sync", tag="a", fn=lambda h: h.dma_start(out=bqbn[:], in_=bqbn_d))
        pb.dma("sync", tag="a", fn=lambda h: h.dma_start(out=bqbr[:], in_=bqbr_d))
        pb.dma("sync", tag="a", fn=lambda h: h.dma_start(out=bkva[:], in_=bkva_d))
        pb.dma("sync", tag="a", fn=lambda h: h.dma_start(out=exp32[:], in_=exp32_d))
        pb.wait_dma("tensor", "gpsimd", tag="w")
        pb.wait_dma("tensor", "gpsimd", tag="a")
        pb.wait_dma("tensor", "sync", tag="a")
        pb.wait_dma("scalar", "sync", tag="a")

        for j in range(2):
            for qh in range(2):
                pb.emit("tensor", lambda h, j=j, qh=qh: nc.tensor.matmul(
                    out=psB[0:64, qh * 512:qh * 512 + 512],
                    lhsT=exp32[:, j * 64:(j + 1) * 64],
                    rhs=csr[:, j * NOWN + qh * 512: j * NOWN + qh * 512 + 512],
                    start=True, stop=True))
            pb.barrier(["tensor"], ["vector"])
            pb.emit("vector", lambda h, j=j: h.tensor_copy(
                out=cs2[0:64, j * NOWN:(j + 1) * NOWN], in_=psB[0:64, 0:1024]))
            pb.barrier(["vector"], ["tensor"])

        for t in range(4):
            for qh in range(2):
                p = psA[:, (t * 2 + qh) % 2 * 512:((t * 2 + qh) % 2 * 512) + 512]
                for c in range(NC_D):
                    pb.emit("tensor", lambda h, t=t, qh=qh, c=c, p=p: nc.tensor.matmul(
                        out=p, lhsT=wqa[:, c * Q_LORA + t * 128: c * Q_LORA + (t + 1) * 128],
                        rhs=xT[:, c * NOWN + qh * 512: c * NOWN + qh * 512 + 512],
                        start=(c == 0), stop=(c == NC_D - 1)))
                pb.barrier(["tensor"], ["scalar"])
                pb.emit("scalar", lambda h, t=t, qh=qh, p=p: h.activation(
                    out=qlat[:, t * NOWN + qh * 512: t * NOWN + qh * 512 + 512],
                    in_=p, func=AF.Identity, bias=bqa[:, t:t + 1], scale=1.0))
                pb.barrier(["scalar"], ["tensor"])

        for t in range(6):
            rows = 128 if t < 4 else 64
            bcol = bkva[:, t:t + 1] if t < 4 else bkva[0:64, t:t + 1]
            for qh in range(2):
                p = psA[:, (qh % 2) * 512:(qh % 2) * 512 + 512]
                for c in range(NC_D):
                    pb.emit("tensor", lambda h, t=t, qh=qh, c=c, p=p, rows=rows: nc.tensor.matmul(
                        out=p[0:rows, :],
                        lhsT=wkva[:, c * 640 + t * 128: c * 640 + t * 128 + rows] if t < 5
                        else wkva[:, c * 640 + 576: c * 640 + 640],
                        rhs=xT[:, c * NOWN + qh * 512: c * NOWN + qh * 512 + 512],
                        start=(c == 0), stop=(c == NC_D - 1)))
                pb.barrier(["tensor"], ["scalar"])
                if t < 5:
                    pb.emit("scalar", lambda h, t=t, qh=qh, p=p, rows=rows, bcol=bcol: h.activation(
                        out=ckvs[0:rows, t * NOWN + qh * 512: t * NOWN + qh * 512 + 512],
                        in_=p[0:rows, :], func=AF.Identity, bias=bcol, scale=1.0))
                else:
                    pb.emit("scalar", lambda h, qh=qh, p=p, bcol=bcol: h.activation(
                        out=kpesw[0:64, qh * 512:qh * 512 + 512],
                        in_=p[0:64, :], func=AF.Identity, bias=bcol, scale=1.0))
                pb.barrier(["scalar"], ["tensor"])

        for t in range(NH):
            for qh in range(2):
                p = psA[:, ((t * 2 + qh) % 2) * 512:(((t * 2 + qh) % 2) * 512) + 512]
                for c in range(4):
                    pb.emit("tensor", lambda h, t=t, qh=qh, c=c, p=p: nc.tensor.matmul(
                        out=p, lhsT=wqb[:, c * 4096 + t * 128: c * 4096 + (t + 1) * 128],
                        rhs=qlat[:, c * NOWN + qh * 512: c * NOWN + qh * 512 + 512],
                        start=(c == 0), stop=(c == 3)))
                pb.barrier(["tensor"], ["scalar"])
                pb.emit("scalar", lambda h, t=t, qh=qh, p=p: h.activation(
                    out=qn[:, t * NOWN + qh * 512: t * NOWN + qh * 512 + 512],
                    in_=p, func=AF.Identity, bias=bqbn[:, t:t + 1], scale=1.0))
                pb.barrier(["scalar"], ["tensor"])
        pb.barrier(["scalar"], ["sync"])
        for c in range(NH):
            pb.dma("sync", tag="s", fn=lambda h, c=c: h.dma_start(
                out=qnT_o[c * 128:(c + 1) * 128, :], in_=qn[:, c * NOWN:(c + 1) * NOWN]))

        for hh in range(NH):
            for qh in range(2):
                for c in range(4):
                    pb.emit("tensor", lambda h, hh=hh, qh=qh, c=c: nc.tensor.matmul(
                        out=psB[0:64, qh * 512:qh * 512 + 512],
                        lhsT=wqb[:, c * 4096 + 2048 + hh * 64: c * 4096 + 2048 + (hh + 1) * 64],
                        rhs=qlat[:, c * NOWN + qh * 512: c * NOWN + qh * 512 + 512],
                        start=(c == 0), stop=(c == 3)))
                for c in range(4):
                    pb.emit("tensor", lambda h, hh=hh, qh=qh, c=c: nc.tensor.matmul(
                        out=psA[0:64, qh * 512:qh * 512 + 512],
                        lhsT=wqb[:, c * 4096 + 3072 + hh * 64: c * 4096 + 3072 + (hh + 1) * 64],
                        rhs=qlat[:, c * NOWN + qh * 512: c * NOWN + qh * 512 + 512],
                        start=(c == 0), stop=(c == 3)))
            pb.barrier(["tensor"], ["scalar"])
            pb.wait_dma("scalar", "gpsimd", tag="s")
            pb.emit("scalar", lambda h, hh=hh: h.activation(
                out=qrp[0:64, 0:NOWN], in_=psB[0:64, 0:NOWN],
                func=AF.Identity, bias=bqbr[0:64, hh:hh + 1], scale=1.0))
            pb.emit("scalar", lambda h, hh=hh: h.activation(
                out=qrsw[0:64, 0:NOWN], in_=psA[0:64, 0:NOWN],
                func=AF.Identity, bias=bqbr[0:64, 16 + hh:17 + hh], scale=1.0))
            pb.barrier(["scalar"], ["tensor", "vector"])
            _emit_rope(pb, qrp, qrsw, qro, cs2, NOWN)
            pb.barrier(["vector"], ["gpsimd"])
            pb.dma("gpsimd", tag="s", fn=lambda h, hh=hh: h.dma_start(
                out=qrT_o[hh * 64:(hh + 1) * 64, :], in_=qro[0:64, 0:NOWN]))

        pb.barrier(["scalar"], ["vector"])
        pb.wait_dma("vector", "gpsimd", tag="s")
        pb.emit("vector", lambda h: h.tensor_copy(out=qrp[0:64, 0:NOWN],
                                                  in_=ckvs[0:64, 4 * NOWN:5 * NOWN]))
        pb.barrier(["vector"], ["vector"])
        _emit_rope(pb, qrp, kpesw, qro, cs2, NOWN)
        pb.barrier(["vector"], ["gpsimd"])
        pb.dma("gpsimd", tag="s", fn=lambda h: h.dma_start(
            out=ckvT_o[512:576, :], in_=qro[0:64, 0:NOWN]))
        for t in range(4):
            pb.dma("gpsimd", tag="s", fn=lambda h, t=t: h.dma_start(
                out=ckvT_o[t * 128:(t + 1) * 128, :], in_=ckvs[:, t * NOWN:(t + 1) * NOWN]))
        pb.wait_dma("gpsimd", "gpsimd", tag="s")
        pb.wait_dma("sync", "sync", tag="s")
        pb.finalize(block)
    return nc


def _emit_rope(pb, pre, presw, out, cs2, n):
    nc = pb.nc
    V = "vector"
    pb.emit(V, lambda h: h.tensor_tensor(out=out[0:64, 0:n], in0=pre[0:64, 0:n],
                                         in1=cs2[0:64, 0:n], op=ALU.mult))
    pb.emit(V, lambda h: h.tensor_tensor(out=presw[0:64, 0:n], in0=presw[0:64, 0:n],
                                         in1=cs2[0:64, n:2 * n], op=ALU.mult))
    pb.barrier([V], [V])
    pb.emit(V, lambda h: h.tensor_tensor(out=out[0:64, 0:n], in0=out[0:64, 0:n],
                                         in1=presw[0:64, 0:n], op=ALU.add))
    pb.barrier([V], [V])



def _rope_tables():
    inv = 1.0 / (1e6 ** (np.arange(0, ROPE, 2, dtype=np.float64) / ROPE))
    t = np.arange(S, dtype=np.float64)
    fr = np.outer(t, inv)
    return np.cos(fr).astype(np.float32), np.sin(fr).astype(np.float32)


def kernel(**inp):
    if USE_FUSED:
        return kernel_fused(inp)
    f32 = np.float32
    x = np.asarray(inp["L0_attn_x"], f32).reshape(NTOK, D)
    xT_full = np.ascontiguousarray(x.T)

    wqa = np.ascontiguousarray(np.asarray(inp["L0_attn_wq_a_w"], f32))
    bqa_c = _bcols(inp["L0_attn_wq_a_b"])
    wqb = np.asarray(inp["L0_attn_wq_b_w"], f32)
    bqb = np.asarray(inp["L0_attn_wq_b_b"], f32)
    perm_n = np.concatenate([np.arange(192 * h, 192 * h + 128) for h in range(NH)])
    perm_r = np.concatenate(
        [np.concatenate([np.arange(192 * h + 128, 192 * h + 192, 2),
                         np.arange(192 * h + 129, 192 * h + 192, 2)]) for h in range(NH)])
    perm_rs = np.concatenate(
        [np.concatenate([np.arange(192 * h + 129, 192 * h + 192, 2),
                         np.arange(192 * h + 128, 192 * h + 192, 2)]) for h in range(NH)])
    perm = np.concatenate([perm_n, perm_r, perm_rs])
    wqb_p = np.ascontiguousarray(wqb[:, perm])
    bqb_p = bqb[perm]
    bqbn_c = _bcols(bqb_p[:2048])
    bqbr_c = np.ascontiguousarray(bqb_p[2048:].reshape(32, 64).T)
    wkva = np.asarray(inp["L0_attn_wkv_a_w"], f32)
    bkva = np.asarray(inp["L0_attn_wkv_a_b"], f32)
    pk_r = np.concatenate([512 + np.arange(0, 64, 2), 512 + np.arange(1, 64, 2)])
    pk_rs = np.concatenate([512 + np.arange(1, 64, 2), 512 + np.arange(0, 64, 2)])
    permk = np.concatenate([np.arange(512), pk_r, pk_rs])
    wkva_p = np.ascontiguousarray(wkva[:, permk])
    bkva_p = bkva[permk]
    bkva_c = np.zeros((128, 6), f32)
    bkva_c[:, 0:4] = bkva_p[:512].reshape(4, 128).T
    bkva_c[0:64, 4] = bkva_p[512:576]
    bkva_c[0:64, 5] = bkva_p[576:640]
    exp32 = np.zeros((32, 128), f32)
    for p in range(64):
        exp32[p % 32, p] = 1.0
        exp32[p % 32, 64 + p] = -1.0 if p < 32 else 1.0
    exp32 = exp32.astype(BF)
    cosf, sinf = _rope_tables()
    cosT, sinT = np.ascontiguousarray(cosf.T), np.ascontiguousarray(sinf.T)

    wkvb = np.ascontiguousarray(np.asarray(inp["L0_attn_wkv_b_w"], f32))
    bkvb = np.asarray(inp["L0_attn_wkv_b_b"], f32)
    bkv_c = _bcols(bkvb)
    bv_row0 = np.ascontiguousarray(bkvb.reshape(NH, 2, 128)[:, 1, :].reshape(1, NH * 128))
    wo0 = np.ascontiguousarray(np.asarray(inp["L0_attn_wo_w"], f32))
    bo0_row = np.ascontiguousarray(np.asarray(inp["L0_attn_wo_b"], f32).reshape(1, D))

    wguT1 = np.ascontiguousarray(np.asarray(inp["L1_ffn_w_gate_up"], f32).T)
    wdT1 = np.ascontiguousarray(np.asarray(inp["L1_ffn_w_down"], f32).T)
    pguT = np.ascontiguousarray(np.asarray(inp["L0_ffn_gate_up_packed"]).T)
    sguT = np.ascontiguousarray(np.asarray(inp["L0_ffn_gate_up_scales"]))
    pdnT = np.ascontiguousarray(np.asarray(inp["L0_ffn_down_packed"]).T)
    sdnT = np.ascontiguousarray(np.asarray(inp["L0_ffn_down_scales"]))
    exp8 = np.zeros((8, 128), f32)
    exp4 = np.zeros((4, 128), f32)
    for p in range(128):
        exp8[p // 16, p] = 1.0
        exp4[p // 32, p] = 1.0
    exp8, exp4 = exp8.astype(BF), exp4.astype(BF)
    codes = [np.asarray(inp[k], f32) for k in
             ("L1_attn_wq_codes", "L1_attn_wk_codes", "L1_attn_wv_codes", "L1_attn_wo_codes")]
    mxs = [np.asarray(inp[k], f32) for k in
           ("L1_attn_wq_scales", "L1_attn_wk_scales", "L1_attn_wv_scales", "L1_attn_wo_scales")]
    bq1_c = _bcols(inp["L1_attn_bq"])
    bk1_c = _bcols(inp["L1_attn_bk"])
    bv1_row = np.ascontiguousarray(np.asarray(inp["L1_attn_bv"], f32).reshape(1, D))
    bo1_row = np.ascontiguousarray(np.asarray(inp["L1_attn_bo"], f32).reshape(1, D))

    CORES = list(range(NCORE))

    nc = build_prepA()
    maps = [{"wgu_sl": np.ascontiguousarray(wguT1[:, c * 2048:(c + 1) * 2048]),
             "wd_sl": np.ascontiguousarray(wdT1[c * 1024:(c + 1) * 1024, :])}
            for c in CORES]
    res = run_bass_kernel_spmd(nc, maps, core_ids=CORES)
    parts = np.concatenate([res.results[c]["partial"] for c in CORES], axis=0)

    nc = build_prepB()
    maps = []
    for c in CORES:
        m = {"parts": parts,
             "wgu1_sl": np.ascontiguousarray(wguT1[:, c * 2048:(c + 1) * 2048]),
             "wd1_sl": np.ascontiguousarray(wdT1[c * 1024:(c + 1) * 1024, :]),
             "pguT_sl": np.ascontiguousarray(pguT[:, c * 1408:(c + 1) * 1408]),
             "sguT_sl": np.ascontiguousarray(sguT[:, c * 1408:(c + 1) * 1408]),
             "pdnT_sl": np.ascontiguousarray(pdnT[:, c * 256:(c + 1) * 256]),
             "sdnT_sl": np.ascontiguousarray(sdnT[:, c * 256:(c + 1) * 256]),
             "exp8": exp8, "exp4": exp4}
        for i in range(4):
            m[f"codes{i}"] = np.ascontiguousarray(codes[i][c * 256:(c + 1) * 256, :]).astype(BF)
            m[f"mxs{i}"] = np.ascontiguousarray(mxs[i][c * 8:(c + 1) * 8, :])
        maps.append(m)
    res = run_bass_kernel_spmd(nc, maps, core_ids=CORES)
    rs = res.results
    sgu = rs[0]["sgu"]
    swd = rs[0]["swd"]
    wgu1q = np.concatenate([rs[c]["wgu1q_sl"] for c in CORES], axis=1)
    wd1q = np.concatenate([rs[c]["wd1q_sl"] for c in CORES], axis=0)
    wgu0 = np.concatenate([rs[c]["wgu0_sl"] for c in CORES], axis=1)
    wdn0 = np.concatenate([rs[c]["wdn0_sl"] for c in CORES], axis=1)
    deq = [np.concatenate([rs[c][f"deq{i}"] for c in CORES], axis=0) for i in range(4)]

    b0 = np.empty((44, NC_D, 128, 256), dtype=BF)
    for sp in range(44):
        b0[:, :, :, 0:128][sp] = wgu0[:, sp * 128:(sp + 1) * 128].reshape(NC_D, 128, 128)
        b0[:, :, :, 128:256][sp] = wgu0[:, I0 + sp * 128: I0 + (sp + 1) * 128].reshape(NC_D, 128, 128)
    b1 = np.empty((16, NC_D, 128, 1024), dtype=BF)
    for sp in range(16):
        b1[:, :, :, 0:512][sp] = wgu1q[:, sp * 512:(sp + 1) * 512].reshape(NC_D, 128, 512)
        b1[:, :, :, 512:1024][sp] = wgu1q[:, F1 + sp * 512: F1 + (sp + 1) * 512].reshape(NC_D, 128, 512)

    nc = build_qkv0()
    maps = [{"xT": np.ascontiguousarray(xT_full[:, c * NOWN:(c + 1) * NOWN]),
             "wqa": wqa, "bqa": bqa_c, "wqb_p": wqb_p, "bqbn": bqbn_c, "bqbr": bqbr_c,
             "wkva_p": wkva_p, "bkva": bkva_c,
             "cosT": np.ascontiguousarray(cosT[:, (c % 2) * NOWN:(c % 2) * NOWN + NOWN]),
             "sinT": np.ascontiguousarray(sinT[:, (c % 2) * NOWN:(c % 2) * NOWN + NOWN]),
             "exp32": exp32} for c in CORES]
    res = run_bass_kernel_spmd(nc, maps, core_ids=CORES)
    q0 = res.results

    nc = build_attn0()
    maps = []
    for c in CORES:
        b = c // 2
        ckv_full = np.concatenate([q0[2 * b]["ckvT"], q0[2 * b + 1]["ckvT"]], axis=1)
        maps.append({"qnT": q0[c]["qnT"], "qrT": q0[c]["qrT"], "ckvT": ckv_full,
                     "wkvb": wkvb, "bkv": bkv_c, "bvrow": bv_row0,
                     "wo": wo0, "bo": bo0_row,
                     "xres": np.ascontiguousarray(x[c * NOWN:(c + 1) * NOWN, :])})
    res = run_bass_kernel_spmd(nc, maps, core_ids=CORES)
    s1 = [res.results[c]["out"] for c in CORES]

    nc = build_ffn0()
    maps = [{"x": s1[c], "xT": np.ascontiguousarray(s1[c].T),
             "wgu": b0, "wd": wdn0} for c in CORES]
    res = run_bass_kernel_spmd(nc, maps, core_ids=CORES)
    s2 = [res.results[c]["out"] for c in CORES]

    nc = build_qkv1()
    maps = [{"xT": np.ascontiguousarray(s2[c].T),
             "wq": deq[0], "wk": deq[1], "wv": deq[2],
             "bq": bq1_c, "bk": bk1_c, "bv": bv1_row} for c in CORES]
    res = run_bass_kernel_spmd(nc, maps, core_ids=CORES)
    q1 = res.results

    nc = build_attn1()
    maps = []
    for c in CORES:
        b = c // 2
        kT_full = np.concatenate([q1[2 * b]["kT"], q1[2 * b + 1]["kT"]], axis=1)
        v_full = np.concatenate([q1[2 * b]["v"], q1[2 * b + 1]["v"]], axis=0)
        vblk = np.ascontiguousarray(
            v_full.reshape(S // 128, 128, NH, V_HD).transpose(2, 1, 0, 3).reshape(NH, 128, S))
        maps.append({"qT": q1[c]["qT"], "kTf": kT_full, "vblk": vblk,
                     "wo": deq[3], "bo": bo1_row, "xres": s2[c]})
    res = run_bass_kernel_spmd(nc, maps, core_ids=CORES)
    s3 = [res.results[c]["out"] for c in CORES]

    nc = build_ffn1()
    maps = [{"x": s3[c], "wgu": b1, "wd": wd1q, "sgu": sgu, "swd": swd}
            for c in CORES]
    res = run_bass_kernel_spmd(nc, maps, core_ids=CORES)
    out = np.concatenate([res.results[c]["out"] for c in CORES], axis=0)
    return out.reshape(B, S, D)



def build_fused():
    nc = bass.Bass("TRN2", target_bir_lowering=False)
    f = {}
    def ein(name, shape, dt):
        f[name] = nc.dram_tensor(name, shape, dt, kind="ExternalInput").ap()
    def itn(name, shape, dt):
        f[name] = nc.dram_tensor(name, shape, dt, kind="Internal")
    ein("x", [NOWN, D], F32)
    ein("xT", [D, NOWN], F32)
    ein("wqa", [D, Q_LORA], F32); ein("bqa", [128, 4], F32)
    ein("wqb_p", [Q_LORA, 4096], F32); ein("bqbn", [128, 16], F32); ein("bqbr", [64, 32], F32)
    ein("wkva_p", [D, 640], F32); ein("bkva", [128, 6], F32)
    ein("cosT", [32, NOWN], F32); ein("sinT", [32, NOWN], F32)
    ein("exp32", [32, 128], BF16)
    ein("wkvb", [KV_LORA, NH * 256], F32); ein("bkv", [128, 32], F32)
    ein("bvrow", [1, NH * V_HD], F32); ein("wo0", [D, D], F32); ein("bo0", [1, D], F32)
    ein("wgu1_sl", [256, 2 * F1], F32)
    ein("wd1_sl", [1024, D], F32)
    ein("pgu_sl", [128, 2816], I32)
    ein("srep_gu", [128, 2816], F16)
    ein("pdn_sl", [88, 2048], I32)
    ein("srep_dn", [88, 2048], F16)
    for i in range(4):
        ein(f"codes{i}", [256, D], F32)
        ein(f"mxrep{i}", [256, D], F32)
    ein("bq1", [128, 16], F32); ein("bk1", [128, 16], F32)
    ein("bv1", [1, D], F32); ein("bo1", [1, D], F32)
    out_d = nc.dram_tensor("out", [NOWN, D], F32, kind="ExternalOutput").ap()
    f["_out_ap"] = out_d

    itn("prt_i", [1, 2], F32); itn("g_prt", [1, 2], F32)
    itn("tgu1_i", [256, 2 * F1], BF16); itn("g_gu1", [D, 2 * F1], BF16)
    itn("twd1_i", [1024, D], BF16); itn("g_wd1", [F1, D], BF16)
    itn("dgu0_i", [256, 2 * I0], BF16); itn("g_gu0", [D, 2 * I0], BF16)
    itn("ddn0_i", [704, D], BF16); itn("g_dn0", [I0, D], BF16)
    for i in range(4):
        itn(f"deq{i}_i", [256, D], BF16)
        itn(f"g_deq{i}", [D, D], BF16)
    itn("qnT_i", [NH * NOPE, NOWN], BF16)
    itn("qrT_i", [NH * ROPE, NOWN], BF16)
    itn("ckv_i", [KV_LORA + ROPE, NOWN], BF16); itn("g_ckv", [2 * (KV_LORA + ROPE), NOWN], BF16)
    itn("kT_i", [D, NOWN], BF16); itn("g_kT", [2 * D, NOWN], BF16)
    itn("vb_i", [NH, 128, 8 * V_HD], BF16); itn("g_vb", [2 * NH, 128, 8 * V_HD], BF16)
    itn("str_i", [NOWN, D], BF16)
    itn("h0_i", [I0, NOWN], BF16)
    itn("h0b_i", [I0, NOWN], BF16)
    itn("xq_i", [NOWN, D], BF16)
    itn("hq_i", [NOWN, F1], BF16)
    itn("hq2_i", [NOWN, F1], BF16)

    with ExitStack() as ctx:
        e = ctx.enter_context
        STR = e(nc.sbuf_tensor("STR", [128, 8 * D], BF16))
        AT = e(nc.sbuf_tensor("AT", [128, NC_D * NOWN], BF16))
        B1 = e(nc.sbuf_tensor("B1", [128, 16384], BF16))
        B2 = e(nc.sbuf_tensor("B2", [128, 16384], BF16))
        B3 = e(nc.sbuf_tensor("B3", [128, 16384], BF16))
        CK = e(nc.sbuf_tensor("CK", [128, 12288], BF16))
        kh = e(nc.sbuf_tensor("kh", [128, 2048], BF16))
        vh = e(nc.sbuf_tensor("vh", [128, 2048], BF16))
        qnh = e(nc.sbuf_tensor("qnh", [128, 1024], BF16))
        qrp2 = e(nc.sbuf_tensor("qrp2", [128, 1024], BF16))
        srepg = CK[:, 0:2816]
        srepd = CK[:, 4096:6144]
        qrp = CK[0:64, 5120:6144]
        qrsw = CK[0:64, 6144:7168]
        qro = CK[0:64, 7168:8192]
        kpesw = e(nc.sbuf_tensor("kpesw", [64, NOWN], BF16))
        cs2 = CK[0:64, 10240:12288]
        csr = CK[0:32, 8192:10240]
        exp32 = e(nc.sbuf_tensor("exp32s", [32, 128], BF16))
        bqa = e(nc.sbuf_tensor("bqa_s", [128, 4], F32))
        bqbn = e(nc.sbuf_tensor("bqbn_s", [128, 16], F32))
        bqbr = e(nc.sbuf_tensor("bqbr_s", [64, 32], F32))
        bkva = e(nc.sbuf_tensor("bkva_s", [128, 6], F32))
        bkv = e(nc.sbuf_tensor("bkv_s", [128, 32], F32))
        bq1 = e(nc.sbuf_tensor("bq1_s", [128, 16], F32))
        bk1 = e(nc.sbuf_tensor("bk1_s", [128, 16], F32))
        bvb = e(nc.sbuf_tensor("bvb_s", [128, 2048], BF16))
        bob = bvb
        rowb = CK[0:1, 10240:12288]
        ones1 = e(nc.sbuf_tensor("ones1_s", [1, 128], BF16))
        onesc = e(nc.sbuf_tensor("onesc_s", [128, 1], BF16))
        ones2 = e(nc.sbuf_tensor("ones2_s", [128, 128], BF16))
        onescf = e(nc.sbuf_tensor("onescf_s", [128, 1], F32))
        shcol = e(nc.sbuf_tensor("shcol_s", [128, 1], F32))
        cols = e(nc.sbuf_tensor("cols_s", [128, 32], F32))
        scc = e(nc.sbuf_tensor("scc_s", [128, 2], F32))
        sc2 = e(nc.sbuf_tensor("sc2_s", [1, 4], F32))
        prtt = e(nc.sbuf_tensor("prtt_s", [8, 2], F32))
        onc8 = e(nc.sbuf_tensor("onc8_s", [8, 1], F32))
        ones1f = e(nc.sbuf_tensor("ones1f_s", [1, 128], F32))
        rrow = CK[0:1, 10240:11264]
        rbc = CK[:, 11264:12288]
        f32a = CK.ap().bitcast(F32)[:, 0:2048]
        f32b = CK.ap().bitcast(F32)[:, 2048:4096]
        PS1 = e(nc.psum_tensor("PS1", [128, 2048], F32))
        PS2 = e(nc.psum_tensor("PS2", [128, 2048], F32))
        rrow32 = PS1[0:1, 1024:2048]
        ccsem = e(nc.semaphore("ccsem"))
        block = e(nc.Block())
        pb = PB(nc, ctx)
        cc_count = [0]

        def fence():
            pb.barrier(list(ENGS), list(ENGS))
            for q in ("sync", "gpsimd", "scalar"):
                for t in ("d", "w", "a", "s", "t", "w2", "a2", "t2"):
                    for eng in ENGS:
                        pb.wait_dma(eng, q, tag=t)

        def collective(kind, op, groups, ins, outs):
            def run(h):
                return h.collective_compute(kind, op, replica_groups=groups,
                                            ins=ins, outs=outs).then_inc(ccsem)
            pb.ops["gpsimd"].append(run)
            pb.lastbox["gpsimd"] = None
            cc_count[0] += 1
            n = cc_count[0]
            for eng in ENGS:
                pb.ops[eng].append(lambda h, n=n: h.wait_ge(ccsem, n))

        ccn = {}

        def collective_async(kind, op, groups, ins, outs):
            def run(h):
                return h.collective_compute(kind, op, replica_groups=groups,
                                            ins=ins, outs=outs).then_inc(ccsem)
            pb.ops["gpsimd"].append(run)
            pb.lastbox["gpsimd"] = None
            cc_count[0] += 1
            return cc_count[0]

        def wait_cc(eng, n):
            pb.ops[eng].append(lambda h, n=n: h.wait_ge(ccsem, n))

        V, SC, T, G, SY = "vector", "scalar", "tensor", "gpsimd", "sync"

        for c in range(8):
            pb.dma(G, tag="a", fn=lambda h, c=c: h.dma_start(
                out=STR[:, c * D:(c + 1) * D], in_=f["x"][c * 128:(c + 1) * 128, :]))
        for c in range(NC_D):
            pb.dma(G, tag="a", fn=lambda h, c=c: h.dma_start(
                out=AT[:, c * NOWN:(c + 1) * NOWN], in_=f["xT"][c * 128:(c + 1) * 128, :]))
        for (dst, src) in ((bqa, "bqa"), (bqbn, "bqbn"), (bqbr, "bqbr"), (bkva, "bkva"),
                           (bkv, "bkv"), (bq1, "bq1"), (bk1, "bk1"), (exp32, "exp32")):
            pb.dma(SY, tag="a", fn=lambda h, dst=dst, src=src: h.dma_start(
                out=dst[:], in_=f[src]))
        pb.dma(G, tag="a", fn=lambda h: h.dma_start(out=csr[:, 0:NOWN], in_=f["cosT"]))
        pb.dma(G, tag="a", fn=lambda h: h.dma_start(out=csr[:, NOWN:], in_=f["sinT"]))
        pb.dma(G, tag="a", fn=lambda h: h.dma_start(out=srepg[:], in_=f["srep_gu"]))
        pb.dma(G, tag="a", fn=lambda h: h.dma_start(out=srepd[0:88, :], in_=f["srep_dn"]))
        pb.emit(V, lambda h: h.memset(ones1[:], 1.0))
        pb.emit(V, lambda h: h.memset(onesc[:], 1.0))
        pb.emit(V, lambda h: h.memset(ones2[:], 1.0))
        pb.emit(V, lambda h: h.memset(onescf[:], 1.0))
        pb.emit(V, lambda h: h.memset(onc8[:], 1.0))
        pb.emit(V, lambda h: h.memset(ones1f[:], 1.0))
        fence()

        wtf = B1.ap().bitcast(F32)
        tiles1 = ([(f["wgu1_sl"], t, w0, 0) for t in range(2)
                   for w0 in range(0, 2 * F1, 2048)] +
                  [(f["wd1_sl"], t, w0, 1) for t in range(8)
                   for w0 in range(0, D, 2048)])
        lsl1 = lambda i: wtf[:, (i % 2) * 2048:(i % 2) * 2048 + 2048]
        lt1 = lambda i: "a" if i % 2 == 0 else "a2"
        cl1 = {}
        vm1 = {}
        pb.emit(V, lambda h: h.memset(cols[:, 0:4], 0.0))
        pb.barrier([V], [V])
        for i in range(2):
            src, t, w0, col = tiles1[i]
            cl1[i] = pb.dma(SY, tag=lt1(i), fn=lambda h, src=src, t=t, w0=w0, i=i:
                            h.dma_start(
                out=lsl1(i), in_=src[t * 128:(t + 1) * 128, w0:w0 + 2048]))
        for i, (src, t, w0, col) in enumerate(tiles1):
            pb.wait_dma(V, SY, tag=lt1(i), upto=cl1[i])
            pb.barrier([V], [V])
            pb.emit(V, lambda h, i=i: h.reduce_sum(
                out=cols[:, 2:3], in_=lsl1(i), axis=AX, apply_absolute_value=True))
            vm1[i] = pb.mark(V)
            pb.barrier([V], [V])
            pb.emit(V, lambda h, col=col: h.tensor_tensor(
                out=cols[:, col:col + 1], in0=cols[:, col:col + 1],
                in1=cols[:, 2:3], op=ALU.add))
            pb.barrier([V], [V])
            if i + 2 < len(tiles1):
                ii = i + 2
                src2, t2, w02, _ = tiles1[ii]
                pb.wait_mark(SY, vm1[i])
                cl1[ii] = pb.dma(SY, tag=lt1(ii), fn=lambda h, src2=src2, t2=t2,
                                 w02=w02, ii=ii: h.dma_start(
                    out=lsl1(ii), in_=src2[t2 * 128:(t2 + 1) * 128, w02:w02 + 2048]))
        pb.barrier([V], [T])
        for j in range(2):
            pb.emit(T, lambda h, j=j: nc.tensor.matmul(
                out=PS1[0:1, j:j + 1], lhsT=cols[:, j:j + 1], rhs=onescf[:, 0:1],
                start=True, stop=True))
        pb.barrier([T], [V])
        pb.emit(V, lambda h: h.tensor_copy(out=sc2[0:1, 0:2], in_=PS1[0:1, 0:2]))
        pb.barrier([V], [G])
        pb.dma(G, tag="s", fn=lambda h: h.dma_start(out=f["prt_i"].ap(), in_=sc2[0:1, 0:2]))
        pb.wait_dma(G, G, tag="s")
        collective("AllReduce", ALU.add, [list(range(NCORE))],
                   [f["prt_i"].ap()], [f["g_prt"].ap()])
        pb.dma(SY, tag="a", fn=lambda h: h.dma_start(out=sc2[0:1, 0:2], in_=f["g_prt"].ap()))
        pb.wait_dma(V, SY, tag="a")
        pb.emit(V, lambda h: h.tensor_scalar(
            out=sc2[0:1, 0:1], in0=sc2[0:1, 0:1], scalar1=1.0 / (2 * F1 * D), scalar2=1e-5,
            op0=ALU.mult, op1=ALU.max))
        pb.emit(V, lambda h: h.tensor_scalar(
            out=sc2[0:1, 1:2], in0=sc2[0:1, 1:2], scalar1=1.0 / (D * F1), scalar2=1e-5,
            op0=ALU.mult, op1=ALU.max))
        pb.barrier([V], [V])
        pb.emit(V, lambda h: h.reciprocal(out=sc2[0:1, 2:4], in_=sc2[0:1, 0:2]))
        pb.barrier([V], [T])
        pb.emit(T, lambda h: nc.tensor.matmul(
            out=PS1[:, 4:6], lhsT=ones1f[0:1, :], rhs=sc2[0:1, 2:4], start=True, stop=True))
        pb.barrier([T], [V])
        pb.emit(V, lambda h: h.tensor_copy(out=scc[:, 0:2], in_=PS1[:, 4:6]))
        pb.barrier([V], [T])
        pb.emit(T, lambda h: nc.tensor.matmul(
            out=PS1[:, 6:8], lhsT=ones1f[0:1, :], rhs=sc2[0:1, 0:2], start=True, stop=True))
        pb.barrier([T], [V])
        pb.emit(V, lambda h: h.tensor_copy(out=cols[:, 30:32], in_=PS1[:, 6:8]))
        fence()

        f32s = B2.ap().bitcast(F32)
        bfo = B3
        iv = wtf.bitcast(I32)
        tiles2 = ([(f["wgu1_sl"], f["tgu1_i"], t, w0, 0) for t in range(2)
                   for w0 in range(0, 2 * F1, 2048)] +
                  [(f["wd1_sl"], f["twd1_i"], t, w0, 1) for t in range(8)
                   for w0 in range(0, D, 2048)])
        fsl2 = lambda i: f32s[:, (i % 2) * 2048:(i % 2) * 2048 + 2048]
        ivl2 = lambda i: iv[:, (i % 2) * 2048:(i % 2) * 2048 + 2048]
        osl2 = lambda i: bfo[:, (i % 2) * 2048:(i % 2) * 2048 + 2048]
        st2 = lambda i: "s" if i % 2 == 0 else "d"
        cl2 = {}
        st2c = {}
        vq2 = {}
        for i in range(2):
            src, dst, t, w0, col = tiles2[i]
            cl2[i] = pb.dma(SY, tag=lt1(i), fn=lambda h, src=src, t=t, w0=w0, i=i:
                            h.dma_start(
                out=lsl1(i), in_=src[t * 128:(t + 1) * 128, w0:w0 + 2048]))
        for i, (src, dst, t, w0, col) in enumerate(tiles2):
            pb.wait_dma(SC, SY, tag=lt1(i), upto=cl2[i])
            pb.barrier([V], [SC])
            pb.emit(SC, lambda h, col=col, i=i: h.activation(
                out=fsl2(i), in_=lsl1(i), func=AF.Identity,
                scale=scc[:, col:col + 1]))
            pb.barrier([SC], [V])
            pb.emit(V, lambda h, i=i: h.tensor_scalar(
                out=fsl2(i), in0=fsl2(i), scalar1=1.0, scalar2=-1.0,
                op0=ALU.min, op1=ALU.max))
            pb.barrier([V], [V])
            pb.emit(V, lambda h, i=i: h.tensor_copy(out=ivl2(i), in_=fsl2(i)))
            pb.barrier([V], [V])
            if i >= 2:
                pb.wait_dma(V, G, tag=st2(i), upto=st2c[i - 2])
            pb.emit(V, lambda h, i=i: h.tensor_copy(out=osl2(i), in_=ivl2(i)))
            vq2[i] = pb.mark(V)
            pb.barrier([V], [G])
            st2c[i] = pb.dma(G, tag=st2(i), fn=lambda h, dst=dst, t=t, w0=w0, i=i:
                             h.dma_start(
                out=dst.ap()[t * 128:(t + 1) * 128, w0:w0 + 2048], in_=osl2(i)))
            if i + 2 < len(tiles2):
                ii = i + 2
                src2, _, t2, w02, _ = tiles2[ii]
                pb.wait_mark(SY, vq2[i])
                cl2[ii] = pb.dma(SY, tag=lt1(ii), fn=lambda h, src2=src2, t2=t2,
                                 w02=w02, ii=ii: h.dma_start(
                    out=lsl1(ii), in_=src2[t2 * 128:(t2 + 1) * 128, w02:w02 + 2048]))
        fence()

        pk_i = B1.ap().bitcast(I32)
        nib_i = B2.ap().bitcast(I32)
        nbf = B3[:, 0:4096]
        d1 = B3[:, 4096:8192]
        d2 = B3[:, 8192:12288]
        d3 = B3[:, 12288:16384]

        kkc = [0]
        st3 = lambda kk: "s" if kk % 2 == 0 else "d"
        d1o = lambda kk: B2[:, 8192 + (kk % 2) * 4096: 8192 + (kk % 2) * 4096 + 4096]
        st3c = {}

        def decode(packed_src, rows, wid, srep, out_view, in_split4=False):
            pb.wait_dma(SY, SY, tag="a")
            pb.wait_dma(SY, SY, tag="a2")
            pb.barrier([V], [SY])
            pb.dma(SY, tag="a", fn=lambda h: h.dma_start(
                out=pk_i[0:rows, 0:wid], in_=packed_src))
            pb.wait_dma(V, SY, tag="a")
            for k in range(8):
                kk = kkc[0]
                kkc[0] += 1
                pb.barrier([V], [V])
                pb.emit(V, lambda h, k=k, rows=rows, wid=wid: h.tensor_scalar(
                    out=nib_i[0:rows, 0:wid], in0=pk_i[0:rows, 0:wid],
                    scalar1=4 * k, scalar2=15,
                    op0=ALU.logical_shift_right, op1=ALU.bitwise_and))
                pb.barrier([V], [V])
                pb.emit(V, lambda h, rows=rows, wid=wid: h.tensor_copy(
                    out=nbf[0:rows, 0:wid], in_=nib_i[0:rows, 0:wid]))
                pb.barrier([V], [V])
                pb.emit(V, lambda h, rows=rows, wid=wid: h.tensor_scalar(
                    out=d1[0:rows, 0:wid], in0=nbf[0:rows, 0:wid],
                    scalar1=8.0, scalar2=-2.0, op0=ALU.is_ge, op1=ALU.mult))
                pb.barrier([V], [V])
                pb.emit(V, lambda h, rows=rows, wid=wid: h.scalar_tensor_tensor(
                    out=d2[0:rows, 0:wid], in0=d1[0:rows, 0:wid], scalar=4.0,
                    in1=nbf[0:rows, 0:wid], op0=ALU.mult, op1=ALU.add))
                pb.barrier([V], [V])
                pb.emit(V, lambda h, rows=rows, wid=wid: h.tensor_scalar(
                    out=d3[0:rows, 0:wid], in0=d2[0:rows, 0:wid],
                    scalar1=2.0, scalar2=-4.0, op0=ALU.mult, op1=ALU.add))
                pb.barrier([V], [V])
                pb.emit(V, lambda h, rows=rows, wid=wid: h.tensor_tensor(
                    out=d3[0:rows, 0:wid], in0=d2[0:rows, 0:wid],
                    in1=d3[0:rows, 0:wid], op=ALU.max))
                pb.barrier([V], [V])
                pb.emit(V, lambda h, rows=rows, wid=wid: h.tensor_scalar(
                    out=d2[0:rows, 0:wid], in0=d2[0:rows, 0:wid],
                    scalar1=4.0, scalar2=-16.0, op0=ALU.mult, op1=ALU.add))
                pb.barrier([V], [V])
                pb.emit(V, lambda h, rows=rows, wid=wid: h.tensor_tensor(
                    out=d3[0:rows, 0:wid], in0=d3[0:rows, 0:wid],
                    in1=d2[0:rows, 0:wid], op=ALU.max))
                pb.barrier([V], [V])
                pb.emit(V, lambda h, rows=rows, wid=wid: h.tensor_scalar(
                    out=d3[0:rows, 0:wid], in0=d3[0:rows, 0:wid], scalar1=0.5,
                    scalar2=None, op0=ALU.mult))
                pb.barrier([V], [V])
                if kk >= 2:
                    pb.wait_dma(V, G, tag=st3(kk), upto=st3c[kk - 2])
                pb.emit(V, lambda h, rows=rows, wid=wid, kk=kk: h.scalar_tensor_tensor(
                    out=d1o(kk)[0:rows, 0:wid], in0=d1[0:rows, 0:wid], scalar=1.0,
                    in1=d3[0:rows, 0:wid], op0=ALU.add, op1=ALU.mult))
                pb.barrier([V], [V])
                pb.emit(V, lambda h, rows=rows, wid=wid, srep=srep, kk=kk: h.tensor_tensor(
                    out=d1o(kk)[0:rows, 0:wid], in0=d1o(kk)[0:rows, 0:wid],
                    in1=srep[0:rows, 0:wid], op=ALU.mult))
                pb.barrier([V], [G])
                if in_split4:
                    for q4 in range(4):
                        st3c[kk] = pb.dma(G, tag=st3(kk), fn=lambda h, k=k, q4=q4,
                                          kk=kk: h.dma_start(
                            out=out_view[(q4, k)],
                            in_=d1o(kk)[32 * q4:32 * (q4 + 1), 0:2816]))
                else:
                    st3c[kk] = pb.dma(G, tag=st3(kk), fn=lambda h, k=k, rows=rows,
                                      wid=wid, kk=kk: h.dma_start(
                        out=out_view[k], in_=d1o(kk)[0:rows, 0:wid]))

        gu_view = f["dgu0_i"].ap().rearrange("(cp k) (q j) -> q k cp j", k=8, q=4)
        decode(f["pgu_sl"], 128, 2816, srepg,
               {(q4, k): gu_view[q4, k] for q4 in range(4) for k in range(8)},
               in_split4=True)
        dn_view = f["ddn0_i"].ap().rearrange("(cp k) o -> k cp o", k=8)
        decode(f["pdn_sl"], 88, 2048, srepd, [dn_view[k] for k in range(8)])
        fence()

        for i in range(4):
            for ch in range(2):
                pb.barrier([V], [SY, G])
                pb.dma(G, tag="a", fn=lambda h, i=i, ch=ch: h.dma_start(
                    out=B1[:, 0:2048], in_=f[f"codes{i}"][ch * 128:(ch + 1) * 128, :]))
                pb.dma(G, tag="a", fn=lambda h, i=i, ch=ch: h.dma_start(
                    out=B2[:, 0:2048], in_=f[f"mxrep{i}"][ch * 128:(ch + 1) * 128, :]))
                pb.wait_dma(V, G, tag="a")
                pb.wait_dma(V, G, tag="s")
                pb.emit(V, lambda h: h.tensor_tensor(
                    out=B3[:, 0:2048], in0=B1[:, 0:2048], in1=B2[:, 0:2048], op=ALU.mult))
                pb.barrier([V], [G])
                pb.dma(G, tag="s", fn=lambda h, i=i, ch=ch: h.dma_start(
                    out=f[f"deq{i}_i"].ap()[ch * 128:(ch + 1) * 128, :], in_=B3[:, 0:2048]))
        pb.wait_dma(G, G, tag="s")
        fence()
        _fused_part2(nc, f, pb, locals())
        pb.finalize(block)
    return nc


def _fused_part2(nc, f, pb, L):
    V, SC, T, G, SY = "vector", "scalar", "tensor", "gpsimd", "sync"
    STR, AT, B1, B2, B3, CK = L["STR"], L["AT"], L["B1"], L["B2"], L["B3"], L["CK"]
    PS1, PS2 = L["PS1"], L["PS2"]
    fence = L["fence"]
    cs2loc = L["cs2"]; csr = L["csr"]; exp32 = L["exp32"]
    bqa, bqbn, bqbr, bkva = L["bqa"], L["bqbn"], L["bqbr"], L["bkva"]
    bkv, bvb, bob, rowb = L["bkv"], L["bvb"], L["bob"], L["rowb"]
    ones1, onesc, shcol, cols = L["ones1"], L["onesc"], L["shcol"], L["cols"]
    ones2 = L["ones2"]
    qrp, qrsw, qro, kpesw = L["qrp"], L["qrsw"], L["qro"], L["kpesw"]
    kh, vh, qnh, qrp2 = L["kh"], L["vh"], L["qnh"], L["qrp2"]
    rrow, rrow32, rbc = L["rrow"], L["rrow32"], L["rbc"]
    f32a, f32b = L["f32a"], L["f32b"]

    wqb = B1
    qn = B2
    wqa_s = B3[:, 0:8192]
    qlat = B3[:, 12288:16384]
    wkva_s = B3[:, 0:10240]
    ckvs = CK[:, 0:5120]
    cs2 = cs2loc

    for c in range(4):
        pb.dma(G, tag="w", fn=lambda h, c=c: h.dma_start(
            out=wqb[:, c * 4096:(c + 1) * 4096], in_=f["wqb_p"][c * 128:(c + 1) * 128, :]))
    for c in range(NC_D):
        pb.dma(G, tag="w", fn=lambda h, c=c: h.dma_start(
            out=wqa_s[:, c * 512:(c + 1) * 512], in_=f["wqa"][c * 128:(c + 1) * 128, :]))
    pb.wait_dma(T, G, tag="w")
    for j in range(2):
        for qh in range(2):
            pb.emit(T, lambda h, j=j, qh=qh: nc.tensor.matmul(
                out=PS2[0:64, qh * 512:qh * 512 + 512],
                lhsT=exp32[:, j * 64:(j + 1) * 64],
                rhs=csr[:, j * NOWN + qh * 512: j * NOWN + qh * 512 + 512],
                start=True, stop=True))
        pb.barrier([T], [V])
        pb.emit(V, lambda h, j=j: h.tensor_copy(
            out=cs2[0:64, j * NOWN:(j + 1) * NOWN], in_=PS2[0:64, 0:1024]))
        pb.barrier([V], [T])
    svm5 = {}
    for t in range(4):
        for qh in range(2):
            idx = t * 2 + qh
            p = PS1[:, (idx % 4) * 512:(idx % 4) * 512 + 512]
            if idx >= 4:
                pb.wait_mark(T, svm5[idx - 4])
            for c in range(NC_D):
                pb.emit(T, lambda h, t=t, qh=qh, c=c, p=p: nc.tensor.matmul(
                    out=p, lhsT=wqa_s[:, c * 512 + t * 128: c * 512 + (t + 1) * 128],
                    rhs=AT[:, c * NOWN + qh * 512: c * NOWN + qh * 512 + 512],
                    start=(c == 0), stop=(c == NC_D - 1)))
            tm5 = pb.mark(T)
            pb.wait_mark(SC, tm5)
            pb.emit(SC, lambda h, t=t, qh=qh, p=p: h.activation(
                out=qlat[:, t * NOWN + qh * 512: t * NOWN + qh * 512 + 512],
                in_=p, func=AF.Identity, bias=bqa[:, t:t + 1], scale=1.0))
            svm5[idx] = pb.mark(SC)
    pb.barrier([T], [G])
    for c in range(NC_D):
        pb.dma(G, tag="w", fn=lambda h, c=c: h.dma_start(
            out=wkva_s[:, c * 640:(c + 1) * 640], in_=f["wkva_p"][c * 128:(c + 1) * 128, :]))
    pb.wait_dma(T, G, tag="w")
    svc5 = {}
    for t in range(6):
        rows = 128 if t < 4 else 64
        bcol = bkva[:, t:t + 1] if t < 4 else bkva[0:64, t:t + 1]
        for qh in range(2):
            idx = t * 2 + qh
            p = PS1[:, (idx % 4) * 512:(idx % 4) * 512 + 512]
            if idx >= 4:
                pb.wait_mark(T, svc5[idx - 4])
            elif t == 0 and qh == 0:
                pb.wait_mark(T, svm5[7])
            for c in range(NC_D):
                pb.emit(T, lambda h, t=t, qh=qh, c=c, p=p, rows=rows: nc.tensor.matmul(
                    out=p[0:rows, :],
                    lhsT=wkva_s[:, c * 640 + t * 128: c * 640 + t * 128 + rows] if t < 5
                    else wkva_s[:, c * 640 + 576: c * 640 + 640],
                    rhs=AT[:, c * NOWN + qh * 512: c * NOWN + qh * 512 + 512],
                    start=(c == 0), stop=(c == NC_D - 1)))
            tm5 = pb.mark(T)
            pb.wait_mark(SC, tm5)
            if t < 5:
                pb.emit(SC, lambda h, t=t, qh=qh, p=p, rows=rows, bcol=bcol: h.activation(
                    out=ckvs[0:rows, t * NOWN + qh * 512: t * NOWN + qh * 512 + 512],
                    in_=p[0:rows, :], func=AF.Identity, bias=bcol, scale=1.0))
            else:
                pb.emit(SC, lambda h, qh=qh, p=p, bcol=bcol: h.activation(
                    out=kpesw[0:64, qh * 512:qh * 512 + 512],
                    in_=p[0:64, :], func=AF.Identity, bias=bcol, scale=1.0))
            svc5[idx] = pb.mark(SC)
    svn5 = {}
    for t in range(NH):
        for qh in range(2):
            idx = t * 2 + qh
            p = PS1[:, (idx % 4) * 512:(idx % 4) * 512 + 512]
            if idx >= 4:
                pb.wait_mark(T, svn5[idx - 4])
            elif t == 0 and qh == 0:
                pb.wait_mark(T, svc5[11])
            for c in range(4):
                pb.emit(T, lambda h, t=t, qh=qh, c=c, p=p: nc.tensor.matmul(
                    out=p, lhsT=wqb[:, c * 4096 + t * 128: c * 4096 + (t + 1) * 128],
                    rhs=qlat[:, c * NOWN + qh * 512: c * NOWN + qh * 512 + 512],
                    start=(c == 0), stop=(c == 3)))
            tm5 = pb.mark(T)
            pb.wait_mark(SC, tm5)
            pb.emit(SC, lambda h, t=t, qh=qh, p=p: h.activation(
                out=qn[:, t * NOWN + qh * 512: t * NOWN + qh * 512 + 512],
                in_=p, func=AF.Identity, bias=bqbn[:, t:t + 1], scale=1.0))
            svn5[idx] = pb.mark(SC)
    pb.barrier([SC], [SY, T])
    for c in range(NH):
        pb.dma(SY, tag="s", fn=lambda h, c=c: h.dma_start(
            out=f["qnT_i"].ap()[c * 128:(c + 1) * 128, :], in_=qn[:, c * NOWN:(c + 1) * NOWN]))
    for hh in range(NH):
        for qh in range(2):
            for c in range(4):
                pb.emit(T, lambda h, hh=hh, qh=qh, c=c: nc.tensor.matmul(
                    out=PS2[0:64, qh * 512:qh * 512 + 512],
                    lhsT=wqb[:, c * 4096 + 2048 + hh * 64: c * 4096 + 2048 + (hh + 1) * 64],
                    rhs=qlat[:, c * NOWN + qh * 512: c * NOWN + qh * 512 + 512],
                    start=(c == 0), stop=(c == 3)))
            for c in range(4):
                pb.emit(T, lambda h, hh=hh, qh=qh, c=c: nc.tensor.matmul(
                    out=PS1[0:64, qh * 512:qh * 512 + 512],
                    lhsT=wqb[:, c * 4096 + 3072 + hh * 64: c * 4096 + 3072 + (hh + 1) * 64],
                    rhs=qlat[:, c * NOWN + qh * 512: c * NOWN + qh * 512 + 512],
                    start=(c == 0), stop=(c == 3)))
        pb.barrier([T], [SC])
        pb.wait_dma(SC, G, tag="s")
        pb.barrier([V], [SC])
        pb.emit(SC, lambda h, hh=hh: h.activation(
            out=qrp[0:64, 0:NOWN], in_=PS2[0:64, 0:NOWN],
            func=AF.Identity, bias=bqbr[0:64, hh:hh + 1], scale=1.0))
        pb.emit(SC, lambda h, hh=hh: h.activation(
            out=qrsw[0:64, 0:NOWN], in_=PS1[0:64, 0:NOWN],
            func=AF.Identity, bias=bqbr[0:64, 16 + hh:17 + hh], scale=1.0))
        pb.barrier([SC], [T, V])
        _emit_rope(pb, qrp, qrsw, qro, cs2, NOWN)
        pb.barrier([V], [G])
        pb.dma(G, tag="s", fn=lambda h, hh=hh: h.dma_start(
            out=f["qrT_i"].ap()[hh * 64:(hh + 1) * 64, :], in_=qro[0:64, 0:NOWN]))
    pb.barrier([SC], [V])
    pb.wait_dma(V, G, tag="s")
    pb.emit(V, lambda h: h.tensor_copy(out=qrp[0:64, 0:NOWN], in_=ckvs[0:64, 4 * NOWN:5 * NOWN]))
    pb.barrier([V], [V])
    _emit_rope(pb, qrp, kpesw, qro, cs2, NOWN)
    pb.barrier([V], [G])
    pb.dma(G, tag="s", fn=lambda h: h.dma_start(
        out=f["ckv_i"].ap()[512:576, :], in_=qro[0:64, 0:NOWN]))
    pb.barrier([SC], [G])
    for t in range(4):
        pb.dma(G, tag="s", fn=lambda h, t=t: h.dma_start(
            out=f["ckv_i"].ap()[t * 128:(t + 1) * 128, :], in_=ckvs[:, t * NOWN:(t + 1) * NOWN]))
    pb.wait_dma(G, G, tag="s")
    L["collective"]("AllGather", ALU.bypass, [[0, 1], [2, 3], [4, 5], [6, 7]],
                    [f["ckv_i"].ap()], [f["g_ckv"].ap()])
    ca, ccn = L["collective_async"], L["ccn"]
    g8 = [list(range(8))]
    ccn["gu0"] = ca("AllGather", ALU.bypass, g8, [f["dgu0_i"].ap()], [f["g_gu0"].ap()])
    ccn["dn0"] = ca("AllGather", ALU.bypass, g8, [f["ddn0_i"].ap()], [f["g_dn0"].ap()])
    for i in range(4):
        ccn[f"deq{i}"] = ca("AllGather", ALU.bypass, g8,
                            [f[f"deq{i}_i"].ap()], [f[f"g_deq{i}"].ap()])
    ccn["gu1"] = ca("AllGather", ALU.bypass, g8, [f["tgu1_i"].ap()], [f["g_gu1"].ap()])
    ccn["wd1"] = ca("AllGather", ALU.bypass, g8, [f["twd1_i"].ap()], [f["g_wd1"].ap()])
    fence()

    wkv = B3
    pt = B1
    ao = B2
    ckv = CK[:, 0:10240]
    NKT = S // 128
    NCC = KV_LORA // 128
    gc = f["g_ckv"].ap()

    for t in range(4):
        for b in range(2):
            pb.dma(SY, tag="a", fn=lambda h, t=t, b=b: h.dma_start(
                out=ckv[:, t * 2048 + b * 1024: t * 2048 + (b + 1) * 1024],
                in_=gc[b * 576 + t * 128: b * 576 + (t + 1) * 128, :]))
    for b in range(2):
        pb.dma(SY, tag="a", fn=lambda h, b=b: h.dma_start(
            out=ckv[0:64, 4 * 2048 + b * 1024: 4 * 2048 + (b + 1) * 1024],
            in_=gc[b * 576 + 512: b * 576 + 576, :]))
        pb.dma(SY, tag="a", fn=lambda h, b=b: h.dma_start(
            out=ckv[64:128, 4 * 2048 + b * 1024: 4 * 2048 + (b + 1) * 1024],
            in_=gc[b * 576 + 512: b * 576 + 576, :]))
    for c in range(NCC):
        pb.dma(G, tag="w", fn=lambda h, c=c: h.dma_start(
            out=wkv[:, c * 4096:(c + 1) * 4096], in_=f["wkvb"][c * 128:(c + 1) * 128, :]))
    pb.dma(G, tag="a", fn=lambda h: h.dma_start(out=rowb[:], in_=f["bvrow"]))
    qnb = lambda hh: AT[:, (hh % 2) * 1024:(hh % 2) * 1024 + 1024]
    qrb = lambda hp: AT[:, 2048 + (hp % 2) * 1024: 2048 + (hp % 2) * 1024 + 1024]
    qtg = lambda hh: "a" if hh % 2 == 0 else "a2"
    rtg = lambda hp: "t" if hp % 2 == 0 else "t2"
    qcnt = {}
    rcnt = {}
    am0 = {}
    vao0 = {}

    def emit_q_loads(hh):
        qcnt[hh] = pb.dma(SY, tag=qtg(hh), fn=lambda h, hh=hh: h.dma_start(
            out=qnb(hh), in_=f["qnT_i"].ap()[hh * 128:(hh + 1) * 128, :]))

    def emit_r_loads(hp):
        rcnt[hp] = pb.dma(SY, tag=rtg(hp), fn=lambda h, hp=hp: h.dma_start(
            out=qrb(hp), in_=f["qrT_i"].ap()[hp * 128:(hp + 1) * 128, :]))

    emit_q_loads(0)
    emit_q_loads(1)
    emit_r_loads(0)
    emit_r_loads(1)
    pb.emit(V, lambda h: h.memset(shcol := L["shcol"][:], -SHIFT0))
    pb.barrier([V], [SC])
    pb.wait_dma(V, G, tag="a")
    pb.barrier([V], [T])
    pb.wait_dma(T, G, tag="w")
    pb.wait_dma(T, SY, tag="a", upto=qcnt[0])
    for s4 in range(4):
        pb.emit(T, lambda h, s4=s4: nc.tensor.matmul(
            out=PS2[:, 0:512], lhsT=ones1[:], rhs=rowb[:, s4 * 512:(s4 + 1) * 512],
            start=True, stop=True))
        pb.barrier([T], [V])
        pb.emit(V, lambda h, s4=s4: h.tensor_copy(
            out=bvb[:, s4 * 512:(s4 + 1) * 512], in_=PS2[:, 0:512]))
        pb.barrier([V], [T])
    shcol = L["shcol"]
    for hh in range(NH):
        kex = {}
        for ks in range(4):
            if ks >= 2:
                pb.wait_mark(T, kex[ks - 2])
            for c in range(NCC):
                pb.emit(T, lambda h, hh=hh, ks=ks, c=c: nc.tensor.matmul(
                    out=PS1[:, 1024 + (ks % 2) * 512:1024 + (ks % 2) * 512 + 512],
                    lhsT=wkv[:, c * 4096 + hh * 256: c * 4096 + hh * 256 + 128],
                    rhs=ckv[:, c * 2048 + ks * 512: c * 2048 + ks * 512 + 512],
                    start=(c == 0), stop=(c == NCC - 1)))
            km = pb.mark(T)
            pb.wait_mark(SC, km)
            pb.emit(SC, lambda h, hh=hh, ks=ks: h.activation(
                out=kh[:, ks * 512:(ks + 1) * 512],
                in_=PS1[:, 1024 + (ks % 2) * 512:1024 + (ks % 2) * 512 + 512],
                func=AF.Identity, bias=bkv[:, 2 * hh:2 * hh + 1], scale=1.0))
            kex[ks] = pb.mark(SC)
        vexg = {}
        for g in range(NKT // 4):
            gb = 1024 + (g % 2) * 512
            if g < 2:
                pb.wait_mark(T, kex[2 + g])
            else:
                pb.wait_mark(T, vexg[g - 2])
            for k4 in range(4):
                kt = g * 4 + k4
                for c in range(NCC):
                    pb.emit(T, lambda h, hh=hh, kt=kt, k4=k4, c=c, gb=gb:
                            nc.tensor.matmul(
                        out=PS1[:, gb + k4 * 128:gb + k4 * 128 + 128],
                        lhsT=ckv[:, c * 2048 + kt * 128: c * 2048 + (kt + 1) * 128],
                        rhs=wkv[:, c * 4096 + hh * 256 + 128: c * 4096 + hh * 256 + 256],
                        start=(c == 0), stop=(c == NCC - 1)))
            vm = pb.mark(T)
            pb.wait_mark(V, vm)
            for k4 in range(4):
                kt = g * 4 + k4
                pb.emit(V, lambda h, hh=hh, kt=kt, k4=k4, gb=gb: h.tensor_tensor(
                    out=vh[:, kt * 128:(kt + 1) * 128],
                    in0=PS1[:, gb + k4 * 128:gb + k4 * 128 + 128],
                    in1=bvb[:, hh * 128:(hh + 1) * 128], op=ALU.add))
            vexg[g] = pb.mark(V)
        pb.barrier([SC, V], [T])
        pb.wait_dma(T, SY, tag=qtg(hh), upto=qcnt[hh])
        pb.wait_dma(T, SY, tag=rtg(hh // 2), upto=rcnt[hh // 2])
        exp_mk = {}
        def emit_spv0(kt, hh=hh):
            pb.wait_mark(T, exp_mk[kt])
            if kt == 0 and hh >= 1:
                pb.wait_mark(T, vao0[hh - 1])
            for qh in range(2):
                pb.emit(T, lambda h, kt=kt, qh=qh: nc.tensor.matmul(
                    out=PS2[:, qh * 512:qh * 512 + 512], lhsT=ones2[:],
                    rhs=pt[:, kt * NOWN + qh * 512: kt * NOWN + qh * 512 + 512],
                    start=(kt == 0), stop=(kt == NKT - 1)))
                pb.emit(T, lambda h, kt=kt, qh=qh: nc.tensor.matmul(
                    out=PS2[:, 1024 + qh * 512:1024 + qh * 512 + 512],
                    lhsT=vh[:, kt * 128:(kt + 1) * 128],
                    rhs=pt[:, kt * NOWN + qh * 512: kt * NOWN + qh * 512 + 512],
                    start=(kt == 0), stop=(kt == NKT - 1)))
        for kt in range(NKT):
            hb2 = (kt % 2) * 1024
            if kt >= 2:
                pb.wait_mark(T, exp_mk[kt - 2])
            for qh in range(2):
                pb.emit(T, lambda h, hh=hh, kt=kt, qh=qh, hb2=hb2: nc.tensor.matmul(
                    out=PS1[:, hb2 + qh * 512:hb2 + qh * 512 + 512],
                    lhsT=kh[:, kt * 128:(kt + 1) * 128],
                    rhs=qnb(hh)[:, qh * 512:qh * 512 + 512],
                    start=True, stop=False))
                pb.emit(T, lambda h, hh=hh, kt=kt, qh=qh, hb2=hb2: nc.tensor.matmul(
                    out=PS1[:, hb2 + qh * 512:hb2 + qh * 512 + 512],
                    lhsT=ckv[64 * (hh % 2):64 * (hh % 2) + 64,
                             4 * 2048 + kt * 128: 4 * 2048 + (kt + 1) * 128],
                    rhs=qrb(hh // 2)[64 * (hh % 2):64 * (hh % 2) + 64,
                                     qh * 512:qh * 512 + 512],
                    start=False, stop=True))
            qk_mk = pb.mark(T)
            pb.wait_mark(SC, qk_mk)
            pb.emit(SC, lambda h, kt=kt, hb2=hb2: h.activation(
                out=pt[:, kt * NOWN:(kt + 1) * NOWN], in_=PS1[:, hb2:hb2 + 1024],
                func=AF.Exp, bias=shcol[:, 0:1], scale=ISQ_HD))
            exp_mk[kt] = pb.mark(SC)
            if kt >= 1:
                emit_spv0(kt - 1)
        emit_spv0(NKT - 1)
        am0[hh] = pb.mark(T)
        if hh + 2 < NH:
            pb.wait_mark(SY, am0[hh])
            emit_q_loads(hh + 2)
        if hh % 2 == 1 and hh // 2 + 2 < NH // 2:
            pb.wait_mark(SY, am0[hh])
            emit_r_loads(hh // 2 + 2)
        pb.barrier([T], [V])
        pb.barrier([SC], [V])
        pb.emit(V, lambda h: h.reciprocal(out=PS1[:, 0:1024], in_=PS2[:, 0:1024]))
        pb.barrier([V], [V])
        pb.emit(V, lambda h: h.tensor_copy(out=rbc[:, 0:1024], in_=PS1[:, 0:1024]))
        pb.barrier([V], [V])
        for qh in range(2):
            pb.emit(V, lambda h, hh=hh, qh=qh: h.tensor_tensor(
                out=ao[:, hh * NOWN + qh * 512: hh * NOWN + qh * 512 + 512],
                in0=PS2[:, 1024 + qh * 512:1024 + qh * 512 + 512],
                in1=rbc[:, qh * 512:qh * 512 + 512], op=ALU.mult))
        vao0[hh] = pb.mark(V)
    wos = B3
    pb.wait_mark(T, vao0[NH - 1])
    pb.barrier([T], [G])
    pb.barrier([V], [G])
    pb.dma(G, tag="a", fn=lambda h: h.dma_start(out=rowb[:], in_=f["bo0"]))
    pb.wait_dma(T, G, tag="a")
    for s4 in range(4):
        pb.emit(T, lambda h, s4=s4: nc.tensor.matmul(
            out=PS2[:, 0:512], lhsT=ones1[:], rhs=rowb[:, s4 * 512:(s4 + 1) * 512],
            start=True, stop=True))
        pb.barrier([T], [V])
        pb.emit(V, lambda h, s4=s4: h.tensor_copy(
            out=bob[:, s4 * 512:(s4 + 1) * 512], in_=PS2[:, 0:512]))
        pb.barrier([V], [T])
    wosb = lambda i: wos[:, (i % 2) * 8192:(i % 2) * 8192 + 8192]
    wtag = lambda i: "w" if i % 2 == 0 else "w2"
    pb.barrier([T], [G])
    cw = {}
    for c in range(NC_D):
        cw[0] = pb.dma(G, tag=wtag(0), fn=lambda h, c=c: h.dma_start(
            out=wosb(0)[:, c * 512:(c + 1) * 512],
            in_=f["wo0"][c * 128:(c + 1) * 128, 0:512]))
    wtoks = {}
    wvm = {}
    for s4 in range(4):
        if s4 + 1 < 4:
            if s4 >= 1:
                pb.wait_mark(G, wtoks[s4 - 1])
            for c in range(NC_D):
                cw[s4 + 1] = pb.dma(G, tag=wtag(s4 + 1), fn=lambda h, c=c, s4=s4: h.dma_start(
                    out=wosb(s4 + 1)[:, c * 512:(c + 1) * 512],
                    in_=f["wo0"][c * 128:(c + 1) * 128, (s4 + 1) * 512:(s4 + 2) * 512]))
        pb.wait_dma(T, G, tag=wtag(s4), upto=cw[s4])
        for qt in range(8):
            gidx = s4 * 8 + qt
            p_off = (gidx % 4) * 512
            if gidx >= 4:
                pb.wait_mark(T, wvm[gidx - 4])
            for c in range(NC_D):
                pb.emit(T, lambda h, qt=qt, s4=s4, c=c, p_off=p_off: nc.tensor.matmul(
                    out=PS1[:, p_off:p_off + 512],
                    lhsT=ao[:, c * NOWN + qt * 128: c * NOWN + (qt + 1) * 128],
                    rhs=wosb(s4)[:, c * 512:(c + 1) * 512],
                    start=(c == 0), stop=(c == NC_D - 1)))
            wtm = pb.mark(T)
            pb.wait_mark(V, wtm)
            pb.emit(V, lambda h, qt=qt, s4=s4, p_off=p_off: h.tensor_tensor(
                out=pt[:, qt * 2048 + s4 * 512: qt * 2048 + s4 * 512 + 512],
                in0=PS1[:, p_off:p_off + 512],
                in1=bob[:, s4 * 512:(s4 + 1) * 512], op=ALU.add))
            wvm[gidx] = pb.mark(V)
        wtoks[s4] = pb.mark(T)
    for qt in range(8):
        pb.barrier([V], [V])
        pb.emit(V, lambda h, qt=qt: h.tensor_tensor(
            out=pt[:, qt * 2048: qt * 2048 + 2048],
            in0=pt[:, qt * 2048: qt * 2048 + 2048],
            in1=STR[:, qt * D:(qt + 1) * D], op=ALU.add))
        _fused_ln(pb, pt, qt * 2048, f32a, cols, STR[:, qt * D:(qt + 1) * D])
    fence()

    for c in range(8):
        pb.dma(G, tag="s", fn=lambda h, c=c: h.dma_start(
            out=f["str_i"].ap()[c * 128:(c + 1) * 128, :], in_=STR[:, c * D:(c + 1) * D]))
    pb.wait_dma(SY, G, tag="s")
    for c in range(NC_D):
        pb.dma(SY, tag="t", fn=lambda h, c=c: h.dma_start(
            out=AT[:, c * NOWN:(c + 1) * NOWN],
            in_=f["str_i"].ap()[:, c * 128:(c + 1) * 128], transpose=True))
    pb.wait_dma(T, SY, tag="t")
    gu0 = f["g_gu0"].ap()
    dn0 = f["g_dn0"].ap()
    evmk = {}
    stcnt0 = {}
    stg0 = lambda sg: ("s", "d", "a", "a2")[sg % 4]
    for pas in range(2):
        def emit_wload(sg, pas=pas):
            buf = (sg % 2) * 8192
            tg = "w" if sg % 2 == 0 else "w2"
            cnt = None
            for cc in range(NC_D):
                cnt = pb.dma(G, tag=tg, fn=lambda h, sg=sg, cc=cc, buf=buf, pas=pas:
                             h.dma_start(
                    out=B3[:, buf + cc * 512: buf + (cc + 1) * 512],
                    in_=gu0[cc * 128:(cc + 1) * 128,
                            pas * I0 + sg * 512: pas * I0 + (sg + 1) * 512]))
            return cnt
        pb.barrier([T], [G])
        if pas == 0:
            L["wait_cc"](G, L["ccn"]["gu0"])
        wcnt = {0: emit_wload(0)}
        wtk = {}
        for sg in range(11):
            buf = (sg % 2) * 8192
            if sg + 1 < 11:
                if sg >= 1:
                    pb.wait_mark(G, wtk[sg - 1])
                wcnt[sg + 1] = emit_wload(sg + 1)
            pb.wait_dma(T, G, tag=("w" if sg % 2 == 0 else "w2"), upto=wcnt[sg])
            prev = (pas, sg - 1) if sg >= 1 else ((pas - 1, 10) if pas == 1 else None)
            for sp in range(4):
                pgrp = (PS1 if sp < 2 else PS2)
                off = (sp % 2) * 1024
                if sp in (0, 2) and prev is not None:
                    pb.wait_mark(T, evmk[prev][sp // 2])
                for qh in range(2):
                    for cc in range(NC_D):
                        pb.emit(T, lambda h, sg=sg, sp=sp, qh=qh, cc=cc, buf=buf,
                                pgrp=pgrp, off=off: nc.tensor.matmul(
                            out=pgrp[:, off + qh * 512: off + qh * 512 + 512],
                            lhsT=B3[:, buf + cc * 512 + sp * 128: buf + cc * 512 + (sp + 1) * 128],
                            rhs=AT[:, cc * NOWN + qh * 512: cc * NOWN + qh * 512 + 512],
                            start=(cc == 0), stop=(cc == NC_D - 1)))
            wtk[sg] = pb.mark(T)
            pb.barrier([T], [SC, V])
            pvst = (pas, sg - 4) if sg >= 4 else ((0, sg + 7) if pas == 1 else None)
            if pvst is not None:
                pb.wait_dma(SC, G, tag=stg0(pvst[1]), upto=stcnt0[pvst])
                pb.wait_dma(V, G, tag=stg0(pvst[1]), upto=stcnt0[pvst])
            mk1 = mk2 = None
            for sp in range(4):
                pgrp = (PS1 if sp < 2 else PS2)
                off = (sp % 2) * 1024
                it = sg * 4 + sp
                if pas == 0:
                    pb.emit(SC, lambda h, it=it, pgrp=pgrp, off=off: h.activation(
                        out=B1[:, (it % 16) * 1024:(it % 16) * 1024 + 1024],
                        in_=pgrp[:, off:off + 1024], func=AF.Silu))
                else:
                    pb.wait_dma(V, SY, tag="a")
                    pb.emit(V, lambda h, it=it, pgrp=pgrp, off=off: h.tensor_tensor(
                        out=B1[:, (it % 16) * 1024:(it % 16) * 1024 + 1024],
                        in0=pgrp[:, off:off + 1024],
                        in1=B2[:, (it % 16) * 1024:(it % 16) * 1024 + 1024], op=ALU.mult))
                if sp == 1:
                    mk1 = pb.mark(SC if pas == 0 else V)
                elif sp == 3:
                    mk2 = pb.mark(SC if pas == 0 else V)
            evmk[(pas, sg)] = (mk1, mk2)
            if pas == 0:
                pb.barrier([SC], [G])
            else:
                pb.barrier([V], [G])
            hqo0 = f["h0_i"].ap() if pas == 0 else f["h0b_i"].ap()
            for sp in range(4):
                it = sg * 4 + sp
                stcnt0[(pas, sg)] = pb.dma(G, tag=stg0(sg), fn=lambda h, it=it,
                                           hqo0=hqo0: h.dma_start(
                    out=hqo0[it * 128:(it + 1) * 128, :],
                    in_=B1[:, (it % 16) * 1024:(it % 16) * 1024 + 1024]))
            if pas == 1 and sg < 10:
                pb.barrier([V], [SY])
                for sp in range(4):
                    it = (sg + 1) * 4 + sp
                    pb.dma(SY, tag="a", fn=lambda h, it=it: h.dma_start(
                        out=B2[:, (it % 16) * 1024:(it % 16) * 1024 + 1024],
                        in_=f["h0_i"].ap()[it * 128:(it + 1) * 128, :]))
        if pas == 0:
            for tgx in ("s", "d", "a", "a2"):
                pb.wait_dma(SY, G, tag=tgx)
            pb.barrier([V, SC], [SY])
            for sp in range(4):
                pb.dma(SY, tag="a", fn=lambda h, sp=sp: h.dma_start(
                    out=B2[:, sp * 1024:(sp + 1) * 1024],
                    in_=f["h0_i"].ap()[sp * 128:(sp + 1) * 128, :]))
    fence()
    NCH0, IC40 = 11, 4
    wtn = lambda c: "w" if c % 2 == 0 else "w2"
    htn = lambda c: "a" if c % 2 == 0 else "a2"
    wv = lambda c: B3[:, (c % 2) * 8192:(c % 2) * 8192 + 8192]
    hv = lambda c: CK[:, (c % 2) * 4096:(c % 2) * 4096 + 4096]
    B1f = B1.ap().bitcast(F32)
    B2f = B2.ap().bitcast(F32)
    accv = lambda qt: (B1f if qt < 4 else B2f)[:, (qt % 4) * 2048:(qt % 4) * 2048 + 2048]
    sq32 = CK.ap().bitcast(F32)[:, 4096:6144]
    wcn = {}
    hcn = {}

    def emit_dn0_loads(c):
        for j in range(IC40):
            ic = c * IC40 + j
            wcn[c] = pb.dma(G, tag=wtn(c), fn=lambda h, c=c, j=j, ic=ic: h.dma_start(
                out=wv(c)[:, j * 2048:(j + 1) * 2048],
                in_=dn0[ic * 128:(ic + 1) * 128, :]))
            hcn[c] = pb.dma(SY, tag=htn(c), fn=lambda h, c=c, j=j, ic=ic: h.dma_start(
                out=hv(c)[:, j * 1024:(j + 1) * 1024],
                in_=f["h0b_i"].ap()[ic * 128:(ic + 1) * 128, :]))

    pb.barrier([T, V, SC], [G, SY])
    for tgx in ("s", "d", "a", "a2"):
        pb.wait_dma(V, G, tag=tgx)
        pb.wait_dma(SY, G, tag=tgx)
    L["wait_cc"](G, L["ccn"]["dn0"])
    emit_dn0_loads(0)
    tmk = {}
    vem = {}
    for c in range(NCH0):
        if c + 1 < NCH0:
            if c >= 1:
                pb.wait_mark(G, tmk[(c - 1, 7)])
                pb.wait_mark(SY, tmk[(c - 1, 7)])
            emit_dn0_loads(c + 1)
        pb.wait_dma(T, G, tag=wtn(c), upto=wcn[c])
        pb.wait_dma(T, SY, tag=htn(c), upto=hcn[c])
        for qt in range(8):
            p = PS1 if qt % 2 == 0 else PS2
            pu = (c, qt - 2) if qt >= 2 else ((c - 1, qt + 6) if c >= 1 else None)
            if pu is not None:
                pb.wait_mark(T, vem[pu])
            for j in range(IC40):
                for os in range(4):
                    pb.emit(T, lambda h, c=c, j=j, qt=qt, os=os, p=p: nc.tensor.matmul(
                        out=p[:, os * 512:(os + 1) * 512],
                        lhsT=hv(c)[:, j * 1024 + qt * 128: j * 1024 + (qt + 1) * 128],
                        rhs=wv(c)[:, j * 2048 + os * 512: j * 2048 + (os + 1) * 512],
                        start=(j == 0), stop=(j == IC40 - 1)))
            tmk[(c, qt)] = pb.mark(T)
            pb.wait_mark(V, tmk[(c, qt)])
            if c == 0:
                pb.emit(V, lambda h, qt=qt, p=p: h.tensor_copy(
                    out=accv(qt), in_=p[:, 0:2048]))
            else:
                pb.emit(V, lambda h, qt=qt, p=p: h.tensor_tensor(
                    out=accv(qt), in0=p[:, 0:2048], in1=accv(qt), op=ALU.add))
            vem[(c, qt)] = pb.mark(V)
    for qt in range(8):
        pb.barrier([V], [SC, V])
        pb.emit(SC, lambda h, qt=qt: h.activation(out=sq32[:, 0:2048], in_=accv(qt),
                                                  func=AF.Square))
        pb.barrier([SC], [V])
        pb.emit(V, lambda h: h.reduce_sum(out=cols[:, 8:9], in_=sq32[:, 0:2048], axis=AX))
        pb.barrier([V], [V])
        pb.emit(V, lambda h: h.tensor_scalar(
            out=cols[:, 9:10], in0=cols[:, 8:9], scalar1=1.0 / D, scalar2=1e-6,
            op0=ALU.mult, op1=ALU.add))
        pb.barrier([V], [SC])
        pb.emit(SC, lambda h: h.activation(out=cols[:, 9:10], in_=cols[:, 9:10],
                                           func=AF.Sqrt))
        pb.barrier([SC], [V])
        pb.emit(V, lambda h: h.reciprocal(out=cols[:, 10:11], in_=cols[:, 9:10]))
        pb.barrier([V], [V])
        pb.emit(V, lambda h, qt=qt: h.scalar_tensor_tensor(
            out=accv(qt), in0=accv(qt), scalar=cols[:, 10:11],
            in1=STR[:, qt * D:(qt + 1) * D], op0=ALU.mult, op1=ALU.add))
        _fused_ln(pb, accv(qt), 0, sq32, cols, STR[:, qt * D:(qt + 1) * D])
    fence()
    _fused_part3(nc, f, pb, L)


def _fused_ln(pb, s_sb, s_off, scratch, cols, out_view):
    nc = pb.nc
    V, SC = "vector", "scalar"
    pb.barrier([V], [V])
    for j in range(4):
        pb.emit(V, lambda h, j=j: h.bn_stats(
            out=scratch[:, j * 6:(j + 1) * 6],
            in_=s_sb[:, s_off + j * 512: s_off + (j + 1) * 512]))
    pb.barrier([V], [V])
    pb.emit(V, lambda h: h.bn_aggr(out=cols[:, 2:4], in_=scratch[:, 0:24]))
    pb.barrier([V], [V])
    pb.emit(V, lambda h: h.tensor_scalar_add(out=cols[:, 5:6], in0=cols[:, 3:4],
                                             scalar1=1e-5))
    pb.barrier([V], [SC])
    pb.emit(SC, lambda h: h.activation(out=cols[:, 5:6], in_=cols[:, 5:6], func=AF.Sqrt))
    pb.barrier([SC], [V])
    pb.emit(V, lambda h: h.reciprocal(out=cols[:, 6:7], in_=cols[:, 5:6]))
    pb.barrier([V], [V])
    pb.emit(V, lambda h: h.tensor_scalar(
        out=out_view, in0=s_sb[:, s_off:s_off + D], scalar1=cols[:, 2:3],
        scalar2=cols[:, 6:7], op0=ALU.subtract, op1=ALU.mult))
    pb.barrier([V], [V])


def _fused_part3(nc, f, pb, L):
    V, SC, T, G, SY = "vector", "scalar", "tensor", "gpsimd", "sync"
    STR, AT, B1, B2, B3, CK = L["STR"], L["AT"], L["B1"], L["B2"], L["B3"], L["CK"]
    PS1, PS2 = L["PS1"], L["PS2"]
    fence = L["fence"]
    bq1, bk1, bvb, bob, rowb = L["bq1"], L["bk1"], L["bvb"], L["bob"], L["rowb"]
    ones1, onesc, shcol, cols, scc = L["ones1"], L["onesc"], L["shcol"], L["cols"], L["scc"]
    ones2 = L["ones2"]
    kh, vh, qnh = L["kh"], L["vh"], L["qnh"]
    rrow, rrow32, rbc = L["rrow"], L["rrow32"], L["rbc"]
    f32a, f32b = L["f32a"], L["f32b"]
    ca, ccn, wait_cc = L["collective_async"], L["ccn"], L["wait_cc"]

    for c in range(8):
        pb.dma(G, tag="s", fn=lambda h, c=c: h.dma_start(
            out=f["str_i"].ap()[c * 128:(c + 1) * 128, :], in_=STR[:, c * D:(c + 1) * D]))
    pb.wait_dma(SY, G, tag="s")
    for c in range(NC_D):
        pb.dma(SY, tag="t", fn=lambda h, c=c: h.dma_start(
            out=AT[:, c * NOWN:(c + 1) * NOWN],
            in_=f["str_i"].ap()[:, c * 128:(c + 1) * 128], transpose=True))
    pb.wait_dma(T, SY, tag="t")
    pb.dma(G, tag="a", fn=lambda h: h.dma_start(out=rowb[:], in_=f["bv1"]))
    pb.wait_dma(T, G, tag="a")
    for s4 in range(4):
        pb.emit(T, lambda h, s4=s4: nc.tensor.matmul(
            out=PS2[:, 0:512], lhsT=ones1[:], rhs=rowb[:, s4 * 512:(s4 + 1) * 512],
            start=True, stop=True))
        pb.barrier([T], [V])
        pb.emit(V, lambda h, s4=s4: h.tensor_copy(
            out=bvb[:, s4 * 512:(s4 + 1) * 512], in_=PS2[:, 0:512]))
        pb.barrier([V], [T])

    pb.barrier([T], [G])
    qT = B2
    seqH = [(0, 0), (0, 1), (1, 0), (1, 1), (2, 0), (2, 1)]

    def wtile(H, c):
        if c < 12:
            return (B3[:, 4096 + c * 1024: 4096 + (c + 1) * 1024] if H % 2 == 0
                    else CK[:, c * 1024:(c + 1) * 1024])
        return B3[:, (c - 12) * 1024:(c - 12) * 1024 + 1024]

    bt = lambda H: "w" if H % 2 == 0 else "w2"
    c12 = {}
    c4 = {}
    hmk = {}

    def load12(H):
        widx, half = seqH[H]
        for c in range(12):
            c12[H] = pb.dma(G, tag=bt(H), fn=lambda h, H=H, widx=widx, half=half,
                            c=c: h.dma_start(
                out=wtile(H, c),
                in_=f[f"g_deq{widx}"].ap()[c * 128:(c + 1) * 128,
                                           half * 1024:(half + 1) * 1024]))

    def load4(H):
        widx, half = seqH[H]
        for c in range(12, 16):
            c4[H] = pb.dma(G, tag="d", fn=lambda h, H=H, widx=widx, half=half,
                           c=c: h.dma_start(
                out=wtile(H, c),
                in_=f[f"g_deq{widx}"].ap()[c * 128:(c + 1) * 128,
                                           half * 1024:(half + 1) * 1024]))

    def half_prelude(H):
        pb.wait_dma(T, G, tag=bt(H), upto=c12[H])
        pb.wait_dma(T, G, tag="d", upto=c4[H])
        if H + 1 < 6:
            if H >= 1:
                pb.wait_mark(G, hmk[H - 1])
            if seqH[H + 1][0] != seqH[H][0]:
                wait_cc(G, ccn[f"deq{seqH[H + 1][0]}"])
            load12(H + 1)

    def half_post(H):
        hmk[H] = pb.mark(T)
        if H + 1 < 6:
            pb.wait_mark(G, hmk[H])
            load4(H + 1)

    def proj1(widx, bias_cols, out_sb, out_dram):
        for half in range(2):
            H = widx * 2 + half
            half_prelude(H)
            svm = {}
            for ft in range(8):
                gt = half * 8 + ft
                for qh in range(2):
                    idx = ft * 2 + qh
                    p = PS1[:, (idx % 4) * 512:(idx % 4) * 512 + 512]
                    if idx >= 4:
                        pb.wait_mark(T, svm[idx - 4])
                    for c in range(NC_D):
                        pb.emit(T, lambda h, H=H, ft=ft, qh=qh, c=c, p=p:
                                nc.tensor.matmul(
                            out=p,
                            lhsT=wtile(H, c)[:, ft * 128:(ft + 1) * 128],
                            rhs=AT[:, c * NOWN + qh * 512: c * NOWN + qh * 512 + 512],
                            start=(c == 0), stop=(c == NC_D - 1)))
                    ptm = pb.mark(T)
                    pb.wait_mark(SC, ptm)
                    pb.emit(SC, lambda h, gt=gt, qh=qh, p=p, bias_cols=bias_cols:
                            h.activation(
                        out=out_sb[:, gt * NOWN + qh * 512: gt * NOWN + qh * 512 + 512],
                        in_=p, func=AF.Identity, bias=bias_cols[:, gt:gt + 1], scale=1.0))
                    svm[idx] = pb.mark(SC)
            pb.barrier([SC], [T])
            half_post(H)
        if out_dram is not None:
            pb.barrier([SC], [G])
            for c in range(NC_D):
                pb.dma(G, tag="s", fn=lambda h, c=c, out_dram=out_dram: h.dma_start(
                    out=out_dram[c * 128:(c + 1) * 128, :],
                    in_=out_sb[:, c * NOWN:(c + 1) * NOWN]))

    wait_cc(G, ccn["deq0"])
    load12(0)
    load4(0)
    proj1(0, bq1, qT, None)
    proj1(1, bk1, B1, f["kT_i"].ap())
    pb.wait_dma(G, G, tag="s")
    ccn["kT"] = ca("AllGather", ALU.bypass, [[0, 1], [2, 3], [4, 5], [6, 7]],
                   [f["kT_i"].ap()], [f["g_kT"].ap()])
    for half in range(2):
        H = 4 + half
        half_prelude(H)
        vvm = {}
        for hg in range(2):
            for kt in range(8):
                vidx = hg * 8 + kt
                bank = PS1 if vidx % 2 == 0 else PS2
                p = bank[:, ((vidx // 2) % 4) * 512:((vidx // 2) % 4) * 512 + 512]
                if vidx >= 8:
                    pb.wait_mark(T, vvm[vidx - 8])
                for c in range(NC_D):
                    pb.emit(T, lambda h, H=H, hg=hg, kt=kt, c=c, p=p: nc.tensor.matmul(
                        out=p,
                        lhsT=AT[:, c * NOWN + kt * 128: c * NOWN + (kt + 1) * 128],
                        rhs=wtile(H, c)[:, hg * 512:(hg + 1) * 512],
                        start=(c == 0), stop=(c == NC_D - 1)))
                vtm = pb.mark(T)
                pb.wait_mark(V, vtm)
                pb.wait_dma(V, G, tag="s")
                for h4 in range(4):
                    hh = half * 8 + hg * 4 + h4
                    pb.emit(V, lambda h, hh=hh, kt=kt, h4=h4, p=p: h.tensor_tensor(
                        out=B1[:, (hh % 8) * 2048 + kt * 128 + 1024 * 0:
                                (hh % 8) * 2048 + kt * 128 + 128],
                        in0=p[:, h4 * 128:(h4 + 1) * 128],
                        in1=bvb[:, hh * 128:(hh + 1) * 128], op=ALU.add))
                vvm[vidx] = pb.mark(V)
            pb.barrier([V], [G])
            for h4 in range(4):
                hh = half * 8 + hg * 4 + h4
                pb.dma(G, tag="s", fn=lambda h, hh=hh: h.dma_start(
                    out=f["vb_i"].ap()[hh, :, 0:1024],
                    in_=B1[:, (hh % 8) * 2048:(hh % 8) * 2048 + 1024]))
        pb.barrier([V], [T])
        half_post(H)
    pb.wait_dma(G, G, tag="s")
    ccn["vb"] = ca("AllGather", ALU.bypass, [[0, 1], [2, 3], [4, 5], [6, 7]],
                   [f["vb_i"].ap().rearrange("h p x -> (h p) x")],
                   [f["g_vb"].ap().rearrange("h p x -> (h p) x")])
    fence()
    wait_cc(G, ccn["kT"])
    wait_cc(G, ccn["vb"])

    pt = B1
    ao = AT
    gkt = f["g_kT"].ap()
    gvb = f["g_vb"].ap()
    NKT = S // 128
    pb.emit(V, lambda h: h.memset(shcol[:], -SHIFT1))
    pb.barrier([V], [SC])
    khb = lambda hh: kh if hh % 2 == 0 else B3[:, 0:2048]
    vhb = lambda hh: vh if hh % 2 == 0 else B3[:, 2048:4096]
    atg = lambda hh: "a" if hh % 2 == 0 else "a2"
    acnt = {}
    am = {}
    vao1 = {}

    def emit_kv_loads(hh):
        for b in range(2):
            acnt[hh] = pb.dma(G, tag=atg(hh), fn=lambda h, hh=hh, b=b: h.dma_start(
                out=khb(hh)[:, b * 1024:(b + 1) * 1024],
                in_=gkt[b * D + hh * 128: b * D + (hh + 1) * 128, :]))
            acnt[hh] = pb.dma(G, tag=atg(hh), fn=lambda h, hh=hh, b=b: h.dma_start(
                out=vhb(hh)[:, b * 1024:(b + 1) * 1024], in_=gvb[b * NH + hh, :, :]))

    emit_kv_loads(0)
    emit_kv_loads(1)
    for hh in range(NH):
        pb.wait_dma(T, G, tag=atg(hh), upto=acnt[hh])
        exp_mk = {}
        def emit_spv1(kt, hh=hh):
            kb, ko = kt // 8, (kt % 8) * 128
            pb.wait_mark(T, exp_mk[kt])
            if kt == 0 and hh >= 1:
                pb.wait_mark(T, vao1[hh - 1])
            for qh in range(2):
                pb.emit(T, lambda h, kt=kt, qh=qh: nc.tensor.matmul(
                    out=PS2[:, qh * 512:qh * 512 + 512], lhsT=ones2[:],
                    rhs=pt[:, kt * NOWN + qh * 512: kt * NOWN + qh * 512 + 512],
                    start=(kt == 0), stop=(kt == NKT - 1)))
                pb.emit(T, lambda h, hh=hh, kt=kt, qh=qh, kb=kb, ko=ko: nc.tensor.matmul(
                    out=PS2[:, 1024 + qh * 512:1024 + qh * 512 + 512],
                    lhsT=vhb(hh)[:, kb * 1024 + ko: kb * 1024 + ko + 128],
                    rhs=pt[:, kt * NOWN + qh * 512: kt * NOWN + qh * 512 + 512],
                    start=(kt == 0), stop=(kt == NKT - 1)))
        for kt in range(NKT):
            kb, ko = kt // 8, (kt % 8) * 128
            hb2 = (kt % 2) * 1024
            if kt >= 2:
                pb.wait_mark(T, exp_mk[kt - 2])
            for qh in range(2):
                pb.emit(T, lambda h, hh=hh, kt=kt, qh=qh, kb=kb, ko=ko, hb2=hb2:
                        nc.tensor.matmul(
                    out=PS1[:, hb2 + qh * 512:hb2 + qh * 512 + 512],
                    lhsT=khb(hh)[:, kb * 1024 + ko: kb * 1024 + ko + 128],
                    rhs=qT[:, hh * NOWN + qh * 512: hh * NOWN + qh * 512 + 512],
                    start=True, stop=True))
            qk_mk = pb.mark(T)
            pb.wait_mark(SC, qk_mk)
            pb.emit(SC, lambda h, kt=kt, hb2=hb2: h.activation(
                out=pt[:, kt * NOWN:(kt + 1) * NOWN], in_=PS1[:, hb2:hb2 + 1024],
                func=AF.Exp, bias=shcol[:, 0:1], scale=ISQ_HD))
            exp_mk[kt] = pb.mark(SC)
            if kt >= 1:
                emit_spv1(kt - 1)
        emit_spv1(NKT - 1)
        am[hh] = pb.mark(T)
        if hh + 2 < NH:
            pb.wait_mark(G, am[hh])
            emit_kv_loads(hh + 2)
        pb.barrier([T], [V])
        pb.emit(V, lambda h: h.reciprocal(out=f32a[:, 0:1024], in_=PS2[:, 0:1024]))
        pb.barrier([V], [V])
        pb.emit(V, lambda h: h.tensor_copy(out=rbc[:, 0:1024], in_=f32a[:, 0:1024]))
        pb.barrier([V], [V])
        for qh in range(2):
            pb.emit(V, lambda h, hh=hh, qh=qh: h.tensor_tensor(
                out=ao[:, hh * NOWN + qh * 512: hh * NOWN + qh * 512 + 512],
                in0=PS2[:, 1024 + qh * 512:1024 + qh * 512 + 512],
                in1=rbc[:, qh * 512:qh * 512 + 512], op=ALU.mult))
        vao1[hh] = pb.mark(V)
    pb.wait_mark(T, vao1[NH - 1])
    pb.barrier([T], [G])
    pb.barrier([V], [G])
    pb.dma(G, tag="a", fn=lambda h: h.dma_start(out=rowb[:], in_=f["bo1"]))
    pb.wait_dma(T, G, tag="a")
    for s4 in range(4):
        pb.emit(T, lambda h, s4=s4: nc.tensor.matmul(
            out=PS2[:, 0:512], lhsT=ones1[:], rhs=rowb[:, s4 * 512:(s4 + 1) * 512],
            start=True, stop=True))
        pb.barrier([T], [V])
        pb.emit(V, lambda h, s4=s4: h.tensor_copy(
            out=bob[:, s4 * 512:(s4 + 1) * 512], in_=PS2[:, 0:512]))
        pb.barrier([V], [T])
    wosb1 = lambda i: B3[:, (i % 2) * 8192:(i % 2) * 8192 + 8192]
    wtag1 = lambda i: "w" if i % 2 == 0 else "w2"
    pb.barrier([T], [G])
    wait_cc(G, ccn["deq3"])
    cw = {}
    wvm = {}
    for c in range(NC_D):
        cw[0] = pb.dma(G, tag=wtag1(0), fn=lambda h, c=c: h.dma_start(
            out=wosb1(0)[:, c * 512:(c + 1) * 512],
            in_=f["g_deq3"].ap()[c * 128:(c + 1) * 128, 0:512]))
    wtoks = {}
    for s4 in range(4):
        if s4 + 1 < 4:
            if s4 >= 1:
                pb.wait_mark(G, wtoks[s4 - 1])
            for c in range(NC_D):
                cw[s4 + 1] = pb.dma(G, tag=wtag1(s4 + 1), fn=lambda h, c=c, s4=s4: h.dma_start(
                    out=wosb1(s4 + 1)[:, c * 512:(c + 1) * 512],
                    in_=f["g_deq3"].ap()[c * 128:(c + 1) * 128,
                                         (s4 + 1) * 512:(s4 + 2) * 512]))
        pb.wait_dma(T, G, tag=wtag1(s4), upto=cw[s4])
        for qt in range(8):
            gidx = s4 * 8 + qt
            p_off = (gidx % 4) * 512
            if gidx >= 4:
                pb.wait_mark(T, wvm[gidx - 4])
            for c in range(NC_D):
                pb.emit(T, lambda h, qt=qt, s4=s4, c=c, p_off=p_off: nc.tensor.matmul(
                    out=PS1[:, p_off:p_off + 512],
                    lhsT=ao[:, c * NOWN + qt * 128: c * NOWN + (qt + 1) * 128],
                    rhs=wosb1(s4)[:, c * 512:(c + 1) * 512],
                    start=(c == 0), stop=(c == NC_D - 1)))
            wtm = pb.mark(T)
            pb.wait_mark(V, wtm)
            pb.emit(V, lambda h, qt=qt, s4=s4, p_off=p_off: h.tensor_tensor(
                out=pt[:, qt * 2048 + s4 * 512: qt * 2048 + s4 * 512 + 512],
                in0=PS1[:, p_off:p_off + 512],
                in1=bob[:, s4 * 512:(s4 + 1) * 512], op=ALU.add))
            wvm[gidx] = pb.mark(V)
        wtoks[s4] = pb.mark(T)
    for qt in range(8):
        pb.barrier([V], [V])
        pb.emit(V, lambda h, qt=qt: h.tensor_tensor(
            out=pt[:, qt * 2048: qt * 2048 + 2048],
            in0=pt[:, qt * 2048: qt * 2048 + 2048],
            in1=STR[:, qt * D:(qt + 1) * D], op=ALU.add))
        _fused_ln(pb, pt, qt * 2048, f32a, cols, STR[:, qt * D:(qt + 1) * D])
    fence()

    gg1 = f["g_gu1"].ap()
    gw1 = f["g_wd1"].ap()
    for qt in range(8):
        pb.emit(V, lambda h, qt=qt: h.reduce_max(
            out=cols[:, qt:qt + 1], in_=STR[:, qt * D:(qt + 1) * D], axis=AX,
            apply_absolute_value=True))
    pb.barrier([V], [V])
    pb.emit(V, lambda h: h.tensor_scalar_max(out=cols[:, 0:8], in0=cols[:, 0:8],
                                             scalar1=1e-5))
    pb.barrier([V], [V])
    pb.emit(V, lambda h: h.tensor_scalar(
        out=cols[:, 8:16], in0=cols[:, 0:8], scalar1=cols[:, 30:31], scalar2=1.0 / 127,
        op0=ALU.mult, op1=ALU.mult))
    pb.emit(V, lambda h: h.reciprocal(out=cols[:, 16:24], in_=cols[:, 0:8]))
    pb.barrier([V], [V])
    pb.emit(V, lambda h: h.tensor_scalar_mul(out=cols[:, 16:24], in0=cols[:, 16:24],
                                             scalar1=127.0))
    pb.barrier([V], [V])
    for qt in range(8):
        pb.wait_dma(V, G, tag="s")
        pb.emit(V, lambda h, qt=qt: h.tensor_scalar_mul(
            out=f32a[:, 0:D], in0=STR[:, qt * D:(qt + 1) * D], scalar1=cols[:, 16 + qt:17 + qt]))
        pb.barrier([V], [V])
        iv = f32b.bitcast(I32)
        pb.emit(V, lambda h, iv=iv: h.tensor_copy(out=iv, in_=f32a[:, 0:D]))
        pb.barrier([V], [V])
        pb.emit(V, lambda h, iv=iv, qt=qt: h.tensor_copy(
            out=B1[:, qt * 2048:(qt + 1) * 2048], in_=iv))
        pb.barrier([V], [G])
        pb.dma(G, tag="s", fn=lambda h, qt=qt: h.dma_start(
            out=f["xq_i"].ap()[qt * 128:(qt + 1) * 128, :],
            in_=B1[:, qt * 2048:(qt + 1) * 2048]))
    pb.wait_dma(SY, G, tag="s")
    for c in range(NC_D):
        pb.dma(SY, tag="t", fn=lambda h, c=c: h.dma_start(
            out=AT[:, c * NOWN:(c + 1) * NOWN],
            in_=f["xq_i"].ap()[:, c * 128:(c + 1) * 128], transpose=True))
    pb.wait_dma(T, SY, tag="t")
    for pas in range(2):
        def emit_wg1(sn, pas=pas):
            cnt = None
            tg = "w" if sn % 2 == 0 else "w2"
            for cc in range(NC_D):
                cnt = pb.dma(G, tag=tg, fn=lambda h, sn=sn, cc=cc, pas=pas: h.dma_start(
                    out=B3[:, (sn % 2) * 8192 + cc * 512:(sn % 2) * 8192 + (cc + 1) * 512],
                    in_=gg1[cc * 128:(cc + 1) * 128,
                            pas * F1 + sn * 512: pas * F1 + (sn + 1) * 512]))
            return cnt
        pb.barrier([T], [G])
        if pas == 0:
            wait_cc(G, ccn["gu1"])
            stcnt = {}
            cpf = {}
        wc1 = {0: emit_wg1(0)}
        wt1 = {}
        hqo = f["hq_i"].ap() if pas == 0 else f["hq2_i"].ap()
        stg_ = lambda s: "s" if s % 2 == 0 else "d"
        pfg_ = lambda i: "a" if i % 2 == 0 else "a2"
        vmk = {}
        vpm = {}
        for s in range(16):
            if s + 1 < 16:
                if s >= 1:
                    pb.wait_mark(G, wt1[s - 1])
                wc1[s + 1] = emit_wg1(s + 1)
            pb.wait_dma(T, G, tag=("w" if s % 2 == 0 else "w2"), upto=wc1[s])
            sl = s % 2
            for qt in range(8):
                idx = s * 8 + qt
                bank = (PS1 if idx % 2 == 0 else PS2)
                p = bank[:, ((idx // 2) % 4) * 512: ((idx // 2) % 4) * 512 + 512]
                if idx >= 8:
                    pb.wait_mark(T, vpm[idx - 8])
                for cc in range(NC_D):
                    pb.emit(T, lambda h, sl=sl, qt=qt, cc=cc, p=p: nc.tensor.matmul(
                        out=p,
                        lhsT=AT[:, cc * NOWN + qt * 128: cc * NOWN + (qt + 1) * 128],
                        rhs=B3[:, sl * 8192 + cc * 512: sl * 8192 + (cc + 1) * 512],
                        start=(cc == 0), stop=(cc == NC_D - 1)))
                if pas == 0:
                    pb.barrier([T], [V])
                    if qt == 0 and s >= 1:
                        pb.wait_dma(V, G, tag=stg_(s - 1), upto=stcnt[(pas, s - 1)])
                    pb.emit(V, lambda h, qt=qt, p=p: h.tensor_scalar_mul(
                        out=f32a[:, 0:512], in0=p, scalar1=cols[:, 8 + qt:9 + qt]))
                    vpm[idx] = pb.mark(V)
                    pb.barrier([V], [SC])
                    pb.emit(SC, lambda h, qt=qt: h.activation(
                        out=B1[:, qt * 512:(qt + 1) * 512], in_=f32a[:, 0:512],
                        func=AF.Silu))
                    pb.barrier([SC], [G, V])
                else:
                    pb.barrier([T], [V])
                    if qt == 0:
                        pv = (pas, s - 1) if s >= 1 else (0, 15)
                        pb.wait_dma(V, G, tag=stg_(pv[1]), upto=stcnt[pv])
                    pb.wait_dma(V, SY, tag=pfg_(idx), upto=cpf[idx])
                    pb.emit(V, lambda h, qt=qt, p=p: h.scalar_tensor_tensor(
                        out=B1[:, qt * 512:(qt + 1) * 512], in0=p,
                        scalar=cols[:, 8 + qt:9 + qt],
                        in1=B2[:, qt * 512:(qt + 1) * 512], op0=ALU.mult, op1=ALU.mult))
                    vpm[idx] = vmk[idx] = pb.mark(V)
                    pb.barrier([V], [V])
                    pb.emit(V, lambda h, qt=qt: h.reduce_max(
                        out=cols[:, 24:25], in_=B1[:, qt * 512:(qt + 1) * 512],
                        axis=AX, apply_absolute_value=True))
                    pb.barrier([V], [V])
                    if s == 0:
                        pb.emit(V, lambda h, qt=qt: h.tensor_copy(
                            out=cols[:, qt:qt + 1], in_=cols[:, 24:25]))
                    else:
                        pb.emit(V, lambda h, qt=qt: h.tensor_tensor(
                            out=cols[:, qt:qt + 1], in0=cols[:, qt:qt + 1],
                            in1=cols[:, 24:25], op=ALU.max))
                    pb.barrier([V], [G])
                stcnt[(pas, s)] = pb.dma(G, tag=stg_(s), fn=lambda h, s=s, qt=qt,
                                         hqo=hqo: h.dma_start(
                    out=hqo[qt * 128:(qt + 1) * 128, s * 512:(s + 1) * 512],
                    in_=B1[:, qt * 512:(qt + 1) * 512]))
                if pas == 1 and idx + 2 < 128:
                    nidx = idx + 2
                    ns, nqt = nidx // 8, nidx % 8
                    pb.wait_mark(SY, vmk[nidx - 2])
                    cpf[nidx] = pb.dma(SY, tag=pfg_(nidx), fn=lambda h, ns=ns, nqt=nqt:
                                       h.dma_start(
                        out=B2[:, nqt * 512:(nqt + 1) * 512],
                        in_=f["hq_i"].ap()[nqt * 128:(nqt + 1) * 128,
                                           ns * 512:(ns + 1) * 512]))
            wt1[s] = pb.mark(T)
        if pas == 0:
            pb.wait_dma(SY, G, tag="s")
            pb.wait_dma(SY, G, tag="d")
            pb.barrier([V, SC], [SY])
            for i0 in range(2):
                cpf[i0] = pb.dma(SY, tag=pfg_(i0), fn=lambda h, i0=i0: h.dma_start(
                    out=B2[:, i0 * 512:(i0 + 1) * 512],
                    in_=f["hq_i"].ap()[i0 * 128:(i0 + 1) * 128, 0:512]))
    fence()
    pb.emit(V, lambda h: h.tensor_scalar_max(out=cols[:, 0:8], in0=cols[:, 0:8],
                                             scalar1=1e-5))
    pb.barrier([V], [V])
    pb.emit(V, lambda h: h.tensor_scalar(
        out=cols[:, 8:16], in0=cols[:, 0:8], scalar1=cols[:, 31:32], scalar2=1.0 / 127,
        op0=ALU.mult, op1=ALU.mult))
    pb.emit(V, lambda h: h.reciprocal(out=cols[:, 16:24], in_=cols[:, 0:8]))
    pb.barrier([V], [V])
    pb.emit(V, lambda h: h.tensor_scalar_mul(out=cols[:, 16:24], in0=cols[:, 16:24],
                                             scalar1=127.0))
    pb.barrier([V], [V])
    lsl = lambda i: B1[:, (i % 2) * 2048:(i % 2) * 2048 + 2048]
    la_ = lambda i: "a" if i % 2 == 0 else "a2"
    st_ = lambda i: "s" if i % 2 == 0 else "d"
    iv = f32b.bitcast(I32)
    cl = {}
    cst = {}
    for it in range(2):
        qt, hf = it // 4, it % 4
        cl[it] = pb.dma(G, tag=la_(it), fn=lambda h, qt=qt, hf=hf, it=it: h.dma_start(
            out=lsl(it), in_=f["hq2_i"].ap()[qt * 128:(qt + 1) * 128,
                                            hf * 2048:(hf + 1) * 2048]))
    for it in range(32):
        qt, hf = it // 4, it % 4
        pb.wait_dma(V, G, tag=la_(it), upto=cl[it])
        pb.barrier([V], [V])
        pb.emit(V, lambda h, qt=qt, it=it: h.tensor_scalar_mul(
            out=f32a[:, 0:2048], in0=lsl(it), scalar1=cols[:, 16 + qt:17 + qt]))
        pb.barrier([V], [V])
        pb.emit(V, lambda h, iv=iv: h.tensor_copy(out=iv, in_=f32a[:, 0:2048]))
        pb.barrier([V], [V])
        pb.emit(V, lambda h, iv=iv, it=it: h.tensor_copy(out=lsl(it), in_=iv))
        pb.barrier([V], [SY])
        cst[it] = pb.dma(SY, tag=st_(it), fn=lambda h, qt=qt, hf=hf, it=it: h.dma_start(
            out=f["hq2_i"].ap()[qt * 128:(qt + 1) * 128, hf * 2048:(hf + 1) * 2048],
            in_=lsl(it)))
        if it + 2 < 32:
            nqt, nhf = (it + 2) // 4, (it + 2) % 4
            pb.wait_dma(G, SY, tag=st_(it), upto=cst[it])
            cl[it + 2] = pb.dma(G, tag=la_(it), fn=lambda h, nqt=nqt, nhf=nhf, it=it:
                                h.dma_start(
                out=lsl(it + 2), in_=f["hq2_i"].ap()[nqt * 128:(nqt + 1) * 128,
                                                    nhf * 2048:(nhf + 1) * 2048]))
    pb.wait_dma(SY, SY, tag="s")
    pb.wait_dma(SY, SY, tag="d")
    NCH, IC4 = 16, 4
    wtn = lambda c: "w" if c % 2 == 0 else "w2"
    htn = lambda c: "t" if c % 2 == 0 else "t2"
    wv = lambda c: B3[:, (c % 2) * 8192:(c % 2) * 8192 + 8192]
    hv = lambda c: CK[:, (c % 2) * 4096:(c % 2) * 4096 + 4096]
    B1f = B1.ap().bitcast(F32)
    B2f = B2.ap().bitcast(F32)
    accv = lambda qt: (B1f if qt < 4 else B2f)[:, (qt % 4) * 2048:(qt % 4) * 2048 + 2048]
    wcn = {}
    hcn = {}

    def emit_down_loads(c):
        for j in range(IC4):
            ic = c * IC4 + j
            wcn[c] = pb.dma(G, tag=wtn(c), fn=lambda h, c=c, j=j, ic=ic: h.dma_start(
                out=wv(c)[:, j * 2048:(j + 1) * 2048],
                in_=gw1[ic * 128:(ic + 1) * 128, :]))
            hcn[c] = pb.dma(SY, tag=htn(c), fn=lambda h, c=c, j=j, ic=ic: h.dma_start(
                out=hv(c)[:, j * 1024:(j + 1) * 1024],
                in_=f["hq2_i"].ap()[:, ic * 128:(ic + 1) * 128], transpose=True))

    pb.barrier([T, V, SC], [G, SY])
    pb.wait_dma(V, SY, tag="s")
    pb.wait_dma(V, SY, tag="d")
    wait_cc(G, ccn["wd1"])
    emit_down_loads(0)
    tmk = {}
    vem = {}
    for c in range(NCH):
        if c + 1 < NCH:
            if c >= 1:
                pb.wait_mark(G, tmk[(c - 1, 7)])
                pb.wait_mark(SY, tmk[(c - 1, 7)])
            emit_down_loads(c + 1)
        pb.wait_dma(T, G, tag=wtn(c), upto=wcn[c])
        pb.wait_dma(T, SY, tag=htn(c), upto=hcn[c])
        for qt in range(8):
            p = PS1 if qt % 2 == 0 else PS2
            pu = (c, qt - 2) if qt >= 2 else ((c - 1, qt + 6) if c >= 1 else None)
            if pu is not None:
                pb.wait_mark(T, vem[pu])
            for j in range(IC4):
                for os in range(4):
                    pb.emit(T, lambda h, c=c, j=j, qt=qt, os=os, p=p: nc.tensor.matmul(
                        out=p[:, os * 512:(os + 1) * 512],
                        lhsT=hv(c)[:, j * 1024 + qt * 128: j * 1024 + (qt + 1) * 128],
                        rhs=wv(c)[:, j * 2048 + os * 512: j * 2048 + (os + 1) * 512],
                        start=(j == 0), stop=(j == IC4 - 1)))
            tmk[(c, qt)] = pb.mark(T)
            pb.wait_mark(V, tmk[(c, qt)])
            if c == 0:
                pb.emit(V, lambda h, qt=qt, p=p: h.tensor_copy(
                    out=accv(qt), in_=p[:, 0:2048]))
            else:
                pb.emit(V, lambda h, qt=qt, p=p: h.tensor_tensor(
                    out=accv(qt), in0=p[:, 0:2048], in1=accv(qt), op=ALU.add))
            vem[(c, qt)] = pb.mark(V)
    for qt in range(8):
        pb.barrier([V], [V])
        pb.emit(V, lambda h, qt=qt: h.scalar_tensor_tensor(
            out=accv(qt), in0=accv(qt), scalar=cols[:, 8 + qt:9 + qt],
            in1=STR[:, qt * D:(qt + 1) * D], op0=ALU.mult, op1=ALU.add))
        _fused_ln_out(pb, accv(qt), f32b, cols, f["_out_ap"][qt * 128:(qt + 1) * 128, :])
    for eng in ENGS:
        pb.wait_dma(eng, G, tag="s")


def _fused_ln_out(pb, s_sb, scratch, cols, out_dram):
    nc = pb.nc
    V, SC, G = "vector", "scalar", "gpsimd"
    pb.wait_dma(SC, G, tag="s")
    pb.wait_dma(V, G, tag="s")
    pb.barrier([V], [V])
    for j in range(4):
        pb.emit(V, lambda h, j=j: h.bn_stats(
            out=scratch[:, j * 6:(j + 1) * 6],
            in_=s_sb[:, j * 512:(j + 1) * 512]))
    pb.barrier([V], [V])
    pb.emit(V, lambda h: h.bn_aggr(out=cols[:, 26:28], in_=scratch[:, 0:24]))
    pb.barrier([V], [V])
    pb.emit(V, lambda h: h.tensor_scalar_add(out=cols[:, 28:29], in0=cols[:, 27:28],
                                             scalar1=1e-5))
    pb.barrier([V], [SC])
    pb.emit(SC, lambda h: h.activation(out=cols[:, 28:29], in_=cols[:, 28:29], func=AF.Sqrt))
    pb.barrier([SC], [V])
    pb.emit(V, lambda h: h.reciprocal(out=cols[:, 29:30], in_=cols[:, 28:29]))
    pb.barrier([V], [V])
    pb.emit(V, lambda h: h.tensor_scalar(
        out=scratch[:, 0:D], in0=s_sb[:, 0:D], scalar1=cols[:, 26:27],
        scalar2=cols[:, 29:30], op0=ALU.subtract, op1=ALU.mult))
    pb.barrier([V], [G])
    pb.dma(G, tag="s", fn=lambda h: h.dma_start(out=out_dram, in_=scratch[:, 0:D]))


USE_FUSED = True


def _host_prep_common(inp):
    f32 = np.float32
    H = {}
    x = np.asarray(inp["L0_attn_x"], f32).reshape(NTOK, D)
    H["x"] = x
    H["xT_full"] = np.ascontiguousarray(x.T)
    H["wqa"] = np.ascontiguousarray(np.asarray(inp["L0_attn_wq_a_w"], f32))
    H["bqa_c"] = _bcols(inp["L0_attn_wq_a_b"])
    wqb = np.asarray(inp["L0_attn_wq_b_w"], f32)
    bqb = np.asarray(inp["L0_attn_wq_b_b"], f32)
    perm_n = np.concatenate([np.arange(192 * h, 192 * h + 128) for h in range(NH)])
    perm_r = np.concatenate(
        [np.concatenate([np.arange(192 * h + 128, 192 * h + 192, 2),
                         np.arange(192 * h + 129, 192 * h + 192, 2)]) for h in range(NH)])
    perm_rs = np.concatenate(
        [np.concatenate([np.arange(192 * h + 129, 192 * h + 192, 2),
                         np.arange(192 * h + 128, 192 * h + 192, 2)]) for h in range(NH)])
    perm = np.concatenate([perm_n, perm_r, perm_rs])
    H["wqb_p"] = np.ascontiguousarray(wqb[:, perm])
    bqb_p = bqb[perm]
    H["bqbn_c"] = _bcols(bqb_p[:2048])
    H["bqbr_c"] = np.ascontiguousarray(bqb_p[2048:].reshape(32, 64).T)
    wkva = np.asarray(inp["L0_attn_wkv_a_w"], f32)
    bkva = np.asarray(inp["L0_attn_wkv_a_b"], f32)
    pk_r = np.concatenate([512 + np.arange(0, 64, 2), 512 + np.arange(1, 64, 2)])
    pk_rs = np.concatenate([512 + np.arange(1, 64, 2), 512 + np.arange(0, 64, 2)])
    permk = np.concatenate([np.arange(512), pk_r, pk_rs])
    H["wkva_p"] = np.ascontiguousarray(wkva[:, permk])
    bkva_p = bkva[permk]
    bkva_c = np.zeros((128, 6), f32)
    bkva_c[:, 0:4] = bkva_p[:512].reshape(4, 128).T
    bkva_c[0:64, 4] = bkva_p[512:576]
    bkva_c[0:64, 5] = bkva_p[576:640]
    H["bkva_c"] = bkva_c
    exp32 = np.zeros((32, 128), f32)
    for p in range(64):
        exp32[p % 32, p] = 1.0
        exp32[p % 32, 64 + p] = -1.0 if p < 32 else 1.0
    H["exp32"] = exp32.astype(BF)
    cosf, sinf = _rope_tables()
    H["cosT"] = np.ascontiguousarray(cosf.T)
    H["sinT"] = np.ascontiguousarray(sinf.T)
    H["wkvb"] = np.ascontiguousarray(np.asarray(inp["L0_attn_wkv_b_w"], f32))
    bkvb = np.asarray(inp["L0_attn_wkv_b_b"], f32)
    H["bkv_c"] = _bcols(bkvb)
    H["bv_row0"] = np.ascontiguousarray(bkvb.reshape(NH, 2, 128)[:, 1, :].reshape(1, NH * 128))
    H["wo0"] = np.ascontiguousarray(np.asarray(inp["L0_attn_wo_w"], f32))
    H["bo0_row"] = np.ascontiguousarray(np.asarray(inp["L0_attn_wo_b"], f32).reshape(1, D))
    H["wguT1"] = np.ascontiguousarray(np.asarray(inp["L1_ffn_w_gate_up"], f32).T)
    H["wdT1"] = np.ascontiguousarray(np.asarray(inp["L1_ffn_w_down"], f32).T)
    H["pguT"] = np.ascontiguousarray(np.asarray(inp["L0_ffn_gate_up_packed"]).T)
    H["sguT"] = np.asarray(inp["L0_ffn_gate_up_scales"])
    H["pdnT"] = np.ascontiguousarray(np.asarray(inp["L0_ffn_down_packed"]).T)
    H["sdnT"] = np.asarray(inp["L0_ffn_down_scales"])
    H["codes"] = [np.asarray(inp[k], f32) for k in
                  ("L1_attn_wq_codes", "L1_attn_wk_codes", "L1_attn_wv_codes",
                   "L1_attn_wo_codes")]
    H["mxs"] = [np.asarray(inp[k], f32) for k in
                ("L1_attn_wq_scales", "L1_attn_wk_scales", "L1_attn_wv_scales",
                 "L1_attn_wo_scales")]
    H["bq1_c"] = _bcols(inp["L1_attn_bq"])
    H["bk1_c"] = _bcols(inp["L1_attn_bk"])
    H["bv1_row"] = np.ascontiguousarray(np.asarray(inp["L1_attn_bv"], f32).reshape(1, D))
    H["bo1_row"] = np.ascontiguousarray(np.asarray(inp["L1_attn_bo"], f32).reshape(1, D))
    return H


def kernel_fused(inp):
    f32 = np.float32
    H = _host_prep_common(inp)
    nc = build_fused()
    maps = []
    for c in range(NCORE):
        sl32 = H["pguT"][32 * c:32 * c + 32, :]
        pgu_sl = np.ascontiguousarray(
            sl32.reshape(32, 4, 2816).transpose(1, 0, 2).reshape(128, 2816))
        p = np.arange(128)
        rows = (2 * c + (p % 32) // 16)
        srep_gu = np.empty((128, 2816), dtype=H["sguT"].dtype)
        for pi in range(128):
            q4 = pi // 32
            srep_gu[pi] = H["sguT"][rows[pi], 2816 * q4:2816 * (q4 + 1)]
        pp = np.arange(88)
        srep_dn = H["sdnT"][(704 * c + 8 * pp) // 128, :]
        m = {
            "x": np.ascontiguousarray(H["x"][c * NOWN:(c + 1) * NOWN]),
            "xT": np.ascontiguousarray(H["xT_full"][:, c * NOWN:(c + 1) * NOWN]),
            "wqa": H["wqa"], "bqa": H["bqa_c"], "wqb_p": H["wqb_p"],
            "bqbn": H["bqbn_c"], "bqbr": H["bqbr_c"],
            "wkva_p": H["wkva_p"], "bkva": H["bkva_c"],
            "cosT": np.ascontiguousarray(H["cosT"][:, (c % 2) * NOWN:(c % 2 + 1) * NOWN]),
            "sinT": np.ascontiguousarray(H["sinT"][:, (c % 2) * NOWN:(c % 2 + 1) * NOWN]),
            "exp32": H["exp32"],
            "wkvb": H["wkvb"], "bkv": H["bkv_c"], "bvrow": H["bv_row0"],
            "wo0": H["wo0"], "bo0": H["bo0_row"],
            "wgu1_sl": np.ascontiguousarray(H["wguT1"][256 * c:256 * (c + 1), :]),
            "wd1_sl": np.ascontiguousarray(H["wdT1"][1024 * c:1024 * (c + 1), :]),
            "pgu_sl": pgu_sl, "srep_gu": np.ascontiguousarray(srep_gu),
            "pdn_sl": np.ascontiguousarray(H["pdnT"][88 * c:88 * (c + 1), :]),
            "srep_dn": np.ascontiguousarray(srep_dn),
            "bq1": H["bq1_c"], "bk1": H["bk1_c"], "bv1": H["bv1_row"], "bo1": H["bo1_row"],
        }
        for i in range(4):
            m[f"codes{i}"] = np.ascontiguousarray(H["codes"][i][256 * c:256 * (c + 1), :])
            m[f"mxrep{i}"] = np.ascontiguousarray(
                np.repeat(H["mxs"][i], 32, axis=0)[256 * c:256 * (c + 1), :])
        maps.append(m)
    res = run_bass_kernel_spmd(nc, maps, core_ids=list(range(NCORE)))
    out = np.concatenate([res.results[c]["out"] for c in range(NCORE)], axis=0)
    return out.reshape(B, S, D)


# version 25
# speedup vs baseline: 1.0871x; 1.0032x over previous
import numpy as np
import ml_dtypes
from contextlib import ExitStack

import concourse.bass as bass
import concourse.mybir as mybir
from concourse.bass_utils import run_bass_kernel_spmd

F32 = mybir.dt.float32
BF16 = mybir.dt.bfloat16
F16 = mybir.dt.float16
I32 = mybir.dt.int32
AF = mybir.ActivationFunctionType
ALU = mybir.AluOpType
AX = mybir.AxisListType.X
BF = ml_dtypes.bfloat16

NCORE = 8
B, S, D = 4, 2048, 2048
NTOK = B * S
NOWN = NTOK // NCORE
NH = 16
Q_LORA = 512
KV_LORA = 512
NOPE = 128
ROPE = 64
QK_HD = NOPE + ROPE
V_HD = 128
I0 = 5632
F1 = 8192
NC_D = D // 128

ENGS = ("sync", "tensor", "scalar", "vector", "gpsimd")


class PB:

    def __init__(self, nc, ctx):
        self.nc = nc
        self.ops = {e: [] for e in ENGS}
        self.barsem = {e: ctx.enter_context(nc.semaphore("bar_" + e)) for e in ENGS}
        self.barcnt = {e: 0 for e in ENGS}
        self.dsem = {}
        self.dcount = {}
        self._ctx = ctx
        for q in ("sync", "gpsimd", "scalar"):
            for t in ("d", "w", "a", "s", "t", "w2", "a2", "t2"):
                key = (q, t)
                self.dsem[key] = ctx.enter_context(nc.semaphore(f"dsem_{q}_{t}"))
                self.dcount[key] = 0
        self.lastbox = {e: None for e in ENGS}
        self.lastbar = {e: None for e in ENGS}
        self.pending = {e: [] for e in ENGS}

    def emit(self, eng, fn):
        box = [None]

        def run(h):
            box[0] = fn(h)
        self.ops[eng].append(run)
        self.lastbox[eng] = box
        self.lastbar[eng] = None
        self.pending[eng] = []
        return box

    def dma(self, queue, fn, tag="d"):
        key = (queue, tag)
        sem = self.dsem[key]

        def run(h):
            return fn(h).then_inc(sem, 16)
        self.ops[queue].append(run)
        self.dcount[key] += 16
        self.lastbox[queue] = None
        self.lastbar[queue] = None
        self.pending[queue] = []
        return self.dcount[key]

    def wait_dma(self, eng, queue, tag="d", upto=None):
        key = (queue, tag)
        n = self.dcount[key] if upto is None else upto
        if n == 0:
            return
        sem = self.dsem[key]
        self.ops[eng].append(lambda h: h.wait_ge(sem, n))
        self.pending[eng].append((sem, n))

    def mark(self, eng):
        box = self.lastbox[eng]
        assert box is not None, f"mark({eng}) with no instruction"
        if self.lastbar[eng] is None:
            self.barcnt[eng] += 1
            self.lastbar[eng] = self.barcnt[eng]
            sem = self.barsem[eng]

            def attach(h, b=box, sem=sem):
                b[0].then_inc(sem, 1)
            self.ops[eng].append(attach)
        return (self.barsem[eng], self.barcnt[eng], tuple(self.pending[eng]))

    def wait_mark(self, eng, token):
        sem, n, pend = token
        self.ops[eng].append(lambda h, sem=sem, n=n: h.wait_ge(sem, n))
        for (s2, n2) in pend:
            self.ops[eng].append(lambda h, s2=s2, n2=n2: h.wait_ge(s2, n2))
            self.pending[eng].append((s2, n2))

    def barrier(self, producers, consumers):
        waits = []
        fwd = []
        for e in producers:
            fwd.extend(self.pending[e])
            box = self.lastbox[e]
            if box is None:
                if self.barcnt[e] > 0:
                    waits.append((e, self.barcnt[e]))
                continue
            if self.lastbar[e] is None:
                self.barcnt[e] += 1
                self.lastbar[e] = self.barcnt[e]
                sem = self.barsem[e]

                def attach(h, b=box, sem=sem):
                    b[0].then_inc(sem, 1)
                self.ops[e].append(attach)
            waits.append((e, self.barcnt[e]))
        fmax = {}
        for (sem, n) in fwd:
            k = id(sem)
            if k not in fmax or fmax[k][1] < n:
                fmax[k] = (sem, n)
        for c in consumers:
            for (e, n) in waits:
                self.ops[c].append(
                    lambda h, sem=self.barsem[e], n=n: h.wait_ge(sem, n))
            for (sem, n) in fmax.values():
                self.ops[c].append(lambda h, sem=sem, n=n: h.wait_ge(sem, n))
                self.pending[c].append((sem, n))

    def finalize(self, block):
        @block.sync
        def _(h):
            for f in self.ops["sync"]:
                f(h)

        @block.tensor
        def _(h):
            for f in self.ops["tensor"]:
                f(h)

        @block.scalar
        def _(h):
            for f in self.ops["scalar"]:
                f(h)

        @block.vector
        def _(h):
            for f in self.ops["vector"]:
                f(h)

        @block.gpsimd
        def _(h):
            for f in self.ops["gpsimd"]:
                f(h)



def build_qkv1():
    nc = bass.Bass("TRN2", target_bir_lowering=False)
    xT_d = nc.dram_tensor("xT", [D, NOWN], BF16, kind="ExternalInput").ap()
    wq_d = nc.dram_tensor("wq", [D, D], BF16, kind="ExternalInput").ap()
    wk_d = nc.dram_tensor("wk", [D, D], BF16, kind="ExternalInput").ap()
    wv_d = nc.dram_tensor("wv", [D, D], BF16, kind="ExternalInput").ap()
    bq_d = nc.dram_tensor("bq", [128, 16], F32, kind="ExternalInput").ap()
    bk_d = nc.dram_tensor("bk", [128, 16], F32, kind="ExternalInput").ap()
    bv_d = nc.dram_tensor("bv", [1, D], F32, kind="ExternalInput").ap()
    qT_o = nc.dram_tensor("qT", [D, NOWN], BF16, kind="ExternalOutput").ap()
    kT_o = nc.dram_tensor("kT", [D, NOWN], BF16, kind="ExternalOutput").ap()
    v_o = nc.dram_tensor("v", [NOWN, D], BF16, kind="ExternalOutput").ap()

    with ExitStack() as ctx:
        e = ctx.enter_context
        xT = e(nc.sbuf_tensor("xT_s", [128, NC_D * NOWN], BF16))
        wa = e(nc.sbuf_tensor("w_a", [128, NC_D * D], BF16))
        bc = e(nc.sbuf_tensor("bc", [128, 16], F32))
        bvr = e(nc.sbuf_tensor("bvr", [1, D], F32))
        bvr_b = e(nc.sbuf_tensor("bvr_b", [1, D], BF16))
        bvb = e(nc.sbuf_tensor("bvb", [128, D], BF16))
        ones1 = e(nc.sbuf_tensor("ones1", [1, 128], BF16))
        outT = e(nc.sbuf_tensor("outT", [128, NC_D * NOWN], BF16))
        ps = [e(nc.psum_tensor(f"ps{i}", [128, 512], F32)) for i in range(2)]
        pbv = e(nc.psum_tensor("pbv", [128, 512], F32))
        block = e(nc.Block())
        pb = PB(nc, ctx)

        for c in range(NC_D):
            pb.dma("sync", tag="a", fn=lambda h, c=c: h.dma_start(
                out=xT[:, c * NOWN:(c + 1) * NOWN], in_=xT_d[c * 128:(c + 1) * 128, :]))
        pb.dma("sync", tag="a", fn=lambda h: h.dma_start(out=bvr[:], in_=bv_d))
        x_loaded = pb.dcount[("sync", "a")]

        pb.emit("vector", lambda h: h.memset(ones1[:], 1.0))
        pb.wait_dma("vector", "sync", tag="a", upto=x_loaded)
        pb.emit("vector", lambda h: h.tensor_copy(out=bvr_b[:], in_=bvr[:]))
        pb.barrier(["vector"], ["tensor"])
        for s4 in range(4):
            pb.emit("tensor", lambda h, s4=s4: nc.tensor.matmul(
                out=pbv[:, 0:512], lhsT=ones1[:], rhs=bvr_b[:, s4 * 512:(s4 + 1) * 512],
                start=True, stop=True))
            pb.barrier(["tensor"], ["vector"])
            pb.emit("vector", lambda h, s4=s4: h.tensor_copy(
                out=bvb[:, s4 * 512:(s4 + 1) * 512], in_=pbv[:, 0:512]))
            pb.barrier(["vector"], ["tensor"])

        def proj_T(w_dram, bias_dram, out_dram, first):
            if not first:
                pb.barrier(["tensor"], ["gpsimd"])
                pb.barrier(["scalar"], ["sync"])
            for c in range(NC_D):
                pb.dma("gpsimd", tag="w", fn=lambda h, c=c: h.dma_start(
                    out=wa[:, c * D:(c + 1) * D], in_=w_dram[c * 128:(c + 1) * 128, :]))
            pb.dma("sync", tag="a", fn=lambda h: h.dma_start(out=bc[:, 0:16], in_=bias_dram))
            if not first:
                pb.wait_dma("scalar", "scalar", tag="s")
            pb.wait_dma("tensor", "gpsimd", tag="w")
            pb.wait_dma("scalar", "sync", tag="a")
            for ft in range(NC_D):
                for qh in range(2):
                    p = ps[(ft * 2 + qh) % 2]
                    for c in range(NC_D):
                        pb.emit("tensor", lambda h, ft=ft, qh=qh, c=c, p=p: nc.tensor.matmul(
                            out=p[:, 0:512],
                            lhsT=wa[:, c * D + ft * 128: c * D + (ft + 1) * 128],
                            rhs=xT[:, c * NOWN + qh * 512: c * NOWN + (qh + 1) * 512],
                            start=(c == 0), stop=(c == NC_D - 1)))
                    pb.barrier(["tensor"], ["scalar"])
                    pb.emit("scalar", lambda h, ft=ft, qh=qh, p=p: h.activation(
                        out=outT[:, ft * NOWN + qh * 512: ft * NOWN + qh * 512 + 512],
                        in_=p[:, 0:512], func=AF.Identity, bias=bc[:, ft:ft + 1], scale=1.0))
                    pb.barrier(["scalar"], ["tensor"])
            pb.barrier(["scalar"], ["scalar"])
            for c in range(NC_D):
                pb.dma("scalar", tag="s", fn=lambda h, c=c, od=out_dram: h.dma_start(
                    out=od[c * 128:(c + 1) * 128, :],
                    in_=outT[:, c * NOWN:(c + 1) * NOWN]))

        proj_T(wq_d, bq_d, qT_o, True)
        proj_T(wk_d, bk_d, kT_o, False)

        pb.barrier(["tensor"], ["gpsimd"])
        for c in range(NC_D):
            pb.dma("gpsimd", tag="w", fn=lambda h, c=c: h.dma_start(
                out=wa[:, c * D:(c + 1) * D], in_=wv_d[c * 128:(c + 1) * 128, :]))
        pb.wait_dma("scalar", "scalar", tag="s")
        pb.barrier(["scalar"], ["vector"])
        pb.wait_dma("tensor", "gpsimd", tag="w")
        for kt in range(8):
            for s4 in range(4):
                p = ps[(kt * 4 + s4) % 2]
                for c in range(NC_D):
                    pb.emit("tensor", lambda h, kt=kt, s4=s4, c=c, p=p: nc.tensor.matmul(
                        out=p[:, 0:512],
                        lhsT=xT[:, c * NOWN + kt * 128: c * NOWN + (kt + 1) * 128],
                        rhs=wa[:, c * D + s4 * 512: c * D + (s4 + 1) * 512],
                        start=(c == 0), stop=(c == NC_D - 1)))
                pb.barrier(["tensor"], ["vector"])
                pb.emit("vector", lambda h, kt=kt, s4=s4, p=p: h.tensor_tensor(
                    out=outT[:, kt * D + s4 * 512: kt * D + (s4 + 1) * 512],
                    in0=p[:, 0:512], in1=bvb[:, s4 * 512:(s4 + 1) * 512], op=ALU.add))
                pb.barrier(["vector"], ["tensor"])
        pb.barrier(["vector"], ["gpsimd"])
        for kt in range(8):
            pb.dma("gpsimd", tag="s", fn=lambda h, kt=kt: h.dma_start(
                out=v_o[kt * 128:(kt + 1) * 128, :], in_=outT[:, kt * D:(kt + 1) * D]))


        pb.wait_dma("scalar", "scalar", tag="s")
        pb.wait_dma("gpsimd", "gpsimd", tag="s")
        pb.finalize(block)
    return nc


def _bcols(b):
    n = b.shape[0] // 128
    return np.ascontiguousarray(np.asarray(b, dtype=np.float32).reshape(n, 128).T)






SHIFT1 = 12.0
ISQ_HD = 1.0 / float(np.sqrt(128.0))


def build_attn1():
    nc = bass.Bass("TRN2", target_bir_lowering=False)
    qT_d = nc.dram_tensor("qT", [D, NOWN], BF16, kind="ExternalInput").ap()
    kT_d = nc.dram_tensor("kTf", [D, S], BF16, kind="ExternalInput").ap()
    vblk_d = nc.dram_tensor("vblk", [NH, 128, S], BF16, kind="ExternalInput").ap()
    wo_d = nc.dram_tensor("wo", [D, D], BF16, kind="ExternalInput").ap()
    bo_d = nc.dram_tensor("bo", [1, D], F32, kind="ExternalInput").ap()
    xres_d = nc.dram_tensor("xres", [NOWN, D], BF16, kind="ExternalInput").ap()
    out_d = nc.dram_tensor("out", [NOWN, D], BF16, kind="ExternalOutput").ap()

    NKT = S // 128

    with ExitStack() as ctx:
        e = ctx.enter_context
        qT = e(nc.sbuf_tensor("qT_s", [128, NH * NOWN], BF16))
        pt = e(nc.sbuf_tensor("pt_s", [128, NKT * NOWN], BF16))
        ao = e(nc.sbuf_tensor("ao_s", [128, NH * NOWN], BF16))
        wos = e(nc.sbuf_tensor("wos", [128, NC_D * D], BF16))
        kv = e(nc.sbuf_tensor("kv_s", [128, 2 * S], BF16))
        xr = e(nc.sbuf_tensor("xr_s", [128, D], BF16))
        sout = e(nc.sbuf_tensor("sout", [128, D], BF16))
        bor = e(nc.sbuf_tensor("bor", [1, D], F32))
        bor_b = e(nc.sbuf_tensor("bor_b", [1, D], BF16))
        bob = e(nc.sbuf_tensor("bob", [128, D], BF16))
        ones1 = e(nc.sbuf_tensor("ones1", [1, 128], BF16))
        onesc = e(nc.sbuf_tensor("onesc", [128, 1], BF16))
        rrow = e(nc.sbuf_tensor("rrow", [1, NOWN], BF16))
        rrow32 = e(nc.sbuf_tensor("rrow32", [1, NOWN], F32))
        rbc = e(nc.sbuf_tensor("rbc", [128, NOWN], BF16))
        cols = e(nc.sbuf_tensor("cols", [128, 8], F32))
        shcol = e(nc.sbuf_tensor("shcol", [128, 1], F32))
        sc = e(nc.psum_tensor("sc", [128, 1024], F32))
        po = e(nc.psum_tensor("po", [128, 1024], F32))
        sums = e(nc.psum_tensor("sums", [1, 1024], F32))
        block = e(nc.Block())
        pb = PB(nc, ctx)

        for c in range(NC_D):
            pb.dma("sync", tag="a", fn=lambda h, c=c: h.dma_start(
                out=qT[:, c * NOWN:(c + 1) * NOWN], in_=qT_d[c * 128:(c + 1) * 128, :]))
        pb.dma("sync", tag="a", fn=lambda h: h.dma_start(out=bor[:], in_=bo_d))
        for c in range(NC_D):
            pb.dma("gpsimd", tag="w", fn=lambda h, c=c: h.dma_start(
                out=wos[:, c * D:(c + 1) * D], in_=wo_d[c * 128:(c + 1) * 128, :]))
        pb.emit("vector", lambda h: h.memset(ones1[:], 1.0))
        pb.emit("vector", lambda h: h.memset(onesc[:], 1.0))
        pb.emit("vector", lambda h: h.memset(shcol[:], -SHIFT1))
        pb.barrier(["vector"], ["scalar"])
        pb.wait_dma("vector", "sync", tag="a")
        pb.emit("vector", lambda h: h.tensor_copy(out=bor_b[:], in_=bor[:]))
        pb.barrier(["vector"], ["tensor"])
        for s4 in range(4):
            pb.emit("tensor", lambda h, s4=s4: nc.tensor.matmul(
                out=sc[:, 0:512], lhsT=ones1[:], rhs=bor_b[:, s4 * 512:(s4 + 1) * 512],
                start=True, stop=True))
            pb.barrier(["tensor"], ["vector"])
            pb.emit("vector", lambda h, s4=s4: h.tensor_copy(
                out=bob[:, s4 * 512:(s4 + 1) * 512], in_=sc[:, 0:512]))
            pb.barrier(["vector"], ["tensor"])

        for hh in range(NH):
            if hh > 0:
                pb.barrier(["tensor"], ["gpsimd"])
            pb.dma("gpsimd", tag="w", fn=lambda h, hh=hh: h.dma_start(
                out=kv[:, 0:S], in_=kT_d[hh * 128:(hh + 1) * 128, :]))
            pb.dma("gpsimd", tag="w", fn=lambda h, hh=hh: h.dma_start(
                out=kv[:, S:2 * S], in_=vblk_d[hh, :, :]))
            pb.wait_dma("tensor", "gpsimd", tag="w")
            for kt in range(NKT):
                for qh in range(2):
                    pb.emit("tensor", lambda h, hh=hh, kt=kt, qh=qh: nc.tensor.matmul(
                        out=sc[:, qh * 512:qh * 512 + 512],
                        lhsT=kv[:, kt * 128:(kt + 1) * 128],
                        rhs=qT[:, hh * NOWN + qh * 512: hh * NOWN + qh * 512 + 512],
                        start=True, stop=True))
                pb.barrier(["tensor"], ["scalar"])
                pb.emit("scalar", lambda h, kt=kt: h.activation(
                    out=pt[:, kt * NOWN:(kt + 1) * NOWN], in_=sc[:, 0:1024],
                    func=AF.Exp, bias=shcol[:, 0:1], scale=ISQ_HD))
                pb.barrier(["scalar"], ["tensor"])
                for qh in range(2):
                    pb.emit("tensor", lambda h, kt=kt, qh=qh: nc.tensor.matmul(
                        out=sums[0:1, qh * 512:qh * 512 + 512], lhsT=onesc[:, 0:1],
                        rhs=pt[:, kt * NOWN + qh * 512: kt * NOWN + qh * 512 + 512],
                        start=(kt == 0), stop=(kt == NKT - 1)))
                    pb.emit("tensor", lambda h, kt=kt, qh=qh: nc.tensor.matmul(
                        out=po[:, qh * 512:qh * 512 + 512],
                        lhsT=kv[:, S + kt * 128: S + (kt + 1) * 128],
                        rhs=pt[:, kt * NOWN + qh * 512: kt * NOWN + qh * 512 + 512],
                        start=(kt == 0), stop=(kt == NKT - 1)))
            pb.barrier(["tensor"], ["vector"])
            pb.barrier(["vector"], ["vector"])
            pb.emit("vector", lambda h: h.reciprocal(out=rrow32[:, 0:512], in_=sums[0:1, 0:512]))
            pb.emit("vector", lambda h: h.reciprocal(out=rrow32[:, 512:1024], in_=sums[0:1, 512:1024]))
            pb.barrier(["vector"], ["vector"])
            pb.emit("vector", lambda h: h.tensor_copy(out=rrow[:, 0:1024], in_=rrow32[:, 0:1024]))
            pb.barrier(["vector"], ["tensor"])
            for qh in range(2):
                pb.emit("tensor", lambda h, qh=qh: nc.tensor.matmul(
                    out=sc[:, qh * 512:qh * 512 + 512], lhsT=ones1[:],
                    rhs=rrow[:, qh * 512:qh * 512 + 512], start=True, stop=True))
            pb.barrier(["tensor"], ["vector"])
            pb.emit("vector", lambda h: h.tensor_copy(out=rbc[:, 0:1024], in_=sc[:, 0:1024]))
            pb.barrier(["vector"], ["vector"])
            for qh in range(2):
                pb.emit("vector", lambda h, hh=hh, qh=qh: h.tensor_tensor(
                    out=ao[:, hh * NOWN + qh * 512: hh * NOWN + qh * 512 + 512],
                    in0=po[:, qh * 512:qh * 512 + 512],
                    in1=rbc[:, qh * 512:qh * 512 + 512], op=ALU.mult))
            pb.barrier(["vector"], ["tensor"])

        pb.wait_dma("tensor", "gpsimd", tag="w")
        for qt in range(8):
            pb.barrier(["vector"], ["sync"])
            pb.dma("sync", tag="a", fn=lambda h, qt=qt: h.dma_start(
                out=xr[:, :], in_=xres_d[qt * 128:(qt + 1) * 128, :]))
            for s4 in range(4):
                for c in range(NC_D):
                    pb.emit("tensor", lambda h, qt=qt, s4=s4, c=c: nc.tensor.matmul(
                        out=sc[:, (s4 % 2) * 512:(s4 % 2) * 512 + 512],
                        lhsT=ao[:, c * NOWN + qt * 128: c * NOWN + (qt + 1) * 128],
                        rhs=wos[:, c * D + s4 * 512: c * D + (s4 + 1) * 512],
                        start=(c == 0), stop=(c == NC_D - 1)))
                pb.barrier(["tensor"], ["vector"])
                if s4 == 0:
                    pb.wait_dma("vector", "sync", tag="a")
                pb.emit("vector", lambda h, qt=qt, s4=s4: h.tensor_tensor(
                    out=sout[:, s4 * 512:(s4 + 1) * 512],
                    in0=sc[:, (s4 % 2) * 512:(s4 % 2) * 512 + 512],
                    in1=bob[:, s4 * 512:(s4 + 1) * 512], op=ALU.add))
                pb.barrier(["vector"], ["vector"])
                pb.emit("vector", lambda h, qt=qt, s4=s4: h.tensor_tensor(
                    out=sout[:, s4 * 512:(s4 + 1) * 512],
                    in0=sout[:, s4 * 512:(s4 + 1) * 512],
                    in1=xr[:, s4 * 512:(s4 + 1) * 512], op=ALU.add))
                pb.barrier(["vector"], ["tensor"])
            _emit_ln(pb, sout, pt, cols, qt, out_d)
        pb.wait_dma("gpsimd", "gpsimd", tag="s")
        pb.wait_dma("sync", "sync", tag="a")
        pb.finalize(block)
    return nc


def _emit_ln(pb, s_sb, scratch, cols, qt, out_d):
    nc = pb.nc
    V, SC = "vector", "scalar"
    pb.wait_dma(SC, "gpsimd", tag="s")
    pb.wait_dma(V, "gpsimd", tag="s")
    pb.barrier([V], [V])
    pb.emit(V, lambda h: h.reduce_sum(out=cols[:, 0:1], in_=s_sb[:, 0:D], axis=AX))
    pb.barrier([V], [SC, V])
    pb.emit(SC, lambda h: h.activation(
        out=scratch[:, 0:D], in_=s_sb[:, 0:D], func=AF.Square))
    pb.emit(V, lambda h: h.tensor_scalar_mul(out=cols[:, 2:3], in0=cols[:, 0:1],
                                             scalar1=1.0 / D))
    pb.barrier([SC, V], [V])
    pb.emit(V, lambda h: h.reduce_sum(out=cols[:, 1:2], in_=scratch[:, 0:D], axis=AX))
    pb.emit(V, lambda h: h.tensor_tensor(out=cols[:, 4:5], in0=cols[:, 2:3],
                                         in1=cols[:, 2:3], op=ALU.mult))
    pb.barrier([V], [V])
    pb.emit(V, lambda h: h.tensor_scalar(out=cols[:, 3:4], in0=cols[:, 1:2],
                                         scalar1=1.0 / D, scalar2=None,
                                         op0=ALU.mult))
    pb.barrier([V], [V])
    pb.emit(V, lambda h: h.tensor_tensor(out=cols[:, 4:5], in0=cols[:, 3:4],
                                         in1=cols[:, 4:5], op=ALU.subtract))
    pb.barrier([V], [V])
    pb.emit(V, lambda h: h.tensor_scalar_add(out=cols[:, 5:6], in0=cols[:, 4:5],
                                             scalar1=1e-5))
    pb.barrier([V], [SC])
    pb.emit(SC, lambda h: h.activation(out=cols[:, 5:6], in_=cols[:, 5:6],
                                       func=AF.Sqrt))
    pb.barrier([SC], [V])
    pb.emit(V, lambda h: h.reciprocal(out=cols[:, 6:7], in_=cols[:, 5:6]))
    pb.barrier([V], [V])
    pb.emit(V, lambda h: h.tensor_scalar(
        out=scratch[:, 0:D], in0=s_sb[:, 0:D], scalar1=cols[:, 2:3],
        scalar2=cols[:, 6:7], op0=ALU.subtract, op1=ALU.mult))
    pb.barrier([V], ["gpsimd"])
    pb.dma("gpsimd", tag="s", fn=lambda h: h.dma_start(
        out=out_d[qt * 128:(qt + 1) * 128, :], in_=scratch[:, 0:D]))



SHIFT0 = 6.0


def build_attn0():
    nc = bass.Bass("TRN2", target_bir_lowering=False)
    qnT_d = nc.dram_tensor("qnT", [NH * NOPE, NOWN], BF16, kind="ExternalInput").ap()
    qrT_d = nc.dram_tensor("qrT", [NH * ROPE, NOWN], BF16, kind="ExternalInput").ap()
    ckvT_d = nc.dram_tensor("ckvT", [KV_LORA + ROPE, S], BF16, kind="ExternalInput").ap()
    wkvb_d = nc.dram_tensor("wkvb", [KV_LORA, NH * 256], F32, kind="ExternalInput").ap()
    bkv_d = nc.dram_tensor("bkv", [128, 32], F32, kind="ExternalInput").ap()
    bvrow_d = nc.dram_tensor("bvrow", [1, NH * V_HD], F32, kind="ExternalInput").ap()
    wo_d = nc.dram_tensor("wo", [D, D], F32, kind="ExternalInput").ap()
    bo_d = nc.dram_tensor("bo", [1, D], F32, kind="ExternalInput").ap()
    xres_d = nc.dram_tensor("xres", [NOWN, D], F32, kind="ExternalInput").ap()
    out_d = nc.dram_tensor("out", [NOWN, D], BF16, kind="ExternalOutput").ap()

    NKT = S // 128
    NCC = KV_LORA // 128

    with ExitStack() as ctx:
        e = ctx.enter_context
        qnT = e(nc.sbuf_tensor("qnT_s", [128, NH * NOWN], BF16))
        qrT = e(nc.sbuf_tensor("qrT_s", [128, 8 * NOWN], BF16))
        ckv = e(nc.sbuf_tensor("ckv_s", [128, 5 * S], BF16))
        wkv = e(nc.sbuf_tensor("wkv_s", [128, NCC * 4096], BF16))
        pt = e(nc.sbuf_tensor("pt_s", [128, NKT * NOWN], BF16))
        ao = e(nc.sbuf_tensor("ao_s", [128, NH * NOWN], BF16))
        kh = e(nc.sbuf_tensor("kh_s", [128, S], BF16))
        vh = e(nc.sbuf_tensor("vh_s", [128, S], BF16))
        bkc = e(nc.sbuf_tensor("bkc", [128, 32], F32))
        bvrb = e(nc.sbuf_tensor("bvrb", [1, 2048], BF16))
        bvb = e(nc.sbuf_tensor("bvb", [128, 2048], BF16))
        borb = e(nc.sbuf_tensor("borb", [1, D], BF16))
        bob = e(nc.sbuf_tensor("bob", [128, D], BF16))
        ones1 = e(nc.sbuf_tensor("ones1", [1, 128], BF16))
        onesc = e(nc.sbuf_tensor("onesc", [128, 1], BF16))
        shcol = e(nc.sbuf_tensor("shcol", [128, 1], F32))
        rrow = e(nc.sbuf_tensor("rrow", [1, NOWN], BF16))
        rrow32 = e(nc.sbuf_tensor("rrow32", [1, NOWN], F32))
        rbc = e(nc.sbuf_tensor("rbc", [128, NOWN], BF16))
        cols = e(nc.sbuf_tensor("cols", [128, 8], F32))
        sc = e(nc.psum_tensor("sc", [128, 1024], F32))
        po = e(nc.psum_tensor("po", [128, 1024], F32))
        sums = e(nc.psum_tensor("sums", [1, 1024], F32))
        pk = e(nc.psum_tensor("pk", [128, 1024], F32))
        block = e(nc.Block())
        pb = PB(nc, ctx)

        for c in range(NH):
            pb.dma("sync", tag="a", fn=lambda h, c=c: h.dma_start(
                out=qnT[:, c * NOWN:(c + 1) * NOWN], in_=qnT_d[c * 128:(c + 1) * 128, :]))
        for c in range(8):
            pb.dma("sync", tag="a", fn=lambda h, c=c: h.dma_start(
                out=qrT[:, c * NOWN:(c + 1) * NOWN], in_=qrT_d[c * 128:(c + 1) * 128, :]))
        for c in range(4):
            pb.dma("sync", tag="a", fn=lambda h, c=c: h.dma_start(
                out=ckv[:, c * S:(c + 1) * S], in_=ckvT_d[c * 128:(c + 1) * 128, :]))
        pb.dma("sync", tag="a", fn=lambda h: h.dma_start(
            out=ckv[0:64, 4 * S:5 * S], in_=ckvT_d[512:576, :]))
        pb.dma("sync", tag="a", fn=lambda h: h.dma_start(
            out=ckv[64:128, 4 * S:5 * S], in_=ckvT_d[512:576, :]))
        pb.dma("sync", tag="a", fn=lambda h: h.dma_start(out=bkc[:], in_=bkv_d))
        pb.dma("gpsimd", tag="a", fn=lambda h: h.dma_start(out=bvrb[:], in_=bvrow_d))
        pb.dma("gpsimd", tag="a", fn=lambda h: h.dma_start(out=borb[:], in_=bo_d))
        for c in range(NCC):
            pb.dma("gpsimd", tag="w", fn=lambda h, c=c: h.dma_start(
                out=wkv[:, c * 4096:(c + 1) * 4096], in_=wkvb_d[c * 128:(c + 1) * 128, :]))
        pb.emit("vector", lambda h: h.memset(ones1[:], 1.0))
        pb.emit("vector", lambda h: h.memset(onesc[:], 1.0))
        pb.emit("vector", lambda h: h.memset(shcol[:], -SHIFT0))
        pb.barrier(["vector"], ["scalar"])
        pb.wait_dma("vector", "sync", tag="a")
        pb.barrier(["vector"], ["tensor"])
        pb.wait_dma("tensor", "gpsimd", tag="w")
        for (row, dst) in ((bvrb, bvb), (borb, bob)):
            for s4 in range(4):
                pb.emit("tensor", lambda h, s4=s4, row=row: nc.tensor.matmul(
                    out=pk[:, 0:512], lhsT=ones1[:], rhs=row[:, s4 * 512:(s4 + 1) * 512],
                    start=True, stop=True))
                pb.barrier(["tensor"], ["vector"])
                pb.emit("vector", lambda h, s4=s4, dst=dst: h.tensor_copy(
                    out=dst[:, s4 * 512:(s4 + 1) * 512], in_=pk[:, 0:512]))
                pb.barrier(["vector"], ["tensor"])

        pb.wait_dma("tensor", "gpsimd", tag="w")
        pb.wait_dma("scalar", "sync", tag="a")

        for hh in range(NH):
            if hh > 0:
                pb.barrier(["tensor"], ["scalar", "vector"])
            for ks in range(4):
                for c in range(NCC):
                    pb.emit("tensor", lambda h, hh=hh, ks=ks, c=c: nc.tensor.matmul(
                        out=pk[:, (ks % 2) * 512:(ks % 2) * 512 + 512],
                        lhsT=wkv[:, c * 4096 + hh * 256: c * 4096 + hh * 256 + 128],
                        rhs=ckv[:, c * S + ks * 512: c * S + ks * 512 + 512],
                        start=(c == 0), stop=(c == NCC - 1)))
                pb.barrier(["tensor"], ["scalar"])
                pb.emit("scalar", lambda h, hh=hh, ks=ks: h.activation(
                    out=kh[:, ks * 512:(ks + 1) * 512],
                    in_=pk[:, (ks % 2) * 512:(ks % 2) * 512 + 512],
                    func=AF.Identity, bias=bkc[:, 2 * hh:2 * hh + 1], scale=1.0))
                pb.barrier(["scalar"], ["tensor"])
            for kt in range(NKT):
                for c in range(NCC):
                    pb.emit("tensor", lambda h, hh=hh, kt=kt, c=c: nc.tensor.matmul(
                        out=pk[:, (kt % 4) * 128 + 512:(kt % 4) * 128 + 640],
                        lhsT=ckv[:, c * S + kt * 128: c * S + (kt + 1) * 128],
                        rhs=wkv[:, c * 4096 + hh * 256 + 128: c * 4096 + hh * 256 + 256],
                        start=(c == 0), stop=(c == NCC - 1)))
                pb.barrier(["tensor"], ["vector"])
                pb.emit("vector", lambda h, hh=hh, kt=kt: h.tensor_tensor(
                    out=vh[:, kt * 128:(kt + 1) * 128],
                    in0=pk[:, (kt % 4) * 128 + 512:(kt % 4) * 128 + 640],
                    in1=bvb[:, hh * 128:(hh + 1) * 128], op=ALU.add))
                pb.barrier(["vector"], ["tensor"])
            pb.barrier(["scalar", "vector"], ["tensor"])
            for kt in range(NKT):
                for qh in range(2):
                    pb.emit("tensor", lambda h, hh=hh, kt=kt, qh=qh: nc.tensor.matmul(
                        out=sc[:, qh * 512:qh * 512 + 512],
                        lhsT=kh[:, kt * 128:(kt + 1) * 128],
                        rhs=qnT[:, hh * NOWN + qh * 512: hh * NOWN + qh * 512 + 512],
                        start=True, stop=False))
                    pb.emit("tensor", lambda h, hh=hh, kt=kt, qh=qh: nc.tensor.matmul(
                        out=sc[:, qh * 512:qh * 512 + 512],
                        lhsT=ckv[64 * (hh % 2):64 * (hh % 2) + 64,
                                 4 * S + kt * 128: 4 * S + (kt + 1) * 128],
                        rhs=qrT[64 * (hh % 2):64 * (hh % 2) + 64,
                                (hh // 2) * NOWN + qh * 512: (hh // 2) * NOWN + qh * 512 + 512],
                        start=False, stop=True))
                pb.barrier(["tensor"], ["scalar"])
                pb.emit("scalar", lambda h, kt=kt: h.activation(
                    out=pt[:, kt * NOWN:(kt + 1) * NOWN], in_=sc[:, 0:1024],
                    func=AF.Exp, bias=shcol[:, 0:1], scale=ISQ_HD))
                pb.barrier(["scalar"], ["tensor"])
                for qh in range(2):
                    pb.emit("tensor", lambda h, kt=kt, qh=qh: nc.tensor.matmul(
                        out=sums[0:1, qh * 512:qh * 512 + 512], lhsT=onesc[:, 0:1],
                        rhs=pt[:, kt * NOWN + qh * 512: kt * NOWN + qh * 512 + 512],
                        start=(kt == 0), stop=(kt == NKT - 1)))
                    pb.emit("tensor", lambda h, kt=kt, qh=qh: nc.tensor.matmul(
                        out=po[:, qh * 512:qh * 512 + 512],
                        lhsT=vh[:, kt * 128:(kt + 1) * 128],
                        rhs=pt[:, kt * NOWN + qh * 512: kt * NOWN + qh * 512 + 512],
                        start=(kt == 0), stop=(kt == NKT - 1)))
            pb.barrier(["tensor"], ["vector"])
            pb.barrier(["vector"], ["vector"])
            pb.emit("vector", lambda h: h.reciprocal(out=rrow32[:, 0:512], in_=sums[0:1, 0:512]))
            pb.emit("vector", lambda h: h.reciprocal(out=rrow32[:, 512:1024], in_=sums[0:1, 512:1024]))
            pb.barrier(["vector"], ["vector"])
            pb.emit("vector", lambda h: h.tensor_copy(out=rrow[:, 0:1024], in_=rrow32[:, 0:1024]))
            pb.barrier(["vector"], ["tensor"])
            for qh in range(2):
                pb.emit("tensor", lambda h, qh=qh: nc.tensor.matmul(
                    out=sc[:, qh * 512:qh * 512 + 512], lhsT=ones1[:],
                    rhs=rrow[:, qh * 512:qh * 512 + 512], start=True, stop=True))
            pb.barrier(["tensor"], ["vector"])
            pb.emit("vector", lambda h: h.tensor_copy(out=rbc[:, 0:1024], in_=sc[:, 0:1024]))
            pb.barrier(["vector"], ["vector"])
            for qh in range(2):
                pb.emit("vector", lambda h, hh=hh, qh=qh: h.tensor_tensor(
                    out=ao[:, hh * NOWN + qh * 512: hh * NOWN + qh * 512 + 512],
                    in0=po[:, qh * 512:qh * 512 + 512],
                    in1=rbc[:, qh * 512:qh * 512 + 512], op=ALU.mult))
            pb.barrier(["vector"], ["tensor"])

        wos = qnT
        for s4 in range(4):
            pb.barrier(["tensor"], ["gpsimd"])
            for c in range(NC_D):
                pb.dma("gpsimd", tag="w", fn=lambda h, c=c, s4=s4: h.dma_start(
                    out=wos[:, c * 512:(c + 1) * 512],
                    in_=wo_d[c * 128:(c + 1) * 128, s4 * 512:(s4 + 1) * 512]))
            pb.wait_dma("tensor", "gpsimd", tag="w")
            for qt in range(8):
                p_half = (qt % 2) * 512
                for c in range(NC_D):
                    pb.emit("tensor", lambda h, qt=qt, s4=s4, c=c, p_half=p_half: nc.tensor.matmul(
                        out=sc[:, p_half:p_half + 512],
                        lhsT=ao[:, c * NOWN + qt * 128: c * NOWN + (qt + 1) * 128],
                        rhs=wos[:, c * 512:(c + 1) * 512],
                        start=(c == 0), stop=(c == NC_D - 1)))
                pb.barrier(["tensor"], ["vector"])
                pb.emit("vector", lambda h, qt=qt, s4=s4, p_half=p_half: h.tensor_tensor(
                    out=pt[:, qt * 2048 + s4 * 512: qt * 2048 + s4 * 512 + 512],
                    in0=sc[:, p_half:p_half + 512],
                    in1=bob[:, s4 * 512:(s4 + 1) * 512], op=ALU.add))
                pb.barrier(["vector"], ["tensor"])

        for qt in range(8):
            pb.barrier(["vector"], ["gpsimd"])
            pb.dma("gpsimd", tag="a", fn=lambda h, qt=qt: h.dma_start(
                out=vh[:, 0:D], in_=xres_d[qt * 128:(qt + 1) * 128, :]))
            pb.wait_dma("vector", "gpsimd", tag="a")
            pb.barrier(["vector"], ["vector"])
            pb.emit("vector", lambda h, qt=qt: h.tensor_tensor(
                out=pt[:, qt * 2048: qt * 2048 + 2048],
                in0=pt[:, qt * 2048: qt * 2048 + 2048],
                in1=vh[:, 0:D], op=ALU.add))
            _emit_ln2(pb, pt, qt * 2048, kh, cols, qt, out_d)
        pb.wait_dma("gpsimd", "gpsimd", tag="s")
        pb.wait_dma("sync", "sync", tag="a")
        pb.finalize(block)
    return nc


def _emit_ln2(pb, s_sb, s_off, scratch, cols, qt, out_d):
    nc = pb.nc
    V, SC = "vector", "scalar"
    pb.wait_dma(SC, "gpsimd", tag="s")
    pb.wait_dma(V, "gpsimd", tag="s")
    pb.barrier([V], [V])
    pb.emit(V, lambda h: h.reduce_sum(out=cols[:, 0:1], in_=s_sb[:, s_off:s_off + D], axis=AX))
    pb.barrier([V], [SC, V])
    pb.emit(SC, lambda h: h.activation(
        out=scratch[:, 0:D], in_=s_sb[:, s_off:s_off + D], func=AF.Square))
    pb.emit(V, lambda h: h.tensor_scalar_mul(out=cols[:, 2:3], in0=cols[:, 0:1],
                                             scalar1=1.0 / D))
    pb.barrier([SC, V], [V])
    pb.emit(V, lambda h: h.reduce_sum(out=cols[:, 1:2], in_=scratch[:, 0:D], axis=AX))
    pb.emit(V, lambda h: h.tensor_tensor(out=cols[:, 4:5], in0=cols[:, 2:3],
                                         in1=cols[:, 2:3], op=ALU.mult))
    pb.barrier([V], [V])
    pb.emit(V, lambda h: h.tensor_scalar(out=cols[:, 3:4], in0=cols[:, 1:2],
                                         scalar1=1.0 / D, scalar2=None, op0=ALU.mult))
    pb.barrier([V], [V])
    pb.emit(V, lambda h: h.tensor_tensor(out=cols[:, 4:5], in0=cols[:, 3:4],
                                         in1=cols[:, 4:5], op=ALU.subtract))
    pb.barrier([V], [V])
    pb.emit(V, lambda h: h.tensor_scalar_add(out=cols[:, 5:6], in0=cols[:, 4:5],
                                             scalar1=1e-5))
    pb.barrier([V], [SC])
    pb.emit(SC, lambda h: h.activation(out=cols[:, 5:6], in_=cols[:, 5:6], func=AF.Sqrt))
    pb.barrier([SC], [V])
    pb.emit(V, lambda h: h.reciprocal(out=cols[:, 6:7], in_=cols[:, 5:6]))
    pb.barrier([V], [V])
    pb.emit(V, lambda h: h.tensor_scalar(
        out=scratch[:, 0:D], in0=s_sb[:, s_off:s_off + D], scalar1=cols[:, 2:3],
        scalar2=cols[:, 6:7], op0=ALU.subtract, op1=ALU.mult))
    pb.barrier([V], ["gpsimd"])
    pb.dma("gpsimd", tag="s", fn=lambda h: h.dma_start(
        out=out_d[qt * 128:(qt + 1) * 128, :], in_=scratch[:, 0:D]))



def build_ffn0():
    nc = bass.Bass("TRN2", target_bir_lowering=False)
    x_d = nc.dram_tensor("x", [NOWN, D], BF16, kind="ExternalInput").ap()
    xT_d = nc.dram_tensor("xT", [D, NOWN], BF16, kind="ExternalInput").ap()
    wgu_d = nc.dram_tensor("wgu", [44, NC_D, 128, 256], BF16, kind="ExternalInput").ap()
    wd_d = nc.dram_tensor("wd", [I0, D], BF16, kind="ExternalInput").ap()
    out_d = nc.dram_tensor("out", [NOWN, D], BF16, kind="ExternalOutput").ap()
    NSP = 44

    with ExitStack() as ctx:
        e = ctx.enter_context
        xT = e(nc.sbuf_tensor("xT_s", [128, NC_D * NOWN], BF16))
        hb = e(nc.sbuf_tensor("hb_s", [128, NSP * NOWN], BF16))
        wg = e(nc.sbuf_tensor("wg_s", [128, NC_D * 256], BF16))
        wd = e(nc.sbuf_tensor("wd_s", [128, D], BF16))
        tg = e(nc.sbuf_tensor("tg_s", [128, NOWN], BF16))
        yb = e(nc.sbuf_tensor("yb_s", [128, D], BF16))
        xc = e(nc.sbuf_tensor("xc_s", [128, D], BF16))
        scr = e(nc.sbuf_tensor("scr_s", [128, D], BF16))
        cols = e(nc.sbuf_tensor("cols", [128, 12], F32))
        pgu = e(nc.psum_tensor("pgu", [128, 2048], F32))
        pd = e(nc.psum_tensor("pd", [128, 2048], F32))
        pg = pgu
        pd2 = pgu
        block = e(nc.Block())
        pb = PB(nc, ctx)

        for c in range(NC_D):
            pb.dma("sync", tag="a", fn=lambda h, c=c: h.dma_start(
                out=xT[:, c * NOWN:(c + 1) * NOWN], in_=xT_d[c * 128:(c + 1) * 128, :]))
        pb.wait_dma("tensor", "sync", tag="a")

        for s in range(NSP):
            if s > 0:
                pb.barrier(["tensor"], ["gpsimd"])
            for cc in range(NC_D):
                pb.dma("gpsimd", tag="w", fn=lambda h, s=s, cc=cc: h.dma_start(
                    out=wg[:, cc * 256:(cc + 1) * 256], in_=wgu_d[s, cc]))
            pb.wait_dma("tensor", "gpsimd", tag="w")
            for qh in range(2):
                for cc in range(NC_D):
                    pb.emit("tensor", lambda h, s=s, qh=qh, cc=cc: nc.tensor.matmul(
                        out=pg[:, qh * 512:qh * 512 + 512],
                        lhsT=wg[:, cc * 256:cc * 256 + 128],
                        rhs=xT[:, cc * NOWN + qh * 512: cc * NOWN + qh * 512 + 512],
                        start=(cc == 0), stop=(cc == NC_D - 1)))
                for cc in range(NC_D):
                    pb.emit("tensor", lambda h, s=s, qh=qh, cc=cc: nc.tensor.matmul(
                        out=pgu[:, 1024 + qh * 512:1024 + qh * 512 + 512],
                        lhsT=wg[:, cc * 256 + 128:cc * 256 + 256],
                        rhs=xT[:, cc * NOWN + qh * 512: cc * NOWN + qh * 512 + 512],
                        start=(cc == 0), stop=(cc == NC_D - 1)))
            pb.barrier(["tensor"], ["scalar"])
            pb.emit("scalar", lambda h: h.activation(
                out=tg[:, 0:NOWN], in_=pg[:, 0:NOWN], func=AF.Silu))
            pb.barrier(["scalar"], ["vector"])
            pb.emit("vector", lambda h, s=s: h.tensor_tensor(
                out=hb[:, s * NOWN:(s + 1) * NOWN], in0=pgu[:, 1024:1024 + NOWN],
                in1=tg[:, 0:NOWN], op=ALU.mult))
            pb.barrier(["vector"], ["tensor"])

        for qtp in range(4):
            for ic in range(NSP):
                pb.barrier(["tensor"], ["gpsimd"])
                pb.dma("gpsimd", tag="w", fn=lambda h, ic=ic: h.dma_start(
                    out=wd[:, :], in_=wd_d[ic * 128:(ic + 1) * 128, :]))
                pb.wait_dma("tensor", "gpsimd", tag="w")
                for qt2 in range(2):
                    p = pd if qt2 == 0 else pd2
                    qt = qtp * 2 + qt2
                    for os in range(4):
                        pb.emit("tensor", lambda h, ic=ic, qt=qt, os=os, p=p: nc.tensor.matmul(
                            out=p[:, os * 512:(os + 1) * 512],
                            lhsT=hb[:, ic * NOWN + qt * 128: ic * NOWN + (qt + 1) * 128],
                            rhs=wd[:, os * 512:(os + 1) * 512],
                            start=(ic == 0), stop=(ic == NSP - 1)))
            for qt2 in range(2):
                p = pd if qt2 == 0 else pd2
                qt = qtp * 2 + qt2
                pb.barrier(["tensor"], ["vector", "gpsimd"])
                pb.wait_dma("vector", "gpsimd", tag="a")
                pb.barrier(["vector"], ["vector", "gpsimd"])
                pb.dma("gpsimd", tag="a", fn=lambda h, qt=qt: h.dma_start(
                    out=xc[:, :], in_=x_d[qt * 128:(qt + 1) * 128, :]))
                pb.emit("vector", lambda h, p=p: h.tensor_copy(out=yb[:, 0:D], in_=p[:, 0:D]))
                pb.barrier(["vector"], ["scalar", "vector"])
                pb.wait_dma("scalar", "gpsimd", tag="s")
                pb.emit("scalar", lambda h: h.activation(out=scr[:, 0:D], in_=yb[:, 0:D],
                                                         func=AF.Square))
                pb.barrier(["scalar"], ["vector"])
                pb.emit("vector", lambda h: h.reduce_sum(out=cols[:, 8:9], in_=scr[:, 0:D], axis=AX))
                pb.barrier(["vector"], ["vector"])
                pb.emit("vector", lambda h: h.tensor_scalar(
                    out=cols[:, 9:10], in0=cols[:, 8:9], scalar1=1.0 / D, scalar2=1e-6,
                    op0=ALU.mult, op1=ALU.add))
                pb.barrier(["vector"], ["scalar"])
                pb.emit("scalar", lambda h: h.activation(out=cols[:, 9:10], in_=cols[:, 9:10],
                                                         func=AF.Sqrt))
                pb.barrier(["scalar"], ["vector"])
                pb.emit("vector", lambda h: h.reciprocal(out=cols[:, 10:11], in_=cols[:, 9:10]))
                pb.wait_dma("vector", "gpsimd", tag="a")
                pb.barrier(["vector"], ["vector"])
                pb.emit("vector", lambda h: h.scalar_tensor_tensor(
                    out=yb[:, 0:D], in0=yb[:, 0:D], scalar=cols[:, 10:11], in1=xc[:, 0:D],
                    op0=ALU.mult, op1=ALU.add))
                _emit_ln2(pb, yb, 0, scr, cols, qt, out_d)
        pb.wait_dma("gpsimd", "gpsimd", tag="s")
        pb.finalize(block)
    return nc



FFN1_DEBUG = False


def build_ffn1():
    nc = bass.Bass("TRN2", target_bir_lowering=False)
    _ikind = "ExternalOutput" if FFN1_DEBUG else "Internal"
    x_d = nc.dram_tensor("x", [NOWN, D], BF16, kind="ExternalInput").ap()
    wgu_d = nc.dram_tensor("wgu", [16, NC_D, 128, 1024], BF16, kind="ExternalInput").ap()
    wd_d = nc.dram_tensor("wd", [F1, D], BF16, kind="ExternalInput").ap()
    sgu_d = nc.dram_tensor("sgu", [1, 1], F32, kind="ExternalInput").ap()
    swd_d = nc.dram_tensor("swd", [1, 1], F32, kind="ExternalInput").ap()
    out_d = nc.dram_tensor("out", [NOWN, D], F32, kind="ExternalOutput").ap()
    xq_dram = nc.dram_tensor("xq_i", [NOWN, D], BF16, kind=_ikind).ap()
    hq_dram = nc.dram_tensor("hq_i", [NOWN, F1], BF16, kind=_ikind).ap()
    NIC = F1 // 128

    with ExitStack() as ctx:
        e = ctx.enter_context
        x3 = e(nc.sbuf_tensor("x3_s", [128, 8 * D], BF16))
        xqT = e(nc.sbuf_tensor("xqT_s", [128, NC_D * NOWN], BF16))
        wgs = e(nc.sbuf_tensor("wgs_s", [128, NC_D * 1024], BF16))
        hqt = e(nc.sbuf_tensor("hqt_s", [128, NOWN], BF16))
        wdc = e(nc.sbuf_tensor("wdc_s", [128, D], BF16))
        f32a = e(nc.sbuf_tensor("f32a", [128, D], F32))
        f32b = e(nc.sbuf_tensor("f32b", [128, D], F32))
        bfa = e(nc.sbuf_tensor("bfa", [128, F1 // 4], BF16))
        cols = e(nc.sbuf_tensor("colsq", [128, 28], F32))
        sca = e(nc.sbuf_tensor("sca", [128, 2], F32))
        ones1 = e(nc.sbuf_tensor("ones1q", [1, 128], BF16))
        srow = e(nc.sbuf_tensor("srow", [1, 2], F32))
        srowb = e(nc.sbuf_tensor("srowb", [1, 2], BF16))
        pg = e(nc.psum_tensor("pgq", [128, 2048], F32))
        pd = e(nc.psum_tensor("pdq", [128, 2048], F32))
        pd2 = pg
        block = e(nc.Block())
        pb = PB(nc, ctx)

        for c in range(8):
            pb.dma("sync", tag="a", fn=lambda h, c=c: h.dma_start(
                out=x3[:, c * D:(c + 1) * D], in_=x_d[c * 128:(c + 1) * 128, :]))
        pb.dma("sync", tag="a", fn=lambda h: h.dma_start(out=srow[0:1, 0:1], in_=sgu_d))
        pb.dma("sync", tag="a", fn=lambda h: h.dma_start(out=srow[0:1, 1:2], in_=swd_d))
        pb.emit("vector", lambda h: h.memset(ones1[:], 1.0))
        pb.wait_dma("vector", "sync", tag="a")
        pb.emit("vector", lambda h: h.tensor_copy(out=srowb[:], in_=srow[:]))
        pb.barrier(["vector"], ["tensor"])
        pb.emit("tensor", lambda h: nc.tensor.matmul(
            out=pg[:, 0:2], lhsT=ones1[:], rhs=srowb[0:1, 0:2], start=True, stop=True))
        pb.barrier(["tensor"], ["vector"])
        pb.emit("vector", lambda h: h.tensor_copy(out=sca[:, 0:2], in_=pg[:, 0:2]))
        pb.barrier(["vector"], ["vector", "tensor"])

        for qt in range(8):
            pb.emit("vector", lambda h, qt=qt: h.reduce_max(
                out=cols[:, qt:qt + 1], in_=x3[:, qt * D:(qt + 1) * D], axis=AX,
                apply_absolute_value=True))
        pb.barrier(["vector"], ["vector"])
        pb.emit("vector", lambda h: h.tensor_scalar_max(out=cols[:, 0:8], in0=cols[:, 0:8],
                                                        scalar1=1e-5))
        pb.barrier(["vector"], ["vector"])
        pb.emit("vector", lambda h: h.tensor_scalar(
            out=cols[:, 8:16], in0=cols[:, 0:8], scalar1=sca[:, 0:1], scalar2=1.0 / 127,
            op0=ALU.mult, op1=ALU.mult))
        pb.emit("vector", lambda h: h.reciprocal(out=cols[:, 16:24], in_=cols[:, 0:8]))
        pb.barrier(["vector"], ["vector"])
        pb.emit("vector", lambda h: h.tensor_scalar_mul(out=cols[:, 16:24], in0=cols[:, 16:24],
                                                        scalar1=127.0))
        pb.barrier(["vector"], ["vector"])
        for qt in range(8):
            pb.wait_dma("vector", "gpsimd", tag="s")
            pb.emit("vector", lambda h, qt=qt: h.tensor_scalar_mul(
                out=f32a[:, 0:D], in0=x3[:, qt * D:(qt + 1) * D], scalar1=cols[:, 16 + qt:17 + qt]))
            pb.barrier(["vector"], ["vector"])
            iv = f32b.bitcast(I32)
            pb.emit("vector", lambda h, iv=iv: h.tensor_copy(out=iv, in_=f32a[:, 0:D]))
            pb.barrier(["vector"], ["vector"])
            pb.emit("vector", lambda h, iv=iv: h.tensor_copy(out=bfa[:, 0:D], in_=iv))
            pb.barrier(["vector"], ["gpsimd"])
            pb.dma("gpsimd", tag="s", fn=lambda h, qt=qt: h.dma_start(
                out=xq_dram[qt * 128:(qt + 1) * 128, :], in_=bfa[:, 0:D]))
        pb.wait_dma("sync", "gpsimd", tag="s")
        for c in range(NC_D):
            pb.dma("sync", tag="t", fn=lambda h, c=c: h.dma_start(
                out=xqT[:, c * NOWN:(c + 1) * NOWN],
                in_=xq_dram[:, c * 128:(c + 1) * 128], transpose=True))
        pb.wait_dma("tensor", "sync", tag="t")

        for s in range(16):
            pb.barrier(["tensor"], ["gpsimd"])
            for cc in range(NC_D):
                pb.dma("gpsimd", tag="w", fn=lambda h, s=s, cc=cc: h.dma_start(
                    out=wgs[:, cc * 1024:(cc + 1) * 1024], in_=wgu_d[s, cc]))
            pb.wait_dma("tensor", "gpsimd", tag="w")
            for qt in range(8):
                for cc in range(NC_D):
                    pb.emit("tensor", lambda h, s=s, qt=qt, cc=cc: nc.tensor.matmul(
                        out=pg[:, 0:512],
                        lhsT=xqT[:, cc * NOWN + qt * 128: cc * NOWN + (qt + 1) * 128],
                        rhs=wgs[:, cc * 1024:cc * 1024 + 512],
                        start=(cc == 0), stop=(cc == NC_D - 1)))
                for cc in range(NC_D):
                    pb.emit("tensor", lambda h, s=s, qt=qt, cc=cc: nc.tensor.matmul(
                        out=pg[:, 512:1024],
                        lhsT=xqT[:, cc * NOWN + qt * 128: cc * NOWN + (qt + 1) * 128],
                        rhs=wgs[:, cc * 1024 + 512:cc * 1024 + 1024],
                        start=(cc == 0), stop=(cc == NC_D - 1)))
                pb.barrier(["tensor"], ["vector"])
                pb.barrier(["vector"], ["vector"])
                pb.emit("vector", lambda h, qt=qt: h.tensor_scalar_mul(
                    out=f32a[:, 0:512], in0=pg[:, 0:512], scalar1=cols[:, 8 + qt:9 + qt]))
                pb.emit("vector", lambda h, qt=qt: h.tensor_scalar_mul(
                    out=f32a[:, 512:1024], in0=pg[:, 512:1024], scalar1=cols[:, 8 + qt:9 + qt]))
                pb.barrier(["vector"], ["scalar", "tensor"])
                pb.emit("scalar", lambda h: h.activation(
                    out=f32b[:, 0:512], in_=f32a[:, 0:512], func=AF.Silu))
                pb.barrier(["scalar"], ["vector"])
                pb.wait_dma("vector", "gpsimd", tag="s")
                pb.emit("vector", lambda h: h.tensor_tensor(
                    out=bfa[:, 0:512], in0=f32b[:, 0:512], in1=f32a[:, 512:1024],
                    op=ALU.mult))
                pb.barrier(["vector"], ["vector"])
                pb.emit("vector", lambda h, s=s, qt=qt: h.reduce_max(
                    out=cols[:, 24:25], in_=bfa[:, 0:512], axis=AX,
                    apply_absolute_value=True))
                pb.barrier(["vector"], ["vector"])
                if s == 0:
                    pb.emit("vector", lambda h, qt=qt: h.tensor_copy(
                        out=cols[:, qt:qt + 1], in_=cols[:, 24:25]))
                else:
                    pb.emit("vector", lambda h, qt=qt: h.tensor_tensor(
                        out=cols[:, qt:qt + 1], in0=cols[:, qt:qt + 1],
                        in1=cols[:, 24:25], op=ALU.max))
                pb.barrier(["vector"], ["gpsimd"])
                pb.dma("gpsimd", tag="s", fn=lambda h, s=s, qt=qt: h.dma_start(
                    out=hq_dram[qt * 128:(qt + 1) * 128, s * 512:(s + 1) * 512],
                    in_=bfa[:, 0:512]))

        pb.barrier(["vector"], ["vector"])
        pb.emit("vector", lambda h: h.tensor_scalar_max(out=cols[:, 0:8], in0=cols[:, 0:8],
                                                        scalar1=1e-5))
        pb.barrier(["vector"], ["vector"])
        pb.emit("vector", lambda h: h.tensor_scalar(
            out=cols[:, 8:16], in0=cols[:, 0:8], scalar1=sca[:, 1:2], scalar2=1.0 / 127,
            op0=ALU.mult, op1=ALU.mult))
    
        pb.emit("vector", lambda h: h.reciprocal(out=cols[:, 16:24], in_=cols[:, 0:8]))
        pb.barrier(["vector"], ["vector"])
        pb.emit("vector", lambda h: h.tensor_scalar_mul(out=cols[:, 16:24], in0=cols[:, 16:24],
                                                        scalar1=127.0))
        pb.barrier(["vector"], ["vector"])
        for qt in range(8):
            for hf in range(4):
                pb.wait_dma("gpsimd", "gpsimd", tag="s")
                pb.barrier(["vector"], ["gpsimd"])
                pb.dma("gpsimd", tag="a", fn=lambda h, qt=qt, hf=hf: h.dma_start(
                    out=bfa[:, 0:2048],
                    in_=hq_dram[qt * 128:(qt + 1) * 128, hf * 2048:(hf + 1) * 2048]))
                pb.wait_dma("vector", "gpsimd", tag="a")
                pb.emit("vector", lambda h, qt=qt: h.tensor_scalar_mul(
                    out=f32a[:, 0:2048], in0=bfa[:, 0:2048], scalar1=cols[:, 16 + qt:17 + qt]))
                pb.barrier(["vector"], ["vector"])
                iv = f32b.bitcast(I32)
                pb.emit("vector", lambda h, iv=iv: h.tensor_copy(out=iv, in_=f32a[:, 0:2048]))
                pb.barrier(["vector"], ["vector"])
                pb.emit("vector", lambda h, iv=iv: h.tensor_copy(out=bfa[:, 0:2048], in_=iv))
                pb.barrier(["vector"], ["gpsimd"])
                pb.dma("gpsimd", tag="s", fn=lambda h, qt=qt, hf=hf: h.dma_start(
                    out=hq_dram[qt * 128:(qt + 1) * 128, hf * 2048:(hf + 1) * 2048],
                    in_=bfa[:, 0:2048]))
        pb.wait_dma("sync", "gpsimd", tag="s")

        for qtp in range(4):
            for ic in range(NIC):
                pb.barrier(["tensor"], ["sync", "gpsimd"])
                pb.dma("gpsimd", tag="w", fn=lambda h, ic=ic: h.dma_start(
                    out=wdc[:, :], in_=wd_d[ic * 128:(ic + 1) * 128, :]))
                pb.dma("sync", tag="t", fn=lambda h, ic=ic: h.dma_start(
                    out=hqt[:, 0:NOWN], in_=hq_dram[:, ic * 128:(ic + 1) * 128],
                    transpose=True))
                pb.wait_dma("tensor", "gpsimd", tag="w")
                pb.wait_dma("tensor", "sync", tag="t")
                for qt2 in range(2):
                    p = pd if qt2 == 0 else pd2
                    qt = qtp * 2 + qt2
                    for os in range(4):
                        pb.emit("tensor", lambda h, ic=ic, qt=qt, os=os, p=p: nc.tensor.matmul(
                            out=p[:, os * 512:(os + 1) * 512],
                            lhsT=hqt[:, qt * 128:(qt + 1) * 128],
                            rhs=wdc[:, os * 512:(os + 1) * 512],
                            start=(ic == 0), stop=(ic == NIC - 1)))
            for qt2 in range(2):
                p = pd if qt2 == 0 else pd2
                qt = qtp * 2 + qt2
                pb.barrier(["tensor"], ["vector"])
                pb.barrier(["vector"], ["vector"])
                pb.emit("vector", lambda h, qt=qt, p=p: h.scalar_tensor_tensor(
                    out=f32a[:, 0:D], in0=p[:, 0:D], scalar=cols[:, 8 + qt:9 + qt],
                    in1=x3[:, qt * D:(qt + 1) * D], op0=ALU.mult, op1=ALU.add))
                _emit_ln2(pb, f32a, 0, f32b, cols, qt, out_d)
        pb.wait_dma("gpsimd", "gpsimd", tag="s")
        pb.finalize(block)
    return nc



def build_prepA():
    nc = bass.Bass("TRN2", target_bir_lowering=False)
    wgu_d = nc.dram_tensor("wgu_sl", [D, D], F32, kind="ExternalInput").ap()
    wd_d = nc.dram_tensor("wd_sl", [1024, D], F32, kind="ExternalInput").ap()
    out_d = nc.dram_tensor("partial", [1, 2], F32, kind="ExternalOutput").ap()

    with ExitStack() as ctx:
        e = ctx.enter_context
        wt = e(nc.sbuf_tensor("wt", [128, 2 * D], F32))
        acc = e(nc.sbuf_tensor("acc", [128, 4], F32))
        onesc = e(nc.sbuf_tensor("onescp", [128, 1], F32))
        res = e(nc.sbuf_tensor("res", [1, 2], F32))
        ps = e(nc.psum_tensor("psp", [1, 2], F32))
        block = e(nc.Block())
        pb = PB(nc, ctx)

        pb.emit("vector", lambda h: h.memset(acc[:, 0:4], 0.0))
        pb.emit("vector", lambda h: h.memset(onesc[:], 1.0))
        pb.barrier(["vector"], ["vector"])
        for j, (src, nt, col) in enumerate(((wgu_d, 16, 0), (wd_d, 8, 1))):
            for t in range(nt):
                buf = (t % 2) * D
                pb.wait_dma("vector", "sync", tag="a")
                pb.barrier(["vector"], ["sync"])
                pb.dma("sync", tag="a", fn=lambda h, t=t, src=src, buf=buf: h.dma_start(
                    out=wt[:, buf:buf + D], in_=src[t * 128:(t + 1) * 128, :]))
                pb.wait_dma("vector", "sync", tag="a")
                pb.emit("vector", lambda h, buf=buf: h.reduce_sum(
                    out=acc[:, 2:3], in_=wt[:, buf:buf + D], axis=AX,
                    apply_absolute_value=True))
                pb.barrier(["vector"], ["vector"])
                pb.emit("vector", lambda h, col=col: h.tensor_tensor(
                    out=acc[:, col:col + 1], in0=acc[:, col:col + 1],
                    in1=acc[:, 2:3], op=ALU.add))
                pb.barrier(["vector"], ["vector"])
        pb.barrier(["vector"], ["tensor"])
        for j in range(2):
            pb.emit("tensor", lambda h, j=j: nc.tensor.matmul(
                out=ps[0:1, j:j + 1], lhsT=acc[:, j:j + 1], rhs=onesc[:, 0:1],
                start=True, stop=True))
        pb.barrier(["tensor"], ["vector"])
        pb.emit("vector", lambda h: h.tensor_copy(out=res[:], in_=ps[0:1, 0:2]))
        pb.barrier(["vector"], ["gpsimd"])
        pb.dma("gpsimd", tag="s", fn=lambda h: h.dma_start(out=out_d, in_=res[:]))
        pb.wait_dma("gpsimd", "gpsimd", tag="s")
        pb.finalize(block)
    return nc



def build_prepB():
    nc = bass.Bass("TRN2", target_bir_lowering=False)
    parts_d = nc.dram_tensor("parts", [8, 2], F32, kind="ExternalInput").ap()
    wgu1_d = nc.dram_tensor("wgu1_sl", [D, D], F32, kind="ExternalInput").ap()
    wd1_d = nc.dram_tensor("wd1_sl", [1024, D], F32, kind="ExternalInput").ap()
    wgu1q_o = nc.dram_tensor("wgu1q_sl", [D, D], BF16, kind="ExternalOutput").ap()
    wd1q_o = nc.dram_tensor("wd1q_sl", [1024, D], BF16, kind="ExternalOutput").ap()
    sgu_o = nc.dram_tensor("sgu", [1, 1], F32, kind="ExternalOutput").ap()
    swd_o = nc.dram_tensor("swd", [1, 1], F32, kind="ExternalOutput").ap()
    pgu_d = nc.dram_tensor("pguT_sl", [256, 1408], I32, kind="ExternalInput").ap()
    sgu0_d = nc.dram_tensor("sguT_sl", [16, 1408], F16, kind="ExternalInput").ap()
    pdn_d = nc.dram_tensor("pdnT_sl", [704, 256], I32, kind="ExternalInput").ap()
    sdn_d = nc.dram_tensor("sdnT_sl", [44, 256], F16, kind="ExternalInput").ap()
    exp8_d = nc.dram_tensor("exp8", [8, 128], BF16, kind="ExternalInput").ap()
    wgu0_o = nc.dram_tensor("wgu0_sl", [D, 1408], BF16, kind="ExternalOutput").ap()
    wdn0_o = nc.dram_tensor("wdn0_sl", [I0, 256], BF16, kind="ExternalOutput").ap()
    exp4_d = nc.dram_tensor("exp4", [4, 128], BF16, kind="ExternalInput").ap()
    cq_d = [nc.dram_tensor(f"codes{i}", [256, D], BF16, kind="ExternalInput").ap()
            for i in range(4)]
    sq_d = [nc.dram_tensor(f"mxs{i}", [8, D], F32, kind="ExternalInput").ap()
            for i in range(4)]
    dq_o = [nc.dram_tensor(f"deq{i}", [256, D], BF16, kind="ExternalOutput").ap()
            for i in range(4)]

    with ExitStack() as ctx:
        e = ctx.enter_context
        wt = e(nc.sbuf_tensor("wtp", [128, D], F32))
        f32s = e(nc.sbuf_tensor("f32sp", [128, D], F32))
        bfo = e(nc.sbuf_tensor("bfop", [128, 2048], BF16))
        pk = e(nc.sbuf_tensor("pkp", [128, 1408], I32))
        nib = e(nc.sbuf_tensor("nibp", [128, 1408], I32))
        nbf = e(nc.sbuf_tensor("nbfp", [128, 1408], BF16))
        d1 = e(nc.sbuf_tensor("d1p", [128, 1408], BF16))
        d2 = e(nc.sbuf_tensor("d2p", [128, 1408], BF16))
        d3 = e(nc.sbuf_tensor("d3p", [128, 1408], BF16))
        srep = e(nc.sbuf_tensor("srepp", [128, 1408], BF16))
        srow = e(nc.sbuf_tensor("srowp", [16, 2048], BF16))
        expt = e(nc.sbuf_tensor("exptp", [8, 128], BF16))
        prt = e(nc.sbuf_tensor("prtp", [8, 2], F32))
        onesc = e(nc.sbuf_tensor("onescb", [8, 1], F32))
        ones1 = e(nc.sbuf_tensor("ones1b", [1, 128], F32))
        sc2 = e(nc.sbuf_tensor("sc2", [1, 4], F32))
        scc = e(nc.sbuf_tensor("sccb", [128, 2], F32))
        pps = e(nc.psum_tensor("ppsb", [128, 2048], F32))
        block = e(nc.Block())
        pb = PB(nc, ctx)

        pb.dma("sync", tag="a", fn=lambda h: h.dma_start(out=prt[:], in_=parts_d))
        pb.emit("vector", lambda h: h.memset(onesc[:], 1.0))
        pb.emit("vector", lambda h: h.memset(ones1[:], 1.0))
        pb.wait_dma("tensor", "sync", tag="a")
        pb.barrier(["vector"], ["tensor"])
        for j in range(2):
            pb.emit("tensor", lambda h, j=j: nc.tensor.matmul(
                out=pps[0:1, j:j + 1], lhsT=prt[:, j:j + 1], rhs=onesc[:, 0:1],
                start=True, stop=True))
        pb.barrier(["tensor"], ["vector"])
        pb.emit("vector", lambda h: h.tensor_scalar(
            out=sc2[0:1, 0:1], in0=pps[0:1, 0:1], scalar1=1.0 / (2 * F1 * D), scalar2=1e-5,
            op0=ALU.mult, op1=ALU.max))
        pb.emit("vector", lambda h: h.tensor_scalar(
            out=sc2[0:1, 1:2], in0=pps[0:1, 1:2], scalar1=1.0 / (D * F1), scalar2=1e-5,
            op0=ALU.mult, op1=ALU.max))
        pb.barrier(["vector"], ["vector", "gpsimd"])
        pb.emit("vector", lambda h: h.reciprocal(out=sc2[0:1, 2:4], in_=sc2[0:1, 0:2]))
        pb.barrier(["vector"], ["tensor", "gpsimd"])
        pb.dma("gpsimd", tag="s", fn=lambda h: h.dma_start(out=sgu_o, in_=sc2[0:1, 0:1]))
        pb.dma("gpsimd", tag="s", fn=lambda h: h.dma_start(out=swd_o, in_=sc2[0:1, 1:2]))
        pb.emit("tensor", lambda h: nc.tensor.matmul(
            out=pps[:, 4:6], lhsT=ones1[0:1, :], rhs=sc2[0:1, 2:4], start=True, stop=True))
        pb.barrier(["tensor"], ["vector"])
        pb.emit("vector", lambda h: h.tensor_copy(out=scc[:, 0:2], in_=pps[:, 4:6]))
        pb.barrier(["vector"], ["scalar", "tensor"])

        def quant_tiles(src, dst, nt, col):
            for t in range(nt):
                pb.wait_dma("scalar", "sync", tag="a")
                pb.barrier(["vector"], ["sync"])
                pb.wait_dma("sync", "gpsimd", tag="s")
                pb.dma("sync", tag="a", fn=lambda h, t=t, src=src: h.dma_start(
                    out=wt[:, :], in_=src[t * 128:(t + 1) * 128, :]))
                pb.wait_dma("scalar", "sync", tag="a")
                pb.emit("scalar", lambda h, col=col: h.activation(
                    out=f32s[:, 0:D], in_=wt[:, 0:D], func=AF.Identity,
                    scale=scc[:, col:col + 1]))
                pb.barrier(["scalar"], ["vector"])
                pb.emit("vector", lambda h: h.tensor_scalar(
                    out=f32s[:, 0:D], in0=f32s[:, 0:D], scalar1=1.0, scalar2=-1.0,
                    op0=ALU.min, op1=ALU.max))
                pb.barrier(["vector"], ["vector"])
                iv = wt.ap().bitcast(I32)
                pb.emit("vector", lambda h, iv=iv: h.tensor_copy(out=iv, in_=f32s[:, 0:D]))
                pb.barrier(["vector"], ["vector"])
                pb.emit("vector", lambda h, iv=iv: h.tensor_copy(out=bfo[:, 0:D], in_=iv))
                pb.barrier(["vector"], ["gpsimd", "scalar"])
                pb.dma("gpsimd", tag="s", fn=lambda h, t=t, dst=dst: h.dma_start(
                    out=dst[t * 128:(t + 1) * 128, :], in_=bfo[:, 0:D]))
                pb.wait_dma("vector", "gpsimd", tag="s")
                pb.barrier(["vector"], ["vector"])

        quant_tiles(wgu1_d, wgu1q_o, 16, 0)
        quant_tiles(wd1_d, wd1q_o, 8, 1)

        def decode(packed_d, scales_d, out_view, n_pchunks, owid, ngrp_tot, last_rows):
            pb.wait_dma("vector", "sync", tag="a")
            pb.barrier(["vector"], ["sync"])
            pb.dma("sync", tag="a", fn=lambda h: h.dma_start(out=expt[:], in_=exp8_d))
            pb.wait_dma("tensor", "sync", tag="a")
            for ch in range(n_pchunks):
                rows = last_rows if ch == n_pchunks - 1 else 128
                ngrp = (rows + 15) // 16
                pb.barrier(["tensor"], ["gpsimd"])
                pb.dma("gpsimd", tag="a", fn=lambda h, ch=ch, ngrp=ngrp: h.dma_start(
                    out=srow[0:ngrp, 0:owid],
                    in_=scales_d[8 * ch:8 * ch + ngrp, 0:owid]))
                pb.wait_dma("tensor", "gpsimd", tag="a")
                g0 = 0
                pb.barrier(["vector"], ["tensor"])
                nmm = (owid + 511) // 512
                for mi in range(nmm):
                    wid = min(512, owid - mi * 512)
                    pb.emit("tensor", lambda h, mi=mi, wid=wid, g0=g0, ngrp=ngrp, rows=rows:
                            nc.tensor.matmul(
                                out=pps[0:rows, mi * 512:mi * 512 + wid],
                                lhsT=expt[0:ngrp, 0:rows],
                                rhs=srow[g0:g0 + ngrp, mi * 512:mi * 512 + wid],
                                start=True, stop=True))
                pb.barrier(["tensor"], ["scalar"])
                pb.emit("scalar", lambda h, owid=owid, rows=rows: h.activation(
                    out=srep[0:rows, 0:owid], in_=pps[0:rows, 0:owid],
                    func=AF.Identity, scale=0.5))
                pb.barrier(["scalar"], ["sync", "vector"])
                pb.dma("sync", tag="a", fn=lambda h, ch=ch, rows=rows, owid=owid:
                       h.dma_start(out=pk[0:rows, 0:owid],
                                   in_=packed_d[ch * 128:ch * 128 + rows, 0:owid]))
                pb.wait_dma("vector", "sync", tag="a")
                for k in range(8):
                    V = "vector"
                    pb.wait_dma(V, "gpsimd", tag="s")
                    pb.barrier([V], [V])
                    pb.emit(V, lambda h, k=k, rows=rows, owid=owid: h.tensor_scalar(
                        out=nib[0:rows, 0:owid], in0=pk[0:rows, 0:owid],
                        scalar1=4 * k, scalar2=15,
                        op0=ALU.logical_shift_right, op1=ALU.bitwise_and))
                    pb.barrier([V], [V])
                    pb.emit(V, lambda h, rows=rows, owid=owid: h.tensor_copy(
                        out=nbf[0:rows, 0:owid], in_=nib[0:rows, 0:owid]))
                    pb.barrier([V], [V])
                    pb.emit(V, lambda h, rows=rows, owid=owid: h.tensor_scalar(
                        out=d1[0:rows, 0:owid], in0=nbf[0:rows, 0:owid],
                        scalar1=8.0, scalar2=-2.0, op0=ALU.is_ge, op1=ALU.mult))
                    pb.barrier([V], [V])
                    pb.emit(V, lambda h, rows=rows, owid=owid: h.scalar_tensor_tensor(
                        out=d2[0:rows, 0:owid], in0=d1[0:rows, 0:owid], scalar=4.0,
                        in1=nbf[0:rows, 0:owid], op0=ALU.mult, op1=ALU.add))
                    pb.barrier([V], [V])
                    pb.emit(V, lambda h, rows=rows, owid=owid: h.tensor_scalar(
                        out=d3[0:rows, 0:owid], in0=d2[0:rows, 0:owid],
                        scalar1=2.0, scalar2=-4.0, op0=ALU.mult, op1=ALU.add))
                    pb.barrier([V], [V])
                    pb.emit(V, lambda h, rows=rows, owid=owid: h.tensor_tensor(
                        out=d3[0:rows, 0:owid], in0=d2[0:rows, 0:owid],
                        in1=d3[0:rows, 0:owid], op=ALU.max))
                    pb.barrier([V], [V])
                    pb.emit(V, lambda h, rows=rows, owid=owid: h.tensor_scalar(
                        out=d2[0:rows, 0:owid], in0=d2[0:rows, 0:owid],
                        scalar1=4.0, scalar2=-16.0, op0=ALU.mult, op1=ALU.add))
                    pb.barrier([V], [V])
                    pb.emit(V, lambda h, rows=rows, owid=owid: h.tensor_tensor(
                        out=d3[0:rows, 0:owid], in0=d3[0:rows, 0:owid],
                        in1=d2[0:rows, 0:owid], op=ALU.max))
                    pb.barrier([V], [V])
                    pb.emit(V, lambda h, rows=rows, owid=owid: h.scalar_tensor_tensor(
                        out=d1[0:rows, 0:owid], in0=d1[0:rows, 0:owid], scalar=1.0,
                        in1=d3[0:rows, 0:owid], op0=ALU.add, op1=ALU.mult))
                    pb.barrier([V], [V])
                    pb.emit(V, lambda h, rows=rows, owid=owid: h.tensor_tensor(
                        out=d1[0:rows, 0:owid], in0=d1[0:rows, 0:owid],
                        in1=srep[0:rows, 0:owid], op=ALU.mult))
                    pb.barrier([V], ["gpsimd"])
                    pb.dma("gpsimd", tag="s", fn=lambda h, ch=ch, k=k, rows=rows, owid=owid:
                           h.dma_start(out=out_view[k, ch * 128:ch * 128 + rows, 0:owid],
                                       in_=d1[0:rows, 0:owid]))

        wgu0_view = wgu0_o.rearrange("(cp k) o -> k cp o", k=8)
        decode(pgu_d, sgu0_d, wgu0_view, 2, 1408, 16, 128)
        wdn0_view = wdn0_o.rearrange("(cp k) o -> k cp o", k=8)
        decode(pdn_d, sdn_d, wdn0_view, 6, 256, 44, 64)

        pb.barrier(["tensor"], ["sync"])
        pb.dma("sync", tag="a", fn=lambda h: h.dma_start(out=expt[0:4, :], in_=exp4_d))
        for i in range(4):
            for ch in range(2):
                pb.wait_dma("vector", "sync", tag="a")
                pb.barrier(["vector", "tensor"], ["sync", "gpsimd"])
                pb.dma("gpsimd", tag="a", fn=lambda h, i=i, ch=ch: h.dma_start(
                    out=bfo[:, 0:D], in_=cq_d[i][ch * 128:(ch + 1) * 128, :]))
                pb.dma("sync", tag="a", fn=lambda h, i=i, ch=ch: h.dma_start(
                    out=wt[0:4, 0:D], in_=sq_d[i][4 * ch:4 * ch + 4, :]))
                pb.wait_dma("vector", "sync", tag="a")
                pb.emit("vector", lambda h: h.tensor_copy(out=srow[0:4, 0:D], in_=wt[0:4, 0:D]))
                pb.barrier(["vector"], ["tensor"])
                pb.wait_dma("tensor", "sync", tag="a")
                for mi in range(4):
                    pb.emit("tensor", lambda h, mi=mi: nc.tensor.matmul(
                        out=pps[:, mi * 512:(mi + 1) * 512], lhsT=expt[0:4, :],
                        rhs=srow[0:4, mi * 512:(mi + 1) * 512], start=True, stop=True))
                pb.barrier(["tensor"], ["vector"])
                pb.wait_dma("vector", "gpsimd", tag="a")
                pb.wait_dma("vector", "gpsimd", tag="s")
                pb.emit("vector", lambda h: h.tensor_tensor(
                    out=bfo[:, 0:D], in0=pps[:, 0:D], in1=bfo[:, 0:D], op=ALU.mult))
                pb.barrier(["vector"], ["gpsimd"])
                pb.dma("gpsimd", tag="s", fn=lambda h, i=i, ch=ch: h.dma_start(
                    out=dq_o[i][ch * 128:(ch + 1) * 128, :], in_=bfo[:, 0:D]))
        pb.wait_dma("gpsimd", "gpsimd", tag="s")
        pb.wait_dma("sync", "gpsimd", tag="s")
        pb.finalize(block)
    return nc



def build_qkv0():
    nc = bass.Bass("TRN2", target_bir_lowering=False)
    xT_d = nc.dram_tensor("xT", [D, NOWN], F32, kind="ExternalInput").ap()
    wqa_d = nc.dram_tensor("wqa", [D, Q_LORA], F32, kind="ExternalInput").ap()
    bqa_d = nc.dram_tensor("bqa", [128, 4], F32, kind="ExternalInput").ap()
    wqb_d = nc.dram_tensor("wqb_p", [Q_LORA, 4096], F32, kind="ExternalInput").ap()
    bqbn_d = nc.dram_tensor("bqbn", [128, 16], F32, kind="ExternalInput").ap()
    bqbr_d = nc.dram_tensor("bqbr", [64, 32], F32, kind="ExternalInput").ap()
    wkva_d = nc.dram_tensor("wkva_p", [D, 640], F32, kind="ExternalInput").ap()
    bkva_d = nc.dram_tensor("bkva", [128, 6], F32, kind="ExternalInput").ap()
    cos_d = nc.dram_tensor("cosT", [32, NOWN], F32, kind="ExternalInput").ap()
    sin_d = nc.dram_tensor("sinT", [32, NOWN], F32, kind="ExternalInput").ap()
    exp32_d = nc.dram_tensor("exp32", [32, 128], BF16, kind="ExternalInput").ap()
    qnT_o = nc.dram_tensor("qnT", [NH * NOPE, NOWN], BF16, kind="ExternalOutput").ap()
    qrT_o = nc.dram_tensor("qrT", [NH * ROPE, NOWN], BF16, kind="ExternalOutput").ap()
    ckvT_o = nc.dram_tensor("ckvT", [KV_LORA + ROPE, NOWN], BF16, kind="ExternalOutput").ap()

    with ExitStack() as ctx:
        e = ctx.enter_context
        xT = e(nc.sbuf_tensor("xTq", [128, NC_D * NOWN], BF16))
        wqa = e(nc.sbuf_tensor("wqaq", [128, NC_D * Q_LORA], BF16))
        wqb = e(nc.sbuf_tensor("wqbq", [128, 4 * 4096], BF16))
        wkva = e(nc.sbuf_tensor("wkvaq", [128, NC_D * 640], BF16))
        qlat = e(nc.sbuf_tensor("qlatq", [128, 4 * NOWN], BF16))
        ckvs = e(nc.sbuf_tensor("ckvsq", [128, 5 * NOWN], BF16))
        qn = e(nc.sbuf_tensor("qnq", [128, NH * NOWN], BF16))
        qrp = e(nc.sbuf_tensor("qrpq", [64, NOWN], BF16))
        qrsw = e(nc.sbuf_tensor("qrswq", [64, NOWN], BF16))
        kpesw = e(nc.sbuf_tensor("kpeswq", [64, NOWN], BF16))
        qro = e(nc.sbuf_tensor("qroq", [64, NOWN], BF16))
        cs2 = e(nc.sbuf_tensor("cs2q", [64, 2 * NOWN], BF16))
        csr = e(nc.sbuf_tensor("csrq", [32, 2 * NOWN], BF16))
        bqa = e(nc.sbuf_tensor("bqaq", [128, 4], F32))
        bqbn = e(nc.sbuf_tensor("bqbnq", [128, 16], F32))
        bqbr = e(nc.sbuf_tensor("bqbrq", [64, 32], F32))
        bkva = e(nc.sbuf_tensor("bkvaq", [128, 6], F32))
        exp32 = e(nc.sbuf_tensor("exp32q", [32, 128], BF16))
        psA = e(nc.psum_tensor("psAq", [128, 1024], F32))
        psB = e(nc.psum_tensor("psBq", [128, 1024], F32))
        block = e(nc.Block())
        pb = PB(nc, ctx)

        for c in range(NC_D):
            pb.dma("gpsimd", tag="a", fn=lambda h, c=c: h.dma_start(
                out=xT[:, c * NOWN:(c + 1) * NOWN], in_=xT_d[c * 128:(c + 1) * 128, :]))
        for c in range(NC_D):
            pb.dma("gpsimd", tag="w", fn=lambda h, c=c: h.dma_start(
                out=wqa[:, c * Q_LORA:(c + 1) * Q_LORA], in_=wqa_d[c * 128:(c + 1) * 128, :]))
        for c in range(4):
            pb.dma("gpsimd", tag="w", fn=lambda h, c=c: h.dma_start(
                out=wqb[:, c * 4096:(c + 1) * 4096], in_=wqb_d[c * 128:(c + 1) * 128, :]))
        for c in range(NC_D):
            pb.dma("gpsimd", tag="w", fn=lambda h, c=c: h.dma_start(
                out=wkva[:, c * 640:(c + 1) * 640], in_=wkva_d[c * 128:(c + 1) * 128, :]))
        pb.dma("gpsimd", tag="w", fn=lambda h: h.dma_start(out=csr[:, 0:NOWN], in_=cos_d))
        pb.dma("gpsimd", tag="w", fn=lambda h: h.dma_start(out=csr[:, NOWN:2 * NOWN], in_=sin_d))
        pb.dma("sync", tag="a", fn=lambda h: h.dma_start(out=bqa[:], in_=bqa_d))
        pb.dma("sync", tag="a", fn=lambda h: h.dma_start(out=bqbn[:], in_=bqbn_d))
        pb.dma("sync", tag="a", fn=lambda h: h.dma_start(out=bqbr[:], in_=bqbr_d))
        pb.dma("sync", tag="a", fn=lambda h: h.dma_start(out=bkva[:], in_=bkva_d))
        pb.dma("sync", tag="a", fn=lambda h: h.dma_start(out=exp32[:], in_=exp32_d))
        pb.wait_dma("tensor", "gpsimd", tag="w")
        pb.wait_dma("tensor", "gpsimd", tag="a")
        pb.wait_dma("tensor", "sync", tag="a")
        pb.wait_dma("scalar", "sync", tag="a")

        for j in range(2):
            for qh in range(2):
                pb.emit("tensor", lambda h, j=j, qh=qh: nc.tensor.matmul(
                    out=psB[0:64, qh * 512:qh * 512 + 512],
                    lhsT=exp32[:, j * 64:(j + 1) * 64],
                    rhs=csr[:, j * NOWN + qh * 512: j * NOWN + qh * 512 + 512],
                    start=True, stop=True))
            pb.barrier(["tensor"], ["vector"])
            pb.emit("vector", lambda h, j=j: h.tensor_copy(
                out=cs2[0:64, j * NOWN:(j + 1) * NOWN], in_=psB[0:64, 0:1024]))
            pb.barrier(["vector"], ["tensor"])

        for t in range(4):
            for qh in range(2):
                p = psA[:, (t * 2 + qh) % 2 * 512:((t * 2 + qh) % 2 * 512) + 512]
                for c in range(NC_D):
                    pb.emit("tensor", lambda h, t=t, qh=qh, c=c, p=p: nc.tensor.matmul(
                        out=p, lhsT=wqa[:, c * Q_LORA + t * 128: c * Q_LORA + (t + 1) * 128],
                        rhs=xT[:, c * NOWN + qh * 512: c * NOWN + qh * 512 + 512],
                        start=(c == 0), stop=(c == NC_D - 1)))
                pb.barrier(["tensor"], ["scalar"])
                pb.emit("scalar", lambda h, t=t, qh=qh, p=p: h.activation(
                    out=qlat[:, t * NOWN + qh * 512: t * NOWN + qh * 512 + 512],
                    in_=p, func=AF.Identity, bias=bqa[:, t:t + 1], scale=1.0))
                pb.barrier(["scalar"], ["tensor"])

        for t in range(6):
            rows = 128 if t < 4 else 64
            bcol = bkva[:, t:t + 1] if t < 4 else bkva[0:64, t:t + 1]
            for qh in range(2):
                p = psA[:, (qh % 2) * 512:(qh % 2) * 512 + 512]
                for c in range(NC_D):
                    pb.emit("tensor", lambda h, t=t, qh=qh, c=c, p=p, rows=rows: nc.tensor.matmul(
                        out=p[0:rows, :],
                        lhsT=wkva[:, c * 640 + t * 128: c * 640 + t * 128 + rows] if t < 5
                        else wkva[:, c * 640 + 576: c * 640 + 640],
                        rhs=xT[:, c * NOWN + qh * 512: c * NOWN + qh * 512 + 512],
                        start=(c == 0), stop=(c == NC_D - 1)))
                pb.barrier(["tensor"], ["scalar"])
                if t < 5:
                    pb.emit("scalar", lambda h, t=t, qh=qh, p=p, rows=rows, bcol=bcol: h.activation(
                        out=ckvs[0:rows, t * NOWN + qh * 512: t * NOWN + qh * 512 + 512],
                        in_=p[0:rows, :], func=AF.Identity, bias=bcol, scale=1.0))
                else:
                    pb.emit("scalar", lambda h, qh=qh, p=p, bcol=bcol: h.activation(
                        out=kpesw[0:64, qh * 512:qh * 512 + 512],
                        in_=p[0:64, :], func=AF.Identity, bias=bcol, scale=1.0))
                pb.barrier(["scalar"], ["tensor"])

        for t in range(NH):
            for qh in range(2):
                p = psA[:, ((t * 2 + qh) % 2) * 512:(((t * 2 + qh) % 2) * 512) + 512]
                for c in range(4):
                    pb.emit("tensor", lambda h, t=t, qh=qh, c=c, p=p: nc.tensor.matmul(
                        out=p, lhsT=wqb[:, c * 4096 + t * 128: c * 4096 + (t + 1) * 128],
                        rhs=qlat[:, c * NOWN + qh * 512: c * NOWN + qh * 512 + 512],
                        start=(c == 0), stop=(c == 3)))
                pb.barrier(["tensor"], ["scalar"])
                pb.emit("scalar", lambda h, t=t, qh=qh, p=p: h.activation(
                    out=qn[:, t * NOWN + qh * 512: t * NOWN + qh * 512 + 512],
                    in_=p, func=AF.Identity, bias=bqbn[:, t:t + 1], scale=1.0))
                pb.barrier(["scalar"], ["tensor"])
        pb.barrier(["scalar"], ["sync"])
        for c in range(NH):
            pb.dma("sync", tag="s", fn=lambda h, c=c: h.dma_start(
                out=qnT_o[c * 128:(c + 1) * 128, :], in_=qn[:, c * NOWN:(c + 1) * NOWN]))

        for hh in range(NH):
            for qh in range(2):
                for c in range(4):
                    pb.emit("tensor", lambda h, hh=hh, qh=qh, c=c: nc.tensor.matmul(
                        out=psB[0:64, qh * 512:qh * 512 + 512],
                        lhsT=wqb[:, c * 4096 + 2048 + hh * 64: c * 4096 + 2048 + (hh + 1) * 64],
                        rhs=qlat[:, c * NOWN + qh * 512: c * NOWN + qh * 512 + 512],
                        start=(c == 0), stop=(c == 3)))
                for c in range(4):
                    pb.emit("tensor", lambda h, hh=hh, qh=qh, c=c: nc.tensor.matmul(
                        out=psA[0:64, qh * 512:qh * 512 + 512],
                        lhsT=wqb[:, c * 4096 + 3072 + hh * 64: c * 4096 + 3072 + (hh + 1) * 64],
                        rhs=qlat[:, c * NOWN + qh * 512: c * NOWN + qh * 512 + 512],
                        start=(c == 0), stop=(c == 3)))
            pb.barrier(["tensor"], ["scalar"])
            pb.wait_dma("scalar", "gpsimd", tag="s")
            pb.emit("scalar", lambda h, hh=hh: h.activation(
                out=qrp[0:64, 0:NOWN], in_=psB[0:64, 0:NOWN],
                func=AF.Identity, bias=bqbr[0:64, hh:hh + 1], scale=1.0))
            pb.emit("scalar", lambda h, hh=hh: h.activation(
                out=qrsw[0:64, 0:NOWN], in_=psA[0:64, 0:NOWN],
                func=AF.Identity, bias=bqbr[0:64, 16 + hh:17 + hh], scale=1.0))
            pb.barrier(["scalar"], ["tensor", "vector"])
            _emit_rope(pb, qrp, qrsw, qro, cs2, NOWN)
            pb.barrier(["vector"], ["gpsimd"])
            pb.dma("gpsimd", tag="s", fn=lambda h, hh=hh: h.dma_start(
                out=qrT_o[hh * 64:(hh + 1) * 64, :], in_=qro[0:64, 0:NOWN]))

        pb.barrier(["scalar"], ["vector"])
        pb.wait_dma("vector", "gpsimd", tag="s")
        pb.emit("vector", lambda h: h.tensor_copy(out=qrp[0:64, 0:NOWN],
                                                  in_=ckvs[0:64, 4 * NOWN:5 * NOWN]))
        pb.barrier(["vector"], ["vector"])
        _emit_rope(pb, qrp, kpesw, qro, cs2, NOWN)
        pb.barrier(["vector"], ["gpsimd"])
        pb.dma("gpsimd", tag="s", fn=lambda h: h.dma_start(
            out=ckvT_o[512:576, :], in_=qro[0:64, 0:NOWN]))
        for t in range(4):
            pb.dma("gpsimd", tag="s", fn=lambda h, t=t: h.dma_start(
                out=ckvT_o[t * 128:(t + 1) * 128, :], in_=ckvs[:, t * NOWN:(t + 1) * NOWN]))
        pb.wait_dma("gpsimd", "gpsimd", tag="s")
        pb.wait_dma("sync", "sync", tag="s")
        pb.finalize(block)
    return nc


def _emit_rope(pb, pre, presw, out, cs2, n):
    nc = pb.nc
    V = "vector"
    pb.emit(V, lambda h: h.tensor_tensor(out=out[0:64, 0:n], in0=pre[0:64, 0:n],
                                         in1=cs2[0:64, 0:n], op=ALU.mult))
    pb.emit(V, lambda h: h.tensor_tensor(out=presw[0:64, 0:n], in0=presw[0:64, 0:n],
                                         in1=cs2[0:64, n:2 * n], op=ALU.mult))
    pb.barrier([V], [V])
    pb.emit(V, lambda h: h.tensor_tensor(out=out[0:64, 0:n], in0=out[0:64, 0:n],
                                         in1=presw[0:64, 0:n], op=ALU.add))
    pb.barrier([V], [V])



def _rope_tables():
    inv = 1.0 / (1e6 ** (np.arange(0, ROPE, 2, dtype=np.float64) / ROPE))
    t = np.arange(S, dtype=np.float64)
    fr = np.outer(t, inv)
    return np.cos(fr).astype(np.float32), np.sin(fr).astype(np.float32)


def kernel(**inp):
    if USE_FUSED:
        return kernel_fused(inp)
    f32 = np.float32
    x = np.asarray(inp["L0_attn_x"], f32).reshape(NTOK, D)
    xT_full = np.ascontiguousarray(x.T)

    wqa = np.ascontiguousarray(np.asarray(inp["L0_attn_wq_a_w"], f32))
    bqa_c = _bcols(inp["L0_attn_wq_a_b"])
    wqb = np.asarray(inp["L0_attn_wq_b_w"], f32)
    bqb = np.asarray(inp["L0_attn_wq_b_b"], f32)
    perm_n = np.concatenate([np.arange(192 * h, 192 * h + 128) for h in range(NH)])
    perm_r = np.concatenate(
        [np.concatenate([np.arange(192 * h + 128, 192 * h + 192, 2),
                         np.arange(192 * h + 129, 192 * h + 192, 2)]) for h in range(NH)])
    perm_rs = np.concatenate(
        [np.concatenate([np.arange(192 * h + 129, 192 * h + 192, 2),
                         np.arange(192 * h + 128, 192 * h + 192, 2)]) for h in range(NH)])
    perm = np.concatenate([perm_n, perm_r, perm_rs])
    wqb_p = np.ascontiguousarray(wqb[:, perm])
    bqb_p = bqb[perm]
    bqbn_c = _bcols(bqb_p[:2048])
    bqbr_c = np.ascontiguousarray(bqb_p[2048:].reshape(32, 64).T)
    wkva = np.asarray(inp["L0_attn_wkv_a_w"], f32)
    bkva = np.asarray(inp["L0_attn_wkv_a_b"], f32)
    pk_r = np.concatenate([512 + np.arange(0, 64, 2), 512 + np.arange(1, 64, 2)])
    pk_rs = np.concatenate([512 + np.arange(1, 64, 2), 512 + np.arange(0, 64, 2)])
    permk = np.concatenate([np.arange(512), pk_r, pk_rs])
    wkva_p = np.ascontiguousarray(wkva[:, permk])
    bkva_p = bkva[permk]
    bkva_c = np.zeros((128, 6), f32)
    bkva_c[:, 0:4] = bkva_p[:512].reshape(4, 128).T
    bkva_c[0:64, 4] = bkva_p[512:576]
    bkva_c[0:64, 5] = bkva_p[576:640]
    exp32 = np.zeros((32, 128), f32)
    for p in range(64):
        exp32[p % 32, p] = 1.0
        exp32[p % 32, 64 + p] = -1.0 if p < 32 else 1.0
    exp32 = exp32.astype(BF)
    cosf, sinf = _rope_tables()
    cosT, sinT = np.ascontiguousarray(cosf.T), np.ascontiguousarray(sinf.T)

    wkvb = np.ascontiguousarray(np.asarray(inp["L0_attn_wkv_b_w"], f32))
    bkvb = np.asarray(inp["L0_attn_wkv_b_b"], f32)
    bkv_c = _bcols(bkvb)
    bv_row0 = np.ascontiguousarray(bkvb.reshape(NH, 2, 128)[:, 1, :].reshape(1, NH * 128))
    wo0 = np.ascontiguousarray(np.asarray(inp["L0_attn_wo_w"], f32))
    bo0_row = np.ascontiguousarray(np.asarray(inp["L0_attn_wo_b"], f32).reshape(1, D))

    wguT1 = np.ascontiguousarray(np.asarray(inp["L1_ffn_w_gate_up"], f32).T)
    wdT1 = np.ascontiguousarray(np.asarray(inp["L1_ffn_w_down"], f32).T)
    pguT = np.ascontiguousarray(np.asarray(inp["L0_ffn_gate_up_packed"]).T)
    sguT = np.ascontiguousarray(np.asarray(inp["L0_ffn_gate_up_scales"]))
    pdnT = np.ascontiguousarray(np.asarray(inp["L0_ffn_down_packed"]).T)
    sdnT = np.ascontiguousarray(np.asarray(inp["L0_ffn_down_scales"]))
    exp8 = np.zeros((8, 128), f32)
    exp4 = np.zeros((4, 128), f32)
    for p in range(128):
        exp8[p // 16, p] = 1.0
        exp4[p // 32, p] = 1.0
    exp8, exp4 = exp8.astype(BF), exp4.astype(BF)
    codes = [np.asarray(inp[k], f32) for k in
             ("L1_attn_wq_codes", "L1_attn_wk_codes", "L1_attn_wv_codes", "L1_attn_wo_codes")]
    mxs = [np.asarray(inp[k], f32) for k in
           ("L1_attn_wq_scales", "L1_attn_wk_scales", "L1_attn_wv_scales", "L1_attn_wo_scales")]
    bq1_c = _bcols(inp["L1_attn_bq"])
    bk1_c = _bcols(inp["L1_attn_bk"])
    bv1_row = np.ascontiguousarray(np.asarray(inp["L1_attn_bv"], f32).reshape(1, D))
    bo1_row = np.ascontiguousarray(np.asarray(inp["L1_attn_bo"], f32).reshape(1, D))

    CORES = list(range(NCORE))

    nc = build_prepA()
    maps = [{"wgu_sl": np.ascontiguousarray(wguT1[:, c * 2048:(c + 1) * 2048]),
             "wd_sl": np.ascontiguousarray(wdT1[c * 1024:(c + 1) * 1024, :])}
            for c in CORES]
    res = run_bass_kernel_spmd(nc, maps, core_ids=CORES)
    parts = np.concatenate([res.results[c]["partial"] for c in CORES], axis=0)

    nc = build_prepB()
    maps = []
    for c in CORES:
        m = {"parts": parts,
             "wgu1_sl": np.ascontiguousarray(wguT1[:, c * 2048:(c + 1) * 2048]),
             "wd1_sl": np.ascontiguousarray(wdT1[c * 1024:(c + 1) * 1024, :]),
             "pguT_sl": np.ascontiguousarray(pguT[:, c * 1408:(c + 1) * 1408]),
             "sguT_sl": np.ascontiguousarray(sguT[:, c * 1408:(c + 1) * 1408]),
             "pdnT_sl": np.ascontiguousarray(pdnT[:, c * 256:(c + 1) * 256]),
             "sdnT_sl": np.ascontiguousarray(sdnT[:, c * 256:(c + 1) * 256]),
             "exp8": exp8, "exp4": exp4}
        for i in range(4):
            m[f"codes{i}"] = np.ascontiguousarray(codes[i][c * 256:(c + 1) * 256, :]).astype(BF)
            m[f"mxs{i}"] = np.ascontiguousarray(mxs[i][c * 8:(c + 1) * 8, :])
        maps.append(m)
    res = run_bass_kernel_spmd(nc, maps, core_ids=CORES)
    rs = res.results
    sgu = rs[0]["sgu"]
    swd = rs[0]["swd"]
    wgu1q = np.concatenate([rs[c]["wgu1q_sl"] for c in CORES], axis=1)
    wd1q = np.concatenate([rs[c]["wd1q_sl"] for c in CORES], axis=0)
    wgu0 = np.concatenate([rs[c]["wgu0_sl"] for c in CORES], axis=1)
    wdn0 = np.concatenate([rs[c]["wdn0_sl"] for c in CORES], axis=1)
    deq = [np.concatenate([rs[c][f"deq{i}"] for c in CORES], axis=0) for i in range(4)]

    b0 = np.empty((44, NC_D, 128, 256), dtype=BF)
    for sp in range(44):
        b0[:, :, :, 0:128][sp] = wgu0[:, sp * 128:(sp + 1) * 128].reshape(NC_D, 128, 128)
        b0[:, :, :, 128:256][sp] = wgu0[:, I0 + sp * 128: I0 + (sp + 1) * 128].reshape(NC_D, 128, 128)
    b1 = np.empty((16, NC_D, 128, 1024), dtype=BF)
    for sp in range(16):
        b1[:, :, :, 0:512][sp] = wgu1q[:, sp * 512:(sp + 1) * 512].reshape(NC_D, 128, 512)
        b1[:, :, :, 512:1024][sp] = wgu1q[:, F1 + sp * 512: F1 + (sp + 1) * 512].reshape(NC_D, 128, 512)

    nc = build_qkv0()
    maps = [{"xT": np.ascontiguousarray(xT_full[:, c * NOWN:(c + 1) * NOWN]),
             "wqa": wqa, "bqa": bqa_c, "wqb_p": wqb_p, "bqbn": bqbn_c, "bqbr": bqbr_c,
             "wkva_p": wkva_p, "bkva": bkva_c,
             "cosT": np.ascontiguousarray(cosT[:, (c % 2) * NOWN:(c % 2) * NOWN + NOWN]),
             "sinT": np.ascontiguousarray(sinT[:, (c % 2) * NOWN:(c % 2) * NOWN + NOWN]),
             "exp32": exp32} for c in CORES]
    res = run_bass_kernel_spmd(nc, maps, core_ids=CORES)
    q0 = res.results

    nc = build_attn0()
    maps = []
    for c in CORES:
        b = c // 2
        ckv_full = np.concatenate([q0[2 * b]["ckvT"], q0[2 * b + 1]["ckvT"]], axis=1)
        maps.append({"qnT": q0[c]["qnT"], "qrT": q0[c]["qrT"], "ckvT": ckv_full,
                     "wkvb": wkvb, "bkv": bkv_c, "bvrow": bv_row0,
                     "wo": wo0, "bo": bo0_row,
                     "xres": np.ascontiguousarray(x[c * NOWN:(c + 1) * NOWN, :])})
    res = run_bass_kernel_spmd(nc, maps, core_ids=CORES)
    s1 = [res.results[c]["out"] for c in CORES]

    nc = build_ffn0()
    maps = [{"x": s1[c], "xT": np.ascontiguousarray(s1[c].T),
             "wgu": b0, "wd": wdn0} for c in CORES]
    res = run_bass_kernel_spmd(nc, maps, core_ids=CORES)
    s2 = [res.results[c]["out"] for c in CORES]

    nc = build_qkv1()
    maps = [{"xT": np.ascontiguousarray(s2[c].T),
             "wq": deq[0], "wk": deq[1], "wv": deq[2],
             "bq": bq1_c, "bk": bk1_c, "bv": bv1_row} for c in CORES]
    res = run_bass_kernel_spmd(nc, maps, core_ids=CORES)
    q1 = res.results

    nc = build_attn1()
    maps = []
    for c in CORES:
        b = c // 2
        kT_full = np.concatenate([q1[2 * b]["kT"], q1[2 * b + 1]["kT"]], axis=1)
        v_full = np.concatenate([q1[2 * b]["v"], q1[2 * b + 1]["v"]], axis=0)
        vblk = np.ascontiguousarray(
            v_full.reshape(S // 128, 128, NH, V_HD).transpose(2, 1, 0, 3).reshape(NH, 128, S))
        maps.append({"qT": q1[c]["qT"], "kTf": kT_full, "vblk": vblk,
                     "wo": deq[3], "bo": bo1_row, "xres": s2[c]})
    res = run_bass_kernel_spmd(nc, maps, core_ids=CORES)
    s3 = [res.results[c]["out"] for c in CORES]

    nc = build_ffn1()
    maps = [{"x": s3[c], "wgu": b1, "wd": wd1q, "sgu": sgu, "swd": swd}
            for c in CORES]
    res = run_bass_kernel_spmd(nc, maps, core_ids=CORES)
    out = np.concatenate([res.results[c]["out"] for c in CORES], axis=0)
    return out.reshape(B, S, D)



def build_fused():
    nc = bass.Bass("TRN2", target_bir_lowering=False)
    f = {}
    def ein(name, shape, dt):
        f[name] = nc.dram_tensor(name, shape, dt, kind="ExternalInput").ap()
    def itn(name, shape, dt):
        f[name] = nc.dram_tensor(name, shape, dt, kind="Internal")
    ein("x", [NOWN, D], F32)
    ein("xT", [D, NOWN], F32)
    ein("wqa", [D, Q_LORA], F32); ein("bqa", [128, 4], F32)
    ein("wqb_p", [Q_LORA, 4096], F32); ein("bqbn", [128, 16], F32); ein("bqbr", [64, 32], F32)
    ein("wkva_p", [D, 640], F32); ein("bkva", [128, 6], F32)
    ein("cosT", [32, NOWN], F32); ein("sinT", [32, NOWN], F32)
    ein("exp32", [32, 128], BF16)
    ein("wkvb", [KV_LORA, NH * 256], F32); ein("bkv", [128, 32], F32)
    ein("bvrow", [1, NH * V_HD], F32); ein("wo0", [D, D], F32); ein("bo0", [1, D], F32)
    ein("wgu1_sl", [256, 2 * F1], F32)
    ein("wd1_sl", [1024, D], F32)
    ein("pgu_sl", [128, 2816], I32)
    ein("srep_gu", [128, 2816], F16)
    ein("pdn_sl", [88, 2048], I32)
    ein("srep_dn", [88, 2048], F16)
    for i in range(4):
        ein(f"codes{i}", [256, D], F32)
        ein(f"mxrep{i}", [256, D], F32)
    ein("bq1", [128, 16], F32); ein("bk1", [128, 16], F32)
    ein("bv1", [1, D], F32); ein("bo1", [1, D], F32)
    out_d = nc.dram_tensor("out", [NOWN, D], F32, kind="ExternalOutput").ap()
    f["_out_ap"] = out_d

    itn("prt_i", [1, 2], F32); itn("g_prt", [1, 2], F32)
    itn("tgu1_i", [256, 2 * F1], BF16); itn("g_gu1", [D, 2 * F1], BF16)
    itn("twd1_i", [1024, D], BF16); itn("g_wd1", [F1, D], BF16)
    itn("dgu0_i", [256, 2 * I0], BF16); itn("g_gu0", [D, 2 * I0], BF16)
    itn("ddn0_i", [704, D], BF16); itn("g_dn0", [I0, D], BF16)
    for i in range(4):
        itn(f"deq{i}_i", [256, D], BF16)
        itn(f"g_deq{i}", [D, D], BF16)
    itn("qnT_i", [NH * NOPE, NOWN], BF16)
    itn("qrT_i", [NH * ROPE, NOWN], BF16)
    itn("ckv_i", [KV_LORA + ROPE, NOWN], BF16); itn("g_ckv", [2 * (KV_LORA + ROPE), NOWN], BF16)
    itn("kT_i", [D, NOWN], BF16); itn("g_kT", [2 * D, NOWN], BF16)
    itn("vb_i", [NH, 128, 8 * V_HD], BF16); itn("g_vb", [2 * NH, 128, 8 * V_HD], BF16)
    itn("str_i", [NOWN, D], BF16)
    itn("h0_i", [I0, NOWN], BF16)
    itn("h0b_i", [I0, NOWN], BF16)
    itn("xq_i", [NOWN, D], BF16)
    itn("hq_i", [NOWN, F1], BF16)
    itn("hq2_i", [NOWN, F1], BF16)

    with ExitStack() as ctx:
        e = ctx.enter_context
        STR = e(nc.sbuf_tensor("STR", [128, 8 * D], BF16))
        AT = e(nc.sbuf_tensor("AT", [128, NC_D * NOWN], BF16))
        B1 = e(nc.sbuf_tensor("B1", [128, 16384], BF16))
        B2 = e(nc.sbuf_tensor("B2", [128, 16384], BF16))
        B3 = e(nc.sbuf_tensor("B3", [128, 16384], BF16))
        CK = e(nc.sbuf_tensor("CK", [128, 12288], BF16))
        kh = e(nc.sbuf_tensor("kh", [128, 2048], BF16))
        vh = e(nc.sbuf_tensor("vh", [128, 2048], BF16))
        qnh = e(nc.sbuf_tensor("qnh", [128, 1024], BF16))
        qrp2 = e(nc.sbuf_tensor("qrp2", [128, 1024], BF16))
        srepg = CK[:, 0:2816]
        srepd = CK[:, 4096:6144]
        qrp = CK[0:64, 5120:6144]
        qrsw = CK[0:64, 6144:7168]
        qro = CK[0:64, 7168:8192]
        kpesw = e(nc.sbuf_tensor("kpesw", [64, NOWN], BF16))
        cs2 = CK[0:64, 10240:12288]
        csr = CK[0:32, 8192:10240]
        exp32 = e(nc.sbuf_tensor("exp32s", [32, 128], BF16))
        bqa = e(nc.sbuf_tensor("bqa_s", [128, 4], F32))
        bqbn = e(nc.sbuf_tensor("bqbn_s", [128, 16], F32))
        bqbr = e(nc.sbuf_tensor("bqbr_s", [64, 32], F32))
        bkva = e(nc.sbuf_tensor("bkva_s", [128, 6], F32))
        bkv = e(nc.sbuf_tensor("bkv_s", [128, 32], F32))
        bq1 = e(nc.sbuf_tensor("bq1_s", [128, 16], F32))
        bk1 = e(nc.sbuf_tensor("bk1_s", [128, 16], F32))
        bvb = e(nc.sbuf_tensor("bvb_s", [128, 2048], BF16))
        bob = bvb
        rowb = CK[0:1, 10240:12288]
        ones1 = e(nc.sbuf_tensor("ones1_s", [1, 128], BF16))
        onesc = e(nc.sbuf_tensor("onesc_s", [128, 1], BF16))
        ones2 = e(nc.sbuf_tensor("ones2_s", [128, 128], BF16))
        onescf = e(nc.sbuf_tensor("onescf_s", [128, 1], F32))
        shcol = e(nc.sbuf_tensor("shcol_s", [128, 1], F32))
        cols = e(nc.sbuf_tensor("cols_s", [128, 32], F32))
        scc = e(nc.sbuf_tensor("scc_s", [128, 2], F32))
        sc2 = e(nc.sbuf_tensor("sc2_s", [1, 4], F32))
        prtt = e(nc.sbuf_tensor("prtt_s", [8, 2], F32))
        onc8 = e(nc.sbuf_tensor("onc8_s", [8, 1], F32))
        ones1f = e(nc.sbuf_tensor("ones1f_s", [1, 128], F32))
        rrow = CK[0:1, 10240:11264]
        rbc = CK[:, 11264:12288]
        f32a = CK.ap().bitcast(F32)[:, 0:2048]
        f32b = CK.ap().bitcast(F32)[:, 2048:4096]
        PS1 = e(nc.psum_tensor("PS1", [128, 2048], F32))
        PS2 = e(nc.psum_tensor("PS2", [128, 2048], F32))
        rrow32 = PS1[0:1, 1024:2048]
        ccsem = e(nc.semaphore("ccsem"))
        block = e(nc.Block())
        pb = PB(nc, ctx)
        cc_count = [0]

        def fence():
            pb.barrier(list(ENGS), list(ENGS))
            for q in ("sync", "gpsimd", "scalar"):
                for t in ("d", "w", "a", "s", "t", "w2", "a2", "t2"):
                    for eng in ENGS:
                        pb.wait_dma(eng, q, tag=t)

        def collective(kind, op, groups, ins, outs):
            def run(h):
                return h.collective_compute(kind, op, replica_groups=groups,
                                            ins=ins, outs=outs).then_inc(ccsem)
            pb.ops["gpsimd"].append(run)
            pb.lastbox["gpsimd"] = None
            cc_count[0] += 1
            n = cc_count[0]
            for eng in ENGS:
                pb.ops[eng].append(lambda h, n=n: h.wait_ge(ccsem, n))

        ccn = {}

        def collective_async(kind, op, groups, ins, outs):
            def run(h):
                return h.collective_compute(kind, op, replica_groups=groups,
                                            ins=ins, outs=outs).then_inc(ccsem)
            pb.ops["gpsimd"].append(run)
            pb.lastbox["gpsimd"] = None
            cc_count[0] += 1
            return cc_count[0]

        def wait_cc(eng, n):
            pb.ops[eng].append(lambda h, n=n: h.wait_ge(ccsem, n))

        V, SC, T, G, SY = "vector", "scalar", "tensor", "gpsimd", "sync"

        for c in range(8):
            pb.dma(G, tag="a", fn=lambda h, c=c: h.dma_start(
                out=STR[:, c * D:(c + 1) * D], in_=f["x"][c * 128:(c + 1) * 128, :]))
        for c in range(NC_D):
            pb.dma(G, tag="a", fn=lambda h, c=c: h.dma_start(
                out=AT[:, c * NOWN:(c + 1) * NOWN], in_=f["xT"][c * 128:(c + 1) * 128, :]))
        for (dst, src) in ((bqa, "bqa"), (bqbn, "bqbn"), (bqbr, "bqbr"), (bkva, "bkva"),
                           (bkv, "bkv"), (bq1, "bq1"), (bk1, "bk1"), (exp32, "exp32")):
            pb.dma(SY, tag="a", fn=lambda h, dst=dst, src=src: h.dma_start(
                out=dst[:], in_=f[src]))
        pb.dma(G, tag="a", fn=lambda h: h.dma_start(out=csr[:, 0:NOWN], in_=f["cosT"]))
        pb.dma(G, tag="a", fn=lambda h: h.dma_start(out=csr[:, NOWN:], in_=f["sinT"]))
        pb.dma(G, tag="a", fn=lambda h: h.dma_start(out=srepg[:], in_=f["srep_gu"]))
        pb.dma(G, tag="a", fn=lambda h: h.dma_start(out=srepd[0:88, :], in_=f["srep_dn"]))
        pb.emit(V, lambda h: h.memset(ones1[:], 1.0))
        pb.emit(V, lambda h: h.memset(onesc[:], 1.0))
        pb.emit(V, lambda h: h.memset(ones2[:], 1.0))
        pb.emit(V, lambda h: h.memset(onescf[:], 1.0))
        pb.emit(V, lambda h: h.memset(onc8[:], 1.0))
        pb.emit(V, lambda h: h.memset(ones1f[:], 1.0))
        fence()

        wtf = B1.ap().bitcast(F32)
        tiles1 = ([(f["wgu1_sl"], t, w0, 0) for t in range(2)
                   for w0 in range(0, 2 * F1, 2048)] +
                  [(f["wd1_sl"], t, w0, 1) for t in range(8)
                   for w0 in range(0, D, 2048)])
        lsl1 = lambda i: wtf[:, (i % 2) * 2048:(i % 2) * 2048 + 2048]
        lt1 = lambda i: "a" if i % 2 == 0 else "a2"
        cl1 = {}
        vm1 = {}
        pb.emit(V, lambda h: h.memset(cols[:, 0:4], 0.0))
        pb.barrier([V], [V])
        for i in range(2):
            src, t, w0, col = tiles1[i]
            cl1[i] = pb.dma(SY, tag=lt1(i), fn=lambda h, src=src, t=t, w0=w0, i=i:
                            h.dma_start(
                out=lsl1(i), in_=src[t * 128:(t + 1) * 128, w0:w0 + 2048]))
        for i, (src, t, w0, col) in enumerate(tiles1):
            pb.wait_dma(V, SY, tag=lt1(i), upto=cl1[i])
            pb.barrier([V], [V])
            pb.emit(V, lambda h, i=i: h.reduce_sum(
                out=cols[:, 2:3], in_=lsl1(i), axis=AX, apply_absolute_value=True))
            vm1[i] = pb.mark(V)
            pb.barrier([V], [V])
            pb.emit(V, lambda h, col=col: h.tensor_tensor(
                out=cols[:, col:col + 1], in0=cols[:, col:col + 1],
                in1=cols[:, 2:3], op=ALU.add))
            pb.barrier([V], [V])
            if i + 2 < len(tiles1):
                ii = i + 2
                src2, t2, w02, _ = tiles1[ii]
                pb.wait_mark(SY, vm1[i])
                cl1[ii] = pb.dma(SY, tag=lt1(ii), fn=lambda h, src2=src2, t2=t2,
                                 w02=w02, ii=ii: h.dma_start(
                    out=lsl1(ii), in_=src2[t2 * 128:(t2 + 1) * 128, w02:w02 + 2048]))
        pb.barrier([V], [T])
        for j in range(2):
            pb.emit(T, lambda h, j=j: nc.tensor.matmul(
                out=PS1[0:1, j:j + 1], lhsT=cols[:, j:j + 1], rhs=onescf[:, 0:1],
                start=True, stop=True))
        pb.barrier([T], [V])
        pb.emit(V, lambda h: h.tensor_copy(out=sc2[0:1, 0:2], in_=PS1[0:1, 0:2]))
        pb.barrier([V], [G])
        pb.dma(G, tag="s", fn=lambda h: h.dma_start(out=f["prt_i"].ap(), in_=sc2[0:1, 0:2]))
        pb.wait_dma(G, G, tag="s")
        collective("AllReduce", ALU.add, [list(range(NCORE))],
                   [f["prt_i"].ap()], [f["g_prt"].ap()])
        pb.dma(SY, tag="a", fn=lambda h: h.dma_start(out=sc2[0:1, 0:2], in_=f["g_prt"].ap()))
        pb.wait_dma(V, SY, tag="a")
        pb.emit(V, lambda h: h.tensor_scalar(
            out=sc2[0:1, 0:1], in0=sc2[0:1, 0:1], scalar1=1.0 / (2 * F1 * D), scalar2=1e-5,
            op0=ALU.mult, op1=ALU.max))
        pb.emit(V, lambda h: h.tensor_scalar(
            out=sc2[0:1, 1:2], in0=sc2[0:1, 1:2], scalar1=1.0 / (D * F1), scalar2=1e-5,
            op0=ALU.mult, op1=ALU.max))
        pb.barrier([V], [V])
        pb.emit(V, lambda h: h.reciprocal(out=sc2[0:1, 2:4], in_=sc2[0:1, 0:2]))
        pb.barrier([V], [T])
        pb.emit(T, lambda h: nc.tensor.matmul(
            out=PS1[:, 4:6], lhsT=ones1f[0:1, :], rhs=sc2[0:1, 2:4], start=True, stop=True))
        pb.barrier([T], [V])
        pb.emit(V, lambda h: h.tensor_copy(out=scc[:, 0:2], in_=PS1[:, 4:6]))
        pb.barrier([V], [T])
        pb.emit(T, lambda h: nc.tensor.matmul(
            out=PS1[:, 6:8], lhsT=ones1f[0:1, :], rhs=sc2[0:1, 0:2], start=True, stop=True))
        pb.barrier([T], [V])
        pb.emit(V, lambda h: h.tensor_copy(out=cols[:, 30:32], in_=PS1[:, 6:8]))
        fence()

        f32s = B2.ap().bitcast(F32)
        bfo = B3
        iv = wtf.bitcast(I32)
        tiles2 = ([(f["wgu1_sl"], f["tgu1_i"], t, w0, 0) for t in range(2)
                   for w0 in range(0, 2 * F1, 2048)] +
                  [(f["wd1_sl"], f["twd1_i"], t, w0, 1) for t in range(8)
                   for w0 in range(0, D, 2048)])
        fsl2 = lambda i: f32s[:, (i % 2) * 2048:(i % 2) * 2048 + 2048]
        ivl2 = lambda i: iv[:, (i % 2) * 2048:(i % 2) * 2048 + 2048]
        osl2 = lambda i: bfo[:, (i % 2) * 2048:(i % 2) * 2048 + 2048]
        st2 = lambda i: "s" if i % 2 == 0 else "d"
        cl2 = {}
        st2c = {}
        vq2 = {}
        for i in range(2):
            src, dst, t, w0, col = tiles2[i]
            cl2[i] = pb.dma(SY, tag=lt1(i), fn=lambda h, src=src, t=t, w0=w0, i=i:
                            h.dma_start(
                out=lsl1(i), in_=src[t * 128:(t + 1) * 128, w0:w0 + 2048]))
        for i, (src, dst, t, w0, col) in enumerate(tiles2):
            pb.wait_dma(SC, SY, tag=lt1(i), upto=cl2[i])
            pb.barrier([V], [SC])
            pb.emit(SC, lambda h, col=col, i=i: h.activation(
                out=fsl2(i), in_=lsl1(i), func=AF.Identity,
                scale=scc[:, col:col + 1]))
            pb.barrier([SC], [V])
            pb.emit(V, lambda h, i=i: h.tensor_scalar(
                out=fsl2(i), in0=fsl2(i), scalar1=1.0, scalar2=-1.0,
                op0=ALU.min, op1=ALU.max))
            pb.barrier([V], [V])
            pb.emit(V, lambda h, i=i: h.tensor_copy(out=ivl2(i), in_=fsl2(i)))
            pb.barrier([V], [V])
            if i >= 2:
                pb.wait_dma(V, G, tag=st2(i), upto=st2c[i - 2])
            pb.emit(V, lambda h, i=i: h.tensor_copy(out=osl2(i), in_=ivl2(i)))
            vq2[i] = pb.mark(V)
            pb.barrier([V], [G])
            st2c[i] = pb.dma(G, tag=st2(i), fn=lambda h, dst=dst, t=t, w0=w0, i=i:
                             h.dma_start(
                out=dst.ap()[t * 128:(t + 1) * 128, w0:w0 + 2048], in_=osl2(i)))
            if i + 2 < len(tiles2):
                ii = i + 2
                src2, _, t2, w02, _ = tiles2[ii]
                pb.wait_mark(SY, vq2[i])
                cl2[ii] = pb.dma(SY, tag=lt1(ii), fn=lambda h, src2=src2, t2=t2,
                                 w02=w02, ii=ii: h.dma_start(
                    out=lsl1(ii), in_=src2[t2 * 128:(t2 + 1) * 128, w02:w02 + 2048]))
        fence()

        pk_i = B1.ap().bitcast(I32)
        nib_i = B2.ap().bitcast(I32)
        nbf = B3[:, 0:4096]
        d1 = B3[:, 4096:8192]
        d2 = B3[:, 8192:12288]
        d3 = B3[:, 12288:16384]

        kkc = [0]
        st3 = lambda kk: "s" if kk % 2 == 0 else "d"
        d1o = lambda kk: B2[:, 8192 + (kk % 2) * 4096: 8192 + (kk % 2) * 4096 + 4096]
        st3c = {}

        def decode(packed_src, rows, wid, srep, out_view, in_split4=False):
            pb.wait_dma(SY, SY, tag="a")
            pb.wait_dma(SY, SY, tag="a2")
            pb.barrier([V], [SY])
            pb.dma(SY, tag="a", fn=lambda h: h.dma_start(
                out=pk_i[0:rows, 0:wid], in_=packed_src))
            pb.wait_dma(V, SY, tag="a")
            for k in range(8):
                kk = kkc[0]
                kkc[0] += 1
                pb.barrier([V], [V])
                pb.emit(V, lambda h, k=k, rows=rows, wid=wid: h.tensor_scalar(
                    out=nib_i[0:rows, 0:wid], in0=pk_i[0:rows, 0:wid],
                    scalar1=4 * k, scalar2=15,
                    op0=ALU.logical_shift_right, op1=ALU.bitwise_and))
                pb.barrier([V], [V])
                pb.emit(V, lambda h, rows=rows, wid=wid: h.tensor_copy(
                    out=nbf[0:rows, 0:wid], in_=nib_i[0:rows, 0:wid]))
                pb.barrier([V], [V])
                pb.emit(V, lambda h, rows=rows, wid=wid: h.tensor_scalar(
                    out=d1[0:rows, 0:wid], in0=nbf[0:rows, 0:wid],
                    scalar1=8.0, scalar2=-2.0, op0=ALU.is_ge, op1=ALU.mult))
                pb.barrier([V], [V])
                pb.emit(V, lambda h, rows=rows, wid=wid: h.scalar_tensor_tensor(
                    out=d2[0:rows, 0:wid], in0=d1[0:rows, 0:wid], scalar=4.0,
                    in1=nbf[0:rows, 0:wid], op0=ALU.mult, op1=ALU.add))
                pb.barrier([V], [V])
                pb.emit(V, lambda h, rows=rows, wid=wid: h.tensor_scalar(
                    out=d3[0:rows, 0:wid], in0=d2[0:rows, 0:wid],
                    scalar1=2.0, scalar2=-4.0, op0=ALU.mult, op1=ALU.add))
                pb.barrier([V], [V])
                pb.emit(V, lambda h, rows=rows, wid=wid: h.tensor_tensor(
                    out=d3[0:rows, 0:wid], in0=d2[0:rows, 0:wid],
                    in1=d3[0:rows, 0:wid], op=ALU.max))
                pb.barrier([V], [V])
                pb.emit(V, lambda h, rows=rows, wid=wid: h.tensor_scalar(
                    out=d2[0:rows, 0:wid], in0=d2[0:rows, 0:wid],
                    scalar1=4.0, scalar2=-16.0, op0=ALU.mult, op1=ALU.add))
                pb.barrier([V], [V])
                pb.emit(V, lambda h, rows=rows, wid=wid: h.tensor_tensor(
                    out=d3[0:rows, 0:wid], in0=d3[0:rows, 0:wid],
                    in1=d2[0:rows, 0:wid], op=ALU.max))
                pb.barrier([V], [V])
                pb.emit(V, lambda h, rows=rows, wid=wid: h.tensor_scalar(
                    out=d3[0:rows, 0:wid], in0=d3[0:rows, 0:wid], scalar1=0.5,
                    scalar2=None, op0=ALU.mult))
                pb.barrier([V], [V])
                if kk >= 2:
                    pb.wait_dma(V, G, tag=st3(kk), upto=st3c[kk - 2])
                pb.emit(V, lambda h, rows=rows, wid=wid, kk=kk: h.scalar_tensor_tensor(
                    out=d1o(kk)[0:rows, 0:wid], in0=d1[0:rows, 0:wid], scalar=1.0,
                    in1=d3[0:rows, 0:wid], op0=ALU.add, op1=ALU.mult))
                pb.barrier([V], [V])
                pb.emit(V, lambda h, rows=rows, wid=wid, srep=srep, kk=kk: h.tensor_tensor(
                    out=d1o(kk)[0:rows, 0:wid], in0=d1o(kk)[0:rows, 0:wid],
                    in1=srep[0:rows, 0:wid], op=ALU.mult))
                pb.barrier([V], [G])
                if in_split4:
                    for q4 in range(4):
                        st3c[kk] = pb.dma(G, tag=st3(kk), fn=lambda h, k=k, q4=q4,
                                          kk=kk: h.dma_start(
                            out=out_view[(q4, k)],
                            in_=d1o(kk)[32 * q4:32 * (q4 + 1), 0:2816]))
                else:
                    st3c[kk] = pb.dma(G, tag=st3(kk), fn=lambda h, k=k, rows=rows,
                                      wid=wid, kk=kk: h.dma_start(
                        out=out_view[k], in_=d1o(kk)[0:rows, 0:wid]))

        gu_view = f["dgu0_i"].ap().rearrange("(cp k) (q j) -> q k cp j", k=8, q=4)
        decode(f["pgu_sl"], 128, 2816, srepg,
               {(q4, k): gu_view[q4, k] for q4 in range(4) for k in range(8)},
               in_split4=True)
        dn_view = f["ddn0_i"].ap().rearrange("(cp k) o -> k cp o", k=8)
        decode(f["pdn_sl"], 88, 2048, srepd, [dn_view[k] for k in range(8)])
        fence()

        for i in range(4):
            for ch in range(2):
                pb.barrier([V], [SY, G])
                pb.dma(G, tag="a", fn=lambda h, i=i, ch=ch: h.dma_start(
                    out=B1[:, 0:2048], in_=f[f"codes{i}"][ch * 128:(ch + 1) * 128, :]))
                pb.dma(G, tag="a", fn=lambda h, i=i, ch=ch: h.dma_start(
                    out=B2[:, 0:2048], in_=f[f"mxrep{i}"][ch * 128:(ch + 1) * 128, :]))
                pb.wait_dma(V, G, tag="a")
                pb.wait_dma(V, G, tag="s")
                pb.emit(V, lambda h: h.tensor_tensor(
                    out=B3[:, 0:2048], in0=B1[:, 0:2048], in1=B2[:, 0:2048], op=ALU.mult))
                pb.barrier([V], [G])
                pb.dma(G, tag="s", fn=lambda h, i=i, ch=ch: h.dma_start(
                    out=f[f"deq{i}_i"].ap()[ch * 128:(ch + 1) * 128, :], in_=B3[:, 0:2048]))
        pb.wait_dma(G, G, tag="s")
        fence()
        _fused_part2(nc, f, pb, locals())
        pb.finalize(block)
    return nc


def _fused_part2(nc, f, pb, L):
    V, SC, T, G, SY = "vector", "scalar", "tensor", "gpsimd", "sync"
    STR, AT, B1, B2, B3, CK = L["STR"], L["AT"], L["B1"], L["B2"], L["B3"], L["CK"]
    PS1, PS2 = L["PS1"], L["PS2"]
    fence = L["fence"]
    cs2loc = L["cs2"]; csr = L["csr"]; exp32 = L["exp32"]
    bqa, bqbn, bqbr, bkva = L["bqa"], L["bqbn"], L["bqbr"], L["bkva"]
    bkv, bvb, bob, rowb = L["bkv"], L["bvb"], L["bob"], L["rowb"]
    ones1, onesc, shcol, cols = L["ones1"], L["onesc"], L["shcol"], L["cols"]
    ones2 = L["ones2"]
    qrp, qrsw, qro, kpesw = L["qrp"], L["qrsw"], L["qro"], L["kpesw"]
    kh, vh, qnh, qrp2 = L["kh"], L["vh"], L["qnh"], L["qrp2"]
    rrow, rrow32, rbc = L["rrow"], L["rrow32"], L["rbc"]
    f32a, f32b = L["f32a"], L["f32b"]

    wqb = B1
    qn = B2
    wqa_s = B3[:, 0:8192]
    qlat = B3[:, 12288:16384]
    wkva_s = B3[:, 0:10240]
    ckvs = CK[:, 0:5120]
    cs2 = cs2loc

    for c in range(4):
        pb.dma(G, tag="w", fn=lambda h, c=c: h.dma_start(
            out=wqb[:, c * 4096:(c + 1) * 4096], in_=f["wqb_p"][c * 128:(c + 1) * 128, :]))
    for c in range(NC_D):
        pb.dma(G, tag="w", fn=lambda h, c=c: h.dma_start(
            out=wqa_s[:, c * 512:(c + 1) * 512], in_=f["wqa"][c * 128:(c + 1) * 128, :]))
    pb.wait_dma(T, G, tag="w")
    for j in range(2):
        for qh in range(2):
            pb.emit(T, lambda h, j=j, qh=qh: nc.tensor.matmul(
                out=PS2[0:64, qh * 512:qh * 512 + 512],
                lhsT=exp32[:, j * 64:(j + 1) * 64],
                rhs=csr[:, j * NOWN + qh * 512: j * NOWN + qh * 512 + 512],
                start=True, stop=True))
        pb.barrier([T], [V])
        pb.emit(V, lambda h, j=j: h.tensor_copy(
            out=cs2[0:64, j * NOWN:(j + 1) * NOWN], in_=PS2[0:64, 0:1024]))
        pb.barrier([V], [T])
    svm5 = {}
    for t in range(4):
        for qh in range(2):
            idx = t * 2 + qh
            p = PS1[:, (idx % 4) * 512:(idx % 4) * 512 + 512]
            if idx >= 4:
                pb.wait_mark(T, svm5[idx - 4])
            for c in range(NC_D):
                pb.emit(T, lambda h, t=t, qh=qh, c=c, p=p: nc.tensor.matmul(
                    out=p, lhsT=wqa_s[:, c * 512 + t * 128: c * 512 + (t + 1) * 128],
                    rhs=AT[:, c * NOWN + qh * 512: c * NOWN + qh * 512 + 512],
                    start=(c == 0), stop=(c == NC_D - 1)))
            tm5 = pb.mark(T)
            pb.wait_mark(SC, tm5)
            pb.emit(SC, lambda h, t=t, qh=qh, p=p: h.activation(
                out=qlat[:, t * NOWN + qh * 512: t * NOWN + qh * 512 + 512],
                in_=p, func=AF.Identity, bias=bqa[:, t:t + 1], scale=1.0))
            svm5[idx] = pb.mark(SC)
    pb.barrier([T], [G])
    for c in range(NC_D):
        pb.dma(G, tag="w", fn=lambda h, c=c: h.dma_start(
            out=wkva_s[:, c * 640:(c + 1) * 640], in_=f["wkva_p"][c * 128:(c + 1) * 128, :]))
    pb.wait_dma(T, G, tag="w")
    svc5 = {}
    for t in range(6):
        rows = 128 if t < 4 else 64
        bcol = bkva[:, t:t + 1] if t < 4 else bkva[0:64, t:t + 1]
        for qh in range(2):
            idx = t * 2 + qh
            p = PS1[:, (idx % 4) * 512:(idx % 4) * 512 + 512]
            if idx >= 4:
                pb.wait_mark(T, svc5[idx - 4])
            elif t == 0 and qh == 0:
                pb.wait_mark(T, svm5[7])
            for c in range(NC_D):
                pb.emit(T, lambda h, t=t, qh=qh, c=c, p=p, rows=rows: nc.tensor.matmul(
                    out=p[0:rows, :],
                    lhsT=wkva_s[:, c * 640 + t * 128: c * 640 + t * 128 + rows] if t < 5
                    else wkva_s[:, c * 640 + 576: c * 640 + 640],
                    rhs=AT[:, c * NOWN + qh * 512: c * NOWN + qh * 512 + 512],
                    start=(c == 0), stop=(c == NC_D - 1)))
            tm5 = pb.mark(T)
            pb.wait_mark(SC, tm5)
            if t < 5:
                pb.emit(SC, lambda h, t=t, qh=qh, p=p, rows=rows, bcol=bcol: h.activation(
                    out=ckvs[0:rows, t * NOWN + qh * 512: t * NOWN + qh * 512 + 512],
                    in_=p[0:rows, :], func=AF.Identity, bias=bcol, scale=1.0))
            else:
                pb.emit(SC, lambda h, qh=qh, p=p, bcol=bcol: h.activation(
                    out=kpesw[0:64, qh * 512:qh * 512 + 512],
                    in_=p[0:64, :], func=AF.Identity, bias=bcol, scale=1.0))
            svc5[idx] = pb.mark(SC)
    svn5 = {}
    for t in range(NH):
        for qh in range(2):
            idx = t * 2 + qh
            p = PS1[:, (idx % 4) * 512:(idx % 4) * 512 + 512]
            if idx >= 4:
                pb.wait_mark(T, svn5[idx - 4])
            elif t == 0 and qh == 0:
                pb.wait_mark(T, svc5[11])
            for c in range(4):
                pb.emit(T, lambda h, t=t, qh=qh, c=c, p=p: nc.tensor.matmul(
                    out=p, lhsT=wqb[:, c * 4096 + t * 128: c * 4096 + (t + 1) * 128],
                    rhs=qlat[:, c * NOWN + qh * 512: c * NOWN + qh * 512 + 512],
                    start=(c == 0), stop=(c == 3)))
            tm5 = pb.mark(T)
            pb.wait_mark(SC, tm5)
            pb.emit(SC, lambda h, t=t, qh=qh, p=p: h.activation(
                out=qn[:, t * NOWN + qh * 512: t * NOWN + qh * 512 + 512],
                in_=p, func=AF.Identity, bias=bqbn[:, t:t + 1], scale=1.0))
            svn5[idx] = pb.mark(SC)
    pb.barrier([SC], [SY, T])
    for c in range(NH):
        pb.dma(SY, tag="s", fn=lambda h, c=c: h.dma_start(
            out=f["qnT_i"].ap()[c * 128:(c + 1) * 128, :], in_=qn[:, c * NOWN:(c + 1) * NOWN]))
    for hh in range(NH):
        for qh in range(2):
            for c in range(4):
                pb.emit(T, lambda h, hh=hh, qh=qh, c=c: nc.tensor.matmul(
                    out=PS2[0:64, qh * 512:qh * 512 + 512],
                    lhsT=wqb[:, c * 4096 + 2048 + hh * 64: c * 4096 + 2048 + (hh + 1) * 64],
                    rhs=qlat[:, c * NOWN + qh * 512: c * NOWN + qh * 512 + 512],
                    start=(c == 0), stop=(c == 3)))
            for c in range(4):
                pb.emit(T, lambda h, hh=hh, qh=qh, c=c: nc.tensor.matmul(
                    out=PS1[0:64, qh * 512:qh * 512 + 512],
                    lhsT=wqb[:, c * 4096 + 3072 + hh * 64: c * 4096 + 3072 + (hh + 1) * 64],
                    rhs=qlat[:, c * NOWN + qh * 512: c * NOWN + qh * 512 + 512],
                    start=(c == 0), stop=(c == 3)))
        pb.barrier([T], [SC])
        pb.wait_dma(SC, G, tag="s")
        pb.barrier([V], [SC])
        pb.emit(SC, lambda h, hh=hh: h.activation(
            out=qrp[0:64, 0:NOWN], in_=PS2[0:64, 0:NOWN],
            func=AF.Identity, bias=bqbr[0:64, hh:hh + 1], scale=1.0))
        pb.emit(SC, lambda h, hh=hh: h.activation(
            out=qrsw[0:64, 0:NOWN], in_=PS1[0:64, 0:NOWN],
            func=AF.Identity, bias=bqbr[0:64, 16 + hh:17 + hh], scale=1.0))
        pb.barrier([SC], [T, V])
        _emit_rope(pb, qrp, qrsw, qro, cs2, NOWN)
        pb.barrier([V], [G])
        pb.dma(G, tag="s", fn=lambda h, hh=hh: h.dma_start(
            out=f["qrT_i"].ap()[hh * 64:(hh + 1) * 64, :], in_=qro[0:64, 0:NOWN]))
    pb.barrier([SC], [V])
    pb.wait_dma(V, G, tag="s")
    pb.emit(V, lambda h: h.tensor_copy(out=qrp[0:64, 0:NOWN], in_=ckvs[0:64, 4 * NOWN:5 * NOWN]))
    pb.barrier([V], [V])
    _emit_rope(pb, qrp, kpesw, qro, cs2, NOWN)
    pb.barrier([V], [G])
    pb.dma(G, tag="s", fn=lambda h: h.dma_start(
        out=f["ckv_i"].ap()[512:576, :], in_=qro[0:64, 0:NOWN]))
    pb.barrier([SC], [G])
    for t in range(4):
        pb.dma(G, tag="s", fn=lambda h, t=t: h.dma_start(
            out=f["ckv_i"].ap()[t * 128:(t + 1) * 128, :], in_=ckvs[:, t * NOWN:(t + 1) * NOWN]))
    pb.wait_dma(G, G, tag="s")
    L["collective"]("AllGather", ALU.bypass, [[0, 1], [2, 3], [4, 5], [6, 7]],
                    [f["ckv_i"].ap()], [f["g_ckv"].ap()])
    ca, ccn = L["collective_async"], L["ccn"]
    g8 = [list(range(8))]
    ccn["gu0"] = ca("AllGather", ALU.bypass, g8, [f["dgu0_i"].ap()], [f["g_gu0"].ap()])
    ccn["dn0"] = ca("AllGather", ALU.bypass, g8, [f["ddn0_i"].ap()], [f["g_dn0"].ap()])
    for i in range(4):
        ccn[f"deq{i}"] = ca("AllGather", ALU.bypass, g8,
                            [f[f"deq{i}_i"].ap()], [f[f"g_deq{i}"].ap()])
    ccn["gu1"] = ca("AllGather", ALU.bypass, g8, [f["tgu1_i"].ap()], [f["g_gu1"].ap()])
    ccn["wd1"] = ca("AllGather", ALU.bypass, g8, [f["twd1_i"].ap()], [f["g_wd1"].ap()])
    fence()

    wkv = B3
    pt = B1
    ao = B2
    ckv = CK[:, 0:10240]
    NKT = S // 128
    NCC = KV_LORA // 128
    gc = f["g_ckv"].ap()

    for t in range(4):
        for b in range(2):
            pb.dma(SY, tag="a", fn=lambda h, t=t, b=b: h.dma_start(
                out=ckv[:, t * 2048 + b * 1024: t * 2048 + (b + 1) * 1024],
                in_=gc[b * 576 + t * 128: b * 576 + (t + 1) * 128, :]))
    for b in range(2):
        pb.dma(SY, tag="a", fn=lambda h, b=b: h.dma_start(
            out=ckv[0:64, 4 * 2048 + b * 1024: 4 * 2048 + (b + 1) * 1024],
            in_=gc[b * 576 + 512: b * 576 + 576, :]))
        pb.dma(SY, tag="a", fn=lambda h, b=b: h.dma_start(
            out=ckv[64:128, 4 * 2048 + b * 1024: 4 * 2048 + (b + 1) * 1024],
            in_=gc[b * 576 + 512: b * 576 + 576, :]))
    for c in range(NCC):
        pb.dma(G, tag="w", fn=lambda h, c=c: h.dma_start(
            out=wkv[:, c * 4096:(c + 1) * 4096], in_=f["wkvb"][c * 128:(c + 1) * 128, :]))
    pb.dma(G, tag="a", fn=lambda h: h.dma_start(out=rowb[:], in_=f["bvrow"]))
    qnb = lambda hh: AT[:, (hh % 2) * 1024:(hh % 2) * 1024 + 1024]
    qrb = lambda hp: AT[:, 2048 + (hp % 2) * 1024: 2048 + (hp % 2) * 1024 + 1024]
    qtg = lambda hh: "a" if hh % 2 == 0 else "a2"
    rtg = lambda hp: "t" if hp % 2 == 0 else "t2"
    qcnt = {}
    rcnt = {}
    am0 = {}
    vao0 = {}

    def emit_q_loads(hh):
        qcnt[hh] = pb.dma(SY, tag=qtg(hh), fn=lambda h, hh=hh: h.dma_start(
            out=qnb(hh), in_=f["qnT_i"].ap()[hh * 128:(hh + 1) * 128, :]))

    def emit_r_loads(hp):
        rcnt[hp] = pb.dma(SY, tag=rtg(hp), fn=lambda h, hp=hp: h.dma_start(
            out=qrb(hp), in_=f["qrT_i"].ap()[hp * 128:(hp + 1) * 128, :]))

    emit_q_loads(0)
    emit_q_loads(1)
    emit_r_loads(0)
    emit_r_loads(1)
    pb.emit(V, lambda h: h.memset(shcol := L["shcol"][:], -SHIFT0))
    pb.barrier([V], [SC])
    pb.wait_dma(V, G, tag="a")
    pb.barrier([V], [T])
    pb.wait_dma(T, G, tag="w")
    pb.wait_dma(T, SY, tag="a", upto=qcnt[0])
    for s4 in range(4):
        pb.emit(T, lambda h, s4=s4: nc.tensor.matmul(
            out=PS2[:, 0:512], lhsT=ones1[:], rhs=rowb[:, s4 * 512:(s4 + 1) * 512],
            start=True, stop=True))
        pb.barrier([T], [V])
        pb.emit(V, lambda h, s4=s4: h.tensor_copy(
            out=bvb[:, s4 * 512:(s4 + 1) * 512], in_=PS2[:, 0:512]))
        pb.barrier([V], [T])
    shcol = L["shcol"]
    for hh in range(NH):
        kex = {}
        for ks in range(4):
            if ks >= 2:
                pb.wait_mark(T, kex[ks - 2])
            for c in range(NCC):
                pb.emit(T, lambda h, hh=hh, ks=ks, c=c: nc.tensor.matmul(
                    out=PS1[:, 1024 + (ks % 2) * 512:1024 + (ks % 2) * 512 + 512],
                    lhsT=wkv[:, c * 4096 + hh * 256: c * 4096 + hh * 256 + 128],
                    rhs=ckv[:, c * 2048 + ks * 512: c * 2048 + ks * 512 + 512],
                    start=(c == 0), stop=(c == NCC - 1)))
            km = pb.mark(T)
            pb.wait_mark(SC, km)
            pb.emit(SC, lambda h, hh=hh, ks=ks: h.activation(
                out=kh[:, ks * 512:(ks + 1) * 512],
                in_=PS1[:, 1024 + (ks % 2) * 512:1024 + (ks % 2) * 512 + 512],
                func=AF.Identity, bias=bkv[:, 2 * hh:2 * hh + 1], scale=1.0))
            kex[ks] = pb.mark(SC)
        vexg = {}
        for g in range(NKT // 4):
            gb = 1024 + (g % 2) * 512
            if g < 2:
                pb.wait_mark(T, kex[2 + g])
            else:
                pb.wait_mark(T, vexg[g - 2])
            for k4 in range(4):
                kt = g * 4 + k4
                for c in range(NCC):
                    pb.emit(T, lambda h, hh=hh, kt=kt, k4=k4, c=c, gb=gb:
                            nc.tensor.matmul(
                        out=PS1[:, gb + k4 * 128:gb + k4 * 128 + 128],
                        lhsT=ckv[:, c * 2048 + kt * 128: c * 2048 + (kt + 1) * 128],
                        rhs=wkv[:, c * 4096 + hh * 256 + 128: c * 4096 + hh * 256 + 256],
                        start=(c == 0), stop=(c == NCC - 1)))
            vm = pb.mark(T)
            pb.wait_mark(V, vm)
            for k4 in range(4):
                kt = g * 4 + k4
                pb.emit(V, lambda h, hh=hh, kt=kt, k4=k4, gb=gb: h.tensor_tensor(
                    out=vh[:, kt * 128:(kt + 1) * 128],
                    in0=PS1[:, gb + k4 * 128:gb + k4 * 128 + 128],
                    in1=bvb[:, hh * 128:(hh + 1) * 128], op=ALU.add))
            vexg[g] = pb.mark(V)
        pb.barrier([SC, V], [T])
        pb.wait_dma(T, SY, tag=qtg(hh), upto=qcnt[hh])
        pb.wait_dma(T, SY, tag=rtg(hh // 2), upto=rcnt[hh // 2])
        exp_mk = {}
        def emit_spv0(kt, hh=hh):
            pb.wait_mark(T, exp_mk[kt])
            if kt == 0 and hh >= 1:
                pb.wait_mark(T, vao0[hh - 1])
            for qh in range(2):
                pb.emit(T, lambda h, kt=kt, qh=qh: nc.tensor.matmul(
                    out=PS2[:, qh * 512:qh * 512 + 512], lhsT=ones2[:],
                    rhs=pt[:, kt * NOWN + qh * 512: kt * NOWN + qh * 512 + 512],
                    start=(kt == 0), stop=(kt == NKT - 1)))
                pb.emit(T, lambda h, kt=kt, qh=qh: nc.tensor.matmul(
                    out=PS2[:, 1024 + qh * 512:1024 + qh * 512 + 512],
                    lhsT=vh[:, kt * 128:(kt + 1) * 128],
                    rhs=pt[:, kt * NOWN + qh * 512: kt * NOWN + qh * 512 + 512],
                    start=(kt == 0), stop=(kt == NKT - 1)))
        for kt in range(NKT):
            hb2 = (kt % 2) * 1024
            if kt >= 2:
                pb.wait_mark(T, exp_mk[kt - 2])
            for qh in range(2):
                pb.emit(T, lambda h, hh=hh, kt=kt, qh=qh, hb2=hb2: nc.tensor.matmul(
                    out=PS1[:, hb2 + qh * 512:hb2 + qh * 512 + 512],
                    lhsT=kh[:, kt * 128:(kt + 1) * 128],
                    rhs=qnb(hh)[:, qh * 512:qh * 512 + 512],
                    start=True, stop=False))
                pb.emit(T, lambda h, hh=hh, kt=kt, qh=qh, hb2=hb2: nc.tensor.matmul(
                    out=PS1[:, hb2 + qh * 512:hb2 + qh * 512 + 512],
                    lhsT=ckv[64 * (hh % 2):64 * (hh % 2) + 64,
                             4 * 2048 + kt * 128: 4 * 2048 + (kt + 1) * 128],
                    rhs=qrb(hh // 2)[64 * (hh % 2):64 * (hh % 2) + 64,
                                     qh * 512:qh * 512 + 512],
                    start=False, stop=True))
            qk_mk = pb.mark(T)
            pb.wait_mark(SC, qk_mk)
            pb.emit(SC, lambda h, kt=kt, hb2=hb2: h.activation(
                out=pt[:, kt * NOWN:(kt + 1) * NOWN], in_=PS1[:, hb2:hb2 + 1024],
                func=AF.Exp, bias=shcol[:, 0:1], scale=ISQ_HD))
            exp_mk[kt] = pb.mark(SC)
            if kt >= 1:
                emit_spv0(kt - 1)
        emit_spv0(NKT - 1)
        am0[hh] = pb.mark(T)
        if hh + 2 < NH:
            pb.wait_mark(SY, am0[hh])
            emit_q_loads(hh + 2)
        if hh % 2 == 1 and hh // 2 + 2 < NH // 2:
            pb.wait_mark(SY, am0[hh])
            emit_r_loads(hh // 2 + 2)
        pb.barrier([T], [V])
        pb.barrier([SC], [V])
        pb.emit(V, lambda h: h.reciprocal(out=PS1[:, 0:1024], in_=PS2[:, 0:1024]))
        pb.barrier([V], [V])
        pb.emit(V, lambda h: h.tensor_copy(out=rbc[:, 0:1024], in_=PS1[:, 0:1024]))
        pb.barrier([V], [V])
        for qh in range(2):
            pb.emit(V, lambda h, hh=hh, qh=qh: h.tensor_tensor(
                out=ao[:, hh * NOWN + qh * 512: hh * NOWN + qh * 512 + 512],
                in0=PS2[:, 1024 + qh * 512:1024 + qh * 512 + 512],
                in1=rbc[:, qh * 512:qh * 512 + 512], op=ALU.mult))
        vao0[hh] = pb.mark(V)
    wos = B3
    pb.wait_mark(T, vao0[NH - 1])
    pb.barrier([T], [G])
    pb.barrier([V], [G])
    pb.dma(G, tag="a", fn=lambda h: h.dma_start(out=rowb[:], in_=f["bo0"]))
    pb.wait_dma(T, G, tag="a")
    for s4 in range(4):
        pb.emit(T, lambda h, s4=s4: nc.tensor.matmul(
            out=PS2[:, 0:512], lhsT=ones1[:], rhs=rowb[:, s4 * 512:(s4 + 1) * 512],
            start=True, stop=True))
        pb.barrier([T], [V])
        pb.emit(V, lambda h, s4=s4: h.tensor_copy(
            out=bob[:, s4 * 512:(s4 + 1) * 512], in_=PS2[:, 0:512]))
        pb.barrier([V], [T])
    wosb = lambda i: wos[:, (i % 2) * 8192:(i % 2) * 8192 + 8192]
    wtag = lambda i: "w" if i % 2 == 0 else "w2"
    pb.barrier([T], [G])
    cw = {}
    for c in range(NC_D):
        cw[0] = pb.dma(G, tag=wtag(0), fn=lambda h, c=c: h.dma_start(
            out=wosb(0)[:, c * 512:(c + 1) * 512],
            in_=f["wo0"][c * 128:(c + 1) * 128, 0:512]))
    wtoks = {}
    wvm = {}
    for s4 in range(4):
        if s4 + 1 < 4:
            if s4 >= 1:
                pb.wait_mark(G, wtoks[s4 - 1])
            for c in range(NC_D):
                cw[s4 + 1] = pb.dma(G, tag=wtag(s4 + 1), fn=lambda h, c=c, s4=s4: h.dma_start(
                    out=wosb(s4 + 1)[:, c * 512:(c + 1) * 512],
                    in_=f["wo0"][c * 128:(c + 1) * 128, (s4 + 1) * 512:(s4 + 2) * 512]))
        pb.wait_dma(T, G, tag=wtag(s4), upto=cw[s4])
        for qt in range(8):
            gidx = s4 * 8 + qt
            p_off = (gidx % 4) * 512
            if gidx >= 4:
                pb.wait_mark(T, wvm[gidx - 4])
            for c in range(NC_D):
                pb.emit(T, lambda h, qt=qt, s4=s4, c=c, p_off=p_off: nc.tensor.matmul(
                    out=PS1[:, p_off:p_off + 512],
                    lhsT=ao[:, c * NOWN + qt * 128: c * NOWN + (qt + 1) * 128],
                    rhs=wosb(s4)[:, c * 512:(c + 1) * 512],
                    start=(c == 0), stop=(c == NC_D - 1)))
            wtm = pb.mark(T)
            pb.wait_mark(V, wtm)
            pb.emit(V, lambda h, qt=qt, s4=s4, p_off=p_off: h.tensor_tensor(
                out=pt[:, qt * 2048 + s4 * 512: qt * 2048 + s4 * 512 + 512],
                in0=PS1[:, p_off:p_off + 512],
                in1=bob[:, s4 * 512:(s4 + 1) * 512], op=ALU.add))
            wvm[gidx] = pb.mark(V)
        wtoks[s4] = pb.mark(T)
    for qt in range(8):
        pb.barrier([V], [V])
        pb.emit(V, lambda h, qt=qt: h.tensor_tensor(
            out=pt[:, qt * 2048: qt * 2048 + 2048],
            in0=pt[:, qt * 2048: qt * 2048 + 2048],
            in1=STR[:, qt * D:(qt + 1) * D], op=ALU.add))
        _fused_ln(pb, pt, qt * 2048, f32a, cols, STR[:, qt * D:(qt + 1) * D])
        pb.barrier([V], [G])
        pb.dma(G, tag="s", fn=lambda h, qt=qt: h.dma_start(
            out=f["str_i"].ap()[qt * 128:(qt + 1) * 128, :],
            in_=STR[:, qt * D:(qt + 1) * D]))
    fence()

    pb.wait_dma(SY, G, tag="s")
    for c in range(NC_D):
        pb.dma(SY, tag="t", fn=lambda h, c=c: h.dma_start(
            out=AT[:, c * NOWN:(c + 1) * NOWN],
            in_=f["str_i"].ap()[:, c * 128:(c + 1) * 128], transpose=True))
    pb.wait_dma(T, SY, tag="t")
    gu0 = f["g_gu0"].ap()
    dn0 = f["g_dn0"].ap()
    evmk = {}
    stcnt0 = {}
    stg0 = lambda sg: ("s", "d", "a", "a2")[sg % 4]
    for pas in range(2):
        def emit_wload(sg, pas=pas):
            buf = (sg % 2) * 8192
            tg = "w" if sg % 2 == 0 else "w2"
            cnt = None
            for cc in range(NC_D):
                cnt = pb.dma(G, tag=tg, fn=lambda h, sg=sg, cc=cc, buf=buf, pas=pas:
                             h.dma_start(
                    out=B3[:, buf + cc * 512: buf + (cc + 1) * 512],
                    in_=gu0[cc * 128:(cc + 1) * 128,
                            pas * I0 + sg * 512: pas * I0 + (sg + 1) * 512]))
            return cnt
        pb.barrier([T], [G])
        if pas == 0:
            L["wait_cc"](G, L["ccn"]["gu0"])
        wcnt = {0: emit_wload(0)}
        wtk = {}
        for sg in range(11):
            buf = (sg % 2) * 8192
            if sg + 1 < 11:
                if sg >= 1:
                    pb.wait_mark(G, wtk[sg - 1])
                wcnt[sg + 1] = emit_wload(sg + 1)
            pb.wait_dma(T, G, tag=("w" if sg % 2 == 0 else "w2"), upto=wcnt[sg])
            prev = (pas, sg - 1) if sg >= 1 else ((pas - 1, 10) if pas == 1 else None)
            for sp in range(4):
                pgrp = (PS1 if sp < 2 else PS2)
                off = (sp % 2) * 1024
                if sp in (0, 2) and prev is not None:
                    pb.wait_mark(T, evmk[prev][sp // 2])
                for qh in range(2):
                    for cc in range(NC_D):
                        pb.emit(T, lambda h, sg=sg, sp=sp, qh=qh, cc=cc, buf=buf,
                                pgrp=pgrp, off=off: nc.tensor.matmul(
                            out=pgrp[:, off + qh * 512: off + qh * 512 + 512],
                            lhsT=B3[:, buf + cc * 512 + sp * 128: buf + cc * 512 + (sp + 1) * 128],
                            rhs=AT[:, cc * NOWN + qh * 512: cc * NOWN + qh * 512 + 512],
                            start=(cc == 0), stop=(cc == NC_D - 1)))
            wtk[sg] = pb.mark(T)
            pb.barrier([T], [SC, V])
            pvst = (pas, sg - 4) if sg >= 4 else ((0, sg + 7) if pas == 1 else None)
            if pvst is not None:
                pb.wait_dma(SC, G, tag=stg0(pvst[1]), upto=stcnt0[pvst])
                pb.wait_dma(V, G, tag=stg0(pvst[1]), upto=stcnt0[pvst])
            mk1 = mk2 = None
            for sp in range(4):
                pgrp = (PS1 if sp < 2 else PS2)
                off = (sp % 2) * 1024
                it = sg * 4 + sp
                if pas == 0:
                    pb.emit(SC, lambda h, it=it, pgrp=pgrp, off=off: h.activation(
                        out=B1[:, (it % 16) * 1024:(it % 16) * 1024 + 1024],
                        in_=pgrp[:, off:off + 1024], func=AF.Silu))
                else:
                    pb.wait_dma(V, SY, tag="a")
                    pb.emit(V, lambda h, it=it, pgrp=pgrp, off=off: h.tensor_tensor(
                        out=B1[:, (it % 16) * 1024:(it % 16) * 1024 + 1024],
                        in0=pgrp[:, off:off + 1024],
                        in1=B2[:, (it % 16) * 1024:(it % 16) * 1024 + 1024], op=ALU.mult))
                if sp == 1:
                    mk1 = pb.mark(SC if pas == 0 else V)
                elif sp == 3:
                    mk2 = pb.mark(SC if pas == 0 else V)
            evmk[(pas, sg)] = (mk1, mk2)
            if pas == 0:
                pb.barrier([SC], [G])
            else:
                pb.barrier([V], [G])
            hqo0 = f["h0_i"].ap() if pas == 0 else f["h0b_i"].ap()
            for sp in range(4):
                it = sg * 4 + sp
                stcnt0[(pas, sg)] = pb.dma(G, tag=stg0(sg), fn=lambda h, it=it,
                                           hqo0=hqo0: h.dma_start(
                    out=hqo0[it * 128:(it + 1) * 128, :],
                    in_=B1[:, (it % 16) * 1024:(it % 16) * 1024 + 1024]))
            if pas == 1 and sg < 10:
                pb.barrier([V], [SY])
                for sp in range(4):
                    it = (sg + 1) * 4 + sp
                    pb.dma(SY, tag="a", fn=lambda h, it=it: h.dma_start(
                        out=B2[:, (it % 16) * 1024:(it % 16) * 1024 + 1024],
                        in_=f["h0_i"].ap()[it * 128:(it + 1) * 128, :]))
        if pas == 0:
            for tgx in ("s", "d", "a", "a2"):
                pb.wait_dma(SY, G, tag=tgx)
            pb.barrier([V, SC], [SY])
            for sp in range(4):
                pb.dma(SY, tag="a", fn=lambda h, sp=sp: h.dma_start(
                    out=B2[:, sp * 1024:(sp + 1) * 1024],
                    in_=f["h0_i"].ap()[sp * 128:(sp + 1) * 128, :]))
    fence()
    NCH0, IC40 = 11, 4
    wtn = lambda c: "w" if c % 2 == 0 else "w2"
    htn = lambda c: "a" if c % 2 == 0 else "a2"
    wv = lambda c: B3[:, (c % 2) * 8192:(c % 2) * 8192 + 8192]
    hv = lambda c: CK[:, (c % 2) * 4096:(c % 2) * 4096 + 4096]
    B1f = B1.ap().bitcast(F32)
    B2f = B2.ap().bitcast(F32)
    accv = lambda qt: (B1f if qt < 4 else B2f)[:, (qt % 4) * 2048:(qt % 4) * 2048 + 2048]
    sq32 = CK.ap().bitcast(F32)[:, 4096:6144]
    wcn = {}
    hcn = {}

    def emit_dn0_loads(c):
        for j in range(IC40):
            ic = c * IC40 + j
            wcn[c] = pb.dma(G, tag=wtn(c), fn=lambda h, c=c, j=j, ic=ic: h.dma_start(
                out=wv(c)[:, j * 2048:(j + 1) * 2048],
                in_=dn0[ic * 128:(ic + 1) * 128, :]))
            hcn[c] = pb.dma(SY, tag=htn(c), fn=lambda h, c=c, j=j, ic=ic: h.dma_start(
                out=hv(c)[:, j * 1024:(j + 1) * 1024],
                in_=f["h0b_i"].ap()[ic * 128:(ic + 1) * 128, :]))

    pb.barrier([T, V, SC], [G, SY])
    for tgx in ("s", "d", "a", "a2"):
        pb.wait_dma(V, G, tag=tgx)
        pb.wait_dma(SY, G, tag=tgx)
    L["wait_cc"](G, L["ccn"]["dn0"])
    emit_dn0_loads(0)
    tmk = {}
    vem = {}
    for c in range(NCH0):
        if c + 1 < NCH0:
            if c >= 1:
                pb.wait_mark(G, tmk[(c - 1, 7)])
                pb.wait_mark(SY, tmk[(c - 1, 7)])
            emit_dn0_loads(c + 1)
        pb.wait_dma(T, G, tag=wtn(c), upto=wcn[c])
        pb.wait_dma(T, SY, tag=htn(c), upto=hcn[c])
        for qt in range(8):
            p = PS1 if qt % 2 == 0 else PS2
            pu = (c, qt - 2) if qt >= 2 else ((c - 1, qt + 6) if c >= 1 else None)
            if pu is not None:
                pb.wait_mark(T, vem[pu])
            for j in range(IC40):
                for os in range(4):
                    pb.emit(T, lambda h, c=c, j=j, qt=qt, os=os, p=p: nc.tensor.matmul(
                        out=p[:, os * 512:(os + 1) * 512],
                        lhsT=hv(c)[:, j * 1024 + qt * 128: j * 1024 + (qt + 1) * 128],
                        rhs=wv(c)[:, j * 2048 + os * 512: j * 2048 + (os + 1) * 512],
                        start=(j == 0), stop=(j == IC40 - 1)))
            tmk[(c, qt)] = pb.mark(T)
            pb.wait_mark(V, tmk[(c, qt)])
            if c == 0:
                pb.emit(V, lambda h, qt=qt, p=p: h.tensor_copy(
                    out=accv(qt), in_=p[:, 0:2048]))
            else:
                pb.emit(V, lambda h, qt=qt, p=p: h.tensor_tensor(
                    out=accv(qt), in0=p[:, 0:2048], in1=accv(qt), op=ALU.add))
            vem[(c, qt)] = pb.mark(V)
    for qt in range(8):
        pb.barrier([V], [SC, V])
        pb.emit(SC, lambda h, qt=qt: h.activation(out=sq32[:, 0:2048], in_=accv(qt),
                                                  func=AF.Square))
        pb.barrier([SC], [V])
        pb.emit(V, lambda h: h.reduce_sum(out=cols[:, 8:9], in_=sq32[:, 0:2048], axis=AX))
        pb.barrier([V], [V])
        pb.emit(V, lambda h: h.tensor_scalar(
            out=cols[:, 9:10], in0=cols[:, 8:9], scalar1=1.0 / D, scalar2=1e-6,
            op0=ALU.mult, op1=ALU.add))
        pb.barrier([V], [SC])
        pb.emit(SC, lambda h: h.activation(out=cols[:, 9:10], in_=cols[:, 9:10],
                                           func=AF.Sqrt))
        pb.barrier([SC], [V])
        pb.emit(V, lambda h: h.reciprocal(out=cols[:, 10:11], in_=cols[:, 9:10]))
        pb.barrier([V], [V])
        pb.emit(V, lambda h, qt=qt: h.scalar_tensor_tensor(
            out=accv(qt), in0=accv(qt), scalar=cols[:, 10:11],
            in1=STR[:, qt * D:(qt + 1) * D], op0=ALU.mult, op1=ALU.add))
        _fused_ln(pb, accv(qt), 0, sq32, cols, STR[:, qt * D:(qt + 1) * D])
        pb.barrier([V], [G])
        pb.dma(G, tag="s", fn=lambda h, qt=qt: h.dma_start(
            out=f["str_i"].ap()[qt * 128:(qt + 1) * 128, :],
            in_=STR[:, qt * D:(qt + 1) * D]))
    fence()
    _fused_part3(nc, f, pb, L)


def _fused_ln(pb, s_sb, s_off, scratch, cols, out_view):
    nc = pb.nc
    V, SC = "vector", "scalar"
    pb.barrier([V], [V])
    for j in range(4):
        pb.emit(V, lambda h, j=j: h.bn_stats(
            out=scratch[:, j * 6:(j + 1) * 6],
            in_=s_sb[:, s_off + j * 512: s_off + (j + 1) * 512]))
    pb.barrier([V], [V])
    pb.emit(V, lambda h: h.bn_aggr(out=cols[:, 2:4], in_=scratch[:, 0:24]))
    pb.barrier([V], [V])
    pb.emit(V, lambda h: h.tensor_scalar_add(out=cols[:, 5:6], in0=cols[:, 3:4],
                                             scalar1=1e-5))
    pb.barrier([V], [SC])
    pb.emit(SC, lambda h: h.activation(out=cols[:, 5:6], in_=cols[:, 5:6], func=AF.Sqrt))
    pb.barrier([SC], [V])
    pb.emit(V, lambda h: h.reciprocal(out=cols[:, 6:7], in_=cols[:, 5:6]))
    pb.barrier([V], [V])
    pb.emit(V, lambda h: h.tensor_scalar(
        out=out_view, in0=s_sb[:, s_off:s_off + D], scalar1=cols[:, 2:3],
        scalar2=cols[:, 6:7], op0=ALU.subtract, op1=ALU.mult))
    pb.barrier([V], [V])


def _fused_part3(nc, f, pb, L):
    V, SC, T, G, SY = "vector", "scalar", "tensor", "gpsimd", "sync"
    STR, AT, B1, B2, B3, CK = L["STR"], L["AT"], L["B1"], L["B2"], L["B3"], L["CK"]
    PS1, PS2 = L["PS1"], L["PS2"]
    fence = L["fence"]
    bq1, bk1, bvb, bob, rowb = L["bq1"], L["bk1"], L["bvb"], L["bob"], L["rowb"]
    ones1, onesc, shcol, cols, scc = L["ones1"], L["onesc"], L["shcol"], L["cols"], L["scc"]
    ones2 = L["ones2"]
    kh, vh, qnh = L["kh"], L["vh"], L["qnh"]
    rrow, rrow32, rbc = L["rrow"], L["rrow32"], L["rbc"]
    f32a, f32b = L["f32a"], L["f32b"]
    ca, ccn, wait_cc = L["collective_async"], L["ccn"], L["wait_cc"]

    for c in range(8):
        pb.dma(G, tag="s", fn=lambda h, c=c: h.dma_start(
            out=f["str_i"].ap()[c * 128:(c + 1) * 128, :], in_=STR[:, c * D:(c + 1) * D]))
    pb.wait_dma(SY, G, tag="s")
    for c in range(NC_D):
        pb.dma(SY, tag="t", fn=lambda h, c=c: h.dma_start(
            out=AT[:, c * NOWN:(c + 1) * NOWN],
            in_=f["str_i"].ap()[:, c * 128:(c + 1) * 128], transpose=True))
    pb.wait_dma(T, SY, tag="t")
    pb.dma(G, tag="a", fn=lambda h: h.dma_start(out=rowb[:], in_=f["bv1"]))
    pb.wait_dma(T, G, tag="a")
    for s4 in range(4):
        pb.emit(T, lambda h, s4=s4: nc.tensor.matmul(
            out=PS2[:, 0:512], lhsT=ones1[:], rhs=rowb[:, s4 * 512:(s4 + 1) * 512],
            start=True, stop=True))
        pb.barrier([T], [V])
        pb.emit(V, lambda h, s4=s4: h.tensor_copy(
            out=bvb[:, s4 * 512:(s4 + 1) * 512], in_=PS2[:, 0:512]))
        pb.barrier([V], [T])

    pb.barrier([T], [G])
    qT = B2
    seqH = [(0, 0), (0, 1), (1, 0), (1, 1), (2, 0), (2, 1)]

    def wtile(H, c):
        if c < 12:
            return (B3[:, 4096 + c * 1024: 4096 + (c + 1) * 1024] if H % 2 == 0
                    else CK[:, c * 1024:(c + 1) * 1024])
        return B3[:, (c - 12) * 1024:(c - 12) * 1024 + 1024]

    bt = lambda H: "w" if H % 2 == 0 else "w2"
    c12 = {}
    c4 = {}
    hmk = {}

    def load12(H):
        widx, half = seqH[H]
        for c in range(12):
            c12[H] = pb.dma(G, tag=bt(H), fn=lambda h, H=H, widx=widx, half=half,
                            c=c: h.dma_start(
                out=wtile(H, c),
                in_=f[f"g_deq{widx}"].ap()[c * 128:(c + 1) * 128,
                                           half * 1024:(half + 1) * 1024]))

    def load4(H):
        widx, half = seqH[H]
        for c in range(12, 16):
            c4[H] = pb.dma(G, tag="d", fn=lambda h, H=H, widx=widx, half=half,
                           c=c: h.dma_start(
                out=wtile(H, c),
                in_=f[f"g_deq{widx}"].ap()[c * 128:(c + 1) * 128,
                                           half * 1024:(half + 1) * 1024]))

    def half_prelude(H):
        pb.wait_dma(T, G, tag=bt(H), upto=c12[H])
        pb.wait_dma(T, G, tag="d", upto=c4[H])
        if H + 1 < 6:
            if H >= 1:
                pb.wait_mark(G, hmk[H - 1])
            if seqH[H + 1][0] != seqH[H][0]:
                wait_cc(G, ccn[f"deq{seqH[H + 1][0]}"])
            load12(H + 1)

    def half_post(H):
        hmk[H] = pb.mark(T)
        if H + 1 < 6:
            pb.wait_mark(G, hmk[H])
            load4(H + 1)

    def proj1(widx, bias_cols, out_sb, out_dram):
        for half in range(2):
            H = widx * 2 + half
            half_prelude(H)
            svm = {}
            for ft in range(8):
                gt = half * 8 + ft
                for qh in range(2):
                    idx = ft * 2 + qh
                    p = PS1[:, (idx % 4) * 512:(idx % 4) * 512 + 512]
                    if idx >= 4:
                        pb.wait_mark(T, svm[idx - 4])
                    for c in range(NC_D):
                        pb.emit(T, lambda h, H=H, ft=ft, qh=qh, c=c, p=p:
                                nc.tensor.matmul(
                            out=p,
                            lhsT=wtile(H, c)[:, ft * 128:(ft + 1) * 128],
                            rhs=AT[:, c * NOWN + qh * 512: c * NOWN + qh * 512 + 512],
                            start=(c == 0), stop=(c == NC_D - 1)))
                    ptm = pb.mark(T)
                    pb.wait_mark(SC, ptm)
                    pb.emit(SC, lambda h, gt=gt, qh=qh, p=p, bias_cols=bias_cols:
                            h.activation(
                        out=out_sb[:, gt * NOWN + qh * 512: gt * NOWN + qh * 512 + 512],
                        in_=p, func=AF.Identity, bias=bias_cols[:, gt:gt + 1], scale=1.0))
                    svm[idx] = pb.mark(SC)
            pb.barrier([SC], [T])
            half_post(H)
        if out_dram is not None:
            pb.barrier([SC], [G])
            for c in range(NC_D):
                pb.dma(G, tag="s", fn=lambda h, c=c, out_dram=out_dram: h.dma_start(
                    out=out_dram[c * 128:(c + 1) * 128, :],
                    in_=out_sb[:, c * NOWN:(c + 1) * NOWN]))

    wait_cc(G, ccn["deq0"])
    load12(0)
    load4(0)
    proj1(0, bq1, qT, None)
    proj1(1, bk1, B1, f["kT_i"].ap())
    pb.wait_dma(G, G, tag="s")
    ccn["kT"] = ca("AllGather", ALU.bypass, [[0, 1], [2, 3], [4, 5], [6, 7]],
                   [f["kT_i"].ap()], [f["g_kT"].ap()])
    for half in range(2):
        H = 4 + half
        half_prelude(H)
        vvm = {}
        for hg in range(2):
            for kt in range(8):
                vidx = hg * 8 + kt
                bank = PS1 if vidx % 2 == 0 else PS2
                p = bank[:, ((vidx // 2) % 4) * 512:((vidx // 2) % 4) * 512 + 512]
                if vidx >= 8:
                    pb.wait_mark(T, vvm[vidx - 8])
                for c in range(NC_D):
                    pb.emit(T, lambda h, H=H, hg=hg, kt=kt, c=c, p=p: nc.tensor.matmul(
                        out=p,
                        lhsT=AT[:, c * NOWN + kt * 128: c * NOWN + (kt + 1) * 128],
                        rhs=wtile(H, c)[:, hg * 512:(hg + 1) * 512],
                        start=(c == 0), stop=(c == NC_D - 1)))
                vtm = pb.mark(T)
                pb.wait_mark(V, vtm)
                pb.wait_dma(V, G, tag="s")
                for h4 in range(4):
                    hh = half * 8 + hg * 4 + h4
                    pb.emit(V, lambda h, hh=hh, kt=kt, h4=h4, p=p: h.tensor_tensor(
                        out=B1[:, (hh % 8) * 2048 + kt * 128 + 1024 * 0:
                                (hh % 8) * 2048 + kt * 128 + 128],
                        in0=p[:, h4 * 128:(h4 + 1) * 128],
                        in1=bvb[:, hh * 128:(hh + 1) * 128], op=ALU.add))
                vvm[vidx] = pb.mark(V)
            pb.barrier([V], [G])
            for h4 in range(4):
                hh = half * 8 + hg * 4 + h4
                pb.dma(G, tag="s", fn=lambda h, hh=hh: h.dma_start(
                    out=f["vb_i"].ap()[hh, :, 0:1024],
                    in_=B1[:, (hh % 8) * 2048:(hh % 8) * 2048 + 1024]))
        pb.barrier([V], [T])
        half_post(H)
    pb.wait_dma(G, G, tag="s")
    ccn["vb"] = ca("AllGather", ALU.bypass, [[0, 1], [2, 3], [4, 5], [6, 7]],
                   [f["vb_i"].ap().rearrange("h p x -> (h p) x")],
                   [f["g_vb"].ap().rearrange("h p x -> (h p) x")])
    fence()
    wait_cc(G, ccn["kT"])
    wait_cc(G, ccn["vb"])

    pt = B1
    ao = AT
    gkt = f["g_kT"].ap()
    gvb = f["g_vb"].ap()
    NKT = S // 128
    pb.emit(V, lambda h: h.memset(shcol[:], -SHIFT1))
    pb.barrier([V], [SC])
    khb = lambda hh: kh if hh % 2 == 0 else B3[:, 0:2048]
    vhb = lambda hh: vh if hh % 2 == 0 else B3[:, 2048:4096]
    atg = lambda hh: "a" if hh % 2 == 0 else "a2"
    acnt = {}
    am = {}
    vao1 = {}

    def emit_kv_loads(hh):
        for b in range(2):
            acnt[hh] = pb.dma(G, tag=atg(hh), fn=lambda h, hh=hh, b=b: h.dma_start(
                out=khb(hh)[:, b * 1024:(b + 1) * 1024],
                in_=gkt[b * D + hh * 128: b * D + (hh + 1) * 128, :]))
            acnt[hh] = pb.dma(G, tag=atg(hh), fn=lambda h, hh=hh, b=b: h.dma_start(
                out=vhb(hh)[:, b * 1024:(b + 1) * 1024], in_=gvb[b * NH + hh, :, :]))

    emit_kv_loads(0)
    emit_kv_loads(1)
    for hh in range(NH):
        pb.wait_dma(T, G, tag=atg(hh), upto=acnt[hh])
        exp_mk = {}
        def emit_spv1(kt, hh=hh):
            kb, ko = kt // 8, (kt % 8) * 128
            pb.wait_mark(T, exp_mk[kt])
            if kt == 0 and hh >= 1:
                pb.wait_mark(T, vao1[hh - 1])
            for qh in range(2):
                pb.emit(T, lambda h, kt=kt, qh=qh: nc.tensor.matmul(
                    out=PS2[:, qh * 512:qh * 512 + 512], lhsT=ones2[:],
                    rhs=pt[:, kt * NOWN + qh * 512: kt * NOWN + qh * 512 + 512],
                    start=(kt == 0), stop=(kt == NKT - 1)))
                pb.emit(T, lambda h, hh=hh, kt=kt, qh=qh, kb=kb, ko=ko: nc.tensor.matmul(
                    out=PS2[:, 1024 + qh * 512:1024 + qh * 512 + 512],
                    lhsT=vhb(hh)[:, kb * 1024 + ko: kb * 1024 + ko + 128],
                    rhs=pt[:, kt * NOWN + qh * 512: kt * NOWN + qh * 512 + 512],
                    start=(kt == 0), stop=(kt == NKT - 1)))
        for kt in range(NKT):
            kb, ko = kt // 8, (kt % 8) * 128
            hb2 = (kt % 2) * 1024
            if kt >= 2:
                pb.wait_mark(T, exp_mk[kt - 2])
            for qh in range(2):
                pb.emit(T, lambda h, hh=hh, kt=kt, qh=qh, kb=kb, ko=ko, hb2=hb2:
                        nc.tensor.matmul(
                    out=PS1[:, hb2 + qh * 512:hb2 + qh * 512 + 512],
                    lhsT=khb(hh)[:, kb * 1024 + ko: kb * 1024 + ko + 128],
                    rhs=qT[:, hh * NOWN + qh * 512: hh * NOWN + qh * 512 + 512],
                    start=True, stop=True))
            qk_mk = pb.mark(T)
            pb.wait_mark(SC, qk_mk)
            pb.emit(SC, lambda h, kt=kt, hb2=hb2: h.activation(
                out=pt[:, kt * NOWN:(kt + 1) * NOWN], in_=PS1[:, hb2:hb2 + 1024],
                func=AF.Exp, bias=shcol[:, 0:1], scale=ISQ_HD))
            exp_mk[kt] = pb.mark(SC)
            if kt >= 1:
                emit_spv1(kt - 1)
        emit_spv1(NKT - 1)
        am[hh] = pb.mark(T)
        if hh + 2 < NH:
            pb.wait_mark(G, am[hh])
            emit_kv_loads(hh + 2)
        pb.barrier([T], [V])
        pb.emit(V, lambda h: h.reciprocal(out=f32a[:, 0:1024], in_=PS2[:, 0:1024]))
        pb.barrier([V], [V])
        pb.emit(V, lambda h: h.tensor_copy(out=rbc[:, 0:1024], in_=f32a[:, 0:1024]))
        pb.barrier([V], [V])
        for qh in range(2):
            pb.emit(V, lambda h, hh=hh, qh=qh: h.tensor_tensor(
                out=ao[:, hh * NOWN + qh * 512: hh * NOWN + qh * 512 + 512],
                in0=PS2[:, 1024 + qh * 512:1024 + qh * 512 + 512],
                in1=rbc[:, qh * 512:qh * 512 + 512], op=ALU.mult))
        vao1[hh] = pb.mark(V)
    pb.wait_mark(T, vao1[NH - 1])
    pb.barrier([T], [G])
    pb.barrier([V], [G])
    pb.dma(G, tag="a", fn=lambda h: h.dma_start(out=rowb[:], in_=f["bo1"]))
    pb.wait_dma(T, G, tag="a")
    for s4 in range(4):
        pb.emit(T, lambda h, s4=s4: nc.tensor.matmul(
            out=PS2[:, 0:512], lhsT=ones1[:], rhs=rowb[:, s4 * 512:(s4 + 1) * 512],
            start=True, stop=True))
        pb.barrier([T], [V])
        pb.emit(V, lambda h, s4=s4: h.tensor_copy(
            out=bob[:, s4 * 512:(s4 + 1) * 512], in_=PS2[:, 0:512]))
        pb.barrier([V], [T])
    wosb1 = lambda i: B3[:, (i % 2) * 8192:(i % 2) * 8192 + 8192]
    wtag1 = lambda i: "w" if i % 2 == 0 else "w2"
    pb.barrier([T], [G])
    wait_cc(G, ccn["deq3"])
    cw = {}
    wvm = {}
    for c in range(NC_D):
        cw[0] = pb.dma(G, tag=wtag1(0), fn=lambda h, c=c: h.dma_start(
            out=wosb1(0)[:, c * 512:(c + 1) * 512],
            in_=f["g_deq3"].ap()[c * 128:(c + 1) * 128, 0:512]))
    wtoks = {}
    for s4 in range(4):
        if s4 + 1 < 4:
            if s4 >= 1:
                pb.wait_mark(G, wtoks[s4 - 1])
            for c in range(NC_D):
                cw[s4 + 1] = pb.dma(G, tag=wtag1(s4 + 1), fn=lambda h, c=c, s4=s4: h.dma_start(
                    out=wosb1(s4 + 1)[:, c * 512:(c + 1) * 512],
                    in_=f["g_deq3"].ap()[c * 128:(c + 1) * 128,
                                         (s4 + 1) * 512:(s4 + 2) * 512]))
        pb.wait_dma(T, G, tag=wtag1(s4), upto=cw[s4])
        for qt in range(8):
            gidx = s4 * 8 + qt
            p_off = (gidx % 4) * 512
            if gidx >= 4:
                pb.wait_mark(T, wvm[gidx - 4])
            for c in range(NC_D):
                pb.emit(T, lambda h, qt=qt, s4=s4, c=c, p_off=p_off: nc.tensor.matmul(
                    out=PS1[:, p_off:p_off + 512],
                    lhsT=ao[:, c * NOWN + qt * 128: c * NOWN + (qt + 1) * 128],
                    rhs=wosb1(s4)[:, c * 512:(c + 1) * 512],
                    start=(c == 0), stop=(c == NC_D - 1)))
            wtm = pb.mark(T)
            pb.wait_mark(V, wtm)
            pb.emit(V, lambda h, qt=qt, s4=s4, p_off=p_off: h.tensor_tensor(
                out=pt[:, qt * 2048 + s4 * 512: qt * 2048 + s4 * 512 + 512],
                in0=PS1[:, p_off:p_off + 512],
                in1=bob[:, s4 * 512:(s4 + 1) * 512], op=ALU.add))
            wvm[gidx] = pb.mark(V)
        wtoks[s4] = pb.mark(T)
    for qt in range(8):
        pb.barrier([V], [V])
        pb.emit(V, lambda h, qt=qt: h.tensor_tensor(
            out=pt[:, qt * 2048: qt * 2048 + 2048],
            in0=pt[:, qt * 2048: qt * 2048 + 2048],
            in1=STR[:, qt * D:(qt + 1) * D], op=ALU.add))
        _fused_ln(pb, pt, qt * 2048, f32a, cols, STR[:, qt * D:(qt + 1) * D])
    fence()

    gg1 = f["g_gu1"].ap()
    gw1 = f["g_wd1"].ap()
    for qt in range(8):
        pb.emit(V, lambda h, qt=qt: h.reduce_max(
            out=cols[:, qt:qt + 1], in_=STR[:, qt * D:(qt + 1) * D], axis=AX,
            apply_absolute_value=True))
    pb.barrier([V], [V])
    pb.emit(V, lambda h: h.tensor_scalar_max(out=cols[:, 0:8], in0=cols[:, 0:8],
                                             scalar1=1e-5))
    pb.barrier([V], [V])
    pb.emit(V, lambda h: h.tensor_scalar(
        out=cols[:, 8:16], in0=cols[:, 0:8], scalar1=cols[:, 30:31], scalar2=1.0 / 127,
        op0=ALU.mult, op1=ALU.mult))
    pb.emit(V, lambda h: h.reciprocal(out=cols[:, 16:24], in_=cols[:, 0:8]))
    pb.barrier([V], [V])
    pb.emit(V, lambda h: h.tensor_scalar_mul(out=cols[:, 16:24], in0=cols[:, 16:24],
                                             scalar1=127.0))
    pb.barrier([V], [V])
    for qt in range(8):
        pb.wait_dma(V, G, tag="s")
        pb.emit(V, lambda h, qt=qt: h.tensor_scalar_mul(
            out=f32a[:, 0:D], in0=STR[:, qt * D:(qt + 1) * D], scalar1=cols[:, 16 + qt:17 + qt]))
        pb.barrier([V], [V])
        iv = f32b.bitcast(I32)
        pb.emit(V, lambda h, iv=iv: h.tensor_copy(out=iv, in_=f32a[:, 0:D]))
        pb.barrier([V], [V])
        pb.emit(V, lambda h, iv=iv, qt=qt: h.tensor_copy(
            out=B1[:, qt * 2048:(qt + 1) * 2048], in_=iv))
        pb.barrier([V], [G])
        pb.dma(G, tag="s", fn=lambda h, qt=qt: h.dma_start(
            out=f["xq_i"].ap()[qt * 128:(qt + 1) * 128, :],
            in_=B1[:, qt * 2048:(qt + 1) * 2048]))
    pb.wait_dma(SY, G, tag="s")
    for c in range(NC_D):
        pb.dma(SY, tag="t", fn=lambda h, c=c: h.dma_start(
            out=AT[:, c * NOWN:(c + 1) * NOWN],
            in_=f["xq_i"].ap()[:, c * 128:(c + 1) * 128], transpose=True))
    pb.wait_dma(T, SY, tag="t")
    for pas in range(2):
        def emit_wg1(sn, pas=pas):
            cnt = None
            tg = "w" if sn % 2 == 0 else "w2"
            for cc in range(NC_D):
                cnt = pb.dma(G, tag=tg, fn=lambda h, sn=sn, cc=cc, pas=pas: h.dma_start(
                    out=B3[:, (sn % 2) * 8192 + cc * 512:(sn % 2) * 8192 + (cc + 1) * 512],
                    in_=gg1[cc * 128:(cc + 1) * 128,
                            pas * F1 + sn * 512: pas * F1 + (sn + 1) * 512]))
            return cnt
        pb.barrier([T], [G])
        if pas == 0:
            wait_cc(G, ccn["gu1"])
            stcnt = {}
            cpf = {}
        wc1 = {0: emit_wg1(0)}
        wt1 = {}
        hqo = f["hq_i"].ap() if pas == 0 else f["hq2_i"].ap()
        stg_ = lambda s: "s" if s % 2 == 0 else "d"
        pfg_ = lambda i: "a" if i % 2 == 0 else "a2"
        vmk = {}
        vpm = {}
        for s in range(16):
            if s + 1 < 16:
                if s >= 1:
                    pb.wait_mark(G, wt1[s - 1])
                wc1[s + 1] = emit_wg1(s + 1)
            pb.wait_dma(T, G, tag=("w" if s % 2 == 0 else "w2"), upto=wc1[s])
            sl = s % 2
            for qt in range(8):
                idx = s * 8 + qt
                bank = (PS1 if idx % 2 == 0 else PS2)
                p = bank[:, ((idx // 2) % 4) * 512: ((idx // 2) % 4) * 512 + 512]
                if idx >= 8:
                    pb.wait_mark(T, vpm[idx - 8])
                for cc in range(NC_D):
                    pb.emit(T, lambda h, sl=sl, qt=qt, cc=cc, p=p: nc.tensor.matmul(
                        out=p,
                        lhsT=AT[:, cc * NOWN + qt * 128: cc * NOWN + (qt + 1) * 128],
                        rhs=B3[:, sl * 8192 + cc * 512: sl * 8192 + (cc + 1) * 512],
                        start=(cc == 0), stop=(cc == NC_D - 1)))
                if pas == 0:
                    pb.barrier([T], [V])
                    if qt == 0 and s >= 1:
                        pb.wait_dma(V, G, tag=stg_(s - 1), upto=stcnt[(pas, s - 1)])
                    pb.emit(V, lambda h, qt=qt, p=p: h.tensor_scalar_mul(
                        out=f32a[:, 0:512], in0=p, scalar1=cols[:, 8 + qt:9 + qt]))
                    vpm[idx] = pb.mark(V)
                    pb.barrier([V], [SC])
                    pb.emit(SC, lambda h, qt=qt: h.activation(
                        out=B1[:, qt * 512:(qt + 1) * 512], in_=f32a[:, 0:512],
                        func=AF.Silu))
                    pb.barrier([SC], [G, V])
                else:
                    pb.barrier([T], [V])
                    if qt == 0:
                        pv = (pas, s - 1) if s >= 1 else (0, 15)
                        pb.wait_dma(V, G, tag=stg_(pv[1]), upto=stcnt[pv])
                    pb.wait_dma(V, SY, tag=pfg_(idx), upto=cpf[idx])
                    pb.emit(V, lambda h, qt=qt, p=p: h.scalar_tensor_tensor(
                        out=B1[:, qt * 512:(qt + 1) * 512], in0=p,
                        scalar=cols[:, 8 + qt:9 + qt],
                        in1=B2[:, qt * 512:(qt + 1) * 512], op0=ALU.mult, op1=ALU.mult))
                    vpm[idx] = vmk[idx] = pb.mark(V)
                    pb.barrier([V], [V])
                    pb.emit(V, lambda h, qt=qt: h.reduce_max(
                        out=cols[:, 24:25], in_=B1[:, qt * 512:(qt + 1) * 512],
                        axis=AX, apply_absolute_value=True))
                    pb.barrier([V], [V])
                    if s == 0:
                        pb.emit(V, lambda h, qt=qt: h.tensor_copy(
                            out=cols[:, qt:qt + 1], in_=cols[:, 24:25]))
                    else:
                        pb.emit(V, lambda h, qt=qt: h.tensor_tensor(
                            out=cols[:, qt:qt + 1], in0=cols[:, qt:qt + 1],
                            in1=cols[:, 24:25], op=ALU.max))
                    pb.barrier([V], [G])
                stcnt[(pas, s)] = pb.dma(G, tag=stg_(s), fn=lambda h, s=s, qt=qt,
                                         hqo=hqo: h.dma_start(
                    out=hqo[qt * 128:(qt + 1) * 128, s * 512:(s + 1) * 512],
                    in_=B1[:, qt * 512:(qt + 1) * 512]))
                if pas == 1 and idx + 2 < 128:
                    nidx = idx + 2
                    ns, nqt = nidx // 8, nidx % 8
                    pb.wait_mark(SY, vmk[nidx - 2])
                    cpf[nidx] = pb.dma(SY, tag=pfg_(nidx), fn=lambda h, ns=ns, nqt=nqt:
                                       h.dma_start(
                        out=B2[:, nqt * 512:(nqt + 1) * 512],
                        in_=f["hq_i"].ap()[nqt * 128:(nqt + 1) * 128,
                                           ns * 512:(ns + 1) * 512]))
            wt1[s] = pb.mark(T)
        if pas == 0:
            pb.wait_dma(SY, G, tag="s")
            pb.wait_dma(SY, G, tag="d")
            pb.barrier([V, SC], [SY])
            for i0 in range(2):
                cpf[i0] = pb.dma(SY, tag=pfg_(i0), fn=lambda h, i0=i0: h.dma_start(
                    out=B2[:, i0 * 512:(i0 + 1) * 512],
                    in_=f["hq_i"].ap()[i0 * 128:(i0 + 1) * 128, 0:512]))
    fence()
    pb.emit(V, lambda h: h.tensor_scalar_max(out=cols[:, 0:8], in0=cols[:, 0:8],
                                             scalar1=1e-5))
    pb.barrier([V], [V])
    pb.emit(V, lambda h: h.tensor_scalar(
        out=cols[:, 8:16], in0=cols[:, 0:8], scalar1=cols[:, 31:32], scalar2=1.0 / 127,
        op0=ALU.mult, op1=ALU.mult))
    pb.emit(V, lambda h: h.reciprocal(out=cols[:, 16:24], in_=cols[:, 0:8]))
    pb.barrier([V], [V])
    pb.emit(V, lambda h: h.tensor_scalar_mul(out=cols[:, 16:24], in0=cols[:, 16:24],
                                             scalar1=127.0))
    pb.barrier([V], [V])
    lsl = lambda i: B1[:, (i % 2) * 2048:(i % 2) * 2048 + 2048]
    la_ = lambda i: "a" if i % 2 == 0 else "a2"
    st_ = lambda i: "s" if i % 2 == 0 else "d"
    iv = f32b.bitcast(I32)
    cl = {}
    cst = {}
    for it in range(2):
        qt, hf = it // 4, it % 4
        cl[it] = pb.dma(G, tag=la_(it), fn=lambda h, qt=qt, hf=hf, it=it: h.dma_start(
            out=lsl(it), in_=f["hq2_i"].ap()[qt * 128:(qt + 1) * 128,
                                            hf * 2048:(hf + 1) * 2048]))
    for it in range(32):
        qt, hf = it // 4, it % 4
        pb.wait_dma(V, G, tag=la_(it), upto=cl[it])
        pb.barrier([V], [V])
        pb.emit(V, lambda h, qt=qt, it=it: h.tensor_scalar_mul(
            out=f32a[:, 0:2048], in0=lsl(it), scalar1=cols[:, 16 + qt:17 + qt]))
        pb.barrier([V], [V])
        pb.emit(V, lambda h, iv=iv: h.tensor_copy(out=iv, in_=f32a[:, 0:2048]))
        pb.barrier([V], [V])
        pb.emit(V, lambda h, iv=iv, it=it: h.tensor_copy(out=lsl(it), in_=iv))
        pb.barrier([V], [SY])
        cst[it] = pb.dma(SY, tag=st_(it), fn=lambda h, qt=qt, hf=hf, it=it: h.dma_start(
            out=f["hq2_i"].ap()[qt * 128:(qt + 1) * 128, hf * 2048:(hf + 1) * 2048],
            in_=lsl(it)))
        if it + 2 < 32:
            nqt, nhf = (it + 2) // 4, (it + 2) % 4
            pb.wait_dma(G, SY, tag=st_(it), upto=cst[it])
            cl[it + 2] = pb.dma(G, tag=la_(it), fn=lambda h, nqt=nqt, nhf=nhf, it=it:
                                h.dma_start(
                out=lsl(it + 2), in_=f["hq2_i"].ap()[nqt * 128:(nqt + 1) * 128,
                                                    nhf * 2048:(nhf + 1) * 2048]))
    pb.wait_dma(SY, SY, tag="s")
    pb.wait_dma(SY, SY, tag="d")
    NCH, IC4 = 16, 4
    wtn = lambda c: "w" if c % 2 == 0 else "w2"
    htn = lambda c: "t" if c % 2 == 0 else "t2"
    wv = lambda c: B3[:, (c % 2) * 8192:(c % 2) * 8192 + 8192]
    hv = lambda c: CK[:, (c % 2) * 4096:(c % 2) * 4096 + 4096]
    B1f = B1.ap().bitcast(F32)
    B2f = B2.ap().bitcast(F32)
    accv = lambda qt: (B1f if qt < 4 else B2f)[:, (qt % 4) * 2048:(qt % 4) * 2048 + 2048]
    wcn = {}
    hcn = {}

    def emit_down_loads(c):
        for j in range(IC4):
            ic = c * IC4 + j
            wcn[c] = pb.dma(G, tag=wtn(c), fn=lambda h, c=c, j=j, ic=ic: h.dma_start(
                out=wv(c)[:, j * 2048:(j + 1) * 2048],
                in_=gw1[ic * 128:(ic + 1) * 128, :]))
            hcn[c] = pb.dma(SY, tag=htn(c), fn=lambda h, c=c, j=j, ic=ic: h.dma_start(
                out=hv(c)[:, j * 1024:(j + 1) * 1024],
                in_=f["hq2_i"].ap()[:, ic * 128:(ic + 1) * 128], transpose=True))

    pb.barrier([T, V, SC], [G, SY])
    pb.wait_dma(V, SY, tag="s")
    pb.wait_dma(V, SY, tag="d")
    wait_cc(G, ccn["wd1"])
    emit_down_loads(0)
    tmk = {}
    vem = {}
    for c in range(NCH):
        if c + 1 < NCH:
            if c >= 1:
                pb.wait_mark(G, tmk[(c - 1, 7)])
                pb.wait_mark(SY, tmk[(c - 1, 7)])
            emit_down_loads(c + 1)
        pb.wait_dma(T, G, tag=wtn(c), upto=wcn[c])
        pb.wait_dma(T, SY, tag=htn(c), upto=hcn[c])
        for qt in range(8):
            p = PS1 if qt % 2 == 0 else PS2
            pu = (c, qt - 2) if qt >= 2 else ((c - 1, qt + 6) if c >= 1 else None)
            if pu is not None:
                pb.wait_mark(T, vem[pu])
            for j in range(IC4):
                for os in range(4):
                    pb.emit(T, lambda h, c=c, j=j, qt=qt, os=os, p=p: nc.tensor.matmul(
                        out=p[:, os * 512:(os + 1) * 512],
                        lhsT=hv(c)[:, j * 1024 + qt * 128: j * 1024 + (qt + 1) * 128],
                        rhs=wv(c)[:, j * 2048 + os * 512: j * 2048 + (os + 1) * 512],
                        start=(j == 0), stop=(j == IC4 - 1)))
            tmk[(c, qt)] = pb.mark(T)
            pb.wait_mark(V, tmk[(c, qt)])
            if c == 0:
                pb.emit(V, lambda h, qt=qt, p=p: h.tensor_copy(
                    out=accv(qt), in_=p[:, 0:2048]))
            else:
                pb.emit(V, lambda h, qt=qt, p=p: h.tensor_tensor(
                    out=accv(qt), in0=p[:, 0:2048], in1=accv(qt), op=ALU.add))
            vem[(c, qt)] = pb.mark(V)
    for qt in range(8):
        pb.barrier([V], [V])
        pb.emit(V, lambda h, qt=qt: h.scalar_tensor_tensor(
            out=accv(qt), in0=accv(qt), scalar=cols[:, 8 + qt:9 + qt],
            in1=STR[:, qt * D:(qt + 1) * D], op0=ALU.mult, op1=ALU.add))
        _fused_ln_out(pb, accv(qt), f32b, cols, f["_out_ap"][qt * 128:(qt + 1) * 128, :])
    for eng in ENGS:
        pb.wait_dma(eng, G, tag="s")


def _fused_ln_out(pb, s_sb, scratch, cols, out_dram):
    nc = pb.nc
    V, SC, G = "vector", "scalar", "gpsimd"
    pb.wait_dma(SC, G, tag="s")
    pb.wait_dma(V, G, tag="s")
    pb.barrier([V], [V])
    for j in range(4):
        pb.emit(V, lambda h, j=j: h.bn_stats(
            out=scratch[:, j * 6:(j + 1) * 6],
            in_=s_sb[:, j * 512:(j + 1) * 512]))
    pb.barrier([V], [V])
    pb.emit(V, lambda h: h.bn_aggr(out=cols[:, 26:28], in_=scratch[:, 0:24]))
    pb.barrier([V], [V])
    pb.emit(V, lambda h: h.tensor_scalar_add(out=cols[:, 28:29], in0=cols[:, 27:28],
                                             scalar1=1e-5))
    pb.barrier([V], [SC])
    pb.emit(SC, lambda h: h.activation(out=cols[:, 28:29], in_=cols[:, 28:29], func=AF.Sqrt))
    pb.barrier([SC], [V])
    pb.emit(V, lambda h: h.reciprocal(out=cols[:, 29:30], in_=cols[:, 28:29]))
    pb.barrier([V], [V])
    pb.emit(V, lambda h: h.tensor_scalar(
        out=scratch[:, 0:D], in0=s_sb[:, 0:D], scalar1=cols[:, 26:27],
        scalar2=cols[:, 29:30], op0=ALU.subtract, op1=ALU.mult))
    pb.barrier([V], [G])
    pb.dma(G, tag="s", fn=lambda h: h.dma_start(out=out_dram, in_=scratch[:, 0:D]))


USE_FUSED = True


def _host_prep_common(inp):
    f32 = np.float32
    H = {}
    x = np.asarray(inp["L0_attn_x"], f32).reshape(NTOK, D)
    H["x"] = x
    H["xT_full"] = np.ascontiguousarray(x.T)
    H["wqa"] = np.ascontiguousarray(np.asarray(inp["L0_attn_wq_a_w"], f32))
    H["bqa_c"] = _bcols(inp["L0_attn_wq_a_b"])
    wqb = np.asarray(inp["L0_attn_wq_b_w"], f32)
    bqb = np.asarray(inp["L0_attn_wq_b_b"], f32)
    perm_n = np.concatenate([np.arange(192 * h, 192 * h + 128) for h in range(NH)])
    perm_r = np.concatenate(
        [np.concatenate([np.arange(192 * h + 128, 192 * h + 192, 2),
                         np.arange(192 * h + 129, 192 * h + 192, 2)]) for h in range(NH)])
    perm_rs = np.concatenate(
        [np.concatenate([np.arange(192 * h + 129, 192 * h + 192, 2),
                         np.arange(192 * h + 128, 192 * h + 192, 2)]) for h in range(NH)])
    perm = np.concatenate([perm_n, perm_r, perm_rs])
    H["wqb_p"] = np.ascontiguousarray(wqb[:, perm])
    bqb_p = bqb[perm]
    H["bqbn_c"] = _bcols(bqb_p[:2048])
    H["bqbr_c"] = np.ascontiguousarray(bqb_p[2048:].reshape(32, 64).T)
    wkva = np.asarray(inp["L0_attn_wkv_a_w"], f32)
    bkva = np.asarray(inp["L0_attn_wkv_a_b"], f32)
    pk_r = np.concatenate([512 + np.arange(0, 64, 2), 512 + np.arange(1, 64, 2)])
    pk_rs = np.concatenate([512 + np.arange(1, 64, 2), 512 + np.arange(0, 64, 2)])
    permk = np.concatenate([np.arange(512), pk_r, pk_rs])
    H["wkva_p"] = np.ascontiguousarray(wkva[:, permk])
    bkva_p = bkva[permk]
    bkva_c = np.zeros((128, 6), f32)
    bkva_c[:, 0:4] = bkva_p[:512].reshape(4, 128).T
    bkva_c[0:64, 4] = bkva_p[512:576]
    bkva_c[0:64, 5] = bkva_p[576:640]
    H["bkva_c"] = bkva_c
    exp32 = np.zeros((32, 128), f32)
    for p in range(64):
        exp32[p % 32, p] = 1.0
        exp32[p % 32, 64 + p] = -1.0 if p < 32 else 1.0
    H["exp32"] = exp32.astype(BF)
    cosf, sinf = _rope_tables()
    H["cosT"] = np.ascontiguousarray(cosf.T)
    H["sinT"] = np.ascontiguousarray(sinf.T)
    H["wkvb"] = np.ascontiguousarray(np.asarray(inp["L0_attn_wkv_b_w"], f32))
    bkvb = np.asarray(inp["L0_attn_wkv_b_b"], f32)
    H["bkv_c"] = _bcols(bkvb)
    H["bv_row0"] = np.ascontiguousarray(bkvb.reshape(NH, 2, 128)[:, 1, :].reshape(1, NH * 128))
    H["wo0"] = np.ascontiguousarray(np.asarray(inp["L0_attn_wo_w"], f32))
    H["bo0_row"] = np.ascontiguousarray(np.asarray(inp["L0_attn_wo_b"], f32).reshape(1, D))
    H["wguT1"] = np.ascontiguousarray(np.asarray(inp["L1_ffn_w_gate_up"], f32).T)
    H["wdT1"] = np.ascontiguousarray(np.asarray(inp["L1_ffn_w_down"], f32).T)
    H["pguT"] = np.ascontiguousarray(np.asarray(inp["L0_ffn_gate_up_packed"]).T)
    H["sguT"] = np.asarray(inp["L0_ffn_gate_up_scales"])
    H["pdnT"] = np.ascontiguousarray(np.asarray(inp["L0_ffn_down_packed"]).T)
    H["sdnT"] = np.asarray(inp["L0_ffn_down_scales"])
    H["codes"] = [np.asarray(inp[k], f32) for k in
                  ("L1_attn_wq_codes", "L1_attn_wk_codes", "L1_attn_wv_codes",
                   "L1_attn_wo_codes")]
    H["mxs"] = [np.asarray(inp[k], f32) for k in
                ("L1_attn_wq_scales", "L1_attn_wk_scales", "L1_attn_wv_scales",
                 "L1_attn_wo_scales")]
    H["bq1_c"] = _bcols(inp["L1_attn_bq"])
    H["bk1_c"] = _bcols(inp["L1_attn_bk"])
    H["bv1_row"] = np.ascontiguousarray(np.asarray(inp["L1_attn_bv"], f32).reshape(1, D))
    H["bo1_row"] = np.ascontiguousarray(np.asarray(inp["L1_attn_bo"], f32).reshape(1, D))
    return H


def kernel_fused(inp):
    f32 = np.float32
    H = _host_prep_common(inp)
    nc = build_fused()
    maps = []
    for c in range(NCORE):
        sl32 = H["pguT"][32 * c:32 * c + 32, :]
        pgu_sl = np.ascontiguousarray(
            sl32.reshape(32, 4, 2816).transpose(1, 0, 2).reshape(128, 2816))
        p = np.arange(128)
        rows = (2 * c + (p % 32) // 16)
        srep_gu = np.empty((128, 2816), dtype=H["sguT"].dtype)
        for pi in range(128):
            q4 = pi // 32
            srep_gu[pi] = H["sguT"][rows[pi], 2816 * q4:2816 * (q4 + 1)]
        pp = np.arange(88)
        srep_dn = H["sdnT"][(704 * c + 8 * pp) // 128, :]
        m = {
            "x": np.ascontiguousarray(H["x"][c * NOWN:(c + 1) * NOWN]),
            "xT": np.ascontiguousarray(H["xT_full"][:, c * NOWN:(c + 1) * NOWN]),
            "wqa": H["wqa"], "bqa": H["bqa_c"], "wqb_p": H["wqb_p"],
            "bqbn": H["bqbn_c"], "bqbr": H["bqbr_c"],
            "wkva_p": H["wkva_p"], "bkva": H["bkva_c"],
            "cosT": np.ascontiguousarray(H["cosT"][:, (c % 2) * NOWN:(c % 2 + 1) * NOWN]),
            "sinT": np.ascontiguousarray(H["sinT"][:, (c % 2) * NOWN:(c % 2 + 1) * NOWN]),
            "exp32": H["exp32"],
            "wkvb": H["wkvb"], "bkv": H["bkv_c"], "bvrow": H["bv_row0"],
            "wo0": H["wo0"], "bo0": H["bo0_row"],
            "wgu1_sl": np.ascontiguousarray(H["wguT1"][256 * c:256 * (c + 1), :]),
            "wd1_sl": np.ascontiguousarray(H["wdT1"][1024 * c:1024 * (c + 1), :]),
            "pgu_sl": pgu_sl, "srep_gu": np.ascontiguousarray(srep_gu),
            "pdn_sl": np.ascontiguousarray(H["pdnT"][88 * c:88 * (c + 1), :]),
            "srep_dn": np.ascontiguousarray(srep_dn),
            "bq1": H["bq1_c"], "bk1": H["bk1_c"], "bv1": H["bv1_row"], "bo1": H["bo1_row"],
        }
        for i in range(4):
            m[f"codes{i}"] = np.ascontiguousarray(H["codes"][i][256 * c:256 * (c + 1), :])
            m[f"mxrep{i}"] = np.ascontiguousarray(
                np.repeat(H["mxs"][i], 32, axis=0)[256 * c:256 * (c + 1), :])
        maps.append(m)
    res = run_bass_kernel_spmd(nc, maps, core_ids=list(range(NCORE)))
    out = np.concatenate([res.results[c]["out"] for c in range(NCORE)], axis=0)
    return out.reshape(B, S, D)
